# version 01
# baseline (speedup 1.0000x reference)
import math
from contextlib import ExitStack

import numpy as np
import ml_dtypes

import concourse.bass as bass
import concourse.mybir as mybir
from concourse.bass_utils import run_bass_kernel_spmd

F32 = mybir.dt.float32
BF16 = mybir.dt.bfloat16
AF = mybir.ActivationFunctionType
ALU = mybir.AluOpType
AX = mybir.AxisListType
BF = ml_dtypes.bfloat16

N_CORES = 8
PHASES = 99
B_, T_, D_ = 2, 4096, 2048
OWN, HALO = 1024, 128
TOK = OWN + HALO
NT = TOK // 128
NTB = T_ // 128
NF = D_ // 128
CHW = [512, 512, 128]

_EV = [h for h in range(32) if (h // 4) % 2 == 0]
_OD = [h for h in range(32) if (h // 4) % 2 == 1]
SLOT2HEAD = []
for _t in range(16):
    SLOT2HEAD.append(_EV[_t])
    SLOT2HEAD.append(_OD[_t])


COMPUTE_ENGS = ("tensor", "vector", "scalar")
DMA_QUEUES = ("sync", "gpsimd")


class Prog:
    def __init__(self, nc):
        self.nc = nc
        self.ops = []
        self.last_w = {}
        self.readers = {}
        self.slot_fills = {}
        self.bulk_slots = set()
        self._last_per_key = {}
        self._fence_need = None
        self._fence_engines = set()

    def op(self, engine, fn, reads=(), writes=()):
        assert engine in COMPUTE_ENGS, engine
        return self._add(engine, fn, reads, writes, None)

    def fence(self):
        self._fence_need = set(self._last_per_key.values())
        self._fence_engines = set()

    def dma(self, queue, slot, out, in_, reads=(), writes=(), bulk=False):
        assert queue in DMA_QUEUES, queue
        if bulk:
            self.bulk_slots.add(slot)
        self.slot_fills[slot] = self.slot_fills.get(slot, 0) + 1
        fn = lambda e, o=out, i=in_: e.dma_start(o, i)
        return self._add(queue, fn, reads, writes, (slot, self.slot_fills[slot]))

    def mm(self, out, lhsT, rhs, start, stop, reads=(), writes=()):
        fn = lambda t, o=out, l=lhsT, r=rhs, a=start, c=stop: t.matmul(
            o, l, r, start=a, stop=c, skip_group_check=True)
        return self.op("tensor", fn, reads, writes)

    def trans(self, out, in_, iden, reads=(), writes=()):
        fn = lambda t, o=out, i=in_, d=iden: t.transpose(o, i, d)
        return self.op("tensor", fn, reads, writes)

    def tt(self, out, a, c, aluop, reads=(), writes=()):
        fn = lambda v, o=out, x=a, y=c, p=aluop: v.tensor_tensor(o, x, y, p)
        return self.op("vector", fn, reads, writes)

    def ts(self, out, in0, s1, s2, op0, op1=None, reads=(), writes=()):
        if s2 is None and op1 is None:
            fn = lambda v, o=out, i=in0, a=s1, p0=op0: v.tensor_single_scalar(o, i, a, p0)
        else:
            fn = lambda v, o=out, i=in0, a=s1, c=s2, p0=op0, p1=op1: v.tensor_scalar(
                o, i, a, c, p0, p1)
        return self.op("vector", fn, reads, writes)

    def red(self, out, in_, aluop=ALU.add, negate=False, reads=(), writes=()):
        fn = lambda v, o=out, i=in_, p=aluop, n=negate: v.tensor_reduce(
            o, i, axis=AX.X, op=p, negate=n)
        return self.op("vector", fn, reads, writes)

    def vcopy(self, out, in_, reads=(), writes=()):
        fn = lambda v, o=out, i=in_: v.tensor_copy(o, i)
        return self.op("vector", fn, reads, writes)

    def vmemset(self, ap, val, reads=(), writes=()):
        fn = lambda v, a=ap, x=val: v.memset(a, x)
        return self.op("vector", fn, reads, writes)

    def vrecip(self, out, in_, reads=(), writes=()):
        fn = lambda v, o=out, i=in_: v.reciprocal(o, i)
        return self.op("vector", fn, reads, writes)

    def act(self, out, in_, func, bias=0.0, scale=1.0, reads=(), writes=()):
        fn = lambda s, o=out, i=in_, f=func, bb=bias, c=scale: s.activation(
            o, i, f, bias=bb, scale=c)
        return self.op("scalar", fn, reads, writes)

    def _verify(self, ops):
        streams = {}
        for i, o in enumerate(ops):
            streams.setdefault(o["engine"], []).append(i)
        heads = {e: 0 for e in streams}
        sem = {}
        done = 0
        while done < len(ops):
            progressed = False
            for e, lst in streams.items():
                while heads[e] < len(lst):
                    o = ops[lst[heads[e]]]
                    if any(sem.get(k, 0) < v for k, v in o["waits"]):
                        break
                    if o["inc"]:
                        if o["slot"] is not None:
                            k = "slot:" + o["slot"][0]
                            sem[k] = sem.get(k, 0) + 16
                        else:
                            k = "eng:" + e
                            sem[k] = sem.get(k, 0) + 1
                    heads[e] += 1
                    done += 1
                    progressed = True
            if not progressed:
                msg = []
                for e, lst in streams.items():
                    if heads[e] < len(lst):
                        i = lst[heads[e]]
                        o = ops[i]
                        blocked = [(k, v, sem.get(k, 0)) for k, v in o["waits"]
                                   if sem.get(k, 0) < v]
                        msg.append(f"{e} stuck at op {i} waits {blocked}")
                raise RuntimeError("DEADLOCK:\n" + "\n".join(msg))

    def _add(self, engine, fn, reads, writes, dma_slot):
        i = len(self.ops)
        deps = set()
        if self._fence_need is not None and engine not in self._fence_engines:
            deps |= self._fence_need
            self._fence_engines.add(engine)
        for r in reads:
            j = self.last_w.get(r)
            if j is not None:
                deps.add(j)
        for w in writes:
            j = self.last_w.get(w)
            if j is not None:
                deps.add(j)
            deps.update(self.readers.get(w, ()))
        deps.discard(i)
        self.ops.append(dict(engine=engine, fn=fn, deps=deps, slot=dma_slot,
                             inc=False))
        for r in reads:
            self.readers.setdefault(r, []).append(i)
        for w in writes:
            self.last_w[w] = i
            self.readers[w] = []
        self._last_per_key[dma_slot[0] if dma_slot else engine] = i
        return i

    def build(self):
        nc = self.nc
        ops = self.ops
        for o in ops:
            pruned = {}
            for j in o["deps"]:
                oj = ops[j]
                if (oj["slot"] is None and o["slot"] is None
                        and oj["engine"] == "tensor" and o["engine"] == "tensor"):
                    continue
                key = oj["slot"][0] if oj["slot"] else oj["engine"]
                if key not in pruned or j > pruned[key]:
                    pruned[key] = j
            o["rdeps"] = pruned
        for o in ops:
            for j in o["rdeps"].values():
                ops[j]["inc"] = True
        for o in ops:
            if o["slot"] is not None:
                o["inc"] = True
        cnt, run = {}, {}
        for i, o in enumerate(ops):
            if o["slot"] is None and o["inc"]:
                e = o["engine"]
                run[e] = run.get(e, 0) + 1
                cnt[i] = run[e]
        final_slot = dict(self.slot_fills)
        for o in ops:
            waits = []
            for key, j in sorted(o["rdeps"].items()):
                oj = ops[j]
                if oj["slot"] is not None:
                    slot, fill = oj["slot"]
                    v = 16 * (final_slot[slot] if slot in self.bulk_slots else fill)
                    waits.append(("slot:" + slot, v))
                else:
                    waits.append(("eng:" + oj["engine"], cnt[j]))
            o["waits"] = waits

        self._verify(ops)
        with ExitStack() as ctx:
            sems = {}
            for e in COMPUTE_ENGS:
                sems["eng:" + e] = ctx.enter_context(nc.semaphore("sem_" + e))
            for s in self.slot_fills:
                sems["slot:" + s] = ctx.enter_context(nc.semaphore("sem_" + s))
            block = ctx.enter_context(nc.Block())
            water = {}

            def emit_for(engine_name, eng):
                for o in ops:
                    if o["engine"] != engine_name:
                        continue
                    for key, v in o["waits"]:
                        wk = (engine_name, key)
                        if water.get(wk, 0) >= v:
                            continue
                        water[wk] = v
                        eng.wait_ge(sems[key], v)
                    inst = o["fn"](eng)
                    if o["inc"]:
                        if o["slot"] is not None:
                            inst.then_inc(sems["slot:" + o["slot"][0]], 16)
                        else:
                            inst.then_inc(sems["eng:" + engine_name], 1)
                if engine_name == "sync":
                    for s, n in sorted(self.slot_fills.items()):
                        wk = (engine_name, "slot:" + s)
                        if water.get(wk, 0) < 16 * n:
                            eng.wait_ge(sems["slot:" + s], 16 * n)

            @block.tensor
            def _(t):
                emit_for("tensor", t)

            @block.vector
            def _(v):
                emit_for("vector", v)

            @block.scalar
            def _(s):
                emit_for("scalar", s)

            @block.sync
            def _(s):
                emit_for("sync", s)

            @block.gpsimd
            def _(g):
                emit_for("gpsimd", g)
        return nc



def _rope_tables_l0():
    HD = 64
    inv_freq = 1.0 / (10000.0 ** (np.arange(0, HD, 2, dtype=np.float32) / HD))
    ang = np.arange(T_, dtype=np.float32)[:, None] * inv_freq[None, :]
    return np.cos(ang).astype(np.float32), np.sin(ang).astype(np.float32)


def _rope_tables_l1():
    HEAD_DIM = 64
    ROPE_THETA, SCALING, NTK_ALPHA, NTK_BETA, ICL = 150000.0, 32.0, 1.0, 32.0, 1024
    d_half = HEAD_DIM // 2
    freq = ROPE_THETA ** (np.arange(0, HEAD_DIM, 2, dtype=np.float32) / HEAD_DIM)
    concentration = 0.1 * math.log(SCALING) + 1.0
    low = d_half * math.log(ICL / (NTK_BETA * 2 * math.pi)) / math.log(ROPE_THETA)
    high = d_half * math.log(ICL / (NTK_ALPHA * 2 * math.pi)) / math.log(ROPE_THETA)
    interpolation = 1.0 / (SCALING * freq)
    extrapolation = 1.0 / freq
    ramp = np.clip((np.arange(d_half, dtype=np.float32) - low) / (high - low), 0.0, 1.0)
    mask = 1.0 - ramp
    inv_freq = interpolation * (1.0 - mask) + extrapolation * mask
    freqs = np.outer(np.arange(T_, dtype=np.float32), inv_freq)
    return ((np.cos(freqs) * concentration).astype(np.float32),
            (np.sin(freqs) * concentration).astype(np.float32))


def _l1_masks(core):
    Q0 = (core % 4) * OWN
    qp = np.arange(512)[None, :]
    kp = np.arange(128)[:, None]
    out = np.zeros((2, 5, 128, 512), dtype=BF)
    for qc in range(2):
        for m in range(5):
            r = 128 * m + kp - qp
            valid = (r >= 1) & (r <= 128)
            if Q0 == 0:
                kl = qc * 512 + 128 * m + kp + 0 * qp
                valid = valid & (kl >= 128)
            out[qc, m] = valid.astype(np.float32).astype(BF)
    return out


def _pad_slice(x, lo, hi):
    pad = max(0, -lo)
    body = x[max(lo, 0):hi]
    if pad:
        return np.concatenate([np.zeros((pad,) + x.shape[1:], x.dtype), body], 0)
    return np.ascontiguousarray(body)


def prep_inputs(inp):
    f32 = np.float32
    x = np.asarray(inp["L0_attn_x"], f32)
    cos0, sin0 = _rope_tables_l0()
    cos1, sin1 = _rope_tables_l1()
    iden_bf = np.eye(128, dtype=BF)
    iden_f32 = np.eye(128, dtype=f32)
    ones128 = np.ones((1, 128), f32)
    sel8 = np.zeros((8, 8 * 128), f32)
    for e in range(8):
        sel8[e, e * 128:(e + 1) * 128] = 1.0
    sel32 = np.zeros((32, 32 * 128), f32)
    for e in range(32):
        sel32[e, e * 128:(e + 1) * 128] = 1.0

    ebm = np.asarray(inp["L0_ffn_expert_b"], f32)
    eb_cols = np.ascontiguousarray(
        ebm.reshape(8, 16, 128).transpose(2, 1, 0).reshape(128, 128))

    wq_t0 = np.asarray(inp["L0_attn_Wq"], f32).T.reshape(D_, 32, 64)
    wq_tp = np.ascontiguousarray(wq_t0[:, SLOT2HEAD, :].reshape(D_, D_))
    bq0 = np.asarray(inp["L0_attn_bq"], f32).reshape(32, 64)
    bqp = np.ascontiguousarray(bq0[SLOT2HEAD].reshape(1, D_))
    wo_t0 = np.asarray(inp["L0_attn_Wo"], f32).T.reshape(32, 64, D_)
    wo_tp = np.ascontiguousarray(wo_t0[SLOT2HEAD].reshape(D_, D_))
    shared0 = dict(
        wq_t=wq_tp,
        wk_t=np.ascontiguousarray(np.asarray(inp["L0_attn_Wk"], f32).T),
        wv_t=np.ascontiguousarray(np.asarray(inp["L0_attn_Wv"], f32).T),
        wo_t=wo_tp,
        bq=bqp,
        bk=np.asarray(inp["L0_attn_bk"], f32).reshape(1, -1),
        bv=np.asarray(inp["L0_attn_bv"], f32).reshape(1, -1),
        bo=np.asarray(inp["L0_attn_bo"], f32).reshape(1, -1),
        rw=np.ascontiguousarray(np.asarray(inp["L0_ffn_router_w"], f32)),
        rb=np.asarray(inp["L0_ffn_router_b"], f32).reshape(1, -1),
        ew=np.ascontiguousarray(np.asarray(inp["L0_ffn_expert_w"], f32)),
        eb_cols=eb_cols,
        pw=np.ascontiguousarray(np.asarray(inp["L0_ffn_proj_w"], f32)),
        pb=np.asarray(inp["L0_ffn_proj_b"], f32).reshape(1, -1),
        nw=np.asarray(inp["L0_ffn_norm_w"], f32).reshape(1, -1),
        iden_bf=iden_bf, iden_f32=iden_f32, ones128=ones128, sel8=sel8,
        cos0=cos0, sin0=sin0)
    qkv_t0 = np.asarray(inp["L1_attn_qkv_w"], f32).T
    qp = qkv_t0[:, 0:2048].reshape(D_, 32, 64)
    qkv_tp = np.concatenate([qp[:, SLOT2HEAD, :].reshape(D_, 2048),
                             qkv_t0[:, 2048:]], axis=1)
    qb0 = np.asarray(inp["L1_attn_qkv_b"], f32).reshape(-1)
    qbq = qb0[0:2048].reshape(32, 64)
    qkv_bp = np.concatenate([qbq[SLOT2HEAD].reshape(-1), qb0[2048:]])
    out_t0 = np.asarray(inp["L1_attn_out_w"], f32).T.reshape(32, 64, D_)
    out_tp = np.ascontiguousarray(out_t0[SLOT2HEAD].reshape(D_, D_))
    shared1 = dict(
        qkv_t=np.ascontiguousarray(qkv_tp),
        qkv_b=np.ascontiguousarray(qkv_bp.reshape(1, -1)),
        out_t=out_tp,
        out_b=np.asarray(inp["L1_attn_out_b"], f32).reshape(1, -1),
        sinks=np.asarray(inp["L1_attn_sinks"], f32).reshape(1, -1),
        gw_t=np.ascontiguousarray(np.asarray(inp["L1_ffn_gate_w"], f32).T),
        gb=np.asarray(inp["L1_ffn_gate_bias"], f32).reshape(1, -1),
        fc1_t=np.ascontiguousarray(np.asarray(inp["L1_ffn_fc1_w"], f32).T),
        fc2_t=np.ascontiguousarray(np.asarray(inp["L1_ffn_fc2_w"], f32).T),
        w1_t=np.ascontiguousarray(np.asarray(inp["L1_ffn_w1"], f32).transpose(0, 2, 1)),
        w2_t=np.ascontiguousarray(np.asarray(inp["L1_ffn_w2"], f32).transpose(0, 2, 1)),
        su_t=np.ascontiguousarray(np.asarray(inp["L1_ffn_shared_up_w"], f32).T),
        sd_t=np.ascontiguousarray(np.asarray(inp["L1_ffn_shared_down_w"], f32).T),
        iden_bf=iden_bf, iden_f32=iden_f32, ones128=ones128, sel32=sel32)

    maps0, maps1 = [], []
    for c in range(N_CORES):
        bb, Q0 = c // 4, (c % 4) * OWN
        m0 = dict(shared0)
        m0["xb"] = np.ascontiguousarray(x[bb])
        m0["xc"] = _pad_slice(x[bb], Q0 - HALO, Q0 + OWN)
        m0["cos0c"] = _pad_slice(cos0, Q0 - HALO, Q0 + OWN)
        m0["sin0c"] = _pad_slice(sin0, Q0 - HALO, Q0 + OWN)
        maps0.append(m0)
        m1 = dict(shared1)
        m1["cos1"] = _pad_slice(cos1, Q0 - HALO, Q0 + OWN)
        m1["sin1"] = _pad_slice(sin1, Q0 - HALO, Q0 + OWN)
        m1["masks"] = _l1_masks(c)
        maps1.append(m1)
    return maps0, maps1



def r3(src2d, p=128):
    return src2d.rearrange("(n p) d -> p n d", p=p)


def d3(dstbuf_slice, d):
    return dstbuf_slice.rearrange("p (n d) -> p n d", d=d)



def fmv(t, total):
    return t[:, :].rearrange("p (f c) -> p f c", c=total)


def emit_ln_tile(b, C, xap, xkey, st, stk, out_bf, outkey, eps,
                 mcol=None, rcol=None, statkeys=("m_own", "r_own")):
    scr, sck = C["lnscr"], "lnscr"
    b.red(st[:, 0:1], xap, reads=[xkey], writes=[stk])
    b.tt(scr[:, 0:D_], xap, xap, ALU.mult, reads=[xkey], writes=[sck])
    b.red(st[:, 1:2], scr[:, 0:D_], reads=[sck], writes=[stk])
    b.ts(st[:, 0:1], st[:, 0:1], 1.0 / D_, None, ALU.mult, reads=[stk], writes=[stk])
    b.tt(st[:, 2:3], st[:, 0:1], st[:, 0:1], ALU.mult, reads=[stk], writes=[stk])
    b.ts(st[:, 1:2], st[:, 1:2], 1.0 / D_, None, ALU.mult, reads=[stk], writes=[stk])
    b.tt(st[:, 1:2], st[:, 1:2], st[:, 2:3], ALU.subtract, reads=[stk], writes=[stk])
    b.ts(st[:, 1:2], st[:, 1:2], float(eps), None, ALU.add, reads=[stk], writes=[stk])
    b.act(st[:, 2:3], st[:, 1:2], AF.Ln, reads=[stk], writes=[stk])
    b.act(st[:, 3:4], st[:, 2:3], AF.Exp, scale=-0.5, reads=[stk], writes=[stk])
    b.ts(out_bf, xap, st[:, 0:1], st[:, 3:4], ALU.subtract, ALU.mult,
         reads=[xkey, stk], writes=[outkey])
    if mcol is not None:
        b.vcopy(mcol, st[:, 0:1], reads=[stk], writes=[statkeys[0]])
        b.vcopy(rcol, st[:, 3:4], reads=[stk], writes=[statkeys[1]])


def emit_transpose16(b, C, src_bf, srckey, dstv, dstkey, nft=16):
    for g in range((nft + 7) // 8):
        n = min(8, nft - g * 8)
        pt, ptk = C["pt%d" % (g % 2)], "pt%d" % (g % 2)
        for k in range(n):
            ft = g * 8 + k
            b.trans(pt[:, k * 128:(k + 1) * 128], src_bf[:, ft * 128:(ft + 1) * 128],
                    C["idb"][:], reads=[srckey, "idb"], writes=[ptk])
        b.vcopy(dstv(g, n), pt[:, 0:n * 128].rearrange("p (f c) -> p f c", c=128),
                reads=[ptk], writes=[dstkey])


def emit_rope(b, C, src, srckey, dst, dstkey, costile, sintile, nheads, tmp, tmpkey,
              ckeys=("cosb", "sinb")):
    ca = costile.rearrange("p (o d) -> p o d", o=1).broadcast_to([128, nheads, 32])
    sa = sintile.rearrange("p (o d) -> p o d", o=1).broadcast_to([128, nheads, 32])
    s3 = src.rearrange("p (h d) -> p h d", d=64)
    d3 = dst.rearrange("p (h d) -> p h d", d=64)
    t3 = tmp.rearrange("p (h d) -> p h d", d=64)
    x1, x2 = s3[:, :, 0:32], s3[:, :, 32:64]
    b.tt(t3[:, :, 0:32], x1, ca, ALU.mult, reads=[srckey, ckeys[0]], writes=[tmpkey])
    b.tt(t3[:, :, 32:64], x2, sa, ALU.mult, reads=[srckey, ckeys[1]], writes=[tmpkey])
    b.tt(d3[:, :, 0:32], t3[:, :, 0:32], t3[:, :, 32:64], ALU.subtract,
         reads=[tmpkey], writes=[dstkey])
    b.tt(t3[:, :, 0:32], x2, ca, ALU.mult, reads=[srckey, ckeys[0]], writes=[tmpkey])
    b.tt(t3[:, :, 32:64], x1, sa, ALU.mult, reads=[srckey, ckeys[1]], writes=[tmpkey])
    b.tt(d3[:, :, 32:64], t3[:, :, 0:32], t3[:, :, 32:64], ALU.add,
         reads=[tmpkey], writes=[dstkey])


def emit_global_max(b, nc, ctx, tag, col, colkey, out11, outkey, row, rowkey):
    dscr = nc.dram_tensor("gmax_" + tag, [1, 128], F32, kind="Internal").ap()
    b.dma("sync", "gm" + tag, dscr.rearrange("o p -> p o"), col,
          reads=[colkey], writes=["gmd" + tag])
    b.dma("sync", "gm" + tag + "b", row, dscr, reads=["gmd" + tag], writes=[rowkey])
    b.red(out11, row, aluop=ALU.max, reads=[rowkey], writes=[outkey])



def build_l0(debug=()):
    nc = bass.Bass("TRN2", target_bir_lowering=False, debug=False)
    I = {}

    def di(name, shape, dt=F32):
        I[name] = nc.dram_tensor(name, list(shape), dt, kind="ExternalInput").ap()

    di("xb", (T_, D_)); di("xc", (TOK, D_))
    di("wq_t", (D_, D_)); di("wk_t", (D_, 512)); di("wv_t", (D_, 512))
    di("wo_t", (D_, D_))
    di("bq", (1, D_)); di("bk", (1, 512)); di("bv", (1, 512)); di("bo", (1, D_))
    di("rw", (D_, 8)); di("rb", (1, 8)); di("ew", (8, D_, D_))
    di("eb_cols", (128, 128)); di("pw", (D_, D_)); di("pb", (1, D_)); di("nw", (1, D_))
    di("cos0", (T_, 32)); di("sin0", (T_, 32))
    di("cos0c", (TOK, 32)); di("sin0c", (TOK, 32))
    di("iden_bf", (128, 128), BF16); di("iden_f32", (128, 128))
    di("ones128", (1, 128)); di("sel8", (8, 8 * 128))

    x1 = nc.dram_tensor("x1", [TOK, D_], F32, kind="ExternalOutput").ap()
    gdram = nc.dram_tensor("gscr", [TOK, D_], BF16, kind="Internal").ap()
    DBG = {}

    def dbg(name, shape=None, dt=F32):
        if name in debug and shape is not None and name not in DBG:
            DBG[name] = nc.dram_tensor("dbg_" + name, list(shape), dt,
                                       kind="ExternalOutput").ap()
        return DBG.get(name)

    b = Prog(nc)
    with ExitStack() as ctx:
        emit_l0(b, I, x1, gdram, dbg, nc, ctx)
    return b.build(), sorted(DBG)


def emit_l0(b, I, x1, gdram, dbg, nc, ctx):
    def sb(name, shape, dt=F32):
        return ctx.enter_context(nc.sbuf_tensor(name, list(shape), dt))

    def psg(st, name, shape, dt=F32):
        return st.enter_context(nc.psum_tensor(name, list(shape), dt))

    C = {}
    K_fm = sb("K_fm", [128, 4 * T_], BF16)
    V_tm = sb("V_tm", [128, NTB * 520], BF16)
    m_own = sb("m_own", [128, NT]); r_own = sb("r_own", [128, NT])
    q2acc = sb("q2acc", [128, 32]); k2acc = sb("k2acc", [128, 8])
    negB = sb("negB", [128, 1])
    C["idb"] = sb("idb", [128, 128], BF16)
    onesr = sb("onesr", [1, 128])

    b.dma("sync", "c0", C["idb"][:], I["iden_bf"], writes=["idb"], bulk=True)
    b.dma("sync", "c0", onesr[:], I["ones128"], writes=["onesr"], bulk=True)
    b.vmemset(q2acc[:], 0.0, writes=["q2acc"])
    b.vmemset(k2acc[:], 0.0, writes=["k2acc"])

    sComb = ExitStack()
    with ExitStack() as sAQ:
        xn_fm = sAQ.enter_context(nc.sbuf_tensor("xn_fm", [128, NF * TOK], BF16))
        wT = sAQ.enter_context(nc.sbuf_tensor("wT", [8, TOK], F32))
        scs = sAQ.enter_context(ExitStack())
        cosb = scs.enter_context(nc.sbuf_tensor("cosb", [128, NTB * 32], F32))
        sinb = scs.enter_context(nc.sbuf_tensor("sinb", [128, NTB * 32], F32))
        b.dma("sync", "c0", d3(cosb[:], 32), r3(I["cos0"]),
              writes=["cosb"], bulk=True)
        b.dma("sync", "c0", d3(sinb[:], 32), r3(I["sin0"]),
              writes=["sinb"], bulk=True)

        with ExitStack() as sA:
            def asb(name, shape, dt=F32):
                return sA.enter_context(nc.sbuf_tensor(name, list(shape), dt))

            xt = [asb("xt%d" % i, [128, D_]) for i in range(2)]
            C["lnscr"] = asb("lnscr", [128, D_])
            wkv = asb("wkv", [128, NF * 1024], BF16)
            xnbf = [asb("xnbf0", [128, D_], BF16)]
            stat = [asb("stat%d" % i, [128, 4]) for i in range(2)]
            xnfm_kv = asb("xnfm_kv", [128, NF * 512], BF16)
            kvscr = [asb("kvscr0", [128, 512])]
            kvtmp = [asb("kvtmp0", [128, 512])]
            krs = [asb("krs%d" % i, [128, 512], BF16) for i in range(2)]
            bk_bc = asb("bk_bc", [128, 512])
            bv_bc = asb("bv_bc", [128, 512])
            pkv = [psg(sA, "pkv%d" % i, [128, 512]) for i in range(2)]
            C["pt0"] = psg(sA, "pt0", [128, 1024], BF16)
            C["pt1"] = psg(sA, "pt1", [128, 1024], BF16)

            b.dma("gpsimd", "wkv", d3(wkv[:, 0:NF * 512], 512), r3(I["wk_t"]),
                  writes=["wkv_k"], bulk=True)
            b.dma("gpsimd", "wkv", d3(wkv[:, NF * 512:], 512), r3(I["wv_t"]),
                  writes=["wkv_v"], bulk=True)
            b.dma("sync", "cA", bk_bc[:], I["bk"].broadcast_to([128, 512]),
                  writes=["bk_bc"], bulk=True)
            b.dma("sync", "cA", bv_bc[:], I["bv"].broadcast_to([128, 512]),
                  writes=["bv_bc"], bulk=True)

            for ch in range(NTB // 4):
                for it in range(4):
                    tt = ch * 4 + it
                    bi = tt % 2
                    xtb, xk = xt[bi], "xt%d" % bi
                    b.dma("sync", xk, xtb[:], I["xb"][tt * 128:(tt + 1) * 128, :],
                          writes=[xk])
                    emit_ln_tile(b, C, xtb[:], xk, stat[bi], "stat%d" % bi,
                                 xnbf[0][:], "xnbf0", 1e-5)
                    kvv = fmv(xnfm_kv, 512)
                    emit_transpose16(
                        b, C, xnbf[0][:], "xnbf0",
                        (lambda g, n, it_=it: kvv[:, g * 8:g * 8 + n,
                                                  it_ * 128:(it_ + 1) * 128]),
                        "xnfm_kv")
                for it in range(4):
                    tt = ch * 4 + it
                    bi = it % 2
                    pk, pkk = pkv[bi], "pkv%d" % bi
                    scr, sck = kvscr[0], "kvscr0"
                    tmp, tmk = kvtmp[0], "kvtmp0"
                    for fc in range(NF):
                        b.mm(pk[:], xnfm_kv[:, fc * 512 + it * 128:fc * 512 + (it + 1) * 128],
                             wkv[:, fc * 512:(fc + 1) * 512], fc == 0, fc == NF - 1,
                             reads=["xnfm_kv", "wkv_k"], writes=[pkk])
                    b.tt(scr[:], pk[:], bk_bc[:], ALU.add, reads=[pkk, "bk_bc"], writes=[sck])
                    emit_rope(b, C, scr[:], sck, krs[bi][:], "krs%d" % bi,
                              cosb[:, tt * 32:(tt + 1) * 32], sinb[:, tt * 32:(tt + 1) * 32],
                              8, tmp[:], tmk)
                    pt, ptk = C["pt%d" % bi], "pt%d" % bi
                    for k in range(4):
                        b.trans(pt[:, k * 128:(k + 1) * 128], krs[bi][:, k * 128:(k + 1) * 128],
                                C["idb"][:], reads=["krs%d" % bi, "idb"], writes=[ptk])
                    b.vcopy(fmv(K_fm, T_)[:, 0:4, tt * 128:(tt + 1) * 128],
                            pt[:, 0:512].rearrange("p (f c) -> p f c", c=128),
                            reads=[ptk], writes=["K_fm"])
                    b.tt(tmp[:], krs[bi][:], krs[bi][:], ALU.mult,
                         reads=["krs%d" % bi], writes=[tmk])
                    b.red(scr[:, 0:8], tmp[:].rearrange("p (h d) -> p h d", d=64),
                          aluop=ALU.max, reads=[tmk], writes=[sck])
                    b.tt(k2acc[:], k2acc[:], scr[:, 0:8], ALU.max,
                         reads=[sck, "k2acc"], writes=["k2acc"])
                    for fc in range(NF):
                        b.mm(pk[:], xnfm_kv[:, fc * 512 + it * 128:fc * 512 + (it + 1) * 128],
                             wkv[:, NF * 512 + fc * 512:NF * 512 + (fc + 1) * 512],
                             fc == 0, fc == NF - 1,
                             reads=["xnfm_kv", "wkv_v"], writes=[pkk])
                    vt = V_tm[:, tt * 520:(tt + 1) * 520].rearrange("p (h c) -> p h c", c=65)
                    b.tt(vt[:, :, 0:64], pk[:].rearrange("p (h c) -> p h c", c=64),
                         bv_bc[:].rearrange("p (h c) -> p h c", c=64), ALU.add,
                         reads=[pkk, "bv_bc"], writes=["V_tm"])
                    b.vmemset(vt[:, :, 64:65], 1.0, writes=["V_tm"])

            for ot in range(NT):
                bi = ot % 2
                xtb, xk = xt[bi], "xt%d" % bi
                b.dma("sync", xk, xtb[:], I["xc"][ot * 128:(ot + 1) * 128, :],
                      writes=[xk])
                emit_ln_tile(b, C, xtb[:], xk, stat[bi], "stat%d" % bi,
                             xnbf[0][:], "xnbf0", 1e-5,
                             mcol=m_own[:, ot:ot + 1], rcol=r_own[:, ot:ot + 1])
                xv = fmv(xn_fm, TOK)
                emit_transpose16(
                    b, C, xnbf[0][:], "xnbf0",
                    (lambda g, n, ot_=ot: xv[:, g * 8:g * 8 + n, ot_ * 128:(ot_ + 1) * 128]),
                    "xn_fm")

        b.fence()
        if dbg("xn_fm", (128, NF * TOK)):
            b.dma("gpsimd", "dbg1", dbg("xn_fm"), xn_fm[:], reads=["xn_fm"], writes=[])
        if dbg("K_fm", (128, 4 * T_)):
            b.dma("gpsimd", "dbg2", dbg("K_fm"), K_fm[:], reads=["K_fm"], writes=[])
        if dbg("V_tm", (128, NTB * 520)):
            b.dma("gpsimd", "dbg3", dbg("V_tm"), V_tm[:], reads=["V_tm"], writes=[])

        if PHASES < 2:
            return
        with ExitStack() as sR:
            rw_b = sR.enter_context(nc.sbuf_tensor("rw_b", [128, NF * 8], BF16))
            rw_f = sR.enter_context(nc.sbuf_tensor("rw_f", [128, NF * 8], F32))
            rb_bc = sR.enter_context(nc.sbuf_tensor("rb_bc", [128, 8], F32))
            rgb = sR.enter_context(nc.sbuf_tensor("rgb", [128, 8], BF16))
            rscr = sR.enter_context(nc.sbuf_tensor("rscr", [128, 32], F32))
            rpsum_t = psg(sR, "rpsum", [128, 512])
            rpsum = rpsum_t[:, 0:8]
            rtp_t = psg(sR, "rtp", [128, 1024], BF16)
            rtp = rtp_t[0:8, 0:128]
            b.dma("sync", "cRg", d3(rw_f[:], 8), r3(I["rw"]),
                  writes=["rw_f"], bulk=True)
            b.vcopy(rw_b[:], rw_f[:], reads=["rw_f"], writes=["rw_b"])
            b.dma("sync", "cR", rb_bc[:], I["rb"].broadcast_to([128, 8]),
                  writes=["rb_bc"], bulk=True)

            for ot in range(NT):
                for fc in range(NF):
                    b.mm(rpsum, fmv(xn_fm, TOK)[:, fc, ot * 128:(ot + 1) * 128],
                         rw_b[:, fc * 8:(fc + 1) * 8], fc == 0, fc == NF - 1,
                         reads=["xn_fm", "rw_b"], writes=["rpsum"])
                l, lk = rscr[:, 0:8], "rscr"
                b.tt(l, rpsum, rb_bc[:], ALU.add, reads=["rpsum", "rb_bc"], writes=[lk])
                m1, nm1, m2 = rscr[:, 8:9], rscr[:, 9:10], rscr[:, 10:11]
                e21, den, rcp = rscr[:, 11:12], rscr[:, 12:13], rscr[:, 13:14]
                tq, ge = rscr[:, 16:24], rscr[:, 24:32]
                b.red(m1, l, aluop=ALU.max, reads=[lk], writes=[lk])
                b.red(nm1, l, aluop=ALU.max, negate=True, reads=[lk], writes=[lk])
                b.ts(tq, l, m1, -1e30, ALU.is_equal, ALU.mult, reads=[lk], writes=[lk])
                b.tt(tq, tq, l, ALU.add, reads=[lk], writes=[lk])
                b.red(m2, tq, aluop=ALU.max, reads=[lk], writes=[lk])
                b.act(e21, m2, AF.Exp, bias=nm1, reads=[lk], writes=[lk])
                b.ts(den, e21, 1.0, None, ALU.add, reads=[lk], writes=[lk])
                b.vrecip(rcp, den, reads=[lk], writes=[lk])
                b.act(tq, l, AF.Exp, bias=nm1, reads=[lk], writes=[lk])
                b.ts(ge, l, m2, None, ALU.is_ge, reads=[lk], writes=[lk])
                b.tt(ge, ge, tq, ALU.mult, reads=[lk], writes=[lk])
                b.ts(ge, ge, rcp, None, ALU.mult, reads=[lk], writes=[lk])
                if dbg("rdump", (128, NT * 32)):
                    b.dma("sync", "dbgR", dbg("rdump")[:, ot * 32:(ot + 1) * 32],
                          rscr[:], reads=[lk], writes=[])
                b.vcopy(rgb[:], ge, reads=[lk], writes=["rgb"])
                b.trans(rtp, rgb[:], C["idb"][:], reads=["rgb", "idb"], writes=["rtp"])
                b.vcopy(wT[:, ot * 128:(ot + 1) * 128], rtp, reads=["rtp"], writes=["wT"])

        if dbg("wT", (8, TOK)):
            b.dma("sync", "dbg4", dbg("wT"), wT[:], reads=["wT"], writes=[])

        b.fence()
        if PHASES < 3:
            return
        Q_fm = ctx.enter_context(nc.sbuf_tensor("Q_fm", [128, NF * TOK], BF16,
                                                side="right"))
        with ExitStack() as sQ:
            wqs = [sQ.enter_context(nc.sbuf_tensor("wqs%d" % i, [128, NF * 512], BF16))
                   for i in range(2)]
            qscr = [sQ.enter_context(nc.sbuf_tensor("qscr%d" % i, [128, 512], F32))
                    for i in range(2)]
            qtmp = [sQ.enter_context(nc.sbuf_tensor("qtmp%d" % i, [128, 512], F32))
                    for i in range(2)]
            qrb = [sQ.enter_context(nc.sbuf_tensor("qrb%d" % i, [128, 512], BF16))
                   for i in range(2)]
            bq_bc = sQ.enter_context(nc.sbuf_tensor("bq_bc", [128, D_], F32))
            cosc = sQ.enter_context(nc.sbuf_tensor("cosc", [128, NT * 32], F32))
            sinc = sQ.enter_context(nc.sbuf_tensor("sinc", [128, NT * 32], F32))
            ppq = [psg(sQ, "ppq%d" % i, [128, 512]) for i in range(2)]
            C["pt0"] = psg(sQ, "pt0b", [128, 1024], BF16)
            C["pt1"] = psg(sQ, "pt1b", [128, 1024], BF16)
            b.dma("sync", "cQ", bq_bc[:], I["bq"].broadcast_to([128, D_]),
                  writes=["bq_bc"], bulk=True)
            b.dma("sync", "cQ", d3(cosc[:], 32), r3(I["cos0c"]),
                  writes=["cosc"], bulk=True)
            b.dma("sync", "cQ", d3(sinc[:], 32), r3(I["sin0c"]),
                  writes=["sinc"], bulk=True)
            for ncc in range(4):
                wi = ncc % 2
                wb, wk = wqs[wi], "wqs%d" % wi
                b.dma("gpsimd", wk, d3(wb[:], 512), r3(I["wq_t"][:, ncc * 512:(ncc + 1) * 512]),
                      writes=[wk])
                for ot in range(NT):
                    bi = ot % 2
                    pp, ppk = ppq[bi], "ppq%d" % bi
                    for fc in range(NF):
                        b.mm(pp[:], fmv(xn_fm, TOK)[:, fc, ot * 128:(ot + 1) * 128],
                             wb[:, fc * 512:(fc + 1) * 512], fc == 0, fc == NF - 1,
                             reads=["xn_fm", wk], writes=[ppk])
                    b.tt(qscr[bi][:], pp[:], bq_bc[:, ncc * 512:(ncc + 1) * 512], ALU.add,
                         reads=[ppk, "bq_bc"], writes=["qscr%d" % bi])
                    emit_rope(b, C, qscr[bi][:], "qscr%d" % bi, qrb[bi][:], "qrb%d" % bi,
                              cosc[:, ot * 32:(ot + 1) * 32], sinc[:, ot * 32:(ot + 1) * 32],
                              8, qtmp[bi][:], "qtmp%d" % bi, ckeys=("cosc", "sinc"))
                    b.tt(qtmp[bi][:], qrb[bi][:], qrb[bi][:], ALU.mult,
                         reads=["qrb%d" % bi], writes=["qtmp%d" % bi])
                    b.red(qscr[bi][:, 0:8], qtmp[bi][:].rearrange("p (h d) -> p h d", d=64),
                          aluop=ALU.max, reads=["qtmp%d" % bi], writes=["qscr%d" % bi])
                    b.tt(q2acc[:, ncc * 8:(ncc + 1) * 8], q2acc[:, ncc * 8:(ncc + 1) * 8],
                         qscr[bi][:, 0:8], ALU.max,
                         reads=["qscr%d" % bi, "q2acc"], writes=["q2acc"])
                    pt, ptk = C["pt%d" % bi], "pt%d" % bi
                    for k in range(4):
                        b.trans(pt[:, k * 128:(k + 1) * 128], qrb[bi][:, k * 128:(k + 1) * 128],
                                C["idb"][:], reads=["qrb%d" % bi, "idb"], writes=[ptk])
                    b.vcopy(fmv(Q_fm, TOK)[:, ncc * 4:(ncc + 1) * 4, ot * 128:(ot + 1) * 128],
                            pt[:, 0:512].rearrange("p (f c) -> p f c", c=128),
                            reads=[ptk], writes=["Q_fm"])

        b.fence()
        scs.close()

        if PHASES < 4:
            return
        comb = sComb.enter_context(nc.sbuf_tensor("comb", [128, NF * TOK], BF16,
                                                   side="right"))
        combv = fmv(comb, TOK)
        with ExitStack() as sM:
            wes = [sM.enter_context(nc.sbuf_tensor("wes%d" % i, [128, NF * 128], BF16))
                   for i in range(2)]
            she = [sM.enter_context(nc.sbuf_tensor("she%d" % i, [128, 512], BF16))
                   for i in range(2)]
            mtmp = [sM.enter_context(nc.sbuf_tensor("mtmp%d" % i, [128, 512], BF16))
                    for i in range(2)]
            wbcs = sM.enter_context(nc.sbuf_tensor("wbcs", [128, TOK], F32))
            sel8b = sM.enter_context(nc.sbuf_tensor("sel8b", [8, 8 * 128], F32))
            eb_c = sM.enter_context(nc.sbuf_tensor("eb_c", [128, 128], F32))
            pH = [psg(sM, "pH%d" % i, [128, 512]) for i in range(2)]
            pwb = psg(sM, "pwb", [128, 512])
            b.dma("sync", "cM", sel8b[:], I["sel8"], writes=["sel8b"], bulk=True)
            b.dma("sync", "cM", eb_c[:], I["eb_cols"], writes=["eb_c"], bulk=True)

            for e in range(8):
                for ci, tw in enumerate(CHW):
                    b.mm(pwb[:, 0:tw], sel8b[:, e * 128:(e + 1) * 128],
                         wT[:, ci * 512:ci * 512 + tw], True, True,
                         reads=["sel8b", "wT"], writes=["pwb"])
                    b.vcopy(wbcs[:, ci * 512:ci * 512 + tw], pwb[:, 0:tw],
                            reads=["pwb"], writes=["wbcs"])
                for ft in range(NF):
                    wi = (e * NF + ft) % 2
                    wb, wk = wes[wi], "wes%d" % wi
                    b.dma("gpsimd", wk, d3(wb[:], 128), r3(I["ew"][e, :, ft * 128:(ft + 1) * 128]),
                          writes=[wk])
                    for ci, tw in enumerate(CHW):
                        t0 = ci * 512
                        pi = (ft * 3 + ci) % 2
                        hb, hbk = pH[pi], "pH%d" % pi
                        for fc in range(NF):
                            b.mm(hb[:, 0:tw], wb[:, fc * 128:(fc + 1) * 128],
                                 fmv(xn_fm, TOK)[:, fc, t0:t0 + tw], fc == 0, fc == NF - 1,
                                 reads=[wk, "xn_fm"], writes=[hbk])
                        b.act(she[pi][:, 0:tw], hb[:, 0:tw], AF.Silu,
                              bias=eb_c[:, ft * 8 + e:ft * 8 + e + 1],
                              reads=[hbk, "eb_c"], writes=["she%d" % pi])
                        dst = combv[:, ft, t0:t0 + tw]
                        ck = "comb%d_%d" % (ft, ci)
                        if e == 0:
                            b.tt(dst, she[pi][:, 0:tw], wbcs[:, t0:t0 + tw], ALU.mult,
                                 reads=["she%d" % pi, "wbcs"], writes=[ck])
                        else:
                            b.tt(mtmp[pi][:, 0:tw], she[pi][:, 0:tw], wbcs[:, t0:t0 + tw],
                                 ALU.mult, reads=["she%d" % pi, "wbcs"], writes=["mtmp%d" % pi])
                            b.tt(dst, dst, mtmp[pi][:, 0:tw], ALU.add,
                                 reads=["mtmp%d" % pi, ck], writes=[ck])

    dbg("g_out", (TOK, D_))
    b.fence()
    if dbg("comb", (128, NF * TOK)):
        b.dma("gpsimd", "dbgC", dbg("comb"), comb[:], reads=["comb0_0"], writes=[])
    if PHASES < 5:
        sComb.close()
        return
    with ExitStack() as sG:
        wpj = [sG.enter_context(nc.sbuf_tensor("wpj%d" % i, [128, NF * 512], BF16))
               for i in range(2)]
        pb_bc = sG.enter_context(nc.sbuf_tensor("pb_bc", [128, D_], F32))
        nw_bc = sG.enter_context(nc.sbuf_tensor("nw_bc", [128, D_], F32))
        scrY = [sG.enter_context(nc.sbuf_tensor("scrY0", [128, D_], F32))]
        xo = [sG.enter_context(nc.sbuf_tensor("xo0", [128, D_], F32))]
        gt = [sG.enter_context(nc.sbuf_tensor("gt0", [128, D_], BF16))]
        yst = [sG.enter_context(nc.sbuf_tensor("yst0", [128, 4], F32))]
        pP = [psg(sG, "pP%d" % i, [128, 512]) for i in range(2)]
        b.dma("sync", "cG", pb_bc[:], I["pb"].broadcast_to([128, D_]),
              writes=["pb_bc"], bulk=True)
        b.dma("sync", "cG", nw_bc[:], I["nw"].broadcast_to([128, D_]),
              writes=["nw_bc"], bulk=True)
        for ot in range(NT):
            bi = 0
            sy, syk = scrY[0], "scrY0"
            xb_, xbk = xo[0], "xo0"
            b.dma("sync", xbk, xb_[:], I["xc"][ot * 128:(ot + 1) * 128, :],
                  writes=[xbk])
            for ncc in range(4):
                wi = ncc % 2
                wb, wk = wpj[wi], "wpj%d" % wi
                b.dma("gpsimd", wk, d3(wb[:], 512), r3(I["pw"][:, ncc * 512:(ncc + 1) * 512]),
                      writes=[wk])
                pp, ppk = pP[wi], "pP%d" % wi
                for fc in range(NF):
                    b.mm(pp[:], combv[:, fc, ot * 128:(ot + 1) * 128],
                         wb[:, fc * 512:(fc + 1) * 512], fc == 0, fc == NF - 1,
                         reads=["comb%d_%d" % (fc, min(ot // 4, 2)), wk], writes=[ppk])
                b.tt(sy[:, ncc * 512:(ncc + 1) * 512], pp[:],
                     pb_bc[:, ncc * 512:(ncc + 1) * 512], ALU.add,
                     reads=[ppk, "pb_bc"], writes=[syk])
            b.ts(xb_[:], xb_[:], m_own[:, ot:ot + 1], r_own[:, ot:ot + 1],
                 ALU.subtract, ALU.mult, reads=[xbk, "m_own", "r_own"], writes=[xbk])
            b.tt(sy[:], sy[:], xb_[:], ALU.add, reads=[syk, xbk], writes=[syk])
            st, stk = yst[0], "yst0"
            b.tt(xb_[:], sy[:], sy[:], ALU.mult, reads=[syk, xbk], writes=[xbk])
            b.red(st[:, 0:1], xb_[:], reads=[xbk], writes=[stk])
            b.ts(st[:, 0:1], st[:, 0:1], 1.0 / D_, 1e-6, ALU.mult, ALU.add,
                 reads=[stk], writes=[stk])
            b.act(st[:, 1:2], st[:, 0:1], AF.Ln, reads=[stk], writes=[stk])
            b.act(st[:, 2:3], st[:, 1:2], AF.Exp, scale=-0.5, reads=[stk], writes=[stk])
            b.ts(sy[:], sy[:], st[:, 2:3], None, ALU.mult, reads=[syk, stk], writes=[syk])
            b.tt(sy[:], sy[:], nw_bc[:], ALU.mult, reads=[syk, "nw_bc"], writes=[syk])
            b.act(gt[0][:], sy[:], AF.Silu, reads=[syk], writes=["gt0"])
            b.dma("sync", "gst0", gdram[ot * 128:(ot + 1) * 128, :], gt[0][:],
                  reads=["gt0"], writes=["gd%d" % ot])
            if dbg("g_out"):
                b.dma("gpsimd", "dbgG", dbg("g_out")[ot * 128:(ot + 1) * 128, :],
                      gt[0][:], reads=["gt0"], writes=[])

    b.fence()
    sComb.close()

    if dbg("Q_fm", (128, NF * TOK)):
        b.dma("gpsimd", "dbg5", dbg("Q_fm"), Q_fm[:], reads=["Q_fm"], writes=[])

    if PHASES < 6:
        return
    with ExitStack() as sB:
        bscr = sB.enter_context(nc.sbuf_tensor("bscr", [128, 8], F32))
        brow = sB.enter_context(nc.sbuf_tensor("brow", [1, 128], F32))
        bsc2 = sB.enter_context(nc.sbuf_tensor("bsc2", [1, 4], F32))
        pnb_t = psg(sB, "pnb", [128, 512])
        pnb = pnb_t[:, 0:1]
        b.red(bscr[:, 0:1], q2acc[:], aluop=ALU.max, reads=["q2acc"], writes=["bscr"])
        emit_global_max(b, nc, ctx, "q", bscr[:, 0:1], "bscr", bsc2[0:1, 0:1], "bsc2",
                        brow[0:1, :], "brow")
        b.red(bscr[:, 1:2], k2acc[:], aluop=ALU.max, reads=["k2acc"], writes=["bscr"])
        emit_global_max(b, nc, ctx, "k", bscr[:, 1:2], "bscr", bsc2[0:1, 1:2], "bsc2",
                        brow[0:1, :], "brow")
        b.tt(bsc2[0:1, 2:3], bsc2[0:1, 0:1], bsc2[0:1, 1:2], ALU.mult,
             reads=["bsc2"], writes=["bsc2"])
        b.act(bsc2[0:1, 2:3], bsc2[0:1, 2:3], AF.Ln, reads=["bsc2"], writes=["bsc2"])
        b.act(bsc2[0:1, 3:4], bsc2[0:1, 2:3], AF.Exp, scale=0.5, reads=["bsc2"], writes=["bsc2"])
        b.ts(bsc2[0:1, 3:4], bsc2[0:1, 3:4], -0.125, None, ALU.mult,
             reads=["bsc2"], writes=["bsc2"])
        b.mm(pnb, onesr[:], bsc2[0:1, 3:4], True, True,
             reads=["onesr", "bsc2"], writes=["pnb"])
        b.vcopy(negB[:], pnb, reads=["pnb"], writes=["negB"])
    if dbg("negB", (128, 1)):
        b.dma("sync", "dbg6", dbg("negB"), negB[:], reads=["negB"], writes=[])

    b.fence()
    if PHASES < 7:
        return
    with ExitStack() as sT:
        def tsb(name, shape, dt=F32):
            return sT.enter_context(nc.sbuf_tensor(name, list(shape), dt))

        attnfm = tsb("attnfm", [128, NF * 512], BF16)
        ptl = [tsb("ptl%d" % i, [128, 512], BF16) for i in range(3)]
        rdrow = tsb("rdrow", [1, 512])
        rbc = tsb("rbc", [64, 512])
        wos = [tsb("wos%d" % i, [128, NF * 512], BF16) for i in range(1)]
        bo_bc = tsb("bo_bc", [128, D_])
        sO = [tsb("sO%d" % i, [128, D_]) for i in range(4)]
        gld = [tsb("gld%d" % i, [128, D_], BF16) for i in range(1)]
        gf32 = [tsb("gf32%d" % i, [128, D_]) for i in range(1)]
        xo2 = [tsb("xo2%d" % i, [128, D_]) for i in range(1)]
        pS = [psg(sT, "pS%d" % i, [128, 512]) for i in range(2)]
        pAV = [psg(sT, "pAV%d" % i, [65, 512]) for i in range(2)]
        pbc = psg(sT, "pbc", [64, 512])
        pO = [psg(sT, "pO%d" % i, [128, 512]) for i in range(2)]
        b.dma("sync", "cT", bo_bc[:], I["bo"].broadcast_to([128, D_]),
              writes=["bo_bc"], bulk=True)

        for ci, qw in enumerate(CHW):
            q0 = ci * 512
            for si in range(32):
                h = SLOT2HEAD[si]
                kvh = h // 4
                assert si % 2 == kvh % 2
                av, avk = pAV[si % 2], "pAV%d" % (si % 2)
                qrhs = fmv(Q_fm, TOK)[:, si // 2, q0:q0 + qw][(si % 2) * 64:(si % 2) * 64 + 64, :]
                for kt in range(NTB):
                    sp, spk = pS[kt % 2], "pS%d" % (kt % 2)
                    klhs = fmv(K_fm, T_)[:, kvh // 2, kt * 128:(kt + 1) * 128][
                        (kvh % 2) * 64:(kvh % 2) * 64 + 64, :]
                    b.mm(sp[:, 0:qw], klhs, qrhs, True, True,
                         reads=["K_fm", "Q_fm"], writes=[spk])
                    pb_, pbk = ptl[kt % 3], "ptl%d" % (kt % 3)
                    b.act(pb_[:, 0:qw], sp[:, 0:qw], AF.Exp, bias=negB[:, 0:1],
                          scale=0.125, reads=[spk, "negB"], writes=[pbk])
                    b.mm(av[:, 0:qw], V_tm[:, kt * 520 + kvh * 65:kt * 520 + (kvh + 1) * 65],
                         pb_[:, 0:qw], kt == 0, kt == NTB - 1,
                         reads=["V_tm", pbk], writes=[avk])
                b.vrecip(rdrow[0:1, 0:qw], av[64:65, 0:qw], reads=[avk], writes=["rdrow"])
                b.mm(pbc[:, 0:qw], onesr[0:1, 0:64], rdrow[0:1, 0:qw], True, True,
                     reads=["onesr", "rdrow"], writes=["pbc"])
                b.vcopy(rbc[:, 0:qw], pbc[:, 0:qw], reads=["pbc"], writes=["rbc"])
                dst = fmv(attnfm, 512)[:, si // 2, 0:qw][(si % 2) * 64:(si % 2) * 64 + 64, :]
                b.tt(dst, av[0:64, 0:qw], rbc[:, 0:qw], ALU.mult,
                     reads=[avk, "rbc"], writes=["attnfm"])
            for ncc in range(4):
                wb, wk = wos[0], "wos0"
                b.dma("gpsimd", wk, d3(wb[:], 512), r3(I["wo_t"][:, ncc * 512:(ncc + 1) * 512]),
                      writes=[wk])
                for it in range(qw // 128):
                    pp, ppk = pO[it % 2], "pO%d" % (it % 2)
                    for fc in range(NF):
                        b.mm(pp[:], fmv(attnfm, 512)[:, fc, it * 128:(it + 1) * 128],
                             wb[:, fc * 512:(fc + 1) * 512], fc == 0, fc == NF - 1,
                             reads=["attnfm", wk], writes=[ppk])
                    b.tt(sO[it][:, ncc * 512:(ncc + 1) * 512], pp[:],
                         bo_bc[:, ncc * 512:(ncc + 1) * 512], ALU.add,
                         reads=[ppk, "bo_bc"], writes=["sO%d" % it])
            for it in range(qw // 128):
                ot = ci * 4 + it
                b.dma("sync", "gld0", gld[0][:], gdram[ot * 128:(ot + 1) * 128, :],
                      reads=["gd%d" % ot], writes=["gld0"])
                b.dma("sync", "xo20", xo2[0][:], I["xc"][ot * 128:(ot + 1) * 128, :],
                      writes=["xo20"])
                b.vcopy(gf32[0][:], gld[0][:], reads=["gld0"], writes=["gf320"])
                b.tt(sO[it][:], sO[it][:], gf32[0][:], ALU.mult,
                     reads=["sO%d" % it, "gf320"], writes=["sO%d" % it])
                b.tt(sO[it][:], sO[it][:], xo2[0][:], ALU.add,
                     reads=["sO%d" % it, "xo20"], writes=["sO%d" % it])
                b.dma("sync", "st%d" % it, x1[ot * 128:(ot + 1) * 128, :], sO[it][:],
                      reads=["sO%d" % it], writes=[])



def build_l1(debug=()):
    nc = bass.Bass("TRN2", target_bir_lowering=False, debug=False)
    I = {}

    def di(name, shape, dt=F32):
        I[name] = nc.dram_tensor(name, list(shape), dt, kind="ExternalInput").ap()

    di("x1c", (TOK, D_))
    di("qkv_t", (D_, 3072)); di("qkv_b", (1, 3072))
    di("out_t", (D_, D_)); di("out_b", (1, D_)); di("sinks", (1, 32))
    di("gw_t", (D_, 32)); di("gb", (1, 32))
    di("fc1_t", (D_, 1024)); di("fc2_t", (1024, D_))
    di("w1_t", (32, 1024, 1024)); di("w2_t", (32, 1024, 1024))
    di("su_t", (D_, 4096)); di("sd_t", (4096, D_))
    di("cos1", (TOK, 32)); di("sin1", (TOK, 32))
    di("masks", (2, 5, 128, 512), BF16)
    di("iden_bf", (128, 128), BF16); di("ones128", (1, 128))
    di("sel32", (32, 32 * 128))

    yc = nc.dram_tensor("yc", [OWN, D_], F32, kind="ExternalOutput").ap()
    DBG = {}

    def dbg(name, shape=None, dt=F32):
        if name in debug and shape is not None and name not in DBG:
            DBG[name] = nc.dram_tensor("dbg_" + name, list(shape), dt,
                                       kind="ExternalOutput").ap()
        return DBG.get(name)

    b = Prog(nc)
    with ExitStack() as ctx:
        emit_l1(b, I, yc, dbg, nc, ctx)
    return b.build(), sorted(DBG)


def emit_l1(b, I, yc, dbg, nc, ctx):
    def sb(name, shape, dt=F32):
        return ctx.enter_context(nc.sbuf_tensor(name, list(shape), dt))

    def psg(st, name, shape, dt=F32):
        return st.enter_context(nc.psum_tensor(name, list(shape), dt))

    C = {}
    K_fm = sb("K1_fm", [128, 4 * TOK], BF16)
    V_tm = sb("V1_tm", [128, NT * 520], BF16)
    maskb = sb("maskb", [128, 2 * 5 * 512], BF16)
    m_own = sb("m1_own", [128, NT]); r_own = sb("r1_own", [128, NT])
    q2acc = sb("q2acc", [128, 32]); k2acc = sb("k2acc", [128, 8])
    negB = sb("negB", [128, 1])
    es = sb("es", [1, 32], F32)
    sink_s = sb("sink_s", [1, 32], F32)
    C["idb"] = sb("idb", [128, 128], BF16)
    onesr = sb("onesr", [1, 128])
    g1 = sb("g1", [128, 8 * D_], BF16)
    cwT = sb("cwT", [32, OWN], BF16)

    b.dma("sync", "c0", C["idb"][:], I["iden_bf"], writes=["idb"], bulk=True)
    b.dma("sync", "c0", onesr[:], I["ones128"], writes=["onesr"], bulk=True)
    b.dma("sync", "c0", sink_s[:], I["sinks"], writes=["sink_s"], bulk=True)
    b.dma("sync", "c0", maskb[:].rearrange("p (q m c) -> p q m c", m=5, c=512),
          I["masks"].rearrange("q m p c -> p q m c"), writes=["maskb"], bulk=True)
    b.vmemset(q2acc[:], 0.0, writes=["q2acc"])
    b.vmemset(k2acc[:], 0.0, writes=["k2acc"])

    with ExitStack() as sAQ:
        xn_fm = sAQ.enter_context(nc.sbuf_tensor("xn1_fm", [128, NF * TOK], BF16))
        cosc = sAQ.enter_context(nc.sbuf_tensor("cosc", [128, NT * 32], F32))
        sinc = sAQ.enter_context(nc.sbuf_tensor("sinc", [128, NT * 32], F32))
        b.dma("sync", "c0", d3(cosc[:], 32), r3(I["cos1"]), writes=["cosc"], bulk=True)
        b.dma("sync", "c0", d3(sinc[:], 32), r3(I["sin1"]), writes=["sinc"], bulk=True)

        with ExitStack() as sA:
            xt = [sA.enter_context(nc.sbuf_tensor("xt%d" % i, [128, D_], F32))
                  for i in range(2)]
            C["lnscr"] = sA.enter_context(nc.sbuf_tensor("lnscr", [128, D_], F32))
            xnbf = sA.enter_context(nc.sbuf_tensor("xnbf0", [128, D_], BF16))
            stat = [sA.enter_context(nc.sbuf_tensor("stat%d" % i, [128, 4], F32))
                    for i in range(2)]
            C["pt0"] = psg(sA, "pt0", [128, 1024], BF16)
            C["pt1"] = psg(sA, "pt1", [128, 1024], BF16)
            for ot in range(NT):
                bi = ot % 2
                xtb, xk = xt[bi], "xt%d" % bi
                b.dma("sync", xk, xtb[:], I["x1c"][ot * 128:(ot + 1) * 128, :],
                      writes=[xk])
                emit_ln_tile(b, C, xtb[:], xk, stat[bi], "stat%d" % bi,
                             xnbf[:], "xnbf0", 1e-5,
                             mcol=m_own[:, ot:ot + 1], rcol=r_own[:, ot:ot + 1])
                xv = fmv(xn_fm, TOK)
                emit_transpose16(
                    b, C, xnbf[:], "xnbf0",
                    (lambda g, n, ot_=ot: xv[:, g * 8:g * 8 + n, ot_ * 128:(ot_ + 1) * 128]),
                    "xn_fm")
        b.fence()

        with ExitStack() as sKV:
            wkv = sKV.enter_context(nc.sbuf_tensor("wkv", [128, NF * 1024], BF16))
            kb_bc = sKV.enter_context(nc.sbuf_tensor("kb_bc", [128, 512], F32))
            vb_bc = sKV.enter_context(nc.sbuf_tensor("vb_bc", [128, 512], F32))
            kvscr = sKV.enter_context(nc.sbuf_tensor("kvscr0", [128, 512], F32))
            kvtmp = sKV.enter_context(nc.sbuf_tensor("kvtmp0", [128, 512], F32))
            krs = [sKV.enter_context(nc.sbuf_tensor("krs%d" % i, [128, 512], BF16))
                   for i in range(2)]
            pkv = [psg(sKV, "pkv%d" % i, [128, 512]) for i in range(2)]
            C["pt0"] = psg(sKV, "pt0k", [128, 1024], BF16)
            C["pt1"] = psg(sKV, "pt1k", [128, 1024], BF16)
            b.dma("gpsimd", "wkv", d3(wkv[:, 0:NF * 512], 512),
                  r3(I["qkv_t"][:, 2048:2560]), writes=["wkv_k"], bulk=True)
            b.dma("gpsimd", "wkv", d3(wkv[:, NF * 512:], 512),
                  r3(I["qkv_t"][:, 2560:3072]), writes=["wkv_v"], bulk=True)
            b.dma("sync", "cA", kb_bc[:], I["qkv_b"][:, 2048:2560].broadcast_to([128, 512]),
                  writes=["kb_bc"], bulk=True)
            b.dma("sync", "cA", vb_bc[:], I["qkv_b"][:, 2560:3072].broadcast_to([128, 512]),
                  writes=["vb_bc"], bulk=True)
            for ot in range(NT):
                bi = ot % 2
                pk, pkk = pkv[bi], "pkv%d" % bi
                for fc in range(NF):
                    b.mm(pk[:], fmv(xn_fm, TOK)[:, fc, ot * 128:(ot + 1) * 128],
                         wkv[:, fc * 512:(fc + 1) * 512], fc == 0, fc == NF - 1,
                         reads=["xn_fm", "wkv_k"], writes=[pkk])
                b.tt(kvscr[:], pk[:], kb_bc[:], ALU.add, reads=[pkk, "kb_bc"],
                     writes=["kvscr0"])
                emit_rope(b, C, kvscr[:], "kvscr0", krs[bi][:], "krs%d" % bi,
                          cosc[:, ot * 32:(ot + 1) * 32], sinc[:, ot * 32:(ot + 1) * 32],
                          8, kvtmp[:], "kvtmp0", ckeys=("cosc", "sinc"))
                pt, ptk = C["pt%d" % bi], "pt%d" % bi
                for k in range(4):
                    b.trans(pt[:, k * 128:(k + 1) * 128], krs[bi][:, k * 128:(k + 1) * 128],
                            C["idb"][:], reads=["krs%d" % bi, "idb"], writes=[ptk])
                b.vcopy(fmv(K_fm, TOK)[:, 0:4, ot * 128:(ot + 1) * 128],
                        pt[:, 0:512].rearrange("p (f c) -> p f c", c=128),
                        reads=[ptk], writes=["K_fm"])
                b.tt(kvtmp[:], krs[bi][:], krs[bi][:], ALU.mult,
                     reads=["krs%d" % bi], writes=["kvtmp0"])
                b.red(kvscr[:, 0:8], kvtmp[:].rearrange("p (h d) -> p h d", d=64),
                      aluop=ALU.max, reads=["kvtmp0"], writes=["kvscr0"])
                b.tt(k2acc[:], k2acc[:], kvscr[:, 0:8], ALU.max,
                     reads=["kvscr0", "k2acc"], writes=["k2acc"])
                for fc in range(NF):
                    b.mm(pk[:], fmv(xn_fm, TOK)[:, fc, ot * 128:(ot + 1) * 128],
                         wkv[:, NF * 512 + fc * 512:NF * 512 + (fc + 1) * 512],
                         fc == 0, fc == NF - 1,
                         reads=["xn_fm", "wkv_v"], writes=[pkk])
                vt = V_tm[:, ot * 520:(ot + 1) * 520].rearrange("p (h c) -> p h c", c=65)
                b.tt(vt[:, :, 0:64], pk[:].rearrange("p (h c) -> p h c", c=64),
                     vb_bc[:].rearrange("p (h c) -> p h c", c=64), ALU.add,
                     reads=[pkk, "vb_bc"], writes=["V_tm"])
                b.vmemset(vt[:, :, 64:65], 1.0, writes=["V_tm"])
        b.fence()

        with ExitStack() as sR:
            gw_b = sR.enter_context(nc.sbuf_tensor("gw_b", [128, NF * 32], BF16))
            gw_f = sR.enter_context(nc.sbuf_tensor("gw_f", [128, NF * 32], F32))
            gb_bc = sR.enter_context(nc.sbuf_tensor("gb_bc", [128, 32], F32))
            rscr = sR.enter_context(nc.sbuf_tensor("rscr", [128, 192], F32))
            rgb = sR.enter_context(nc.sbuf_tensor("rgb", [128, 32], BF16))
            rpsum_t = psg(sR, "rpsum", [128, 512])
            rpsum = rpsum_t[:, 0:32]
            rtp_t = psg(sR, "rtp", [128, 1024], BF16)
            rtp = rtp_t[0:32, 0:128]
            b.dma("sync", "cRg", d3(gw_f[:], 32), r3(I["gw_t"]), writes=["gw_f"], bulk=True)
            b.vcopy(gw_b[:], gw_f[:], reads=["gw_f"], writes=["gw_b"])
            b.dma("sync", "cR", gb_bc[:], I["gb"].broadcast_to([128, 32]),
                  writes=["gb_bc"], bulk=True)
            for ot in range(8):
                lt = ot + 1
                for fc in range(NF):
                    b.mm(rpsum, fmv(xn_fm, TOK)[:, fc, lt * 128:(lt + 1) * 128],
                         gw_b[:, fc * 32:(fc + 1) * 32], fc == 0, fc == NF - 1,
                         reads=["xn_fm", "gw_b"], writes=["rpsum"])
                lk = "rscr"
                lg = rscr[:, 0:32]
                sc_ = rscr[:, 32:64]
                sbv = rscr[:, 64:96]
                tq = rscr[:, 96:128]
                gs = rscr[:, 128:132]
                st1 = rscr[:, 132:133]; st2 = rscr[:, 133:134]
                thg = rscr[:, 134:135]; th8 = rscr[:, 135:136]
                ssum = rscr[:, 136:137]; rs = rscr[:, 137:138]
                msk = rscr[:, 128 + 16:128 + 16]
                b.vcopy(lg, rpsum, reads=["rpsum"], writes=[lk])
                b.act(sc_, lg, AF.Exp, scale=-1.0, reads=[lk], writes=[lk])
                b.ts(sc_, sc_, 1.0, None, ALU.add, reads=[lk], writes=[lk])
                b.vrecip(sc_, sc_, reads=[lk], writes=[lk])
                b.tt(sbv, sc_, gb_bc[:], ALU.add, reads=[lk, "gb_bc"], writes=[lk])
                for gi in range(4):
                    sl = sbv[:, gi * 8:(gi + 1) * 8]
                    b.red(st1, sl, aluop=ALU.max, reads=[lk], writes=[lk])
                    b.ts(tq[:, 0:8], sl, st1, -1e30, ALU.is_equal, ALU.mult,
                         reads=[lk], writes=[lk])
                    b.tt(tq[:, 0:8], tq[:, 0:8], sl, ALU.add, reads=[lk], writes=[lk])
                    b.red(st2, tq[:, 0:8], aluop=ALU.max, reads=[lk], writes=[lk])
                    b.tt(gs[:, gi:gi + 1], st1, st2, ALU.add, reads=[lk], writes=[lk])
                b.red(st1, gs, aluop=ALU.max, reads=[lk], writes=[lk])
                b.ts(tq[:, 0:4], gs, st1, -1e30, ALU.is_equal, ALU.mult,
                     reads=[lk], writes=[lk])
                b.tt(tq[:, 0:4], tq[:, 0:4], gs, ALU.add, reads=[lk], writes=[lk])
                b.red(thg, tq[:, 0:4], aluop=ALU.max, reads=[lk], writes=[lk])
                b.ts(gs, gs, thg, None, ALU.is_ge, reads=[lk], writes=[lk])
                gm_exp = gs.rearrange("p (g o) -> p g o", o=1).broadcast_to([128, 4, 8])
                sb3 = sbv.rearrange("p (g e) -> p g e", e=8)
                tq3 = tq.rearrange("p (g e) -> p g e", e=8)
                b.tt(tq3, sb3, gm_exp, ALU.mult, reads=[lk], writes=[lk])
                b.ts(lg, gs.rearrange("p (g o) -> p g o", o=1).broadcast_to([128, 4, 8]),
                     1.0, -1e30, ALU.subtract, ALU.mult, reads=[lk], writes=[lk])
                b.tt(tq, tq, lg, ALU.subtract, reads=[lk], writes=[lk])
                mcp = rscr[:, 96 + 32:96 + 32] if False else None
                b.vcopy(lg, tq, reads=[lk], writes=[lk])
                for rnd in range(8):
                    b.red(th8, tq, aluop=ALU.max, reads=[lk], writes=[lk])
                    if rnd < 7:
                        b.ts(rscr[:, 160:192], tq, th8, -1e30,
                             ALU.is_equal, ALU.mult, reads=[lk], writes=[lk])
                        b.tt(tq, tq, rscr[:, 160:192], ALU.add,
                             reads=[lk], writes=[lk])
                b.ts(lg, lg, th8, None, ALU.is_ge, reads=[lk], writes=[lk])
                b.tt(lg, lg, sc_, ALU.mult, reads=[lk], writes=[lk])
                b.red(ssum, lg, reads=[lk], writes=[lk])
                b.vrecip(rs, ssum, reads=[lk], writes=[lk])
                b.ts(lg, lg, rs, 2.5, ALU.mult, ALU.mult, reads=[lk], writes=[lk])
                b.vcopy(rgb[:], lg, reads=[lk], writes=["rgb"])
                b.trans(rtp, rgb[:], C["idb"][:], reads=["rgb", "idb"], writes=["rtp"])
                b.vcopy(cwT[:, ot * 128:(ot + 1) * 128], rtp, reads=["rtp"], writes=["cwT"])
        b.fence()

        TCH = 256
        NCH = OWN // TCH
        with ExitStack() as sF:
            xlat = sF.enter_context(nc.sbuf_tensor("xlat", [128, 8 * OWN], BF16))
            wstr = [sF.enter_context(nc.sbuf_tensor("wstr%d" % i, [128, 16 * 128], BF16))
                    for i in range(2)]
            wfc1 = [sF.enter_context(nc.sbuf_tensor("wfc1%d" % i, [128, NF * 128], BF16))
                    for i in range(2)]
            hw = sF.enter_context(nc.sbuf_tensor("hw", [128, 8 * TCH], BF16))
            rt = sF.enter_context(nc.sbuf_tensor("rt", [128, TCH], F32))
            rt2 = sF.enter_context(nc.sbuf_tensor("rt2", [128, TCH], F32))
            cwb = sF.enter_context(nc.sbuf_tensor("cwb", [128, TCH], F32))
            sel32b = sF.enter_context(nc.sbuf_tensor("sel32b", [32, 32 * 128], BF16))
            ylat_bf = sF.enter_context(nc.sbuf_tensor("ylat_bf", [128, 8 * TCH], BF16))
            upfm = sF.enter_context(nc.sbuf_tensor("upfm", [128, 32 * TCH], BF16))
            wdn = [sF.enter_context(nc.sbuf_tensor("wdn%d" % i, [128, 32 * 256], BF16))
                   for i in range(2)]
            wf2 = [sF.enter_context(nc.sbuf_tensor("wf20", [128, 8 * 256], BF16))]
            pylat = psg(sF, "pylat", [128, 2048])
            ppre = [psg(sF, "ppre%d" % i, [128, 512]) for i in range(2)]
            pcw = psg(sF, "pcw", [128, 512])
            pdn = ppre
            b.dma("gpsimd", "cF", sel32b[:], I["sel32"], writes=["sel32b"], bulk=True)

            for lt in range(8):
                wi = lt % 2
                wb, wk = wfc1[wi], "wfc1%d" % wi
                b.dma("gpsimd", wk, d3(wb[:], 128),
                      r3(I["fc1_t"][:, lt * 128:(lt + 1) * 128]), writes=[wk])
                for tc in range(2):
                    pp, ppk = ppre[tc % 2], "ppre%d" % (tc % 2)
                    for fc in range(NF):
                        b.mm(pp[:], wb[:, fc * 128:(fc + 1) * 128],
                             fmv(xn_fm, TOK)[:, fc, 128 + tc * 512:128 + (tc + 1) * 512],
                             fc == 0, fc == NF - 1, reads=[wk, "xn_fm"], writes=[ppk])
                    b.vcopy(fmv(xlat, OWN)[:, lt, tc * 512:(tc + 1) * 512][0:128, :],
                            pp[:], reads=[ppk], writes=["xlat"])

            if dbg("xlatd", (128, 8 * OWN), BF16):
                b.dma("sync", "dbgXL", dbg("xlatd"), xlat[:],
                      reads=["xlat"], writes=[])
            for ci in range(NCH):
                t0 = ci * TCH
                for e in range(32):
                    b.mm(pcw[:, 0:TCH], sel32b[:, e * 128:(e + 1) * 128],
                         cwT[:, t0:t0 + TCH], True, True,
                         reads=["sel32b", "cwT"], writes=["pcw"])
                    b.vcopy(cwb[:], pcw[:, 0:TCH], reads=["pcw"], writes=["cwb"])
                    for it_ in range(8):
                        wi = it_ % 2
                        wb, wk = wstr[wi], "wstr%d" % wi
                        b.dma("gpsimd", wk, d3(wb[:, 0:8 * 128], 128),
                              r3(I["w1_t"][e, :, it_ * 128:(it_ + 1) * 128]), writes=[wk])
                        pp, ppk = ppre[it_ % 2], "ppre%d" % (it_ % 2)
                        for lc in range(8):
                            b.mm(pp[:, 0:TCH], wb[:, lc * 128:(lc + 1) * 128],
                                 fmv(xlat, OWN)[:, lc, t0:t0 + TCH],
                                 lc == 0, lc == 7, reads=[wk, "xlat"], writes=[ppk])
                        b.op("scalar",
                             (lambda s, o=rt[:, 0:TCH], i=pp[:, 0:TCH]:
                              s.activation(o, i, AF.Relu)),
                             reads=[ppk], writes=["rt"])
                        b.tt(rt2[:, 0:TCH], rt[:, 0:TCH], rt[:, 0:TCH], ALU.mult,
                             reads=["rt"], writes=["rt2"])
                        b.tt(hw[:, it_ * TCH:(it_ + 1) * TCH], rt2[:, 0:TCH],
                             cwb[:], ALU.mult, reads=["rt2", "cwb"], writes=["hw"])
                    for lt in range(8):
                        wi = lt % 2
                        wb, wk = wstr[wi], "wstr%d" % wi
                        b.dma("gpsimd", wk, d3(wb[:, 0:8 * 128], 128),
                              r3(I["w2_t"][e, :, lt * 128:(lt + 1) * 128]), writes=[wk])
                        for icc in range(8):
                            b.mm(pylat[:, lt * TCH:(lt + 1) * TCH],
                                 wb[:, icc * 128:(icc + 1) * 128],
                                 hw[:, icc * TCH:(icc + 1) * TCH],
                                 (e == 0 and icc == 0 and lt % 2 == 0),
                                 (e == 31 and icc == 7),
                                 reads=[wk, "hw"], writes=["pylat"])
                for lt in range(8):
                    b.vcopy(ylat_bf[:, lt * TCH:(lt + 1) * TCH],
                            pylat[:, lt * TCH:(lt + 1) * TCH],
                            reads=["pylat"], writes=["ylat_bf"])
                if ci == 0 and dbg("ylatd", (128, 8 * 256), BF16):
                    b.dma("sync", "dbgYL", dbg("ylatd"), ylat_bf[:],
                          reads=["ylat_bf"], writes=[])
                for st_ in range(32):
                    wi = st_ % 2
                    wb, wk = wstr[wi], "wstr%d" % wi
                    b.dma("gpsimd", wk, d3(wb[:], 128),
                          r3(I["su_t"][:, st_ * 128:(st_ + 1) * 128]), writes=[wk])
                    pp, ppk = ppre[st_ % 2], "ppre%d" % (st_ % 2)
                    for fc in range(NF):
                        b.mm(pp[:, 0:TCH], wb[:, fc * 128:(fc + 1) * 128],
                             fmv(xn_fm, TOK)[:, fc, 128 + t0:128 + t0 + TCH],
                             fc == 0, fc == NF - 1, reads=[wk, "xn_fm"], writes=[ppk])
                    b.op("scalar",
                         (lambda s, o=rt[:, 0:TCH], i=pp[:, 0:TCH]:
                          s.activation(o, i, AF.Relu)),
                         reads=[ppk], writes=["rt"])
                    b.tt(upfm[:, st_ * TCH:(st_ + 1) * TCH], rt[:, 0:TCH],
                         rt[:, 0:TCH], ALU.mult, reads=["rt"], writes=["upfm"])
                if ci == 0 and dbg("upfmd", (128, 32 * 256), BF16):
                    b.dma("sync", "dbgUP", dbg("upfmd"), upfm[:],
                          reads=["upfm"], writes=[])
                for nc8 in range(8):
                    wi = nc8 % 2
                    wdb, wdk = wdn[wi], "wdn%d" % wi
                    wfb, wfk = wf2[0], "wf20"
                    b.dma("gpsimd", wdk, d3(wdb[:], 256),
                          r3(I["sd_t"][:, nc8 * 256:(nc8 + 1) * 256]),
                          writes=[wdk])
                    b.dma("gpsimd", wfk, d3(wfb[:], 256),
                          r3(I["fc2_t"][:, nc8 * 256:(nc8 + 1) * 256]),
                          writes=[wfk])
                    for it_ in range(TCH // 128):
                        ot = (t0 + it_ * 128) // 128
                        pp, ppk = pdn[it_ % 2], "ppre%d" % (it_ % 2)
                        for sc_ in range(32):
                            b.mm(pp[:, 0:256], upfm[:, sc_ * TCH + it_ * 128:
                                                    sc_ * TCH + (it_ + 1) * 128],
                                 wdb[:, sc_ * 256:(sc_ + 1) * 256],
                                 sc_ == 0, False, reads=["upfm", wdk], writes=[ppk])
                        for lc in range(8):
                            b.mm(pp[:, 0:256], ylat_bf[:, lc * TCH + it_ * 128:
                                                       lc * TCH + (it_ + 1) * 128],
                                 wfb[:, lc * 256:(lc + 1) * 256],
                                 False, lc == 7, reads=["ylat_bf", wfk], writes=[ppk])
                        b.op("scalar",
                             (lambda s, o=g1[:, ot * D_ + nc8 * 256:
                                             ot * D_ + (nc8 + 1) * 256],
                              i=pp[:, 0:256]: s.activation(o, i, AF.Silu)),
                             reads=[ppk], writes=["g1"])
        if dbg("g1d", (128, 8 * D_)):
            b.dma("gpsimd", "dbgG1", dbg("g1d"), g1[:], reads=["g1"], writes=[])
        if dbg("cwTd", (32, OWN)):
            b.dma("gpsimd", "dbgCW", dbg("cwTd"), cwT[:], reads=["cwT"], writes=[])
        b.fence()

        Q_fm = ctx.enter_context(nc.sbuf_tensor("Q1_fm", [128, NF * OWN], BF16,
                                                side="right"))
        with ExitStack() as sQ:
            wqs = [sQ.enter_context(nc.sbuf_tensor("wqs%d" % i, [128, NF * 512], BF16))
                   for i in range(2)]
            qscr = [sQ.enter_context(nc.sbuf_tensor("qscr%d" % i, [128, 512], F32))
                    for i in range(2)]
            qtmp = [sQ.enter_context(nc.sbuf_tensor("qtmp%d" % i, [128, 512], F32))
                    for i in range(2)]
            qrb = [sQ.enter_context(nc.sbuf_tensor("qrb%d" % i, [128, 512], BF16))
                   for i in range(2)]
            qb_bc = sQ.enter_context(nc.sbuf_tensor("qb_bc", [128, D_], F32))
            ppq = [psg(sQ, "ppq%d" % i, [128, 512]) for i in range(2)]
            C["pt0"] = psg(sQ, "pt0b", [128, 1024], BF16)
            C["pt1"] = psg(sQ, "pt1b", [128, 1024], BF16)
            b.dma("sync", "cQ", qb_bc[:], I["qkv_b"][:, 0:2048].broadcast_to([128, D_]),
                  writes=["qb_bc"], bulk=True)
            for ncc in range(4):
                wi = ncc % 2
                wb, wk = wqs[wi], "wqs%d" % wi
                b.dma("gpsimd", wk, d3(wb[:], 512),
                      r3(I["qkv_t"][:, ncc * 512:(ncc + 1) * 512]), writes=[wk])
                for ot in range(8):
                    lt = ot + 1
                    bi = ot % 2
                    pp, ppk = ppq[bi], "ppq%d" % bi
                    for fc in range(NF):
                        b.mm(pp[:], fmv(xn_fm, TOK)[:, fc, lt * 128:(lt + 1) * 128],
                             wb[:, fc * 512:(fc + 1) * 512], fc == 0, fc == NF - 1,
                             reads=["xn_fm", wk], writes=[ppk])
                    b.tt(qscr[bi][:], pp[:], qb_bc[:, ncc * 512:(ncc + 1) * 512], ALU.add,
                         reads=[ppk, "qb_bc"], writes=["qscr%d" % bi])
                    emit_rope(b, C, qscr[bi][:], "qscr%d" % bi, qrb[bi][:], "qrb%d" % bi,
                              cosc[:, lt * 32:(lt + 1) * 32], sinc[:, lt * 32:(lt + 1) * 32],
                              8, qtmp[bi][:], "qtmp%d" % bi, ckeys=("cosc", "sinc"))
                    b.tt(qtmp[bi][:], qrb[bi][:], qrb[bi][:], ALU.mult,
                         reads=["qrb%d" % bi], writes=["qtmp%d" % bi])
                    b.red(qscr[bi][:, 0:8], qtmp[bi][:].rearrange("p (h d) -> p h d", d=64),
                          aluop=ALU.max, reads=["qtmp%d" % bi], writes=["qscr%d" % bi])
                    b.tt(q2acc[:, ncc * 8:(ncc + 1) * 8], q2acc[:, ncc * 8:(ncc + 1) * 8],
                         qscr[bi][:, 0:8], ALU.max,
                         reads=["qscr%d" % bi, "q2acc"], writes=["q2acc"])
                    pt, ptk = C["pt%d" % bi], "pt%d" % bi
                    for k in range(4):
                        b.trans(pt[:, k * 128:(k + 1) * 128], qrb[bi][:, k * 128:(k + 1) * 128],
                                C["idb"][:], reads=["qrb%d" % bi, "idb"], writes=[ptk])
                    b.vcopy(fmv(Q_fm, OWN)[:, ncc * 4:(ncc + 1) * 4, ot * 128:(ot + 1) * 128],
                            pt[:, 0:512].rearrange("p (f c) -> p f c", c=128),
                            reads=[ptk], writes=["Q_fm"])
        b.fence()

    with ExitStack() as sB:
        bscr = sB.enter_context(nc.sbuf_tensor("bscr", [128, 8], F32))
        brow = sB.enter_context(nc.sbuf_tensor("brow", [1, 128], F32))
        bsc2 = sB.enter_context(nc.sbuf_tensor("bsc2", [1, 4], F32))
        pnb_t = psg(sB, "pnb", [128, 512])
        pnb = pnb_t[:, 0:1]
        b.red(bscr[:, 0:1], q2acc[:], aluop=ALU.max, reads=["q2acc"], writes=["bscr"])
        emit_global_max(b, nc, ctx, "q1", bscr[:, 0:1], "bscr", bsc2[0:1, 0:1], "bsc2",
                        brow[0:1, :], "brow")
        b.red(bscr[:, 1:2], k2acc[:], aluop=ALU.max, reads=["k2acc"], writes=["bscr"])
        emit_global_max(b, nc, ctx, "k1", bscr[:, 1:2], "bscr", bsc2[0:1, 1:2], "bsc2",
                        brow[0:1, :], "brow")
        b.tt(bsc2[0:1, 2:3], bsc2[0:1, 0:1], bsc2[0:1, 1:2], ALU.mult,
             reads=["bsc2"], writes=["bsc2"])
        b.act(bsc2[0:1, 2:3], bsc2[0:1, 2:3], AF.Ln, reads=["bsc2"], writes=["bsc2"])
        b.act(bsc2[0:1, 3:4], bsc2[0:1, 2:3], AF.Exp, scale=0.5, reads=["bsc2"], writes=["bsc2"])
        b.ts(bsc2[0:1, 3:4], bsc2[0:1, 3:4], -0.125, None, ALU.mult,
             reads=["bsc2"], writes=["bsc2"])
        b.mm(pnb, onesr[:], bsc2[0:1, 3:4], True, True,
             reads=["onesr", "bsc2"], writes=["pnb"])
        b.vcopy(negB[:], pnb, reads=["pnb"], writes=["negB"])
        b.act(es[:], sink_s[:], AF.Exp, bias=negB[0:1, 0:1],
              reads=["sink_s", "negB"], writes=["es"])
    b.fence()

    with ExitStack() as sT:
        def tsb(name, shape, dt=F32):
            return sT.enter_context(nc.sbuf_tensor(name, list(shape), dt))

        attnfm = tsb("attnfm", [128, NF * 512], BF16)
        ptl = [tsb("ptl%d" % i, [128, 512], BF16) for i in range(3)]
        rdrow = tsb("rdrow", [1, 512])
        rbc = tsb("rbc", [64, 512])
        wos = [tsb("wos%d" % i, [128, NF * 512], BF16) for i in range(1)]
        ob_bc = tsb("ob_bc", [128, D_])
        sO = [tsb("sO%d" % i, [128, D_]) for i in range(4)]
        gf32 = tsb("gf32", [128, D_])
        xo2 = tsb("xo2", [128, D_])
        pS = [psg(sT, "pS%d" % i, [128, 512]) for i in range(2)]
        pAV = [psg(sT, "pAV%d" % i, [65, 512]) for i in range(2)]
        pbc = psg(sT, "pbc", [64, 512])
        pO = [psg(sT, "pO%d" % i, [128, 512]) for i in range(2)]
        b.dma("sync", "cT", ob_bc[:], I["out_b"].broadcast_to([128, D_]),
              writes=["ob_bc"], bulk=True)
        mk3 = maskb[:].rearrange("p (q m c) -> p q m c", m=5, c=512)

        for qc in range(2):
            for si in range(32):
                h = SLOT2HEAD[si]
                kvh = h // 4
                av, avk = pAV[si % 2], "pAV%d" % (si % 2)
                qrhs = fmv(Q_fm, OWN)[:, si // 2, qc * 512:(qc + 1) * 512][
                    (si % 2) * 64:(si % 2) * 64 + 64, :]
                for m in range(5):
                    kt = qc * 4 + m
                    sp, spk = pS[m % 2], "pS%d" % (m % 2)
                    klhs = fmv(K_fm, TOK)[:, kvh // 2, kt * 128:(kt + 1) * 128][
                        (kvh % 2) * 64:(kvh % 2) * 64 + 64, :]
                    b.mm(sp[:], klhs, qrhs, True, True,
                         reads=["K_fm", "Q_fm"], writes=[spk])
                    pb_, pbk = ptl[m % 3], "ptl%d" % (m % 3)
                    b.act(pb_[:], sp[:], AF.Exp, bias=negB[:, 0:1], scale=0.125,
                          reads=[spk, "negB"], writes=[pbk])
                    b.tt(pb_[:], pb_[:], mk3[:, qc, m, :], ALU.mult,
                         reads=[pbk, "maskb"], writes=[pbk])
                    b.mm(av[:], V_tm[:, kt * 520 + kvh * 65:kt * 520 + (kvh + 1) * 65],
                         pb_[:], m == 0, m == 4, reads=["V_tm", pbk], writes=[avk])
                b.ts(rdrow[0:1, :], av[64:65, :], es[0:1, h:h + 1], None, ALU.add,
                     reads=[avk, "es"], writes=["rdrow"])
                b.vrecip(rdrow[0:1, :], rdrow[0:1, :], reads=["rdrow"], writes=["rdrow"])
                b.mm(pbc[:], onesr[0:1, 0:64], rdrow[0:1, :], True, True,
                     reads=["onesr", "rdrow"], writes=["pbc"])
                b.vcopy(rbc[:], pbc[:], reads=["pbc"], writes=["rbc"])
                dst = fmv(attnfm, 512)[:, si // 2, :][(si % 2) * 64:(si % 2) * 64 + 64, :]
                b.tt(dst, av[0:64, :], rbc[:], ALU.mult, reads=[avk, "rbc"],
                     writes=["attnfm"])
            for ncc in range(4):
                wb, wk = wos[0], "wos0"
                b.dma("gpsimd", wk, d3(wb[:], 512),
                      r3(I["out_t"][:, ncc * 512:(ncc + 1) * 512]), writes=[wk])
                for it_ in range(4):
                    pp, ppk = pO[it_ % 2], "pO%d" % (it_ % 2)
                    for fc in range(NF):
                        b.mm(pp[:], fmv(attnfm, 512)[:, fc, it_ * 128:(it_ + 1) * 128],
                             wb[:, fc * 512:(fc + 1) * 512], fc == 0, fc == NF - 1,
                             reads=["attnfm", wk], writes=[ppk])
                    b.tt(sO[it_][:, ncc * 512:(ncc + 1) * 512], pp[:],
                         ob_bc[:, ncc * 512:(ncc + 1) * 512], ALU.add,
                         reads=[ppk, "ob_bc"], writes=["sO%d" % it_])
            for it_ in range(4):
                ot = qc * 4 + it_
                b.dma("sync", "xo2", xo2[:], I["x1c"][128 + ot * 128:128 + (ot + 1) * 128, :],
                      writes=["xo2"])
                b.vcopy(gf32[:], g1[:, ot * D_:(ot + 1) * D_], reads=["g1"], writes=["gf32"])
                b.tt(sO[it_][:], sO[it_][:], gf32[:], ALU.mult,
                     reads=["sO%d" % it_, "gf32"], writes=["sO%d" % it_])
                b.tt(sO[it_][:], sO[it_][:], xo2[:], ALU.add,
                     reads=["sO%d" % it_, "xo2"], writes=["sO%d" % it_])
                b.dma("sync", "st%d" % it_, yc[ot * 128:(ot + 1) * 128, :], sO[it_][:],
                      reads=["sO%d" % it_], writes=[])



def kernel(**inputs):
    maps0, maps1 = prep_inputs(inputs)
    nc0, _ = build_l0()
    res0 = run_bass_kernel_spmd(nc0, maps0, core_ids=list(range(N_CORES)))
    for c in range(N_CORES):
        maps1[c]["x1c"] = np.ascontiguousarray(
            np.asarray(res0.results[c]["x1"], np.float32))
    nc1, _ = build_l1()
    res1 = run_bass_kernel_spmd(nc1, maps1, core_ids=list(range(N_CORES)))
    out = np.zeros((B_, T_, D_), np.float32)
    for c in range(N_CORES):
        bb, Q0 = c // 4, (c % 4) * OWN
        out[bb, Q0:Q0 + OWN] = np.asarray(res1.results[c]["yc"], np.float32)
    return out


# version 02
# speedup vs baseline: 1.0029x; 1.0029x over previous
import math
from contextlib import ExitStack

import numpy as np
import ml_dtypes

import concourse.bass as bass
import concourse.mybir as mybir
from concourse.bass_utils import run_bass_kernel_spmd

F32 = mybir.dt.float32
BF16 = mybir.dt.bfloat16
AF = mybir.ActivationFunctionType
ALU = mybir.AluOpType
AX = mybir.AxisListType
BF = ml_dtypes.bfloat16

N_CORES = 8
PHASES = 99
B_, T_, D_ = 2, 4096, 2048
OWN, HALO = 1024, 128
TOK = OWN + HALO
NT = TOK // 128
NTB = T_ // 128
NF = D_ // 128
CHW = [512, 512, 128]

_EV = [h for h in range(32) if (h // 4) % 2 == 0]
_OD = [h for h in range(32) if (h // 4) % 2 == 1]
SLOT2HEAD = []
for _t in range(16):
    SLOT2HEAD.append(_EV[_t])
    SLOT2HEAD.append(_OD[_t])


COMPUTE_ENGS = ("tensor", "vector", "scalar")
DMA_QUEUES = ("sync", "gpsimd")


class Prog:
    def __init__(self, nc):
        self.nc = nc
        self.ops = []
        self.last_w = {}
        self.readers = {}
        self.slot_fills = {}
        self.bulk_slots = set()
        self._last_per_key = {}
        self._fence_need = None
        self._fence_engines = set()

    def op(self, engine, fn, reads=(), writes=()):
        assert engine in COMPUTE_ENGS, engine
        return self._add(engine, fn, reads, writes, None)

    def fence(self):
        self._fence_need = set(self._last_per_key.values())
        self._fence_engines = set()

    def dma(self, queue, slot, out, in_, reads=(), writes=(), bulk=False):
        assert queue in DMA_QUEUES, queue
        if bulk:
            self.bulk_slots.add(slot)
        self.slot_fills[slot] = self.slot_fills.get(slot, 0) + 1
        fn = lambda e, o=out, i=in_: e.dma_start(o, i)
        return self._add(queue, fn, reads, writes, (slot, self.slot_fills[slot]))

    def mm(self, out, lhsT, rhs, start, stop, reads=(), writes=()):
        fn = lambda t, o=out, l=lhsT, r=rhs, a=start, c=stop: t.matmul(
            o, l, r, start=a, stop=c, skip_group_check=True)
        return self.op("tensor", fn, reads, writes)

    def trans(self, out, in_, iden, reads=(), writes=()):
        fn = lambda t, o=out, i=in_, d=iden: t.transpose(o, i, d)
        return self.op("tensor", fn, reads, writes)

    def tt(self, out, a, c, aluop, reads=(), writes=()):
        fn = lambda v, o=out, x=a, y=c, p=aluop: v.tensor_tensor(o, x, y, p)
        return self.op("vector", fn, reads, writes)

    def ts(self, out, in0, s1, s2, op0, op1=None, reads=(), writes=()):
        if s2 is None and op1 is None:
            fn = lambda v, o=out, i=in0, a=s1, p0=op0: v.tensor_single_scalar(o, i, a, p0)
        else:
            fn = lambda v, o=out, i=in0, a=s1, c=s2, p0=op0, p1=op1: v.tensor_scalar(
                o, i, a, c, p0, p1)
        return self.op("vector", fn, reads, writes)

    def red(self, out, in_, aluop=ALU.add, negate=False, reads=(), writes=()):
        fn = lambda v, o=out, i=in_, p=aluop, n=negate: v.tensor_reduce(
            o, i, axis=AX.X, op=p, negate=n)
        return self.op("vector", fn, reads, writes)

    def vcopy(self, out, in_, reads=(), writes=()):
        fn = lambda v, o=out, i=in_: v.tensor_copy(o, i)
        return self.op("vector", fn, reads, writes)

    def vmemset(self, ap, val, reads=(), writes=()):
        fn = lambda v, a=ap, x=val: v.memset(a, x)
        return self.op("vector", fn, reads, writes)

    def vrecip(self, out, in_, reads=(), writes=()):
        fn = lambda v, o=out, i=in_: v.reciprocal(o, i)
        return self.op("vector", fn, reads, writes)

    def act(self, out, in_, func, bias=0.0, scale=1.0, reads=(), writes=()):
        fn = lambda s, o=out, i=in_, f=func, bb=bias, c=scale: s.activation(
            o, i, f, bias=bb, scale=c)
        return self.op("scalar", fn, reads, writes)

    def _verify(self, ops):
        streams = {}
        for i, o in enumerate(ops):
            streams.setdefault(o["engine"], []).append(i)
        heads = {e: 0 for e in streams}
        sem = {}
        done = 0
        while done < len(ops):
            progressed = False
            for e, lst in streams.items():
                while heads[e] < len(lst):
                    o = ops[lst[heads[e]]]
                    if any(sem.get(k, 0) < v for k, v in o["waits"]):
                        break
                    if o["inc"]:
                        if o["slot"] is not None:
                            k = "slot:" + o["slot"][0]
                            sem[k] = sem.get(k, 0) + 16
                        else:
                            k = "eng:" + e
                            sem[k] = sem.get(k, 0) + 1
                    heads[e] += 1
                    done += 1
                    progressed = True
            if not progressed:
                msg = []
                for e, lst in streams.items():
                    if heads[e] < len(lst):
                        i = lst[heads[e]]
                        o = ops[i]
                        blocked = [(k, v, sem.get(k, 0)) for k, v in o["waits"]
                                   if sem.get(k, 0) < v]
                        msg.append(f"{e} stuck at op {i} waits {blocked}")
                raise RuntimeError("DEADLOCK:\n" + "\n".join(msg))

    def _add(self, engine, fn, reads, writes, dma_slot):
        i = len(self.ops)
        deps = set()
        if self._fence_need is not None and engine not in self._fence_engines:
            deps |= self._fence_need
            self._fence_engines.add(engine)
        for r in reads:
            j = self.last_w.get(r)
            if j is not None:
                deps.add(j)
        for w in writes:
            j = self.last_w.get(w)
            if j is not None:
                deps.add(j)
            deps.update(self.readers.get(w, ()))
        deps.discard(i)
        self.ops.append(dict(engine=engine, fn=fn, deps=deps, slot=dma_slot,
                             inc=False))
        for r in reads:
            self.readers.setdefault(r, []).append(i)
        for w in writes:
            self.last_w[w] = i
            self.readers[w] = []
        self._last_per_key[dma_slot[0] if dma_slot else engine] = i
        return i

    def build(self):
        nc = self.nc
        ops = self.ops
        for o in ops:
            pruned = {}
            for j in o["deps"]:
                oj = ops[j]
                if (oj["slot"] is None and o["slot"] is None
                        and oj["engine"] == "tensor" and o["engine"] == "tensor"):
                    continue
                key = oj["slot"][0] if oj["slot"] else oj["engine"]
                if key not in pruned or j > pruned[key]:
                    pruned[key] = j
            o["rdeps"] = pruned
        for o in ops:
            for j in o["rdeps"].values():
                ops[j]["inc"] = True
        for o in ops:
            if o["slot"] is not None:
                o["inc"] = True
        cnt, run = {}, {}
        for i, o in enumerate(ops):
            if o["slot"] is None and o["inc"]:
                e = o["engine"]
                run[e] = run.get(e, 0) + 1
                cnt[i] = run[e]
        final_slot = dict(self.slot_fills)
        for o in ops:
            waits = []
            for key, j in sorted(o["rdeps"].items()):
                oj = ops[j]
                if oj["slot"] is not None:
                    slot, fill = oj["slot"]
                    v = 16 * (final_slot[slot] if slot in self.bulk_slots else fill)
                    waits.append(("slot:" + slot, v))
                else:
                    waits.append(("eng:" + oj["engine"], cnt[j]))
            o["waits"] = waits

        self._verify(ops)
        with ExitStack() as ctx:
            sems = {}
            for e in COMPUTE_ENGS:
                sems["eng:" + e] = ctx.enter_context(nc.semaphore("sem_" + e))
            for s in self.slot_fills:
                sems["slot:" + s] = ctx.enter_context(nc.semaphore("sem_" + s))
            block = ctx.enter_context(nc.Block())
            water = {}

            def emit_for(engine_name, eng):
                for o in ops:
                    if o["engine"] != engine_name:
                        continue
                    for key, v in o["waits"]:
                        wk = (engine_name, key)
                        if water.get(wk, 0) >= v:
                            continue
                        water[wk] = v
                        eng.wait_ge(sems[key], v)
                    inst = o["fn"](eng)
                    if o["inc"]:
                        if o["slot"] is not None:
                            inst.then_inc(sems["slot:" + o["slot"][0]], 16)
                        else:
                            inst.then_inc(sems["eng:" + engine_name], 1)
                if engine_name == "sync":
                    for s, n in sorted(self.slot_fills.items()):
                        wk = (engine_name, "slot:" + s)
                        if water.get(wk, 0) < 16 * n:
                            eng.wait_ge(sems["slot:" + s], 16 * n)

            @block.tensor
            def _(t):
                emit_for("tensor", t)

            @block.vector
            def _(v):
                emit_for("vector", v)

            @block.scalar
            def _(s):
                emit_for("scalar", s)

            @block.sync
            def _(s):
                emit_for("sync", s)

            @block.gpsimd
            def _(g):
                emit_for("gpsimd", g)
        return nc



def _rope_tables_l0():
    HD = 64
    inv_freq = 1.0 / (10000.0 ** (np.arange(0, HD, 2, dtype=np.float32) / HD))
    ang = np.arange(T_, dtype=np.float32)[:, None] * inv_freq[None, :]
    return np.cos(ang).astype(np.float32), np.sin(ang).astype(np.float32)


def _rope_tables_l1():
    HEAD_DIM = 64
    ROPE_THETA, SCALING, NTK_ALPHA, NTK_BETA, ICL = 150000.0, 32.0, 1.0, 32.0, 1024
    d_half = HEAD_DIM // 2
    freq = ROPE_THETA ** (np.arange(0, HEAD_DIM, 2, dtype=np.float32) / HEAD_DIM)
    concentration = 0.1 * math.log(SCALING) + 1.0
    low = d_half * math.log(ICL / (NTK_BETA * 2 * math.pi)) / math.log(ROPE_THETA)
    high = d_half * math.log(ICL / (NTK_ALPHA * 2 * math.pi)) / math.log(ROPE_THETA)
    interpolation = 1.0 / (SCALING * freq)
    extrapolation = 1.0 / freq
    ramp = np.clip((np.arange(d_half, dtype=np.float32) - low) / (high - low), 0.0, 1.0)
    mask = 1.0 - ramp
    inv_freq = interpolation * (1.0 - mask) + extrapolation * mask
    freqs = np.outer(np.arange(T_, dtype=np.float32), inv_freq)
    return ((np.cos(freqs) * concentration).astype(np.float32),
            (np.sin(freqs) * concentration).astype(np.float32))


def _l1_masks(core):
    Q0 = (core % 4) * OWN
    qp = np.arange(512)[None, :]
    kp = np.arange(128)[:, None]
    out = np.zeros((2, 5, 128, 512), dtype=BF)
    for qc in range(2):
        for m in range(5):
            r = 128 * m + kp - qp
            valid = (r >= 1) & (r <= 128)
            if Q0 == 0:
                kl = qc * 512 + 128 * m + kp + 0 * qp
                valid = valid & (kl >= 128)
            out[qc, m] = valid.astype(np.float32).astype(BF)
    return out


def _pad_slice(x, lo, hi):
    pad = max(0, -lo)
    body = x[max(lo, 0):hi]
    if pad:
        return np.concatenate([np.zeros((pad,) + x.shape[1:], x.dtype), body], 0)
    return np.ascontiguousarray(body)


def prep_inputs(inp):
    f32 = np.float32
    x = np.asarray(inp["L0_attn_x"], f32)
    cos0, sin0 = _rope_tables_l0()
    cos1, sin1 = _rope_tables_l1()
    iden_bf = np.eye(128, dtype=BF)
    iden_f32 = np.eye(128, dtype=f32)
    ones128 = np.ones((1, 128), f32)
    sel8 = np.zeros((8, 8 * 128), f32)
    for e in range(8):
        sel8[e, e * 128:(e + 1) * 128] = 1.0
    sel32 = np.zeros((32, 32 * 128), f32)
    for e in range(32):
        sel32[e, e * 128:(e + 1) * 128] = 1.0

    ebm = np.asarray(inp["L0_ffn_expert_b"], f32)
    eb_cols = np.ascontiguousarray(
        ebm.reshape(8, 16, 128).transpose(2, 1, 0).reshape(128, 128))

    wq_t0 = np.asarray(inp["L0_attn_Wq"], f32).T.reshape(D_, 32, 64)
    wq_tp = np.ascontiguousarray(wq_t0[:, SLOT2HEAD, :].reshape(D_, D_))
    bq0 = np.asarray(inp["L0_attn_bq"], f32).reshape(32, 64)
    bqp = np.ascontiguousarray(bq0[SLOT2HEAD].reshape(1, D_))
    wo_t0 = np.asarray(inp["L0_attn_Wo"], f32).T.reshape(32, 64, D_)
    wo_tp = np.ascontiguousarray(wo_t0[SLOT2HEAD].reshape(D_, D_))
    shared0 = dict(
        wq_t=wq_tp,
        wk_t=np.ascontiguousarray(np.asarray(inp["L0_attn_Wk"], f32).T),
        wv_t=np.ascontiguousarray(np.asarray(inp["L0_attn_Wv"], f32).T),
        wo_t=wo_tp,
        bq=bqp,
        bk=np.asarray(inp["L0_attn_bk"], f32).reshape(1, -1),
        bv=np.asarray(inp["L0_attn_bv"], f32).reshape(1, -1),
        bo=np.asarray(inp["L0_attn_bo"], f32).reshape(1, -1),
        rw=np.ascontiguousarray(np.asarray(inp["L0_ffn_router_w"], f32)),
        rb=np.asarray(inp["L0_ffn_router_b"], f32).reshape(1, -1),
        ew=np.ascontiguousarray(np.asarray(inp["L0_ffn_expert_w"], f32)),
        eb_cols=eb_cols,
        pw=np.ascontiguousarray(np.asarray(inp["L0_ffn_proj_w"], f32)),
        pb=np.asarray(inp["L0_ffn_proj_b"], f32).reshape(1, -1),
        nw=np.asarray(inp["L0_ffn_norm_w"], f32).reshape(1, -1),
        iden_bf=iden_bf, iden_f32=iden_f32, ones128=ones128, sel8=sel8,
        cos0=cos0, sin0=sin0)
    qkv_t0 = np.asarray(inp["L1_attn_qkv_w"], f32).T
    qp = qkv_t0[:, 0:2048].reshape(D_, 32, 64)
    qkv_tp = np.concatenate([qp[:, SLOT2HEAD, :].reshape(D_, 2048),
                             qkv_t0[:, 2048:]], axis=1)
    qb0 = np.asarray(inp["L1_attn_qkv_b"], f32).reshape(-1)
    qbq = qb0[0:2048].reshape(32, 64)
    qkv_bp = np.concatenate([qbq[SLOT2HEAD].reshape(-1), qb0[2048:]])
    out_t0 = np.asarray(inp["L1_attn_out_w"], f32).T.reshape(32, 64, D_)
    out_tp = np.ascontiguousarray(out_t0[SLOT2HEAD].reshape(D_, D_))
    shared1 = dict(
        qkv_t=np.ascontiguousarray(qkv_tp),
        qkv_b=np.ascontiguousarray(qkv_bp.reshape(1, -1)),
        out_t=out_tp,
        out_b=np.asarray(inp["L1_attn_out_b"], f32).reshape(1, -1),
        sinks=np.asarray(inp["L1_attn_sinks"], f32).reshape(1, -1),
        gw_t=np.ascontiguousarray(np.asarray(inp["L1_ffn_gate_w"], f32).T),
        gb=np.asarray(inp["L1_ffn_gate_bias"], f32).reshape(1, -1),
        fc1_t=np.ascontiguousarray(np.asarray(inp["L1_ffn_fc1_w"], f32).T),
        fc2_t=np.ascontiguousarray(np.asarray(inp["L1_ffn_fc2_w"], f32).T),
        w1_t=np.ascontiguousarray(np.asarray(inp["L1_ffn_w1"], f32).transpose(0, 2, 1)),
        w2_t=np.ascontiguousarray(np.asarray(inp["L1_ffn_w2"], f32).transpose(0, 2, 1)),
        su_t=np.ascontiguousarray(np.asarray(inp["L1_ffn_shared_up_w"], f32).T),
        sd_t=np.ascontiguousarray(np.asarray(inp["L1_ffn_shared_down_w"], f32).T),
        iden_bf=iden_bf, iden_f32=iden_f32, ones128=ones128, sel32=sel32)

    maps0, maps1 = [], []
    for c in range(N_CORES):
        bb, Q0 = c // 4, (c % 4) * OWN
        m0 = dict(shared0)
        m0["xb"] = np.ascontiguousarray(x[bb])
        m0["xc"] = _pad_slice(x[bb], Q0 - HALO, Q0 + OWN)
        m0["cos0c"] = _pad_slice(cos0, Q0 - HALO, Q0 + OWN)
        m0["sin0c"] = _pad_slice(sin0, Q0 - HALO, Q0 + OWN)
        maps0.append(m0)
        m1 = dict(shared1)
        m1["cos1"] = _pad_slice(cos1, Q0 - HALO, Q0 + OWN)
        m1["sin1"] = _pad_slice(sin1, Q0 - HALO, Q0 + OWN)
        m1["masks"] = _l1_masks(c)
        maps1.append(m1)
    return maps0, maps1



def r3(src2d, p=128):
    return src2d.rearrange("(n p) d -> p n d", p=p)


def d3(dstbuf_slice, d):
    return dstbuf_slice.rearrange("p (n d) -> p n d", d=d)



def fmv(t, total):
    return t[:, :].rearrange("p (f c) -> p f c", c=total)


def emit_ln_tile(b, C, xap, xkey, st, stk, out_bf, outkey, eps,
                 mcol=None, rcol=None, statkeys=("m_own", "r_own")):
    scr, sck = C["lnscr"], "lnscr"
    b.red(st[:, 0:1], xap, reads=[xkey], writes=[stk])
    b.tt(scr[:, 0:D_], xap, xap, ALU.mult, reads=[xkey], writes=[sck])
    b.red(st[:, 1:2], scr[:, 0:D_], reads=[sck], writes=[stk])
    b.ts(st[:, 0:1], st[:, 0:1], 1.0 / D_, None, ALU.mult, reads=[stk], writes=[stk])
    b.tt(st[:, 2:3], st[:, 0:1], st[:, 0:1], ALU.mult, reads=[stk], writes=[stk])
    b.ts(st[:, 1:2], st[:, 1:2], 1.0 / D_, None, ALU.mult, reads=[stk], writes=[stk])
    b.tt(st[:, 1:2], st[:, 1:2], st[:, 2:3], ALU.subtract, reads=[stk], writes=[stk])
    b.ts(st[:, 1:2], st[:, 1:2], float(eps), None, ALU.add, reads=[stk], writes=[stk])
    b.act(st[:, 2:3], st[:, 1:2], AF.Ln, reads=[stk], writes=[stk])
    b.act(st[:, 3:4], st[:, 2:3], AF.Exp, scale=-0.5, reads=[stk], writes=[stk])
    b.ts(out_bf, xap, st[:, 0:1], st[:, 3:4], ALU.subtract, ALU.mult,
         reads=[xkey, stk], writes=[outkey])
    if mcol is not None:
        b.vcopy(mcol, st[:, 0:1], reads=[stk], writes=[statkeys[0]])
        b.vcopy(rcol, st[:, 3:4], reads=[stk], writes=[statkeys[1]])


def emit_transpose16(b, C, src_bf, srckey, dstv, dstkey, nft=16):
    for g in range((nft + 7) // 8):
        n = min(8, nft - g * 8)
        pt, ptk = C["pt%d" % (g % 2)], "pt%d" % (g % 2)
        for k in range(n):
            ft = g * 8 + k
            b.trans(pt[:, k * 128:(k + 1) * 128], src_bf[:, ft * 128:(ft + 1) * 128],
                    C["idb"][:], reads=[srckey, "idb"], writes=[ptk])
        b.vcopy(dstv(g, n), pt[:, 0:n * 128].rearrange("p (f c) -> p f c", c=128),
                reads=[ptk], writes=[dstkey])


def emit_rope(b, C, src, srckey, dst, dstkey, costile, sintile, nheads, tmp, tmpkey,
              ckeys=("cosb", "sinb")):
    ca = costile.rearrange("p (o d) -> p o d", o=1).broadcast_to([128, nheads, 32])
    sa = sintile.rearrange("p (o d) -> p o d", o=1).broadcast_to([128, nheads, 32])
    s3 = src.rearrange("p (h d) -> p h d", d=64)
    d3 = dst.rearrange("p (h d) -> p h d", d=64)
    t3 = tmp.rearrange("p (h d) -> p h d", d=64)
    x1, x2 = s3[:, :, 0:32], s3[:, :, 32:64]
    b.tt(t3[:, :, 0:32], x1, ca, ALU.mult, reads=[srckey, ckeys[0]], writes=[tmpkey])
    b.tt(t3[:, :, 32:64], x2, sa, ALU.mult, reads=[srckey, ckeys[1]], writes=[tmpkey])
    b.tt(d3[:, :, 0:32], t3[:, :, 0:32], t3[:, :, 32:64], ALU.subtract,
         reads=[tmpkey], writes=[dstkey])
    b.tt(t3[:, :, 0:32], x2, ca, ALU.mult, reads=[srckey, ckeys[0]], writes=[tmpkey])
    b.tt(t3[:, :, 32:64], x1, sa, ALU.mult, reads=[srckey, ckeys[1]], writes=[tmpkey])
    b.tt(d3[:, :, 32:64], t3[:, :, 0:32], t3[:, :, 32:64], ALU.add,
         reads=[tmpkey], writes=[dstkey])


def emit_global_max(b, nc, ctx, tag, col, colkey, out11, outkey, row, rowkey):
    dscr = nc.dram_tensor("gmax_" + tag, [1, 128], F32, kind="Internal").ap()
    b.dma("sync", "gm" + tag, dscr.rearrange("o p -> p o"), col,
          reads=[colkey], writes=["gmd" + tag])
    b.dma("sync", "gm" + tag + "b", row, dscr, reads=["gmd" + tag], writes=[rowkey])
    b.red(out11, row, aluop=ALU.max, reads=[rowkey], writes=[outkey])



def build_l0(debug=()):
    nc = bass.Bass("TRN2", target_bir_lowering=False, debug=False)
    I = {}

    def di(name, shape, dt=F32):
        I[name] = nc.dram_tensor(name, list(shape), dt, kind="ExternalInput").ap()

    di("xb", (T_, D_)); di("xc", (TOK, D_))
    di("wq_t", (D_, D_)); di("wk_t", (D_, 512)); di("wv_t", (D_, 512))
    di("wo_t", (D_, D_))
    di("bq", (1, D_)); di("bk", (1, 512)); di("bv", (1, 512)); di("bo", (1, D_))
    di("rw", (D_, 8)); di("rb", (1, 8)); di("ew", (8, D_, D_))
    di("eb_cols", (128, 128)); di("pw", (D_, D_)); di("pb", (1, D_)); di("nw", (1, D_))
    di("cos0", (T_, 32)); di("sin0", (T_, 32))
    di("cos0c", (TOK, 32)); di("sin0c", (TOK, 32))
    di("iden_bf", (128, 128), BF16); di("iden_f32", (128, 128))
    di("ones128", (1, 128)); di("sel8", (8, 8 * 128))

    x1 = nc.dram_tensor("x1", [TOK, D_], F32, kind="ExternalOutput").ap()
    gdram = nc.dram_tensor("gscr", [TOK, D_], BF16, kind="Internal").ap()
    DBG = {}

    def dbg(name, shape=None, dt=F32):
        if name in debug and shape is not None and name not in DBG:
            DBG[name] = nc.dram_tensor("dbg_" + name, list(shape), dt,
                                       kind="ExternalOutput").ap()
        return DBG.get(name)

    b = Prog(nc)
    with ExitStack() as ctx:
        emit_l0(b, I, x1, gdram, dbg, nc, ctx)
    return b.build(), sorted(DBG)


def emit_l0(b, I, x1, gdram, dbg, nc, ctx):
    def sb(name, shape, dt=F32):
        return ctx.enter_context(nc.sbuf_tensor(name, list(shape), dt))

    def psg(st, name, shape, dt=F32):
        return st.enter_context(nc.psum_tensor(name, list(shape), dt))

    C = {}
    K_fm = sb("K_fm", [128, 4 * T_], BF16)
    V_tm = sb("V_tm", [128, NTB * 520], BF16)
    m_own = sb("m_own", [128, NT]); r_own = sb("r_own", [128, NT])
    q2acc = sb("q2acc", [128, 32]); k2acc = sb("k2acc", [128, 8])
    negB = sb("negB", [128, 1])
    C["idb"] = sb("idb", [128, 128], BF16)
    onesr = sb("onesr", [1, 128])

    b.dma("sync", "c0", C["idb"][:], I["iden_bf"], writes=["idb"], bulk=True)
    b.dma("sync", "c0", onesr[:], I["ones128"], writes=["onesr"], bulk=True)
    b.vmemset(q2acc[:], 0.0, writes=["q2acc"])
    b.vmemset(k2acc[:], 0.0, writes=["k2acc"])

    sComb = ExitStack()
    with ExitStack() as sAQ:
        xn_fm = sAQ.enter_context(nc.sbuf_tensor("xn_fm", [128, NF * TOK], BF16))
        wT = sAQ.enter_context(nc.sbuf_tensor("wT", [8, TOK], F32))
        scs = sAQ.enter_context(ExitStack())
        cosb = scs.enter_context(nc.sbuf_tensor("cosb", [128, NTB * 32], F32))
        sinb = scs.enter_context(nc.sbuf_tensor("sinb", [128, NTB * 32], F32))
        b.dma("sync", "c0", d3(cosb[:], 32), r3(I["cos0"]),
              writes=["cosb"], bulk=True)
        b.dma("sync", "c0", d3(sinb[:], 32), r3(I["sin0"]),
              writes=["sinb"], bulk=True)

        with ExitStack() as sA:
            def asb(name, shape, dt=F32):
                return sA.enter_context(nc.sbuf_tensor(name, list(shape), dt))

            xt = [asb("xt%d" % i, [128, D_]) for i in range(2)]
            C["lnscr"] = asb("lnscr", [128, D_])
            wkv = asb("wkv", [128, NF * 1024], BF16)
            xnbf = [asb("xnbf0", [128, D_], BF16)]
            stat = [asb("stat%d" % i, [128, 4]) for i in range(2)]
            xnfm_kv = asb("xnfm_kv", [128, NF * 512], BF16)
            kvscr = [asb("kvscr0", [128, 512])]
            kvtmp = [asb("kvtmp0", [128, 512])]
            krs = [asb("krs%d" % i, [128, 512], BF16) for i in range(2)]
            bk_bc = asb("bk_bc", [128, 512])
            bv_bc = asb("bv_bc", [128, 512])
            pkv = [psg(sA, "pkv%d" % i, [128, 512]) for i in range(2)]
            C["pt0"] = psg(sA, "pt0", [128, 1024], BF16)
            C["pt1"] = psg(sA, "pt1", [128, 1024], BF16)

            b.dma("gpsimd", "wkv", d3(wkv[:, 0:NF * 512], 512), r3(I["wk_t"]),
                  writes=["wkv_k"], bulk=True)
            b.dma("gpsimd", "wkv", d3(wkv[:, NF * 512:], 512), r3(I["wv_t"]),
                  writes=["wkv_v"], bulk=True)
            b.dma("sync", "cA", bk_bc[:], I["bk"].broadcast_to([128, 512]),
                  writes=["bk_bc"], bulk=True)
            b.dma("sync", "cA", bv_bc[:], I["bv"].broadcast_to([128, 512]),
                  writes=["bv_bc"], bulk=True)

            for ch in range(NTB // 4):
                for it in range(4):
                    tt = ch * 4 + it
                    bi = tt % 2
                    xtb, xk = xt[bi], "xt%d" % bi
                    b.dma("sync", xk, xtb[:], I["xb"][tt * 128:(tt + 1) * 128, :],
                          writes=[xk])
                    emit_ln_tile(b, C, xtb[:], xk, stat[bi], "stat%d" % bi,
                                 xnbf[0][:], "xnbf0", 1e-5)
                    kvv = fmv(xnfm_kv, 512)
                    emit_transpose16(
                        b, C, xnbf[0][:], "xnbf0",
                        (lambda g, n, it_=it: kvv[:, g * 8:g * 8 + n,
                                                  it_ * 128:(it_ + 1) * 128]),
                        "xnfm_kv")
                for it in range(4):
                    tt = ch * 4 + it
                    bi = it % 2
                    pk, pkk = pkv[bi], "pkv%d" % bi
                    scr, sck = kvscr[0], "kvscr0"
                    tmp, tmk = kvtmp[0], "kvtmp0"
                    for fc in range(NF):
                        b.mm(pk[:], xnfm_kv[:, fc * 512 + it * 128:fc * 512 + (it + 1) * 128],
                             wkv[:, fc * 512:(fc + 1) * 512], fc == 0, fc == NF - 1,
                             reads=["xnfm_kv", "wkv_k"], writes=[pkk])
                    b.tt(scr[:], pk[:], bk_bc[:], ALU.add, reads=[pkk, "bk_bc"], writes=[sck])
                    emit_rope(b, C, scr[:], sck, krs[bi][:], "krs%d" % bi,
                              cosb[:, tt * 32:(tt + 1) * 32], sinb[:, tt * 32:(tt + 1) * 32],
                              8, tmp[:], tmk)
                    pt, ptk = C["pt%d" % bi], "pt%d" % bi
                    for k in range(4):
                        b.trans(pt[:, k * 128:(k + 1) * 128], krs[bi][:, k * 128:(k + 1) * 128],
                                C["idb"][:], reads=["krs%d" % bi, "idb"], writes=[ptk])
                    b.vcopy(fmv(K_fm, T_)[:, 0:4, tt * 128:(tt + 1) * 128],
                            pt[:, 0:512].rearrange("p (f c) -> p f c", c=128),
                            reads=[ptk], writes=["K_fm"])
                    b.tt(tmp[:], krs[bi][:], krs[bi][:], ALU.mult,
                         reads=["krs%d" % bi], writes=[tmk])
                    b.red(scr[:, 0:8], tmp[:].rearrange("p (h d) -> p h d", d=64),
                          aluop=ALU.max, reads=[tmk], writes=[sck])
                    b.tt(k2acc[:], k2acc[:], scr[:, 0:8], ALU.max,
                         reads=[sck, "k2acc"], writes=["k2acc"])
                    for fc in range(NF):
                        b.mm(pk[:], xnfm_kv[:, fc * 512 + it * 128:fc * 512 + (it + 1) * 128],
                             wkv[:, NF * 512 + fc * 512:NF * 512 + (fc + 1) * 512],
                             fc == 0, fc == NF - 1,
                             reads=["xnfm_kv", "wkv_v"], writes=[pkk])
                    vt = V_tm[:, tt * 520:(tt + 1) * 520].rearrange("p (h c) -> p h c", c=65)
                    b.tt(vt[:, :, 0:64], pk[:].rearrange("p (h c) -> p h c", c=64),
                         bv_bc[:].rearrange("p (h c) -> p h c", c=64), ALU.add,
                         reads=[pkk, "bv_bc"], writes=["V_tm"])
                    b.vmemset(vt[:, :, 64:65], 1.0, writes=["V_tm"])

            for ot in range(NT):
                bi = ot % 2
                xtb, xk = xt[bi], "xt%d" % bi
                b.dma("sync", xk, xtb[:], I["xc"][ot * 128:(ot + 1) * 128, :],
                      writes=[xk])
                emit_ln_tile(b, C, xtb[:], xk, stat[bi], "stat%d" % bi,
                             xnbf[0][:], "xnbf0", 1e-5,
                             mcol=m_own[:, ot:ot + 1], rcol=r_own[:, ot:ot + 1])
                xv = fmv(xn_fm, TOK)
                emit_transpose16(
                    b, C, xnbf[0][:], "xnbf0",
                    (lambda g, n, ot_=ot: xv[:, g * 8:g * 8 + n, ot_ * 128:(ot_ + 1) * 128]),
                    "xn_fm")

        b.fence()
        if dbg("xn_fm", (128, NF * TOK)):
            b.dma("gpsimd", "dbg1", dbg("xn_fm"), xn_fm[:], reads=["xn_fm"], writes=[])
        if dbg("K_fm", (128, 4 * T_)):
            b.dma("gpsimd", "dbg2", dbg("K_fm"), K_fm[:], reads=["K_fm"], writes=[])
        if dbg("V_tm", (128, NTB * 520)):
            b.dma("gpsimd", "dbg3", dbg("V_tm"), V_tm[:], reads=["V_tm"], writes=[])

        if PHASES < 2:
            return
        with ExitStack() as sR:
            rw_b = sR.enter_context(nc.sbuf_tensor("rw_b", [128, NF * 8], BF16))
            rw_f = sR.enter_context(nc.sbuf_tensor("rw_f", [128, NF * 8], F32))
            rb_bc = sR.enter_context(nc.sbuf_tensor("rb_bc", [128, 8], F32))
            rgb = sR.enter_context(nc.sbuf_tensor("rgb", [128, 8], BF16))
            rscr = sR.enter_context(nc.sbuf_tensor("rscr", [128, 32], F32))
            rpsum_t = psg(sR, "rpsum", [128, 512])
            rpsum = rpsum_t[:, 0:8]
            rtp_t = psg(sR, "rtp", [128, 1024], BF16)
            rtp = rtp_t[0:8, 0:128]
            b.dma("sync", "cRg", d3(rw_f[:], 8), r3(I["rw"]),
                  writes=["rw_f"], bulk=True)
            b.vcopy(rw_b[:], rw_f[:], reads=["rw_f"], writes=["rw_b"])
            b.dma("sync", "cR", rb_bc[:], I["rb"].broadcast_to([128, 8]),
                  writes=["rb_bc"], bulk=True)

            for ot in range(NT):
                for fc in range(NF):
                    b.mm(rpsum, fmv(xn_fm, TOK)[:, fc, ot * 128:(ot + 1) * 128],
                         rw_b[:, fc * 8:(fc + 1) * 8], fc == 0, fc == NF - 1,
                         reads=["xn_fm", "rw_b"], writes=["rpsum"])
                l, lk = rscr[:, 0:8], "rscr"
                b.tt(l, rpsum, rb_bc[:], ALU.add, reads=["rpsum", "rb_bc"], writes=[lk])
                m1, nm1, m2 = rscr[:, 8:9], rscr[:, 9:10], rscr[:, 10:11]
                e21, den, rcp = rscr[:, 11:12], rscr[:, 12:13], rscr[:, 13:14]
                tq, ge = rscr[:, 16:24], rscr[:, 24:32]
                b.red(m1, l, aluop=ALU.max, reads=[lk], writes=[lk])
                b.red(nm1, l, aluop=ALU.max, negate=True, reads=[lk], writes=[lk])
                b.ts(tq, l, m1, -1e30, ALU.is_equal, ALU.mult, reads=[lk], writes=[lk])
                b.tt(tq, tq, l, ALU.add, reads=[lk], writes=[lk])
                b.red(m2, tq, aluop=ALU.max, reads=[lk], writes=[lk])
                b.act(e21, m2, AF.Exp, bias=nm1, reads=[lk], writes=[lk])
                b.ts(den, e21, 1.0, None, ALU.add, reads=[lk], writes=[lk])
                b.vrecip(rcp, den, reads=[lk], writes=[lk])
                b.act(tq, l, AF.Exp, bias=nm1, reads=[lk], writes=[lk])
                b.ts(ge, l, m2, None, ALU.is_ge, reads=[lk], writes=[lk])
                b.tt(ge, ge, tq, ALU.mult, reads=[lk], writes=[lk])
                b.ts(ge, ge, rcp, None, ALU.mult, reads=[lk], writes=[lk])
                if dbg("rdump", (128, NT * 32)):
                    b.dma("sync", "dbgR", dbg("rdump")[:, ot * 32:(ot + 1) * 32],
                          rscr[:], reads=[lk], writes=[])
                b.vcopy(rgb[:], ge, reads=[lk], writes=["rgb"])
                b.trans(rtp, rgb[:], C["idb"][:], reads=["rgb", "idb"], writes=["rtp"])
                b.vcopy(wT[:, ot * 128:(ot + 1) * 128], rtp, reads=["rtp"], writes=["wT"])

        if dbg("wT", (8, TOK)):
            b.dma("sync", "dbg4", dbg("wT"), wT[:], reads=["wT"], writes=[])

        b.fence()
        if PHASES < 3:
            return
        Q_fm = ctx.enter_context(nc.sbuf_tensor("Q_fm", [128, NF * TOK], BF16,
                                                side="right"))
        with ExitStack() as sQ:
            wqs = [sQ.enter_context(nc.sbuf_tensor("wqs%d" % i, [128, NF * 512], BF16))
                   for i in range(2)]
            qscr = [sQ.enter_context(nc.sbuf_tensor("qscr%d" % i, [128, 512], F32))
                    for i in range(2)]
            qtmp = [sQ.enter_context(nc.sbuf_tensor("qtmp%d" % i, [128, 512], F32))
                    for i in range(2)]
            qrb = [sQ.enter_context(nc.sbuf_tensor("qrb%d" % i, [128, 512], BF16))
                   for i in range(2)]
            bq_bc = sQ.enter_context(nc.sbuf_tensor("bq_bc", [128, D_], F32))
            cosc = sQ.enter_context(nc.sbuf_tensor("cosc", [128, NT * 32], F32))
            sinc = sQ.enter_context(nc.sbuf_tensor("sinc", [128, NT * 32], F32))
            ppq = [psg(sQ, "ppq%d" % i, [128, 512]) for i in range(2)]
            C["pt0"] = psg(sQ, "pt0b", [128, 1024], BF16)
            C["pt1"] = psg(sQ, "pt1b", [128, 1024], BF16)
            b.dma("sync", "cQ", bq_bc[:], I["bq"].broadcast_to([128, D_]),
                  writes=["bq_bc"], bulk=True)
            b.dma("sync", "cQ", d3(cosc[:], 32), r3(I["cos0c"]),
                  writes=["cosc"], bulk=True)
            b.dma("sync", "cQ", d3(sinc[:], 32), r3(I["sin0c"]),
                  writes=["sinc"], bulk=True)
            for ncc in range(4):
                wi = ncc % 2
                wb, wk = wqs[wi], "wqs%d" % wi
                b.dma("gpsimd", wk, d3(wb[:], 512), r3(I["wq_t"][:, ncc * 512:(ncc + 1) * 512]),
                      writes=[wk])
                for ot in range(NT):
                    bi = ot % 2
                    pp, ppk = ppq[bi], "ppq%d" % bi
                    for fc in range(NF):
                        b.mm(pp[:], fmv(xn_fm, TOK)[:, fc, ot * 128:(ot + 1) * 128],
                             wb[:, fc * 512:(fc + 1) * 512], fc == 0, fc == NF - 1,
                             reads=["xn_fm", wk], writes=[ppk])
                    b.tt(qscr[bi][:], pp[:], bq_bc[:, ncc * 512:(ncc + 1) * 512], ALU.add,
                         reads=[ppk, "bq_bc"], writes=["qscr%d" % bi])
                    emit_rope(b, C, qscr[bi][:], "qscr%d" % bi, qrb[bi][:], "qrb%d" % bi,
                              cosc[:, ot * 32:(ot + 1) * 32], sinc[:, ot * 32:(ot + 1) * 32],
                              8, qtmp[bi][:], "qtmp%d" % bi, ckeys=("cosc", "sinc"))
                    b.tt(qtmp[bi][:], qrb[bi][:], qrb[bi][:], ALU.mult,
                         reads=["qrb%d" % bi], writes=["qtmp%d" % bi])
                    b.red(qscr[bi][:, 0:8], qtmp[bi][:].rearrange("p (h d) -> p h d", d=64),
                          aluop=ALU.max, reads=["qtmp%d" % bi], writes=["qscr%d" % bi])
                    b.tt(q2acc[:, ncc * 8:(ncc + 1) * 8], q2acc[:, ncc * 8:(ncc + 1) * 8],
                         qscr[bi][:, 0:8], ALU.max,
                         reads=["qscr%d" % bi, "q2acc"], writes=["q2acc"])
                    pt, ptk = C["pt%d" % bi], "pt%d" % bi
                    for k in range(4):
                        b.trans(pt[:, k * 128:(k + 1) * 128], qrb[bi][:, k * 128:(k + 1) * 128],
                                C["idb"][:], reads=["qrb%d" % bi, "idb"], writes=[ptk])
                    b.vcopy(fmv(Q_fm, TOK)[:, ncc * 4:(ncc + 1) * 4, ot * 128:(ot + 1) * 128],
                            pt[:, 0:512].rearrange("p (f c) -> p f c", c=128),
                            reads=[ptk], writes=["Q_fm"])

        b.fence()
        scs.close()

        if PHASES < 4:
            return
        comb = sComb.enter_context(nc.sbuf_tensor("comb", [128, NF * TOK], BF16,
                                                   side="right"))
        combv = fmv(comb, TOK)
        with ExitStack() as sM:
            wes = [sM.enter_context(nc.sbuf_tensor("wes%d" % i, [128, NF * 128], BF16))
                   for i in range(2)]
            she = [sM.enter_context(nc.sbuf_tensor("she%d" % i, [128, 512], BF16))
                   for i in range(2)]
            mtmp = [sM.enter_context(nc.sbuf_tensor("mtmp%d" % i, [128, 512], BF16))
                    for i in range(2)]
            wbcs = sM.enter_context(nc.sbuf_tensor("wbcs", [128, TOK], F32))
            sel8b = sM.enter_context(nc.sbuf_tensor("sel8b", [8, 8 * 128], F32))
            eb_c = sM.enter_context(nc.sbuf_tensor("eb_c", [128, 128], F32))
            pH = [psg(sM, "pH%d" % i, [128, 512]) for i in range(2)]
            pwb = psg(sM, "pwb", [128, 512])
            b.dma("sync", "cM", sel8b[:], I["sel8"], writes=["sel8b"], bulk=True)
            b.dma("sync", "cM", eb_c[:], I["eb_cols"], writes=["eb_c"], bulk=True)

            for e in range(8):
                for ci, tw in enumerate(CHW):
                    b.mm(pwb[:, 0:tw], sel8b[:, e * 128:(e + 1) * 128],
                         wT[:, ci * 512:ci * 512 + tw], True, True,
                         reads=["sel8b", "wT"], writes=["pwb"])
                    b.vcopy(wbcs[:, ci * 512:ci * 512 + tw], pwb[:, 0:tw],
                            reads=["pwb"], writes=["wbcs"])
                for ft in range(NF):
                    wi = (e * NF + ft) % 2
                    wb, wk = wes[wi], "wes%d" % wi
                    b.dma("gpsimd", wk, d3(wb[:], 128), r3(I["ew"][e, :, ft * 128:(ft + 1) * 128]),
                          writes=[wk])
                    for ci, tw in enumerate(CHW):
                        t0 = ci * 512
                        pi = (ft * 3 + ci) % 2
                        hb, hbk = pH[pi], "pH%d" % pi
                        for fc in range(NF):
                            b.mm(hb[:, 0:tw], wb[:, fc * 128:(fc + 1) * 128],
                                 fmv(xn_fm, TOK)[:, fc, t0:t0 + tw], fc == 0, fc == NF - 1,
                                 reads=[wk, "xn_fm"], writes=[hbk])
                        b.act(she[pi][:, 0:tw], hb[:, 0:tw], AF.Silu,
                              bias=eb_c[:, ft * 8 + e:ft * 8 + e + 1],
                              reads=[hbk, "eb_c"], writes=["she%d" % pi])
                        dst = combv[:, ft, t0:t0 + tw]
                        ck = "comb%d_%d" % (ft, ci)
                        if e == 0:
                            b.tt(dst, she[pi][:, 0:tw], wbcs[:, t0:t0 + tw], ALU.mult,
                                 reads=["she%d" % pi, "wbcs"], writes=[ck])
                        else:
                            b.tt(mtmp[pi][:, 0:tw], she[pi][:, 0:tw], wbcs[:, t0:t0 + tw],
                                 ALU.mult, reads=["she%d" % pi, "wbcs"], writes=["mtmp%d" % pi])
                            b.tt(dst, dst, mtmp[pi][:, 0:tw], ALU.add,
                                 reads=["mtmp%d" % pi, ck], writes=[ck])

    dbg("g_out", (TOK, D_))
    b.fence()
    if dbg("comb", (128, NF * TOK)):
        b.dma("gpsimd", "dbgC", dbg("comb"), comb[:], reads=["comb0_0"], writes=[])
    if PHASES < 5:
        sComb.close()
        return
    with ExitStack() as sG:
        wpj = [sG.enter_context(nc.sbuf_tensor("wpj%d" % i, [128, NF * 512], BF16))
               for i in range(2)]
        pb_bc = sG.enter_context(nc.sbuf_tensor("pb_bc", [128, D_], F32))
        nw_bc = sG.enter_context(nc.sbuf_tensor("nw_bc", [128, D_], F32))
        scrY = [sG.enter_context(nc.sbuf_tensor("scrY0", [128, D_], F32))]
        xo = [sG.enter_context(nc.sbuf_tensor("xo0", [128, D_], F32))]
        gt = [sG.enter_context(nc.sbuf_tensor("gt0", [128, D_], BF16))]
        yst = [sG.enter_context(nc.sbuf_tensor("yst0", [128, 4], F32))]
        pP = [psg(sG, "pP%d" % i, [128, 512]) for i in range(2)]
        b.dma("sync", "cG", pb_bc[:], I["pb"].broadcast_to([128, D_]),
              writes=["pb_bc"], bulk=True)
        b.dma("sync", "cG", nw_bc[:], I["nw"].broadcast_to([128, D_]),
              writes=["nw_bc"], bulk=True)
        for ot in range(NT):
            bi = 0
            sy, syk = scrY[0], "scrY0"
            xb_, xbk = xo[0], "xo0"
            b.dma("sync", xbk, xb_[:], I["xc"][ot * 128:(ot + 1) * 128, :],
                  writes=[xbk])
            for ncc in range(4):
                wi = ncc % 2
                wb, wk = wpj[wi], "wpj%d" % wi
                b.dma("gpsimd", wk, d3(wb[:], 512), r3(I["pw"][:, ncc * 512:(ncc + 1) * 512]),
                      writes=[wk])
                pp, ppk = pP[wi], "pP%d" % wi
                for fc in range(NF):
                    b.mm(pp[:], combv[:, fc, ot * 128:(ot + 1) * 128],
                         wb[:, fc * 512:(fc + 1) * 512], fc == 0, fc == NF - 1,
                         reads=["comb%d_%d" % (fc, min(ot // 4, 2)), wk], writes=[ppk])
                b.tt(sy[:, ncc * 512:(ncc + 1) * 512], pp[:],
                     pb_bc[:, ncc * 512:(ncc + 1) * 512], ALU.add,
                     reads=[ppk, "pb_bc"], writes=[syk])
            b.ts(xb_[:], xb_[:], m_own[:, ot:ot + 1], r_own[:, ot:ot + 1],
                 ALU.subtract, ALU.mult, reads=[xbk, "m_own", "r_own"], writes=[xbk])
            b.tt(sy[:], sy[:], xb_[:], ALU.add, reads=[syk, xbk], writes=[syk])
            st, stk = yst[0], "yst0"
            b.tt(xb_[:], sy[:], sy[:], ALU.mult, reads=[syk, xbk], writes=[xbk])
            b.red(st[:, 0:1], xb_[:], reads=[xbk], writes=[stk])
            b.ts(st[:, 0:1], st[:, 0:1], 1.0 / D_, 1e-6, ALU.mult, ALU.add,
                 reads=[stk], writes=[stk])
            b.act(st[:, 1:2], st[:, 0:1], AF.Ln, reads=[stk], writes=[stk])
            b.act(st[:, 2:3], st[:, 1:2], AF.Exp, scale=-0.5, reads=[stk], writes=[stk])
            b.ts(sy[:], sy[:], st[:, 2:3], None, ALU.mult, reads=[syk, stk], writes=[syk])
            b.tt(sy[:], sy[:], nw_bc[:], ALU.mult, reads=[syk, "nw_bc"], writes=[syk])
            b.act(gt[0][:], sy[:], AF.Silu, reads=[syk], writes=["gt0"])
            b.dma("sync", "gst0", gdram[ot * 128:(ot + 1) * 128, :], gt[0][:],
                  reads=["gt0"], writes=["gd%d" % ot])
            if dbg("g_out"):
                b.dma("gpsimd", "dbgG", dbg("g_out")[ot * 128:(ot + 1) * 128, :],
                      gt[0][:], reads=["gt0"], writes=[])

    b.fence()
    sComb.close()

    if dbg("Q_fm", (128, NF * TOK)):
        b.dma("gpsimd", "dbg5", dbg("Q_fm"), Q_fm[:], reads=["Q_fm"], writes=[])

    if PHASES < 6:
        return
    with ExitStack() as sB:
        bscr = sB.enter_context(nc.sbuf_tensor("bscr", [128, 8], F32))
        brow = sB.enter_context(nc.sbuf_tensor("brow", [1, 128], F32))
        bsc2 = sB.enter_context(nc.sbuf_tensor("bsc2", [1, 4], F32))
        pnb_t = psg(sB, "pnb", [128, 512])
        pnb = pnb_t[:, 0:1]
        b.red(bscr[:, 0:1], q2acc[:], aluop=ALU.max, reads=["q2acc"], writes=["bscr"])
        emit_global_max(b, nc, ctx, "q", bscr[:, 0:1], "bscr", bsc2[0:1, 0:1], "bsc2",
                        brow[0:1, :], "brow")
        b.red(bscr[:, 1:2], k2acc[:], aluop=ALU.max, reads=["k2acc"], writes=["bscr"])
        emit_global_max(b, nc, ctx, "k", bscr[:, 1:2], "bscr", bsc2[0:1, 1:2], "bsc2",
                        brow[0:1, :], "brow")
        b.tt(bsc2[0:1, 2:3], bsc2[0:1, 0:1], bsc2[0:1, 1:2], ALU.mult,
             reads=["bsc2"], writes=["bsc2"])
        b.act(bsc2[0:1, 2:3], bsc2[0:1, 2:3], AF.Ln, reads=["bsc2"], writes=["bsc2"])
        b.act(bsc2[0:1, 3:4], bsc2[0:1, 2:3], AF.Exp, scale=0.5, reads=["bsc2"], writes=["bsc2"])
        b.ts(bsc2[0:1, 3:4], bsc2[0:1, 3:4], -0.125, None, ALU.mult,
             reads=["bsc2"], writes=["bsc2"])
        b.mm(pnb, onesr[:], bsc2[0:1, 3:4], True, True,
             reads=["onesr", "bsc2"], writes=["pnb"])
        b.vcopy(negB[:], pnb, reads=["pnb"], writes=["negB"])
    if dbg("negB", (128, 1)):
        b.dma("sync", "dbg6", dbg("negB"), negB[:], reads=["negB"], writes=[])

    b.fence()
    if PHASES < 7:
        return
    with ExitStack() as sT:
        def tsb(name, shape, dt=F32):
            return sT.enter_context(nc.sbuf_tensor(name, list(shape), dt))

        attnfm = tsb("attnfm", [128, NF * 512], BF16)
        ptl = [tsb("ptl%d" % i, [128, 512], BF16) for i in range(3)]
        rdrow = tsb("rdrow", [1, 512])
        rbc = tsb("rbc", [64, 512])
        wos = [tsb("wos%d" % i, [128, NF * 512], BF16) for i in range(1)]
        bo_bc = tsb("bo_bc", [128, D_])
        sO = [tsb("sO%d" % i, [128, D_]) for i in range(4)]
        gld = [tsb("gld%d" % i, [128, D_], BF16) for i in range(1)]
        gf32 = [tsb("gf32%d" % i, [128, D_]) for i in range(1)]
        xo2 = [tsb("xo2%d" % i, [128, D_]) for i in range(1)]
        pS = [psg(sT, "pS%d" % i, [128, 512]) for i in range(2)]
        pAV = [psg(sT, "pAV%d" % i, [65, 512]) for i in range(2)]
        pbc = psg(sT, "pbc", [64, 512])
        pO = [psg(sT, "pO%d" % i, [128, 512]) for i in range(2)]
        b.dma("sync", "cT", bo_bc[:], I["bo"].broadcast_to([128, D_]),
              writes=["bo_bc"], bulk=True)

        for ci, qw in enumerate(CHW):
            q0 = ci * 512
            for si in range(32):
                h = SLOT2HEAD[si]
                kvh = h // 4
                assert si % 2 == kvh % 2
                av, avk = pAV[si % 2], "pAV%d" % (si % 2)
                qrhs = fmv(Q_fm, TOK)[:, si // 2, q0:q0 + qw][(si % 2) * 64:(si % 2) * 64 + 64, :]
                for kt in range(NTB):
                    sp, spk = pS[kt % 2], "pS%d" % (kt % 2)
                    klhs = fmv(K_fm, T_)[:, kvh // 2, kt * 128:(kt + 1) * 128][
                        (kvh % 2) * 64:(kvh % 2) * 64 + 64, :]
                    b.mm(sp[:, 0:qw], klhs, qrhs, True, True,
                         reads=["K_fm", "Q_fm"], writes=[spk])
                    pb_, pbk = ptl[kt % 3], "ptl%d" % (kt % 3)
                    b.act(pb_[:, 0:qw], sp[:, 0:qw], AF.Exp, bias=negB[:, 0:1],
                          scale=0.125, reads=[spk, "negB"], writes=[pbk])
                    b.mm(av[:, 0:qw], V_tm[:, kt * 520 + kvh * 65:kt * 520 + (kvh + 1) * 65],
                         pb_[:, 0:qw], kt == 0, kt == NTB - 1,
                         reads=["V_tm", pbk], writes=[avk])
                b.vrecip(rdrow[0:1, 0:qw], av[64:65, 0:qw], reads=[avk], writes=["rdrow"])
                b.mm(pbc[:, 0:qw], onesr[0:1, 0:64], rdrow[0:1, 0:qw], True, True,
                     reads=["onesr", "rdrow"], writes=["pbc"])
                b.vcopy(rbc[:, 0:qw], pbc[:, 0:qw], reads=["pbc"], writes=["rbc"])
                dst = fmv(attnfm, 512)[:, si // 2, 0:qw][(si % 2) * 64:(si % 2) * 64 + 64, :]
                b.tt(dst, av[0:64, 0:qw], rbc[:, 0:qw], ALU.mult,
                     reads=[avk, "rbc"], writes=["attnfm"])
            for ncc in range(4):
                wb, wk = wos[0], "wos0"
                b.dma("gpsimd", wk, d3(wb[:], 512), r3(I["wo_t"][:, ncc * 512:(ncc + 1) * 512]),
                      writes=[wk])
                for it in range(qw // 128):
                    pp, ppk = pO[it % 2], "pO%d" % (it % 2)
                    for fc in range(NF):
                        b.mm(pp[:], fmv(attnfm, 512)[:, fc, it * 128:(it + 1) * 128],
                             wb[:, fc * 512:(fc + 1) * 512], fc == 0, fc == NF - 1,
                             reads=["attnfm", wk], writes=[ppk])
                    b.tt(sO[it][:, ncc * 512:(ncc + 1) * 512], pp[:],
                         bo_bc[:, ncc * 512:(ncc + 1) * 512], ALU.add,
                         reads=[ppk, "bo_bc"], writes=["sO%d" % it])
            for it in range(qw // 128):
                ot = ci * 4 + it
                b.dma("sync", "gld0", gld[0][:], gdram[ot * 128:(ot + 1) * 128, :],
                      reads=["gd%d" % ot], writes=["gld0"])
                b.dma("sync", "xo20", xo2[0][:], I["xc"][ot * 128:(ot + 1) * 128, :],
                      writes=["xo20"])
                b.vcopy(gf32[0][:], gld[0][:], reads=["gld0"], writes=["gf320"])
                b.tt(sO[it][:], sO[it][:], gf32[0][:], ALU.mult,
                     reads=["sO%d" % it, "gf320"], writes=["sO%d" % it])
                b.tt(sO[it][:], sO[it][:], xo2[0][:], ALU.add,
                     reads=["sO%d" % it, "xo20"], writes=["sO%d" % it])
                b.dma("sync", "st%d" % it, x1[ot * 128:(ot + 1) * 128, :], sO[it][:],
                      reads=["sO%d" % it], writes=[])



def build_l1(debug=()):
    nc = bass.Bass("TRN2", target_bir_lowering=False, debug=False)
    I = {}

    def di(name, shape, dt=F32):
        I[name] = nc.dram_tensor(name, list(shape), dt, kind="ExternalInput").ap()

    di("x1c", (TOK, D_))
    di("qkv_t", (D_, 3072)); di("qkv_b", (1, 3072))
    di("out_t", (D_, D_)); di("out_b", (1, D_)); di("sinks", (1, 32))
    di("gw_t", (D_, 32)); di("gb", (1, 32))
    di("fc1_t", (D_, 1024)); di("fc2_t", (1024, D_))
    di("w1_t", (32, 1024, 1024)); di("w2_t", (32, 1024, 1024))
    di("su_t", (D_, 4096)); di("sd_t", (4096, D_))
    di("cos1", (TOK, 32)); di("sin1", (TOK, 32))
    di("masks", (2, 5, 128, 512), BF16)
    di("iden_bf", (128, 128), BF16); di("ones128", (1, 128))
    di("sel32", (32, 32 * 128))

    yc = nc.dram_tensor("yc", [OWN, D_], F32, kind="ExternalOutput").ap()
    DBG = {}

    def dbg(name, shape=None, dt=F32):
        if name in debug and shape is not None and name not in DBG:
            DBG[name] = nc.dram_tensor("dbg_" + name, list(shape), dt,
                                       kind="ExternalOutput").ap()
        return DBG.get(name)

    b = Prog(nc)
    with ExitStack() as ctx:
        emit_l1(b, I, yc, dbg, nc, ctx)
    return b.build(), sorted(DBG)


def emit_l1(b, I, yc, dbg, nc, ctx):
    def sb(name, shape, dt=F32):
        return ctx.enter_context(nc.sbuf_tensor(name, list(shape), dt))

    def psg(st, name, shape, dt=F32):
        return st.enter_context(nc.psum_tensor(name, list(shape), dt))

    C = {}
    K_fm = sb("K1_fm", [128, 4 * TOK], BF16)
    V_tm = sb("V1_tm", [128, NT * 520], BF16)
    maskb = sb("maskb", [128, 2 * 5 * 512], BF16)
    m_own = sb("m1_own", [128, NT]); r_own = sb("r1_own", [128, NT])
    q2acc = sb("q2acc", [128, 32]); k2acc = sb("k2acc", [128, 8])
    negB = sb("negB", [128, 1])
    es = sb("es", [1, 32], F32)
    sink_s = sb("sink_s", [1, 32], F32)
    C["idb"] = sb("idb", [128, 128], BF16)
    onesr = sb("onesr", [1, 128])
    g1 = sb("g1", [128, 8 * D_], BF16)
    cwT = sb("cwT", [32, OWN], BF16)

    b.dma("sync", "c0L1", C["idb"][:], I["iden_bf"], writes=["idb"], bulk=True)
    b.dma("sync", "c0L1", onesr[:], I["ones128"], writes=["onesr"], bulk=True)
    b.dma("sync", "c0L1", sink_s[:], I["sinks"], writes=["sink_s"], bulk=True)
    b.dma("sync", "c0L1", maskb[:].rearrange("p (q m c) -> p q m c", m=5, c=512),
          I["masks"].rearrange("q m p c -> p q m c"), writes=["maskb"], bulk=True)
    b.vmemset(q2acc[:], 0.0, writes=["q2acc"])
    b.vmemset(k2acc[:], 0.0, writes=["k2acc"])

    with ExitStack() as sAQ:
        xn_fm = sAQ.enter_context(nc.sbuf_tensor("xn1_fm", [128, NF * TOK], BF16))
        cosc = sAQ.enter_context(nc.sbuf_tensor("cosc", [128, NT * 32], F32))
        sinc = sAQ.enter_context(nc.sbuf_tensor("sinc", [128, NT * 32], F32))
        b.dma("sync", "c0L1", d3(cosc[:], 32), r3(I["cos1"]), writes=["cosc"], bulk=True)
        b.dma("sync", "c0L1", d3(sinc[:], 32), r3(I["sin1"]), writes=["sinc"], bulk=True)

        with ExitStack() as sA:
            xt = [sA.enter_context(nc.sbuf_tensor("xt%d" % i, [128, D_], F32))
                  for i in range(2)]
            C["lnscr"] = sA.enter_context(nc.sbuf_tensor("lnscr", [128, D_], F32))
            xnbf = sA.enter_context(nc.sbuf_tensor("xnbf0", [128, D_], BF16))
            stat = [sA.enter_context(nc.sbuf_tensor("stat%d" % i, [128, 4], F32))
                    for i in range(2)]
            C["pt0"] = psg(sA, "pt0", [128, 1024], BF16)
            C["pt1"] = psg(sA, "pt1", [128, 1024], BF16)
            for ot in range(NT):
                bi = ot % 2
                xtb, xk = xt[bi], "xt%d" % bi
                b.dma("sync", xk, xtb[:], I["x1c"][ot * 128:(ot + 1) * 128, :],
                      writes=[xk])
                emit_ln_tile(b, C, xtb[:], xk, stat[bi], "stat%d" % bi,
                             xnbf[:], "xnbf0", 1e-5,
                             mcol=m_own[:, ot:ot + 1], rcol=r_own[:, ot:ot + 1])
                xv = fmv(xn_fm, TOK)
                emit_transpose16(
                    b, C, xnbf[:], "xnbf0",
                    (lambda g, n, ot_=ot: xv[:, g * 8:g * 8 + n, ot_ * 128:(ot_ + 1) * 128]),
                    "xn_fm")
        b.fence()

        with ExitStack() as sKV:
            wkv = sKV.enter_context(nc.sbuf_tensor("wkv1", [128, NF * 1024], BF16))
            kb_bc = sKV.enter_context(nc.sbuf_tensor("kb_bc", [128, 512], F32))
            vb_bc = sKV.enter_context(nc.sbuf_tensor("vb_bc", [128, 512], F32))
            kvscr = sKV.enter_context(nc.sbuf_tensor("kvscr0", [128, 512], F32))
            kvtmp = sKV.enter_context(nc.sbuf_tensor("kvtmp0", [128, 512], F32))
            krs = [sKV.enter_context(nc.sbuf_tensor("krs%d" % i, [128, 512], BF16))
                   for i in range(2)]
            pkv = [psg(sKV, "pkv%d" % i, [128, 512]) for i in range(2)]
            C["pt0"] = psg(sKV, "pt0k", [128, 1024], BF16)
            C["pt1"] = psg(sKV, "pt1k", [128, 1024], BF16)
            b.dma("gpsimd", "wkv1", d3(wkv[:, 0:NF * 512], 512),
                  r3(I["qkv_t"][:, 2048:2560]), writes=["wkv_k"], bulk=True)
            b.dma("gpsimd", "wkv1", d3(wkv[:, NF * 512:], 512),
                  r3(I["qkv_t"][:, 2560:3072]), writes=["wkv_v"], bulk=True)
            b.dma("sync", "cA1", kb_bc[:], I["qkv_b"][:, 2048:2560].broadcast_to([128, 512]),
                  writes=["kb_bc"], bulk=True)
            b.dma("sync", "cA1", vb_bc[:], I["qkv_b"][:, 2560:3072].broadcast_to([128, 512]),
                  writes=["vb_bc"], bulk=True)
            for ot in range(NT):
                bi = ot % 2
                pk, pkk = pkv[bi], "pkv%d" % bi
                for fc in range(NF):
                    b.mm(pk[:], fmv(xn_fm, TOK)[:, fc, ot * 128:(ot + 1) * 128],
                         wkv[:, fc * 512:(fc + 1) * 512], fc == 0, fc == NF - 1,
                         reads=["xn_fm", "wkv_k"], writes=[pkk])
                b.tt(kvscr[:], pk[:], kb_bc[:], ALU.add, reads=[pkk, "kb_bc"],
                     writes=["kvscr0"])
                emit_rope(b, C, kvscr[:], "kvscr0", krs[bi][:], "krs%d" % bi,
                          cosc[:, ot * 32:(ot + 1) * 32], sinc[:, ot * 32:(ot + 1) * 32],
                          8, kvtmp[:], "kvtmp0", ckeys=("cosc", "sinc"))
                pt, ptk = C["pt%d" % bi], "pt%d" % bi
                for k in range(4):
                    b.trans(pt[:, k * 128:(k + 1) * 128], krs[bi][:, k * 128:(k + 1) * 128],
                            C["idb"][:], reads=["krs%d" % bi, "idb"], writes=[ptk])
                b.vcopy(fmv(K_fm, TOK)[:, 0:4, ot * 128:(ot + 1) * 128],
                        pt[:, 0:512].rearrange("p (f c) -> p f c", c=128),
                        reads=[ptk], writes=["K_fm"])
                b.tt(kvtmp[:], krs[bi][:], krs[bi][:], ALU.mult,
                     reads=["krs%d" % bi], writes=["kvtmp0"])
                b.red(kvscr[:, 0:8], kvtmp[:].rearrange("p (h d) -> p h d", d=64),
                      aluop=ALU.max, reads=["kvtmp0"], writes=["kvscr0"])
                b.tt(k2acc[:], k2acc[:], kvscr[:, 0:8], ALU.max,
                     reads=["kvscr0", "k2acc"], writes=["k2acc"])
                for fc in range(NF):
                    b.mm(pk[:], fmv(xn_fm, TOK)[:, fc, ot * 128:(ot + 1) * 128],
                         wkv[:, NF * 512 + fc * 512:NF * 512 + (fc + 1) * 512],
                         fc == 0, fc == NF - 1,
                         reads=["xn_fm", "wkv_v"], writes=[pkk])
                vt = V_tm[:, ot * 520:(ot + 1) * 520].rearrange("p (h c) -> p h c", c=65)
                b.tt(vt[:, :, 0:64], pk[:].rearrange("p (h c) -> p h c", c=64),
                     vb_bc[:].rearrange("p (h c) -> p h c", c=64), ALU.add,
                     reads=[pkk, "vb_bc"], writes=["V_tm"])
                b.vmemset(vt[:, :, 64:65], 1.0, writes=["V_tm"])
        b.fence()

        with ExitStack() as sR:
            gw_b = sR.enter_context(nc.sbuf_tensor("gw_b", [128, NF * 32], BF16))
            gw_f = sR.enter_context(nc.sbuf_tensor("gw_f", [128, NF * 32], F32))
            gb_bc = sR.enter_context(nc.sbuf_tensor("gb_bc", [128, 32], F32))
            rscr = sR.enter_context(nc.sbuf_tensor("rscr", [128, 192], F32))
            rgb = sR.enter_context(nc.sbuf_tensor("rgb", [128, 32], BF16))
            rpsum_t = psg(sR, "rpsum", [128, 512])
            rpsum = rpsum_t[:, 0:32]
            rtp_t = psg(sR, "rtp", [128, 1024], BF16)
            rtp = rtp_t[0:32, 0:128]
            b.dma("sync", "cRg1", d3(gw_f[:], 32), r3(I["gw_t"]), writes=["gw_f"], bulk=True)
            b.vcopy(gw_b[:], gw_f[:], reads=["gw_f"], writes=["gw_b"])
            b.dma("sync", "cR1", gb_bc[:], I["gb"].broadcast_to([128, 32]),
                  writes=["gb_bc"], bulk=True)
            for ot in range(8):
                lt = ot + 1
                for fc in range(NF):
                    b.mm(rpsum, fmv(xn_fm, TOK)[:, fc, lt * 128:(lt + 1) * 128],
                         gw_b[:, fc * 32:(fc + 1) * 32], fc == 0, fc == NF - 1,
                         reads=["xn_fm", "gw_b"], writes=["rpsum"])
                lk = "rscr"
                lg = rscr[:, 0:32]
                sc_ = rscr[:, 32:64]
                sbv = rscr[:, 64:96]
                tq = rscr[:, 96:128]
                gs = rscr[:, 128:132]
                st1 = rscr[:, 132:133]; st2 = rscr[:, 133:134]
                thg = rscr[:, 134:135]; th8 = rscr[:, 135:136]
                ssum = rscr[:, 136:137]; rs = rscr[:, 137:138]
                msk = rscr[:, 128 + 16:128 + 16]
                b.vcopy(lg, rpsum, reads=["rpsum"], writes=[lk])
                b.act(sc_, lg, AF.Exp, scale=-1.0, reads=[lk], writes=[lk])
                b.ts(sc_, sc_, 1.0, None, ALU.add, reads=[lk], writes=[lk])
                b.vrecip(sc_, sc_, reads=[lk], writes=[lk])
                b.tt(sbv, sc_, gb_bc[:], ALU.add, reads=[lk, "gb_bc"], writes=[lk])
                for gi in range(4):
                    sl = sbv[:, gi * 8:(gi + 1) * 8]
                    b.red(st1, sl, aluop=ALU.max, reads=[lk], writes=[lk])
                    b.ts(tq[:, 0:8], sl, st1, -1e30, ALU.is_equal, ALU.mult,
                         reads=[lk], writes=[lk])
                    b.tt(tq[:, 0:8], tq[:, 0:8], sl, ALU.add, reads=[lk], writes=[lk])
                    b.red(st2, tq[:, 0:8], aluop=ALU.max, reads=[lk], writes=[lk])
                    b.tt(gs[:, gi:gi + 1], st1, st2, ALU.add, reads=[lk], writes=[lk])
                b.red(st1, gs, aluop=ALU.max, reads=[lk], writes=[lk])
                b.ts(tq[:, 0:4], gs, st1, -1e30, ALU.is_equal, ALU.mult,
                     reads=[lk], writes=[lk])
                b.tt(tq[:, 0:4], tq[:, 0:4], gs, ALU.add, reads=[lk], writes=[lk])
                b.red(thg, tq[:, 0:4], aluop=ALU.max, reads=[lk], writes=[lk])
                b.ts(gs, gs, thg, None, ALU.is_ge, reads=[lk], writes=[lk])
                gm_exp = gs.rearrange("p (g o) -> p g o", o=1).broadcast_to([128, 4, 8])
                sb3 = sbv.rearrange("p (g e) -> p g e", e=8)
                tq3 = tq.rearrange("p (g e) -> p g e", e=8)
                b.tt(tq3, sb3, gm_exp, ALU.mult, reads=[lk], writes=[lk])
                b.ts(lg, gs.rearrange("p (g o) -> p g o", o=1).broadcast_to([128, 4, 8]),
                     1.0, -1e30, ALU.subtract, ALU.mult, reads=[lk], writes=[lk])
                b.tt(tq, tq, lg, ALU.subtract, reads=[lk], writes=[lk])
                mcp = rscr[:, 96 + 32:96 + 32] if False else None
                b.vcopy(lg, tq, reads=[lk], writes=[lk])
                for rnd in range(8):
                    b.red(th8, tq, aluop=ALU.max, reads=[lk], writes=[lk])
                    if rnd < 7:
                        b.ts(rscr[:, 160:192], tq, th8, -1e30,
                             ALU.is_equal, ALU.mult, reads=[lk], writes=[lk])
                        b.tt(tq, tq, rscr[:, 160:192], ALU.add,
                             reads=[lk], writes=[lk])
                b.ts(lg, lg, th8, None, ALU.is_ge, reads=[lk], writes=[lk])
                b.tt(lg, lg, sc_, ALU.mult, reads=[lk], writes=[lk])
                b.red(ssum, lg, reads=[lk], writes=[lk])
                b.vrecip(rs, ssum, reads=[lk], writes=[lk])
                b.ts(lg, lg, rs, 2.5, ALU.mult, ALU.mult, reads=[lk], writes=[lk])
                b.vcopy(rgb[:], lg, reads=[lk], writes=["rgb"])
                b.trans(rtp, rgb[:], C["idb"][:], reads=["rgb", "idb"], writes=["rtp"])
                b.vcopy(cwT[:, ot * 128:(ot + 1) * 128], rtp, reads=["rtp"], writes=["cwT"])
        b.fence()

        TCH = 256
        NCH = OWN // TCH
        with ExitStack() as sF:
            xlat = sF.enter_context(nc.sbuf_tensor("xlat", [128, 8 * OWN], BF16))
            wstr = [sF.enter_context(nc.sbuf_tensor("wstr%d" % i, [128, 16 * 128], BF16))
                    for i in range(2)]
            wfc1 = [sF.enter_context(nc.sbuf_tensor("wfc1%d" % i, [128, NF * 128], BF16))
                    for i in range(2)]
            hw = sF.enter_context(nc.sbuf_tensor("hw", [128, 8 * TCH], BF16))
            rt = sF.enter_context(nc.sbuf_tensor("rt", [128, TCH], F32))
            rt2 = sF.enter_context(nc.sbuf_tensor("rt2", [128, TCH], F32))
            cwb = sF.enter_context(nc.sbuf_tensor("cwb", [128, TCH], F32))
            sel32b = sF.enter_context(nc.sbuf_tensor("sel32b", [32, 32 * 128], BF16))
            ylat_bf = sF.enter_context(nc.sbuf_tensor("ylat_bf", [128, 8 * TCH], BF16))
            upfm = sF.enter_context(nc.sbuf_tensor("upfm", [128, 32 * TCH], BF16))
            wdn = [sF.enter_context(nc.sbuf_tensor("wdn%d" % i, [128, 32 * 256], BF16))
                   for i in range(2)]
            wf2 = [sF.enter_context(nc.sbuf_tensor("wf20", [128, 8 * 256], BF16))]
            pylat = psg(sF, "pylat", [128, 2048])
            ppre = [psg(sF, "ppre%d" % i, [128, 512]) for i in range(2)]
            pcw = psg(sF, "pcw", [128, 512])
            pdn = ppre
            b.dma("gpsimd", "cF", sel32b[:], I["sel32"], writes=["sel32b"], bulk=True)

            for lt in range(8):
                wi = lt % 2
                wb, wk = wfc1[wi], "wfc1%d" % wi
                b.dma("gpsimd", wk, d3(wb[:], 128),
                      r3(I["fc1_t"][:, lt * 128:(lt + 1) * 128]), writes=[wk])
                for tc in range(2):
                    pp, ppk = ppre[tc % 2], "ppre%d" % (tc % 2)
                    for fc in range(NF):
                        b.mm(pp[:], wb[:, fc * 128:(fc + 1) * 128],
                             fmv(xn_fm, TOK)[:, fc, 128 + tc * 512:128 + (tc + 1) * 512],
                             fc == 0, fc == NF - 1, reads=[wk, "xn_fm"], writes=[ppk])
                    b.vcopy(fmv(xlat, OWN)[:, lt, tc * 512:(tc + 1) * 512][0:128, :],
                            pp[:], reads=[ppk], writes=["xlat"])

            if dbg("xlatd", (128, 8 * OWN), BF16):
                b.dma("sync", "dbgXL", dbg("xlatd"), xlat[:],
                      reads=["xlat"], writes=[])
            for ci in range(NCH):
                t0 = ci * TCH
                for e in range(32):
                    b.mm(pcw[:, 0:TCH], sel32b[:, e * 128:(e + 1) * 128],
                         cwT[:, t0:t0 + TCH], True, True,
                         reads=["sel32b", "cwT"], writes=["pcw"])
                    b.vcopy(cwb[:], pcw[:, 0:TCH], reads=["pcw"], writes=["cwb"])
                    for it_ in range(8):
                        wi = it_ % 2
                        wb, wk = wstr[wi], "wstr%d" % wi
                        b.dma("gpsimd", wk, d3(wb[:, 0:8 * 128], 128),
                              r3(I["w1_t"][e, :, it_ * 128:(it_ + 1) * 128]), writes=[wk])
                        pp, ppk = ppre[it_ % 2], "ppre%d" % (it_ % 2)
                        for lc in range(8):
                            b.mm(pp[:, 0:TCH], wb[:, lc * 128:(lc + 1) * 128],
                                 fmv(xlat, OWN)[:, lc, t0:t0 + TCH],
                                 lc == 0, lc == 7, reads=[wk, "xlat"], writes=[ppk])
                        b.op("scalar",
                             (lambda s, o=rt[:, 0:TCH], i=pp[:, 0:TCH]:
                              s.activation(o, i, AF.Relu)),
                             reads=[ppk], writes=["rt"])
                        b.tt(rt2[:, 0:TCH], rt[:, 0:TCH], rt[:, 0:TCH], ALU.mult,
                             reads=["rt"], writes=["rt2"])
                        b.tt(hw[:, it_ * TCH:(it_ + 1) * TCH], rt2[:, 0:TCH],
                             cwb[:], ALU.mult, reads=["rt2", "cwb"], writes=["hw"])
                    for lt in range(8):
                        wi = lt % 2
                        wb, wk = wstr[wi], "wstr%d" % wi
                        b.dma("gpsimd", wk, d3(wb[:, 0:8 * 128], 128),
                              r3(I["w2_t"][e, :, lt * 128:(lt + 1) * 128]), writes=[wk])
                        for icc in range(8):
                            b.mm(pylat[:, lt * TCH:(lt + 1) * TCH],
                                 wb[:, icc * 128:(icc + 1) * 128],
                                 hw[:, icc * TCH:(icc + 1) * TCH],
                                 (e == 0 and icc == 0 and lt % 2 == 0),
                                 (e == 31 and icc == 7),
                                 reads=[wk, "hw"], writes=["pylat"])
                for lt in range(8):
                    b.vcopy(ylat_bf[:, lt * TCH:(lt + 1) * TCH],
                            pylat[:, lt * TCH:(lt + 1) * TCH],
                            reads=["pylat"], writes=["ylat_bf"])
                if ci == 0 and dbg("ylatd", (128, 8 * 256), BF16):
                    b.dma("sync", "dbgYL", dbg("ylatd"), ylat_bf[:],
                          reads=["ylat_bf"], writes=[])
                for st_ in range(32):
                    wi = st_ % 2
                    wb, wk = wstr[wi], "wstr%d" % wi
                    b.dma("gpsimd", wk, d3(wb[:], 128),
                          r3(I["su_t"][:, st_ * 128:(st_ + 1) * 128]), writes=[wk])
                    pp, ppk = ppre[st_ % 2], "ppre%d" % (st_ % 2)
                    for fc in range(NF):
                        b.mm(pp[:, 0:TCH], wb[:, fc * 128:(fc + 1) * 128],
                             fmv(xn_fm, TOK)[:, fc, 128 + t0:128 + t0 + TCH],
                             fc == 0, fc == NF - 1, reads=[wk, "xn_fm"], writes=[ppk])
                    b.op("scalar",
                         (lambda s, o=rt[:, 0:TCH], i=pp[:, 0:TCH]:
                          s.activation(o, i, AF.Relu)),
                         reads=[ppk], writes=["rt"])
                    b.tt(upfm[:, st_ * TCH:(st_ + 1) * TCH], rt[:, 0:TCH],
                         rt[:, 0:TCH], ALU.mult, reads=["rt"], writes=["upfm"])
                if ci == 0 and dbg("upfmd", (128, 32 * 256), BF16):
                    b.dma("sync", "dbgUP", dbg("upfmd"), upfm[:],
                          reads=["upfm"], writes=[])
                for nc8 in range(8):
                    wi = nc8 % 2
                    wdb, wdk = wdn[wi], "wdn%d" % wi
                    wfb, wfk = wf2[0], "wf20"
                    b.dma("gpsimd", wdk, d3(wdb[:], 256),
                          r3(I["sd_t"][:, nc8 * 256:(nc8 + 1) * 256]),
                          writes=[wdk])
                    b.dma("gpsimd", wfk, d3(wfb[:], 256),
                          r3(I["fc2_t"][:, nc8 * 256:(nc8 + 1) * 256]),
                          writes=[wfk])
                    for it_ in range(TCH // 128):
                        ot = (t0 + it_ * 128) // 128
                        pp, ppk = pdn[it_ % 2], "ppre%d" % (it_ % 2)
                        for sc_ in range(32):
                            b.mm(pp[:, 0:256], upfm[:, sc_ * TCH + it_ * 128:
                                                    sc_ * TCH + (it_ + 1) * 128],
                                 wdb[:, sc_ * 256:(sc_ + 1) * 256],
                                 sc_ == 0, False, reads=["upfm", wdk], writes=[ppk])
                        for lc in range(8):
                            b.mm(pp[:, 0:256], ylat_bf[:, lc * TCH + it_ * 128:
                                                       lc * TCH + (it_ + 1) * 128],
                                 wfb[:, lc * 256:(lc + 1) * 256],
                                 False, lc == 7, reads=["ylat_bf", wfk], writes=[ppk])
                        b.op("scalar",
                             (lambda s, o=g1[:, ot * D_ + nc8 * 256:
                                             ot * D_ + (nc8 + 1) * 256],
                              i=pp[:, 0:256]: s.activation(o, i, AF.Silu)),
                             reads=[ppk], writes=["g1"])
        if dbg("g1d", (128, 8 * D_)):
            b.dma("gpsimd", "dbgG1", dbg("g1d"), g1[:], reads=["g1"], writes=[])
        if dbg("cwTd", (32, OWN)):
            b.dma("gpsimd", "dbgCW", dbg("cwTd"), cwT[:], reads=["cwT"], writes=[])
        b.fence()

        Q_fm = ctx.enter_context(nc.sbuf_tensor("Q1_fm", [128, NF * OWN], BF16,
                                                side="right"))
        with ExitStack() as sQ:
            wqs = [sQ.enter_context(nc.sbuf_tensor("wqs%d" % i, [128, NF * 512], BF16))
                   for i in range(2)]
            qscr = [sQ.enter_context(nc.sbuf_tensor("qscr%d" % i, [128, 512], F32))
                    for i in range(2)]
            qtmp = [sQ.enter_context(nc.sbuf_tensor("qtmp%d" % i, [128, 512], F32))
                    for i in range(2)]
            qrb = [sQ.enter_context(nc.sbuf_tensor("qrb%d" % i, [128, 512], BF16))
                   for i in range(2)]
            qb_bc = sQ.enter_context(nc.sbuf_tensor("qb_bc", [128, D_], F32))
            ppq = [psg(sQ, "ppq%d" % i, [128, 512]) for i in range(2)]
            C["pt0"] = psg(sQ, "pt0b", [128, 1024], BF16)
            C["pt1"] = psg(sQ, "pt1b", [128, 1024], BF16)
            b.dma("sync", "cQ1", qb_bc[:], I["qkv_b"][:, 0:2048].broadcast_to([128, D_]),
                  writes=["qb_bc"], bulk=True)
            for ncc in range(4):
                wi = ncc % 2
                wb, wk = wqs[wi], "wqs%d" % wi
                b.dma("gpsimd", wk, d3(wb[:], 512),
                      r3(I["qkv_t"][:, ncc * 512:(ncc + 1) * 512]), writes=[wk])
                for ot in range(8):
                    lt = ot + 1
                    bi = ot % 2
                    pp, ppk = ppq[bi], "ppq%d" % bi
                    for fc in range(NF):
                        b.mm(pp[:], fmv(xn_fm, TOK)[:, fc, lt * 128:(lt + 1) * 128],
                             wb[:, fc * 512:(fc + 1) * 512], fc == 0, fc == NF - 1,
                             reads=["xn_fm", wk], writes=[ppk])
                    b.tt(qscr[bi][:], pp[:], qb_bc[:, ncc * 512:(ncc + 1) * 512], ALU.add,
                         reads=[ppk, "qb_bc"], writes=["qscr%d" % bi])
                    emit_rope(b, C, qscr[bi][:], "qscr%d" % bi, qrb[bi][:], "qrb%d" % bi,
                              cosc[:, lt * 32:(lt + 1) * 32], sinc[:, lt * 32:(lt + 1) * 32],
                              8, qtmp[bi][:], "qtmp%d" % bi, ckeys=("cosc", "sinc"))
                    b.tt(qtmp[bi][:], qrb[bi][:], qrb[bi][:], ALU.mult,
                         reads=["qrb%d" % bi], writes=["qtmp%d" % bi])
                    b.red(qscr[bi][:, 0:8], qtmp[bi][:].rearrange("p (h d) -> p h d", d=64),
                          aluop=ALU.max, reads=["qtmp%d" % bi], writes=["qscr%d" % bi])
                    b.tt(q2acc[:, ncc * 8:(ncc + 1) * 8], q2acc[:, ncc * 8:(ncc + 1) * 8],
                         qscr[bi][:, 0:8], ALU.max,
                         reads=["qscr%d" % bi, "q2acc"], writes=["q2acc"])
                    pt, ptk = C["pt%d" % bi], "pt%d" % bi
                    for k in range(4):
                        b.trans(pt[:, k * 128:(k + 1) * 128], qrb[bi][:, k * 128:(k + 1) * 128],
                                C["idb"][:], reads=["qrb%d" % bi, "idb"], writes=[ptk])
                    b.vcopy(fmv(Q_fm, OWN)[:, ncc * 4:(ncc + 1) * 4, ot * 128:(ot + 1) * 128],
                            pt[:, 0:512].rearrange("p (f c) -> p f c", c=128),
                            reads=[ptk], writes=["Q_fm"])
        b.fence()

    with ExitStack() as sB:
        bscr = sB.enter_context(nc.sbuf_tensor("bscr", [128, 8], F32))
        brow = sB.enter_context(nc.sbuf_tensor("brow", [1, 128], F32))
        bsc2 = sB.enter_context(nc.sbuf_tensor("bsc2", [1, 4], F32))
        pnb_t = psg(sB, "pnb", [128, 512])
        pnb = pnb_t[:, 0:1]
        b.red(bscr[:, 0:1], q2acc[:], aluop=ALU.max, reads=["q2acc"], writes=["bscr"])
        emit_global_max(b, nc, ctx, "q1", bscr[:, 0:1], "bscr", bsc2[0:1, 0:1], "bsc2",
                        brow[0:1, :], "brow")
        b.red(bscr[:, 1:2], k2acc[:], aluop=ALU.max, reads=["k2acc"], writes=["bscr"])
        emit_global_max(b, nc, ctx, "k1", bscr[:, 1:2], "bscr", bsc2[0:1, 1:2], "bsc2",
                        brow[0:1, :], "brow")
        b.tt(bsc2[0:1, 2:3], bsc2[0:1, 0:1], bsc2[0:1, 1:2], ALU.mult,
             reads=["bsc2"], writes=["bsc2"])
        b.act(bsc2[0:1, 2:3], bsc2[0:1, 2:3], AF.Ln, reads=["bsc2"], writes=["bsc2"])
        b.act(bsc2[0:1, 3:4], bsc2[0:1, 2:3], AF.Exp, scale=0.5, reads=["bsc2"], writes=["bsc2"])
        b.ts(bsc2[0:1, 3:4], bsc2[0:1, 3:4], -0.125, None, ALU.mult,
             reads=["bsc2"], writes=["bsc2"])
        b.mm(pnb, onesr[:], bsc2[0:1, 3:4], True, True,
             reads=["onesr", "bsc2"], writes=["pnb"])
        b.vcopy(negB[:], pnb, reads=["pnb"], writes=["negB"])
        b.act(es[:], sink_s[:], AF.Exp, bias=negB[0:1, 0:1],
              reads=["sink_s", "negB"], writes=["es"])
    b.fence()

    with ExitStack() as sT:
        def tsb(name, shape, dt=F32):
            return sT.enter_context(nc.sbuf_tensor(name, list(shape), dt))

        attnfm = tsb("attnfm", [128, NF * 512], BF16)
        ptl = [tsb("ptl%d" % i, [128, 512], BF16) for i in range(3)]
        rdrow = tsb("rdrow", [1, 512])
        rbc = tsb("rbc", [64, 512])
        wos = [tsb("wos%d" % i, [128, NF * 512], BF16) for i in range(1)]
        ob_bc = tsb("ob_bc", [128, D_])
        sO = [tsb("sO%d" % i, [128, D_]) for i in range(4)]
        gf32 = tsb("gf32", [128, D_])
        xo2 = tsb("xo2", [128, D_])
        pS = [psg(sT, "pS%d" % i, [128, 512]) for i in range(2)]
        pAV = [psg(sT, "pAV%d" % i, [65, 512]) for i in range(2)]
        pbc = psg(sT, "pbc", [64, 512])
        pO = [psg(sT, "pO%d" % i, [128, 512]) for i in range(2)]
        b.dma("sync", "cT1", ob_bc[:], I["out_b"].broadcast_to([128, D_]),
              writes=["ob_bc"], bulk=True)
        mk3 = maskb[:].rearrange("p (q m c) -> p q m c", m=5, c=512)

        for qc in range(2):
            for si in range(32):
                h = SLOT2HEAD[si]
                kvh = h // 4
                av, avk = pAV[si % 2], "pAV%d" % (si % 2)
                qrhs = fmv(Q_fm, OWN)[:, si // 2, qc * 512:(qc + 1) * 512][
                    (si % 2) * 64:(si % 2) * 64 + 64, :]
                for m in range(5):
                    kt = qc * 4 + m
                    sp, spk = pS[m % 2], "pS%d" % (m % 2)
                    klhs = fmv(K_fm, TOK)[:, kvh // 2, kt * 128:(kt + 1) * 128][
                        (kvh % 2) * 64:(kvh % 2) * 64 + 64, :]
                    b.mm(sp[:], klhs, qrhs, True, True,
                         reads=["K_fm", "Q_fm"], writes=[spk])
                    pb_, pbk = ptl[m % 3], "ptl%d" % (m % 3)
                    b.act(pb_[:], sp[:], AF.Exp, bias=negB[:, 0:1], scale=0.125,
                          reads=[spk, "negB"], writes=[pbk])
                    b.tt(pb_[:], pb_[:], mk3[:, qc, m, :], ALU.mult,
                         reads=[pbk, "maskb"], writes=[pbk])
                    b.mm(av[:], V_tm[:, kt * 520 + kvh * 65:kt * 520 + (kvh + 1) * 65],
                         pb_[:], m == 0, m == 4, reads=["V_tm", pbk], writes=[avk])
                b.ts(rdrow[0:1, :], av[64:65, :], es[0:1, h:h + 1], None, ALU.add,
                     reads=[avk, "es"], writes=["rdrow"])
                b.vrecip(rdrow[0:1, :], rdrow[0:1, :], reads=["rdrow"], writes=["rdrow"])
                b.mm(pbc[:], onesr[0:1, 0:64], rdrow[0:1, :], True, True,
                     reads=["onesr", "rdrow"], writes=["pbc"])
                b.vcopy(rbc[:], pbc[:], reads=["pbc"], writes=["rbc"])
                dst = fmv(attnfm, 512)[:, si // 2, :][(si % 2) * 64:(si % 2) * 64 + 64, :]
                b.tt(dst, av[0:64, :], rbc[:], ALU.mult, reads=[avk, "rbc"],
                     writes=["attnfm"])
            for ncc in range(4):
                wb, wk = wos[0], "wos0"
                b.dma("gpsimd", wk, d3(wb[:], 512),
                      r3(I["out_t"][:, ncc * 512:(ncc + 1) * 512]), writes=[wk])
                for it_ in range(4):
                    pp, ppk = pO[it_ % 2], "pO%d" % (it_ % 2)
                    for fc in range(NF):
                        b.mm(pp[:], fmv(attnfm, 512)[:, fc, it_ * 128:(it_ + 1) * 128],
                             wb[:, fc * 512:(fc + 1) * 512], fc == 0, fc == NF - 1,
                             reads=["attnfm", wk], writes=[ppk])
                    b.tt(sO[it_][:, ncc * 512:(ncc + 1) * 512], pp[:],
                         ob_bc[:, ncc * 512:(ncc + 1) * 512], ALU.add,
                         reads=[ppk, "ob_bc"], writes=["sO%d" % it_])
            for it_ in range(4):
                ot = qc * 4 + it_
                b.dma("sync", "xo2", xo2[:], I["x1c"][128 + ot * 128:128 + (ot + 1) * 128, :],
                      writes=["xo2"])
                b.vcopy(gf32[:], g1[:, ot * D_:(ot + 1) * D_], reads=["g1"], writes=["gf32"])
                b.tt(sO[it_][:], sO[it_][:], gf32[:], ALU.mult,
                     reads=["sO%d" % it_, "gf32"], writes=["sO%d" % it_])
                b.tt(sO[it_][:], sO[it_][:], xo2[:], ALU.add,
                     reads=["sO%d" % it_, "xo2"], writes=["sO%d" % it_])
                b.dma("sync", "st%d" % it_, yc[ot * 128:(ot + 1) * 128, :], sO[it_][:],
                      reads=["sO%d" % it_], writes=[])



def build_fused():
    nc = bass.Bass("TRN2", target_bir_lowering=False, debug=False)
    I0, I1 = {}, {}

    def di(d, name, shape, dt=F32):
        d[name] = nc.dram_tensor(name, list(shape), dt, kind="ExternalInput").ap()

    di(I0, "xb", (T_, D_)); di(I0, "xc", (TOK, D_))
    di(I0, "wq_t", (D_, D_)); di(I0, "wk_t", (D_, 512)); di(I0, "wv_t", (D_, 512))
    di(I0, "wo_t", (D_, D_))
    di(I0, "bq", (1, D_)); di(I0, "bk", (1, 512)); di(I0, "bv", (1, 512))
    di(I0, "bo", (1, D_))
    di(I0, "rw", (D_, 8)); di(I0, "rb", (1, 8)); di(I0, "ew", (8, D_, D_))
    di(I0, "eb_cols", (128, 128)); di(I0, "pw", (D_, D_)); di(I0, "pb", (1, D_))
    di(I0, "nw", (1, D_))
    di(I0, "cos0", (T_, 32)); di(I0, "sin0", (T_, 32))
    di(I0, "cos0c", (TOK, 32)); di(I0, "sin0c", (TOK, 32))
    di(I0, "iden_bf", (128, 128), BF16); di(I0, "iden_f32", (128, 128))
    di(I0, "ones128", (1, 128)); di(I0, "sel8", (8, 8 * 128))

    di(I1, "qkv_t", (D_, 3072)); di(I1, "qkv_b", (1, 3072))
    di(I1, "out_t", (D_, D_)); di(I1, "out_b", (1, D_)); di(I1, "sinks", (1, 32))
    di(I1, "gw_t", (D_, 32)); di(I1, "gb", (1, 32))
    di(I1, "fc1_t", (D_, 1024)); di(I1, "fc2_t", (1024, D_))
    di(I1, "w1_t", (32, 1024, 1024)); di(I1, "w2_t", (32, 1024, 1024))
    di(I1, "su_t", (D_, 4096)); di(I1, "sd_t", (4096, D_))
    di(I1, "cos1", (TOK, 32)); di(I1, "sin1", (TOK, 32))
    di(I1, "masks", (2, 5, 128, 512), BF16)
    di(I1, "sel32", (32, 32 * 128))
    I1["iden_bf"] = I0["iden_bf"]
    I1["ones128"] = I0["ones128"]

    x1 = nc.dram_tensor("x1", [TOK, D_], F32, kind="Internal").ap()
    gdram = nc.dram_tensor("gscr", [TOK, D_], BF16, kind="Internal").ap()
    yc = nc.dram_tensor("yc", [OWN, D_], F32, kind="ExternalOutput").ap()
    I1["x1c"] = x1

    nodbg = lambda name, shape=None, dt=F32: None

    class _NCProxy:

        def __init__(self, real):
            self._nc = real

        def __getattr__(self, k):
            return getattr(self._nc, k)

        def sbuf_tensor(self, name, shape, dt, **kw):
            kw.setdefault("allow_name_mangling", True)
            return self._nc.sbuf_tensor(name, shape, dt, **kw)

        def psum_tensor(self, name, shape, dt=F32, **kw):
            kw.setdefault("allow_name_mangling", True)
            return self._nc.psum_tensor(name, shape, dt, **kw)

    ncp = _NCProxy(nc)
    b = Prog(nc)
    with ExitStack() as ctx0:
        emit_l0(b, I0, x1, gdram, nodbg, ncp, ctx0)
    b.fence()
    with ExitStack() as ctx1:
        emit_l1(b, I1, yc, nodbg, ncp, ctx1)
    return b.build()



def kernel(**inputs):
    maps0, maps1 = prep_inputs(inputs)
    nc = build_fused()
    in_maps = []
    for c in range(N_CORES):
        m = dict(maps0[c])
        m.update(maps1[c])
        in_maps.append(m)
    res = run_bass_kernel_spmd(nc, in_maps, core_ids=list(range(N_CORES)))
    out = np.zeros((B_, T_, D_), np.float32)
    for c in range(N_CORES):
        bb, Q0 = c // 4, (c % 4) * OWN
        out[bb, Q0:Q0 + OWN] = np.asarray(res.results[c]["yc"], np.float32)
    return out


# version 03
# speedup vs baseline: 1.0114x; 1.0085x over previous
import math
from contextlib import ExitStack

import numpy as np
import ml_dtypes

import concourse.bass as bass
import concourse.mybir as mybir
from concourse.bass_utils import run_bass_kernel_spmd

F32 = mybir.dt.float32
BF16 = mybir.dt.bfloat16
AF = mybir.ActivationFunctionType
ALU = mybir.AluOpType
AX = mybir.AxisListType
BF = ml_dtypes.bfloat16

N_CORES = 8
PHASES = 99
B_, T_, D_ = 2, 4096, 2048
OWN, HALO = 1024, 128
TOK = OWN + HALO
NT = TOK // 128
NTB = T_ // 128
NF = D_ // 128
CHW = [512, 512, 128]

_EV = [h for h in range(32) if (h // 4) % 2 == 0]
_OD = [h for h in range(32) if (h // 4) % 2 == 1]
SLOT2HEAD = []
for _t in range(16):
    SLOT2HEAD.append(_EV[_t])
    SLOT2HEAD.append(_OD[_t])


COMPUTE_ENGS = ("tensor", "vector", "scalar")
DMA_QUEUES = ("sync", "gpsimd")


class Prog:
    def __init__(self, nc):
        self.nc = nc
        self.ops = []
        self.last_w = {}
        self.readers = {}
        self.slot_fills = {}
        self.bulk_slots = set()
        self._last_per_key = {}
        self._fence_need = None
        self._fence_engines = set()

    def op(self, engine, fn, reads=(), writes=()):
        assert engine in COMPUTE_ENGS, engine
        return self._add(engine, fn, reads, writes, None)

    def fence(self):
        self._fence_need = set(self._last_per_key.values())
        self._fence_engines = set()

    def dma(self, queue, slot, out, in_, reads=(), writes=(), bulk=False):
        assert queue in DMA_QUEUES, queue
        if bulk:
            self.bulk_slots.add(slot)
        self.slot_fills[slot] = self.slot_fills.get(slot, 0) + 1
        fn = lambda e, o=out, i=in_: e.dma_start(o, i)
        return self._add(queue, fn, reads, writes, (slot, self.slot_fills[slot]))

    def mm(self, out, lhsT, rhs, start, stop, reads=(), writes=()):
        fn = lambda t, o=out, l=lhsT, r=rhs, a=start, c=stop: t.matmul(
            o, l, r, start=a, stop=c, skip_group_check=True)
        return self.op("tensor", fn, reads, writes)

    def trans(self, out, in_, iden, reads=(), writes=()):
        fn = lambda t, o=out, i=in_, d=iden: t.transpose(o, i, d)
        return self.op("tensor", fn, reads, writes)

    def tt(self, out, a, c, aluop, reads=(), writes=()):
        fn = lambda v, o=out, x=a, y=c, p=aluop: v.tensor_tensor(o, x, y, p)
        return self.op("vector", fn, reads, writes)

    def ts(self, out, in0, s1, s2, op0, op1=None, reads=(), writes=()):
        if s2 is None and op1 is None:
            fn = lambda v, o=out, i=in0, a=s1, p0=op0: v.tensor_single_scalar(o, i, a, p0)
        else:
            fn = lambda v, o=out, i=in0, a=s1, c=s2, p0=op0, p1=op1: v.tensor_scalar(
                o, i, a, c, p0, p1)
        return self.op("vector", fn, reads, writes)

    def red(self, out, in_, aluop=ALU.add, negate=False, reads=(), writes=()):
        fn = lambda v, o=out, i=in_, p=aluop, n=negate: v.tensor_reduce(
            o, i, axis=AX.X, op=p, negate=n)
        return self.op("vector", fn, reads, writes)

    def vcopy(self, out, in_, reads=(), writes=()):
        fn = lambda v, o=out, i=in_: v.tensor_copy(o, i)
        return self.op("vector", fn, reads, writes)

    def vmemset(self, ap, val, reads=(), writes=()):
        fn = lambda v, a=ap, x=val: v.memset(a, x)
        return self.op("vector", fn, reads, writes)

    def vrecip(self, out, in_, reads=(), writes=()):
        fn = lambda v, o=out, i=in_: v.reciprocal(o, i)
        return self.op("vector", fn, reads, writes)

    def act(self, out, in_, func, bias=0.0, scale=1.0, reads=(), writes=()):
        fn = lambda s, o=out, i=in_, f=func, bb=bias, c=scale: s.activation(
            o, i, f, bias=bb, scale=c)
        return self.op("scalar", fn, reads, writes)

    def _verify(self, ops):
        streams = {}
        for i, o in enumerate(ops):
            streams.setdefault(o["engine"], []).append(i)
        heads = {e: 0 for e in streams}
        sem = {}
        done = 0
        while done < len(ops):
            progressed = False
            for e, lst in streams.items():
                while heads[e] < len(lst):
                    o = ops[lst[heads[e]]]
                    if any(sem.get(k, 0) < v for k, v in o["waits"]):
                        break
                    if o["inc"]:
                        if o["slot"] is not None:
                            k = "slot:" + o["slot"][0]
                            sem[k] = sem.get(k, 0) + 16
                        else:
                            k = "eng:" + e
                            sem[k] = sem.get(k, 0) + 1
                    heads[e] += 1
                    done += 1
                    progressed = True
            if not progressed:
                msg = []
                for e, lst in streams.items():
                    if heads[e] < len(lst):
                        i = lst[heads[e]]
                        o = ops[i]
                        blocked = [(k, v, sem.get(k, 0)) for k, v in o["waits"]
                                   if sem.get(k, 0) < v]
                        msg.append(f"{e} stuck at op {i} waits {blocked}")
                raise RuntimeError("DEADLOCK:\n" + "\n".join(msg))

    def _add(self, engine, fn, reads, writes, dma_slot):
        i = len(self.ops)
        deps = set()
        if self._fence_need is not None and engine not in self._fence_engines:
            deps |= self._fence_need
            self._fence_engines.add(engine)
        for r in reads:
            j = self.last_w.get(r)
            if j is not None:
                deps.add(j)
        for w in writes:
            j = self.last_w.get(w)
            if j is not None:
                deps.add(j)
            deps.update(self.readers.get(w, ()))
        deps.discard(i)
        self.ops.append(dict(engine=engine, fn=fn, deps=deps, slot=dma_slot,
                             inc=False))
        for r in reads:
            self.readers.setdefault(r, []).append(i)
        for w in writes:
            self.last_w[w] = i
            self.readers[w] = []
        self._last_per_key[dma_slot[0] if dma_slot else engine] = i
        return i

    def build(self):
        nc = self.nc
        ops = self.ops
        for o in ops:
            pruned = {}
            for j in o["deps"]:
                oj = ops[j]
                if (oj["slot"] is None and o["slot"] is None
                        and oj["engine"] == "tensor" and o["engine"] == "tensor"):
                    continue
                key = oj["slot"][0] if oj["slot"] else oj["engine"]
                if key not in pruned or j > pruned[key]:
                    pruned[key] = j
            o["rdeps"] = pruned
        for o in ops:
            for j in o["rdeps"].values():
                ops[j]["inc"] = True
        for o in ops:
            if o["slot"] is not None:
                o["inc"] = True
        cnt, run = {}, {}
        for i, o in enumerate(ops):
            if o["slot"] is None and o["inc"]:
                e = o["engine"]
                run[e] = run.get(e, 0) + 1
                cnt[i] = run[e]
        final_slot = dict(self.slot_fills)
        for o in ops:
            waits = []
            for key, j in sorted(o["rdeps"].items()):
                oj = ops[j]
                if oj["slot"] is not None:
                    slot, fill = oj["slot"]
                    v = 16 * (final_slot[slot] if slot in self.bulk_slots else fill)
                    waits.append(("slot:" + slot, v))
                else:
                    waits.append(("eng:" + oj["engine"], cnt[j]))
            o["waits"] = waits

        self._verify(ops)
        with ExitStack() as ctx:
            sems = {}
            for e in COMPUTE_ENGS:
                sems["eng:" + e] = ctx.enter_context(nc.semaphore("sem_" + e))
            for s in self.slot_fills:
                sems["slot:" + s] = ctx.enter_context(nc.semaphore("sem_" + s))
            block = ctx.enter_context(nc.Block())
            water = {}

            def emit_for(engine_name, eng):
                for o in ops:
                    if o["engine"] != engine_name:
                        continue
                    for key, v in o["waits"]:
                        wk = (engine_name, key)
                        if water.get(wk, 0) >= v:
                            continue
                        water[wk] = v
                        eng.wait_ge(sems[key], v)
                    inst = o["fn"](eng)
                    if o["inc"]:
                        if o["slot"] is not None:
                            inst.then_inc(sems["slot:" + o["slot"][0]], 16)
                        else:
                            inst.then_inc(sems["eng:" + engine_name], 1)
                if engine_name == "sync":
                    for s, n in sorted(self.slot_fills.items()):
                        wk = (engine_name, "slot:" + s)
                        if water.get(wk, 0) < 16 * n:
                            eng.wait_ge(sems["slot:" + s], 16 * n)

            @block.tensor
            def _(t):
                emit_for("tensor", t)

            @block.vector
            def _(v):
                emit_for("vector", v)

            @block.scalar
            def _(s):
                emit_for("scalar", s)

            @block.sync
            def _(s):
                emit_for("sync", s)

            @block.gpsimd
            def _(g):
                emit_for("gpsimd", g)
        return nc



def _rope_tables_l0():
    HD = 64
    inv_freq = 1.0 / (10000.0 ** (np.arange(0, HD, 2, dtype=np.float32) / HD))
    ang = np.arange(T_, dtype=np.float32)[:, None] * inv_freq[None, :]
    return np.cos(ang).astype(np.float32), np.sin(ang).astype(np.float32)


def _rope_tables_l1():
    HEAD_DIM = 64
    ROPE_THETA, SCALING, NTK_ALPHA, NTK_BETA, ICL = 150000.0, 32.0, 1.0, 32.0, 1024
    d_half = HEAD_DIM // 2
    freq = ROPE_THETA ** (np.arange(0, HEAD_DIM, 2, dtype=np.float32) / HEAD_DIM)
    concentration = 0.1 * math.log(SCALING) + 1.0
    low = d_half * math.log(ICL / (NTK_BETA * 2 * math.pi)) / math.log(ROPE_THETA)
    high = d_half * math.log(ICL / (NTK_ALPHA * 2 * math.pi)) / math.log(ROPE_THETA)
    interpolation = 1.0 / (SCALING * freq)
    extrapolation = 1.0 / freq
    ramp = np.clip((np.arange(d_half, dtype=np.float32) - low) / (high - low), 0.0, 1.0)
    mask = 1.0 - ramp
    inv_freq = interpolation * (1.0 - mask) + extrapolation * mask
    freqs = np.outer(np.arange(T_, dtype=np.float32), inv_freq)
    return ((np.cos(freqs) * concentration).astype(np.float32),
            (np.sin(freqs) * concentration).astype(np.float32))


def _l1_masks(core):
    Q0 = (core % 4) * OWN
    qp = np.arange(512)[None, :]
    kp = np.arange(128)[:, None]
    out = np.zeros((2, 5, 128, 512), dtype=BF)
    for qc in range(2):
        for m in range(5):
            r = 128 * m + kp - qp
            valid = (r >= 1) & (r <= 128)
            if Q0 == 0:
                kl = qc * 512 + 128 * m + kp + 0 * qp
                valid = valid & (kl >= 128)
            out[qc, m] = valid.astype(np.float32).astype(BF)
    return out


def _pad_slice(x, lo, hi):
    pad = max(0, -lo)
    body = x[max(lo, 0):hi]
    if pad:
        return np.concatenate([np.zeros((pad,) + x.shape[1:], x.dtype), body], 0)
    return np.ascontiguousarray(body)


def prep_inputs(inp):
    f32 = np.float32
    x = np.asarray(inp["L0_attn_x"], f32)
    cos0, sin0 = _rope_tables_l0()
    cos1, sin1 = _rope_tables_l1()
    iden_bf = np.eye(128, dtype=BF)
    iden_f32 = np.eye(128, dtype=f32)
    ones128 = np.ones((1, 128), f32)
    sel8 = np.zeros((8, 8 * 128), f32)
    for e in range(8):
        sel8[e, e * 128:(e + 1) * 128] = 1.0
    sel32 = np.zeros((32, 32 * 128), f32)
    for e in range(32):
        sel32[e, e * 128:(e + 1) * 128] = 1.0

    ebm = np.asarray(inp["L0_ffn_expert_b"], f32)
    eb_cols = np.ascontiguousarray(
        ebm.reshape(8, 16, 128).transpose(2, 1, 0).reshape(128, 128))

    wq_t0 = np.asarray(inp["L0_attn_Wq"], f32).T.reshape(D_, 32, 64)
    wq_tp = np.ascontiguousarray(wq_t0[:, SLOT2HEAD, :].reshape(D_, D_))
    bq0 = np.asarray(inp["L0_attn_bq"], f32).reshape(32, 64)
    bqp = np.ascontiguousarray(bq0[SLOT2HEAD].reshape(1, D_))
    wo_t0 = np.asarray(inp["L0_attn_Wo"], f32).T.reshape(32, 64, D_)
    wo_tp = np.ascontiguousarray(wo_t0[SLOT2HEAD].reshape(D_, D_))
    shared0 = dict(
        wq_t=wq_tp,
        wk_t=np.ascontiguousarray(np.asarray(inp["L0_attn_Wk"], f32).T),
        wv_t=np.ascontiguousarray(np.asarray(inp["L0_attn_Wv"], f32).T),
        wo_t=wo_tp,
        bq=bqp,
        bk=np.asarray(inp["L0_attn_bk"], f32).reshape(1, -1),
        bv=np.asarray(inp["L0_attn_bv"], f32).reshape(1, -1),
        bo=np.asarray(inp["L0_attn_bo"], f32).reshape(1, -1),
        rw=np.ascontiguousarray(np.asarray(inp["L0_ffn_router_w"], f32)),
        rb=np.asarray(inp["L0_ffn_router_b"], f32).reshape(1, -1),
        ew=np.ascontiguousarray(np.asarray(inp["L0_ffn_expert_w"], f32)),
        eb_cols=eb_cols,
        pw=np.ascontiguousarray(np.asarray(inp["L0_ffn_proj_w"], f32)),
        pb=np.asarray(inp["L0_ffn_proj_b"], f32).reshape(1, -1),
        nw=np.asarray(inp["L0_ffn_norm_w"], f32).reshape(1, -1),
        iden_bf=iden_bf, iden_f32=iden_f32, ones128=ones128, sel8=sel8,
        cos0=cos0, sin0=sin0)
    qkv_t0 = np.asarray(inp["L1_attn_qkv_w"], f32).T
    qp = qkv_t0[:, 0:2048].reshape(D_, 32, 64)
    qkv_tp = np.concatenate([qp[:, SLOT2HEAD, :].reshape(D_, 2048),
                             qkv_t0[:, 2048:]], axis=1)
    qb0 = np.asarray(inp["L1_attn_qkv_b"], f32).reshape(-1)
    qbq = qb0[0:2048].reshape(32, 64)
    qkv_bp = np.concatenate([qbq[SLOT2HEAD].reshape(-1), qb0[2048:]])
    out_t0 = np.asarray(inp["L1_attn_out_w"], f32).T.reshape(32, 64, D_)
    out_tp = np.ascontiguousarray(out_t0[SLOT2HEAD].reshape(D_, D_))
    shared1 = dict(
        qkv_t=np.ascontiguousarray(qkv_tp),
        qkv_b=np.ascontiguousarray(qkv_bp.reshape(1, -1)),
        out_t=out_tp,
        out_b=np.asarray(inp["L1_attn_out_b"], f32).reshape(1, -1),
        sinks=np.asarray(inp["L1_attn_sinks"], f32).reshape(1, -1),
        gw_t=np.ascontiguousarray(np.asarray(inp["L1_ffn_gate_w"], f32).T),
        gb=np.asarray(inp["L1_ffn_gate_bias"], f32).reshape(1, -1),
        fc1_t=np.ascontiguousarray(np.asarray(inp["L1_ffn_fc1_w"], f32).T),
        fc2_t=np.ascontiguousarray(np.asarray(inp["L1_ffn_fc2_w"], f32).T),
        w1_t=np.ascontiguousarray(np.asarray(inp["L1_ffn_w1"], f32).transpose(0, 2, 1)),
        w2_t=np.ascontiguousarray(np.asarray(inp["L1_ffn_w2"], f32).transpose(0, 2, 1)),
        su_t=np.ascontiguousarray(np.asarray(inp["L1_ffn_shared_up_w"], f32).T),
        sd_t=np.ascontiguousarray(np.asarray(inp["L1_ffn_shared_down_w"], f32).T),
        iden_bf=iden_bf, iden_f32=iden_f32, ones128=ones128, sel32=sel32)

    maps0, maps1 = [], []
    for c in range(N_CORES):
        bb, Q0 = c // 4, (c % 4) * OWN
        m0 = dict(shared0)
        m0["xb"] = np.ascontiguousarray(x[bb])
        m0["xc"] = _pad_slice(x[bb], Q0 - HALO, Q0 + OWN)
        m0["cos0c"] = _pad_slice(cos0, Q0 - HALO, Q0 + OWN)
        m0["sin0c"] = _pad_slice(sin0, Q0 - HALO, Q0 + OWN)
        maps0.append(m0)
        m1 = dict(shared1)
        m1["cos1"] = _pad_slice(cos1, Q0 - HALO, Q0 + OWN)
        m1["sin1"] = _pad_slice(sin1, Q0 - HALO, Q0 + OWN)
        m1["masks"] = _l1_masks(c)
        maps1.append(m1)
    return maps0, maps1



def r3(src2d, p=128):
    return src2d.rearrange("(n p) d -> p n d", p=p)


def d3(dstbuf_slice, d):
    return dstbuf_slice.rearrange("p (n d) -> p n d", d=d)



def fmv(t, total):
    return t[:, :].rearrange("p (f c) -> p f c", c=total)


def emit_ln_tile(b, C, xap, xkey, st, stk, out_bf, outkey, eps,
                 mcol=None, rcol=None, statkeys=("m_own", "r_own")):
    scr, sck = C["lnscr"], "lnscr"
    b.red(st[:, 0:1], xap, reads=[xkey], writes=[stk])
    b.tt(scr[:, 0:D_], xap, xap, ALU.mult, reads=[xkey], writes=[sck])
    b.red(st[:, 1:2], scr[:, 0:D_], reads=[sck], writes=[stk])
    b.ts(st[:, 0:1], st[:, 0:1], 1.0 / D_, None, ALU.mult, reads=[stk], writes=[stk])
    b.tt(st[:, 2:3], st[:, 0:1], st[:, 0:1], ALU.mult, reads=[stk], writes=[stk])
    b.ts(st[:, 1:2], st[:, 1:2], 1.0 / D_, None, ALU.mult, reads=[stk], writes=[stk])
    b.tt(st[:, 1:2], st[:, 1:2], st[:, 2:3], ALU.subtract, reads=[stk], writes=[stk])
    b.ts(st[:, 1:2], st[:, 1:2], float(eps), None, ALU.add, reads=[stk], writes=[stk])
    b.act(st[:, 2:3], st[:, 1:2], AF.Ln, reads=[stk], writes=[stk])
    b.act(st[:, 3:4], st[:, 2:3], AF.Exp, scale=-0.5, reads=[stk], writes=[stk])
    b.ts(out_bf, xap, st[:, 0:1], st[:, 3:4], ALU.subtract, ALU.mult,
         reads=[xkey, stk], writes=[outkey])
    if mcol is not None:
        b.vcopy(mcol, st[:, 0:1], reads=[stk], writes=[statkeys[0]])
        b.vcopy(rcol, st[:, 3:4], reads=[stk], writes=[statkeys[1]])


def emit_transpose16(b, C, src_bf, srckey, dstv, dstkey, nft=16):
    for g in range((nft + 7) // 8):
        n = min(8, nft - g * 8)
        pt, ptk = C["pt%d" % (g % 2)], "pt%d" % (g % 2)
        for k in range(n):
            ft = g * 8 + k
            b.trans(pt[:, k * 128:(k + 1) * 128], src_bf[:, ft * 128:(ft + 1) * 128],
                    C["idb"][:], reads=[srckey, "idb"], writes=[ptk])
        b.vcopy(dstv(g, n), pt[:, 0:n * 128].rearrange("p (f c) -> p f c", c=128),
                reads=[ptk], writes=[dstkey])


def emit_rope(b, C, src, srckey, dst, dstkey, costile, sintile, nheads, tmp, tmpkey,
              ckeys=("cosb", "sinb")):
    ca = costile.rearrange("p (o d) -> p o d", o=1).broadcast_to([128, nheads, 32])
    sa = sintile.rearrange("p (o d) -> p o d", o=1).broadcast_to([128, nheads, 32])
    s3 = src.rearrange("p (h d) -> p h d", d=64)
    d3 = dst.rearrange("p (h d) -> p h d", d=64)
    t3 = tmp.rearrange("p (h d) -> p h d", d=64)
    x1, x2 = s3[:, :, 0:32], s3[:, :, 32:64]
    b.tt(t3[:, :, 0:32], x1, ca, ALU.mult, reads=[srckey, ckeys[0]], writes=[tmpkey])
    b.tt(t3[:, :, 32:64], x2, sa, ALU.mult, reads=[srckey, ckeys[1]], writes=[tmpkey])
    b.tt(d3[:, :, 0:32], t3[:, :, 0:32], t3[:, :, 32:64], ALU.subtract,
         reads=[tmpkey], writes=[dstkey])
    b.tt(t3[:, :, 0:32], x2, ca, ALU.mult, reads=[srckey, ckeys[0]], writes=[tmpkey])
    b.tt(t3[:, :, 32:64], x1, sa, ALU.mult, reads=[srckey, ckeys[1]], writes=[tmpkey])
    b.tt(d3[:, :, 32:64], t3[:, :, 0:32], t3[:, :, 32:64], ALU.add,
         reads=[tmpkey], writes=[dstkey])


def emit_global_max(b, nc, ctx, tag, col, colkey, out11, outkey, row, rowkey):
    dscr = nc.dram_tensor("gmax_" + tag, [1, 128], F32, kind="Internal").ap()
    b.dma("sync", "gm" + tag, dscr.rearrange("o p -> p o"), col,
          reads=[colkey], writes=["gmd" + tag])
    b.dma("sync", "gm" + tag + "b", row, dscr, reads=["gmd" + tag], writes=[rowkey])
    b.red(out11, row, aluop=ALU.max, reads=[rowkey], writes=[outkey])



def build_l0(debug=()):
    nc = bass.Bass("TRN2", target_bir_lowering=False, debug=False)
    I = {}

    def di(name, shape, dt=F32):
        I[name] = nc.dram_tensor(name, list(shape), dt, kind="ExternalInput").ap()

    di("xb", (T_, D_)); di("xc", (TOK, D_))
    di("wq_t", (D_, D_)); di("wk_t", (D_, 512)); di("wv_t", (D_, 512))
    di("wo_t", (D_, D_))
    di("bq", (1, D_)); di("bk", (1, 512)); di("bv", (1, 512)); di("bo", (1, D_))
    di("rw", (D_, 8)); di("rb", (1, 8)); di("ew", (8, D_, D_))
    di("eb_cols", (128, 128)); di("pw", (D_, D_)); di("pb", (1, D_)); di("nw", (1, D_))
    di("cos0", (T_, 32)); di("sin0", (T_, 32))
    di("cos0c", (TOK, 32)); di("sin0c", (TOK, 32))
    di("iden_bf", (128, 128), BF16); di("iden_f32", (128, 128))
    di("ones128", (1, 128)); di("sel8", (8, 8 * 128))

    x1 = nc.dram_tensor("x1", [TOK, D_], F32, kind="ExternalOutput").ap()
    gdram = nc.dram_tensor("gscr", [TOK, D_], BF16, kind="Internal").ap()
    DBG = {}

    def dbg(name, shape=None, dt=F32):
        if name in debug and shape is not None and name not in DBG:
            DBG[name] = nc.dram_tensor("dbg_" + name, list(shape), dt,
                                       kind="ExternalOutput").ap()
        return DBG.get(name)

    b = Prog(nc)
    with ExitStack() as ctx:
        emit_l0(b, I, x1, gdram, dbg, nc, ctx)
    return b.build(), sorted(DBG)


def emit_l0(b, I, x1, gdram, dbg, nc, ctx):
    def sb(name, shape, dt=F32):
        return ctx.enter_context(nc.sbuf_tensor(name, list(shape), dt))

    def psg(st, name, shape, dt=F32):
        return st.enter_context(nc.psum_tensor(name, list(shape), dt))

    C = {}
    K_fm = sb("K_fm", [128, 4 * T_], BF16)
    V_tm = sb("V_tm", [128, NTB * 520], BF16)
    m_own = sb("m_own", [128, NT]); r_own = sb("r_own", [128, NT])
    q2acc = sb("q2acc", [128, 32]); k2acc = sb("k2acc", [128, 8])
    negB = sb("negB", [128, 1])
    C["idb"] = sb("idb", [128, 128], BF16)
    onesr = sb("onesr", [1, 128])

    b.dma("sync", "c0", C["idb"][:], I["iden_bf"], writes=["idb"], bulk=True)
    b.dma("sync", "c0", onesr[:], I["ones128"], writes=["onesr"], bulk=True)
    b.vmemset(q2acc[:], 0.0, writes=["q2acc"])
    b.vmemset(k2acc[:], 0.0, writes=["k2acc"])

    sComb = ExitStack()
    with ExitStack() as sAQ:
        xn_fm = sAQ.enter_context(nc.sbuf_tensor("xn_fm", [128, NF * TOK], BF16))
        wT = sAQ.enter_context(nc.sbuf_tensor("wT", [8, TOK], F32))
        scs = sAQ.enter_context(ExitStack())
        cosb = scs.enter_context(nc.sbuf_tensor("cosb", [128, NTB * 32], F32))
        sinb = scs.enter_context(nc.sbuf_tensor("sinb", [128, NTB * 32], F32))
        b.dma("sync", "c0", d3(cosb[:], 32), r3(I["cos0"]),
              writes=["cosb"], bulk=True)
        b.dma("sync", "c0", d3(sinb[:], 32), r3(I["sin0"]),
              writes=["sinb"], bulk=True)

        with ExitStack() as sA:
            def asb(name, shape, dt=F32):
                return sA.enter_context(nc.sbuf_tensor(name, list(shape), dt))

            xt = [asb("xt%d" % i, [128, D_]) for i in range(2)]
            C["lnscr"] = asb("lnscr", [128, D_])
            wkv = asb("wkv", [128, NF * 1024], BF16)
            xnbf = [asb("xnbf0", [128, D_], BF16)]
            stat = [asb("stat%d" % i, [128, 4]) for i in range(2)]
            xnfm_kv = asb("xnfm_kv", [128, NF * 512], BF16)
            kvscr = [asb("kvscr0", [128, 512])]
            kvtmp = [asb("kvtmp0", [128, 512])]
            krs = [asb("krs%d" % i, [128, 512], BF16) for i in range(2)]
            bk_bc = asb("bk_bc", [128, 512])
            bv_bc = asb("bv_bc", [128, 512])
            pkv = [psg(sA, "pkv%d" % i, [128, 512]) for i in range(2)]
            C["pt0"] = psg(sA, "pt0", [128, 1024], BF16)
            C["pt1"] = psg(sA, "pt1", [128, 1024], BF16)

            b.dma("gpsimd", "wkv", d3(wkv[:, 0:NF * 512], 512), r3(I["wk_t"]),
                  writes=["wkv_k"], bulk=True)
            b.dma("gpsimd", "wkv", d3(wkv[:, NF * 512:], 512), r3(I["wv_t"]),
                  writes=["wkv_v"], bulk=True)
            b.dma("sync", "cA", bk_bc[:], I["bk"].broadcast_to([128, 512]),
                  writes=["bk_bc"], bulk=True)
            b.dma("sync", "cA", bv_bc[:], I["bv"].broadcast_to([128, 512]),
                  writes=["bv_bc"], bulk=True)

            for ch in range(NTB // 4):
                for it in range(4):
                    tt = ch * 4 + it
                    bi = tt % 2
                    xtb, xk = xt[bi], "xt%d" % bi
                    b.dma("sync", xk, xtb[:], I["xb"][tt * 128:(tt + 1) * 128, :],
                          writes=[xk])
                    emit_ln_tile(b, C, xtb[:], xk, stat[bi], "stat%d" % bi,
                                 xnbf[0][:], "xnbf0", 1e-5)
                    kvv = fmv(xnfm_kv, 512)
                    emit_transpose16(
                        b, C, xnbf[0][:], "xnbf0",
                        (lambda g, n, it_=it: kvv[:, g * 8:g * 8 + n,
                                                  it_ * 128:(it_ + 1) * 128]),
                        "xnfm_kv")
                for it in range(4):
                    tt = ch * 4 + it
                    bi = it % 2
                    pk, pkk = pkv[bi], "pkv%d" % bi
                    scr, sck = kvscr[0], "kvscr0"
                    tmp, tmk = kvtmp[0], "kvtmp0"
                    for fc in range(NF):
                        b.mm(pk[:], xnfm_kv[:, fc * 512 + it * 128:fc * 512 + (it + 1) * 128],
                             wkv[:, fc * 512:(fc + 1) * 512], fc == 0, fc == NF - 1,
                             reads=["xnfm_kv", "wkv_k"], writes=[pkk])
                    b.tt(scr[:], pk[:], bk_bc[:], ALU.add, reads=[pkk, "bk_bc"], writes=[sck])
                    emit_rope(b, C, scr[:], sck, krs[bi][:], "krs%d" % bi,
                              cosb[:, tt * 32:(tt + 1) * 32], sinb[:, tt * 32:(tt + 1) * 32],
                              8, tmp[:], tmk)
                    pt, ptk = C["pt%d" % bi], "pt%d" % bi
                    for k in range(4):
                        b.trans(pt[:, k * 128:(k + 1) * 128], krs[bi][:, k * 128:(k + 1) * 128],
                                C["idb"][:], reads=["krs%d" % bi, "idb"], writes=[ptk])
                    b.vcopy(fmv(K_fm, T_)[:, 0:4, tt * 128:(tt + 1) * 128],
                            pt[:, 0:512].rearrange("p (f c) -> p f c", c=128),
                            reads=[ptk], writes=["K_fm"])
                    b.tt(tmp[:], krs[bi][:], krs[bi][:], ALU.mult,
                         reads=["krs%d" % bi], writes=[tmk])
                    b.red(scr[:, 0:8], tmp[:].rearrange("p (h d) -> p h d", d=64),
                          aluop=ALU.max, reads=[tmk], writes=[sck])
                    b.tt(k2acc[:], k2acc[:], scr[:, 0:8], ALU.max,
                         reads=[sck, "k2acc"], writes=["k2acc"])
                    for fc in range(NF):
                        b.mm(pk[:], xnfm_kv[:, fc * 512 + it * 128:fc * 512 + (it + 1) * 128],
                             wkv[:, NF * 512 + fc * 512:NF * 512 + (fc + 1) * 512],
                             fc == 0, fc == NF - 1,
                             reads=["xnfm_kv", "wkv_v"], writes=[pkk])
                    vt = V_tm[:, tt * 520:(tt + 1) * 520].rearrange("p (h c) -> p h c", c=65)
                    b.tt(vt[:, :, 0:64], pk[:].rearrange("p (h c) -> p h c", c=64),
                         bv_bc[:].rearrange("p (h c) -> p h c", c=64), ALU.add,
                         reads=[pkk, "bv_bc"], writes=["V_tm"])
                    b.vmemset(vt[:, :, 64:65], 1.0, writes=["V_tm"])

            for ot in range(NT):
                bi = ot % 2
                xtb, xk = xt[bi], "xt%d" % bi
                b.dma("sync", xk, xtb[:], I["xc"][ot * 128:(ot + 1) * 128, :],
                      writes=[xk])
                emit_ln_tile(b, C, xtb[:], xk, stat[bi], "stat%d" % bi,
                             xnbf[0][:], "xnbf0", 1e-5,
                             mcol=m_own[:, ot:ot + 1], rcol=r_own[:, ot:ot + 1])
                xv = fmv(xn_fm, TOK)
                emit_transpose16(
                    b, C, xnbf[0][:], "xnbf0",
                    (lambda g, n, ot_=ot: xv[:, g * 8:g * 8 + n, ot_ * 128:(ot_ + 1) * 128]),
                    "xn_fm")

        b.fence()
        if dbg("xn_fm", (128, NF * TOK)):
            b.dma("gpsimd", "dbg1", dbg("xn_fm"), xn_fm[:], reads=["xn_fm"], writes=[])
        if dbg("K_fm", (128, 4 * T_)):
            b.dma("gpsimd", "dbg2", dbg("K_fm"), K_fm[:], reads=["K_fm"], writes=[])
        if dbg("V_tm", (128, NTB * 520)):
            b.dma("gpsimd", "dbg3", dbg("V_tm"), V_tm[:], reads=["V_tm"], writes=[])

        if PHASES < 2:
            return
        with ExitStack() as sR:
            rw_b = sR.enter_context(nc.sbuf_tensor("rw_b", [128, NF * 8], BF16))
            rw_f = sR.enter_context(nc.sbuf_tensor("rw_f", [128, NF * 8], F32))
            rb_bc = sR.enter_context(nc.sbuf_tensor("rb_bc", [128, 8], F32))
            rgb = sR.enter_context(nc.sbuf_tensor("rgb", [128, 8], BF16))
            rscr = sR.enter_context(nc.sbuf_tensor("rscr", [128, 32], F32))
            rpsum_t = psg(sR, "rpsum", [128, 512])
            rpsum = rpsum_t[:, 0:8]
            rtp_t = psg(sR, "rtp", [128, 1024], BF16)
            rtp = rtp_t[0:8, 0:128]
            b.dma("sync", "cRg", d3(rw_f[:], 8), r3(I["rw"]),
                  writes=["rw_f"], bulk=True)
            b.vcopy(rw_b[:], rw_f[:], reads=["rw_f"], writes=["rw_b"])
            b.dma("sync", "cR", rb_bc[:], I["rb"].broadcast_to([128, 8]),
                  writes=["rb_bc"], bulk=True)

            for ot in range(NT):
                for fc in range(NF):
                    b.mm(rpsum, fmv(xn_fm, TOK)[:, fc, ot * 128:(ot + 1) * 128],
                         rw_b[:, fc * 8:(fc + 1) * 8], fc == 0, fc == NF - 1,
                         reads=["xn_fm", "rw_b"], writes=["rpsum"])
                l, lk = rscr[:, 0:8], "rscr"
                b.tt(l, rpsum, rb_bc[:], ALU.add, reads=["rpsum", "rb_bc"], writes=[lk])
                m1, nm1, m2 = rscr[:, 8:9], rscr[:, 9:10], rscr[:, 10:11]
                e21, den, rcp = rscr[:, 11:12], rscr[:, 12:13], rscr[:, 13:14]
                tq, ge = rscr[:, 16:24], rscr[:, 24:32]
                b.red(m1, l, aluop=ALU.max, reads=[lk], writes=[lk])
                b.red(nm1, l, aluop=ALU.max, negate=True, reads=[lk], writes=[lk])
                b.ts(tq, l, m1, -1e30, ALU.is_equal, ALU.mult, reads=[lk], writes=[lk])
                b.tt(tq, tq, l, ALU.add, reads=[lk], writes=[lk])
                b.red(m2, tq, aluop=ALU.max, reads=[lk], writes=[lk])
                b.act(e21, m2, AF.Exp, bias=nm1, reads=[lk], writes=[lk])
                b.ts(den, e21, 1.0, None, ALU.add, reads=[lk], writes=[lk])
                b.vrecip(rcp, den, reads=[lk], writes=[lk])
                b.act(tq, l, AF.Exp, bias=nm1, reads=[lk], writes=[lk])
                b.ts(ge, l, m2, None, ALU.is_ge, reads=[lk], writes=[lk])
                b.tt(ge, ge, tq, ALU.mult, reads=[lk], writes=[lk])
                b.ts(ge, ge, rcp, None, ALU.mult, reads=[lk], writes=[lk])
                if dbg("rdump", (128, NT * 32)):
                    b.dma("sync", "dbgR", dbg("rdump")[:, ot * 32:(ot + 1) * 32],
                          rscr[:], reads=[lk], writes=[])
                b.vcopy(rgb[:], ge, reads=[lk], writes=["rgb"])
                b.trans(rtp, rgb[:], C["idb"][:], reads=["rgb", "idb"], writes=["rtp"])
                b.vcopy(wT[:, ot * 128:(ot + 1) * 128], rtp, reads=["rtp"], writes=["wT"])

        if dbg("wT", (8, TOK)):
            b.dma("sync", "dbg4", dbg("wT"), wT[:], reads=["wT"], writes=[])

        b.fence()
        if PHASES < 3:
            return
        Q_fm = ctx.enter_context(nc.sbuf_tensor("Q_fm", [128, NF * TOK], BF16,
                                                side="right"))
        with ExitStack() as sQ:
            wqs = [sQ.enter_context(nc.sbuf_tensor("wqs%d" % i, [128, NF * 512], BF16))
                   for i in range(2)]
            qscr = [sQ.enter_context(nc.sbuf_tensor("qscr%d" % i, [128, 512], F32))
                    for i in range(2)]
            qtmp = [sQ.enter_context(nc.sbuf_tensor("qtmp%d" % i, [128, 512], F32))
                    for i in range(2)]
            qrb = [sQ.enter_context(nc.sbuf_tensor("qrb%d" % i, [128, 512], BF16))
                   for i in range(2)]
            bq_bc = sQ.enter_context(nc.sbuf_tensor("bq_bc", [128, D_], F32))
            cosc = sQ.enter_context(nc.sbuf_tensor("cosc", [128, NT * 32], F32))
            sinc = sQ.enter_context(nc.sbuf_tensor("sinc", [128, NT * 32], F32))
            ppq = [psg(sQ, "ppq%d" % i, [128, 512]) for i in range(2)]
            C["pt0"] = psg(sQ, "pt0b", [128, 1024], BF16)
            C["pt1"] = psg(sQ, "pt1b", [128, 1024], BF16)
            b.dma("sync", "cQ", bq_bc[:], I["bq"].broadcast_to([128, D_]),
                  writes=["bq_bc"], bulk=True)
            b.dma("sync", "cQ", d3(cosc[:], 32), r3(I["cos0c"]),
                  writes=["cosc"], bulk=True)
            b.dma("sync", "cQ", d3(sinc[:], 32), r3(I["sin0c"]),
                  writes=["sinc"], bulk=True)
            for ncc in range(4):
                wi = ncc % 2
                wb, wk = wqs[wi], "wqs%d" % wi
                b.dma("gpsimd", wk, d3(wb[:], 512), r3(I["wq_t"][:, ncc * 512:(ncc + 1) * 512]),
                      writes=[wk])
                for ot in range(NT):
                    bi = ot % 2
                    pp, ppk = ppq[bi], "ppq%d" % bi
                    for fc in range(NF):
                        b.mm(pp[:], fmv(xn_fm, TOK)[:, fc, ot * 128:(ot + 1) * 128],
                             wb[:, fc * 512:(fc + 1) * 512], fc == 0, fc == NF - 1,
                             reads=["xn_fm", wk], writes=[ppk])
                    b.tt(qscr[bi][:], pp[:], bq_bc[:, ncc * 512:(ncc + 1) * 512], ALU.add,
                         reads=[ppk, "bq_bc"], writes=["qscr%d" % bi])
                    emit_rope(b, C, qscr[bi][:], "qscr%d" % bi, qrb[bi][:], "qrb%d" % bi,
                              cosc[:, ot * 32:(ot + 1) * 32], sinc[:, ot * 32:(ot + 1) * 32],
                              8, qtmp[bi][:], "qtmp%d" % bi, ckeys=("cosc", "sinc"))
                    b.tt(qtmp[bi][:], qrb[bi][:], qrb[bi][:], ALU.mult,
                         reads=["qrb%d" % bi], writes=["qtmp%d" % bi])
                    b.red(qscr[bi][:, 0:8], qtmp[bi][:].rearrange("p (h d) -> p h d", d=64),
                          aluop=ALU.max, reads=["qtmp%d" % bi], writes=["qscr%d" % bi])
                    b.tt(q2acc[:, ncc * 8:(ncc + 1) * 8], q2acc[:, ncc * 8:(ncc + 1) * 8],
                         qscr[bi][:, 0:8], ALU.max,
                         reads=["qscr%d" % bi, "q2acc"], writes=["q2acc"])
                    pt, ptk = C["pt%d" % bi], "pt%d" % bi
                    for k in range(4):
                        b.trans(pt[:, k * 128:(k + 1) * 128], qrb[bi][:, k * 128:(k + 1) * 128],
                                C["idb"][:], reads=["qrb%d" % bi, "idb"], writes=[ptk])
                    b.vcopy(fmv(Q_fm, TOK)[:, ncc * 4:(ncc + 1) * 4, ot * 128:(ot + 1) * 128],
                            pt[:, 0:512].rearrange("p (f c) -> p f c", c=128),
                            reads=[ptk], writes=["Q_fm"])

        b.fence()
        scs.close()

        if PHASES < 4:
            return
        comb = sComb.enter_context(nc.sbuf_tensor("comb", [128, NF * TOK], BF16,
                                                   side="right"))
        combv = fmv(comb, TOK)
        with ExitStack() as sM:
            wes = [sM.enter_context(nc.sbuf_tensor("wes%d" % i, [128, NF * 256], BF16))
                   for i in range(2)]
            she = [sM.enter_context(nc.sbuf_tensor("she0", [128, 512], BF16))] * 2
            mtmp = [sM.enter_context(nc.sbuf_tensor("mtmp0", [128, 512], BF16))] * 2
            wbcs = sM.enter_context(nc.sbuf_tensor("wbcs", [128, TOK], BF16))
            sel8b = sM.enter_context(nc.sbuf_tensor("sel8b", [8, 8 * 128], F32))
            eb_c = sM.enter_context(nc.sbuf_tensor("eb_c", [128, 128], F32))
            pH = [psg(sM, "pH%d" % i, [128, 512]) for i in range(2)]
            pwb = psg(sM, "pwb", [128, 512])
            b.dma("sync", "cM", sel8b[:], I["sel8"], writes=["sel8b"], bulk=True)
            b.dma("sync", "cM", eb_c[:], I["eb_cols"], writes=["eb_c"], bulk=True)

            for e in range(8):
                for ci, tw in enumerate(CHW):
                    b.mm(pwb[:, 0:tw], sel8b[:, e * 128:(e + 1) * 128],
                         wT[:, ci * 512:ci * 512 + tw], True, True,
                         reads=["sel8b", "wT"], writes=["pwb"])
                    b.vcopy(wbcs[:, ci * 512:ci * 512 + tw], pwb[:, 0:tw],
                            reads=["pwb"], writes=["wbcs"])
                for ft4 in range(8):
                    wi = (e * 8 + ft4) % 2
                    wb, wk = wes[wi], "wes%d" % wi
                    b.dma("gpsimd", wk, d3(wb[:], 256),
                          r3(I["ew"][e, :, ft4 * 256:(ft4 + 1) * 256]), writes=[wk])
                    for ftl in range(2):
                        ft = ft4 * 2 + ftl
                        for ci, tw in enumerate(CHW):
                            t0 = ci * 512
                            pi = (ft * 3 + ci) % 2
                            hb, hbk = pH[pi], "pH%d" % pi
                            for fc in range(NF):
                                b.mm(hb[:, 0:tw],
                                     wb[:, fc * 256 + ftl * 128:fc * 256 + (ftl + 1) * 128],
                                     fmv(xn_fm, TOK)[:, fc, t0:t0 + tw], fc == 0, fc == NF - 1,
                                     reads=[wk, "xn_fm"], writes=[hbk])
                            b.act(she[0][:, 0:tw], hb[:, 0:tw], AF.Silu,
                                  bias=eb_c[:, ft * 8 + e:ft * 8 + e + 1],
                                  reads=[hbk, "eb_c"], writes=["she0"])
                            dst = combv[:, ft, t0:t0 + tw]
                            ck = "comb%d_%d" % (ft, ci)
                            if e == 0:
                                b.tt(dst, she[0][:, 0:tw], wbcs[:, t0:t0 + tw], ALU.mult,
                                     reads=["she0", "wbcs"], writes=[ck])
                            else:
                                b.tt(mtmp[0][:, 0:tw], she[0][:, 0:tw], wbcs[:, t0:t0 + tw],
                                     ALU.mult, reads=["she0", "wbcs"],
                                     writes=["mtmp0"])
                                b.tt(dst, dst, mtmp[0][:, 0:tw], ALU.add,
                                     reads=["mtmp0", ck], writes=[ck])

    dbg("g_out", (TOK, D_))
    b.fence()
    if dbg("comb", (128, NF * TOK)):
        b.dma("gpsimd", "dbgC", dbg("comb"), comb[:], reads=["comb0_0"], writes=[])
    if PHASES < 5:
        sComb.close()
        return
    with ExitStack() as sG:
        wpj = [sG.enter_context(nc.sbuf_tensor("wpj%d" % i, [128, NF * 512], BF16))
               for i in range(2)]
        pb_bc = sG.enter_context(nc.sbuf_tensor("pb_bc", [128, D_], F32))
        nw_bc = sG.enter_context(nc.sbuf_tensor("nw_bc", [128, D_], F32))
        scrY = [sG.enter_context(nc.sbuf_tensor("scrY0", [128, D_], F32))]
        xo = [sG.enter_context(nc.sbuf_tensor("xo0", [128, D_], F32))]
        gt = [sG.enter_context(nc.sbuf_tensor("gt0", [128, D_], BF16))]
        yst = [sG.enter_context(nc.sbuf_tensor("yst0", [128, 4], F32))]
        pP = [psg(sG, "pP%d" % i, [128, 512]) for i in range(2)]
        b.dma("sync", "cG", pb_bc[:], I["pb"].broadcast_to([128, D_]),
              writes=["pb_bc"], bulk=True)
        b.dma("sync", "cG", nw_bc[:], I["nw"].broadcast_to([128, D_]),
              writes=["nw_bc"], bulk=True)
        for ot in range(NT):
            bi = 0
            sy, syk = scrY[0], "scrY0"
            xb_, xbk = xo[0], "xo0"
            b.dma("sync", xbk, xb_[:], I["xc"][ot * 128:(ot + 1) * 128, :],
                  writes=[xbk])
            for ncc in range(4):
                wi = ncc % 2
                wb, wk = wpj[wi], "wpj%d" % wi
                b.dma("gpsimd", wk, d3(wb[:], 512), r3(I["pw"][:, ncc * 512:(ncc + 1) * 512]),
                      writes=[wk])
                pp, ppk = pP[wi], "pP%d" % wi
                for fc in range(NF):
                    b.mm(pp[:], combv[:, fc, ot * 128:(ot + 1) * 128],
                         wb[:, fc * 512:(fc + 1) * 512], fc == 0, fc == NF - 1,
                         reads=["comb%d_%d" % (fc, min(ot // 4, 2)), wk], writes=[ppk])
                b.tt(sy[:, ncc * 512:(ncc + 1) * 512], pp[:],
                     pb_bc[:, ncc * 512:(ncc + 1) * 512], ALU.add,
                     reads=[ppk, "pb_bc"], writes=[syk])
            b.ts(xb_[:], xb_[:], m_own[:, ot:ot + 1], r_own[:, ot:ot + 1],
                 ALU.subtract, ALU.mult, reads=[xbk, "m_own", "r_own"], writes=[xbk])
            b.tt(sy[:], sy[:], xb_[:], ALU.add, reads=[syk, xbk], writes=[syk])
            st, stk = yst[0], "yst0"
            b.tt(xb_[:], sy[:], sy[:], ALU.mult, reads=[syk, xbk], writes=[xbk])
            b.red(st[:, 0:1], xb_[:], reads=[xbk], writes=[stk])
            b.ts(st[:, 0:1], st[:, 0:1], 1.0 / D_, 1e-6, ALU.mult, ALU.add,
                 reads=[stk], writes=[stk])
            b.act(st[:, 1:2], st[:, 0:1], AF.Ln, reads=[stk], writes=[stk])
            b.act(st[:, 2:3], st[:, 1:2], AF.Exp, scale=-0.5, reads=[stk], writes=[stk])
            b.ts(sy[:], sy[:], st[:, 2:3], None, ALU.mult, reads=[syk, stk], writes=[syk])
            b.tt(sy[:], sy[:], nw_bc[:], ALU.mult, reads=[syk, "nw_bc"], writes=[syk])
            b.act(gt[0][:], sy[:], AF.Silu, reads=[syk], writes=["gt0"])
            b.dma("sync", "gst0", gdram[ot * 128:(ot + 1) * 128, :], gt[0][:],
                  reads=["gt0"], writes=["gd%d" % ot])
            if dbg("g_out"):
                b.dma("gpsimd", "dbgG", dbg("g_out")[ot * 128:(ot + 1) * 128, :],
                      gt[0][:], reads=["gt0"], writes=[])

    b.fence()
    sComb.close()

    if dbg("Q_fm", (128, NF * TOK)):
        b.dma("gpsimd", "dbg5", dbg("Q_fm"), Q_fm[:], reads=["Q_fm"], writes=[])

    if PHASES < 6:
        return
    with ExitStack() as sB:
        bscr = sB.enter_context(nc.sbuf_tensor("bscr", [128, 8], F32))
        brow = sB.enter_context(nc.sbuf_tensor("brow", [1, 128], F32))
        bsc2 = sB.enter_context(nc.sbuf_tensor("bsc2", [1, 4], F32))
        pnb_t = psg(sB, "pnb", [128, 512])
        pnb = pnb_t[:, 0:1]
        b.red(bscr[:, 0:1], q2acc[:], aluop=ALU.max, reads=["q2acc"], writes=["bscr"])
        emit_global_max(b, nc, ctx, "q", bscr[:, 0:1], "bscr", bsc2[0:1, 0:1], "bsc2",
                        brow[0:1, :], "brow")
        b.red(bscr[:, 1:2], k2acc[:], aluop=ALU.max, reads=["k2acc"], writes=["bscr"])
        emit_global_max(b, nc, ctx, "k", bscr[:, 1:2], "bscr", bsc2[0:1, 1:2], "bsc2",
                        brow[0:1, :], "brow")
        b.tt(bsc2[0:1, 2:3], bsc2[0:1, 0:1], bsc2[0:1, 1:2], ALU.mult,
             reads=["bsc2"], writes=["bsc2"])
        b.act(bsc2[0:1, 2:3], bsc2[0:1, 2:3], AF.Ln, reads=["bsc2"], writes=["bsc2"])
        b.act(bsc2[0:1, 3:4], bsc2[0:1, 2:3], AF.Exp, scale=0.5, reads=["bsc2"], writes=["bsc2"])
        b.ts(bsc2[0:1, 3:4], bsc2[0:1, 3:4], -0.125, None, ALU.mult,
             reads=["bsc2"], writes=["bsc2"])
        b.mm(pnb, onesr[:], bsc2[0:1, 3:4], True, True,
             reads=["onesr", "bsc2"], writes=["pnb"])
        b.vcopy(negB[:], pnb, reads=["pnb"], writes=["negB"])
    if dbg("negB", (128, 1)):
        b.dma("sync", "dbg6", dbg("negB"), negB[:], reads=["negB"], writes=[])

    b.fence()
    if PHASES < 7:
        return
    with ExitStack() as sT:
        def tsb(name, shape, dt=F32):
            return sT.enter_context(nc.sbuf_tensor(name, list(shape), dt))

        attnfm = tsb("attnfm", [128, NF * 512], BF16)
        ptl = [tsb("ptl%d" % i, [128, 512], BF16) for i in range(6)]
        rdrow = tsb("rdrow", [1, 512])
        rbc = tsb("rbc", [64, 512])
        wos = [tsb("wos%d" % i, [128, NF * 512], BF16) for i in range(1)]
        bo_bc = tsb("bo_bc", [128, D_])
        sO = [tsb("sO%d" % i, [128, D_]) for i in range(4)]
        gld = [tsb("gld%d" % i, [128, D_], BF16) for i in range(1)]
        gf32 = [tsb("gf32%d" % i, [128, D_]) for i in range(1)]
        xo2 = [tsb("xo2%d" % i, [128, D_]) for i in range(1)]
        pS = [psg(sT, "pS%d" % i, [128, 512]) for i in range(3)]
        pAV = [psg(sT, "pAV%d" % i, [65, 512]) for i in range(2)]
        pbc = psg(sT, "pbc", [64, 512])
        pO = [psg(sT, "pO%d" % i, [128, 512]) for i in range(2)]
        b.dma("sync", "cT", bo_bc[:], I["bo"].broadcast_to([128, D_]),
              writes=["bo_bc"], bulk=True)

        for ci, qw in enumerate(CHW):
            q0 = ci * 512
            for pr in range(16):
                pair = (2 * pr, 2 * pr + 1)
                for kt in range(NTB):
                    for j, si in enumerate(pair):
                        h = SLOT2HEAD[si]
                        kvh = h // 4
                        assert si % 2 == kvh % 2
                        av, avk = pAV[j], "pAV%d" % j
                        qrhs = fmv(Q_fm, TOK)[:, si // 2, q0:q0 + qw][
                            (si % 2) * 64:(si % 2) * 64 + 64, :]
                        ii = 2 * kt + j
                        sp, spk = pS[ii % 3], "pS%d" % (ii % 3)
                        klhs = fmv(K_fm, T_)[:, kvh // 2, kt * 128:(kt + 1) * 128][
                            (kvh % 2) * 64:(kvh % 2) * 64 + 64, :]
                        b.mm(sp[:, 0:qw], klhs, qrhs, True, True,
                             reads=["K_fm", "Q_fm"], writes=[spk])
                        pb_, pbk = ptl[ii % 6], "ptl%d" % (ii % 6)
                        b.act(pb_[:, 0:qw], sp[:, 0:qw], AF.Exp, bias=negB[:, 0:1],
                              scale=0.125, reads=[spk, "negB"], writes=[pbk])
                        b.mm(av[:, 0:qw],
                             V_tm[:, kt * 520 + kvh * 65:kt * 520 + (kvh + 1) * 65],
                             pb_[:, 0:qw], kt == 0, kt == NTB - 1,
                             reads=["V_tm", pbk], writes=[avk])
                for j, si in enumerate(pair):
                    av, avk = pAV[j], "pAV%d" % j
                    b.vrecip(rdrow[0:1, 0:qw], av[64:65, 0:qw], reads=[avk], writes=["rdrow"])
                    b.mm(pbc[:, 0:qw], onesr[0:1, 0:64], rdrow[0:1, 0:qw], True, True,
                         reads=["onesr", "rdrow"], writes=["pbc"])
                    b.vcopy(rbc[:, 0:qw], pbc[:, 0:qw], reads=["pbc"], writes=["rbc"])
                    dst = fmv(attnfm, 512)[:, si // 2, 0:qw][(si % 2) * 64:(si % 2) * 64 + 64, :]
                    b.tt(dst, av[0:64, 0:qw], rbc[:, 0:qw], ALU.mult,
                         reads=[avk, "rbc"], writes=["attnfm"])
            for ncc in range(4):
                wb, wk = wos[0], "wos0"
                b.dma("gpsimd", wk, d3(wb[:], 512), r3(I["wo_t"][:, ncc * 512:(ncc + 1) * 512]),
                      writes=[wk])
                for it in range(qw // 128):
                    pp, ppk = pO[it % 2], "pO%d" % (it % 2)
                    for fc in range(NF):
                        b.mm(pp[:], fmv(attnfm, 512)[:, fc, it * 128:(it + 1) * 128],
                             wb[:, fc * 512:(fc + 1) * 512], fc == 0, fc == NF - 1,
                             reads=["attnfm", wk], writes=[ppk])
                    b.tt(sO[it][:, ncc * 512:(ncc + 1) * 512], pp[:],
                         bo_bc[:, ncc * 512:(ncc + 1) * 512], ALU.add,
                         reads=[ppk, "bo_bc"], writes=["sO%d" % it])
            for it in range(qw // 128):
                ot = ci * 4 + it
                b.dma("sync", "gld0", gld[0][:], gdram[ot * 128:(ot + 1) * 128, :],
                      reads=["gd%d" % ot], writes=["gld0"])
                b.dma("sync", "xo20", xo2[0][:], I["xc"][ot * 128:(ot + 1) * 128, :],
                      writes=["xo20"])
                b.vcopy(gf32[0][:], gld[0][:], reads=["gld0"], writes=["gf320"])
                b.tt(sO[it][:], sO[it][:], gf32[0][:], ALU.mult,
                     reads=["sO%d" % it, "gf320"], writes=["sO%d" % it])
                b.tt(sO[it][:], sO[it][:], xo2[0][:], ALU.add,
                     reads=["sO%d" % it, "xo20"], writes=["sO%d" % it])
                b.dma("sync", "st%d" % it, x1[ot * 128:(ot + 1) * 128, :], sO[it][:],
                      reads=["sO%d" % it], writes=[])



def build_l1(debug=()):
    nc = bass.Bass("TRN2", target_bir_lowering=False, debug=False)
    I = {}

    def di(name, shape, dt=F32):
        I[name] = nc.dram_tensor(name, list(shape), dt, kind="ExternalInput").ap()

    di("x1c", (TOK, D_))
    di("qkv_t", (D_, 3072)); di("qkv_b", (1, 3072))
    di("out_t", (D_, D_)); di("out_b", (1, D_)); di("sinks", (1, 32))
    di("gw_t", (D_, 32)); di("gb", (1, 32))
    di("fc1_t", (D_, 1024)); di("fc2_t", (1024, D_))
    di("w1_t", (32, 1024, 1024)); di("w2_t", (32, 1024, 1024))
    di("su_t", (D_, 4096)); di("sd_t", (4096, D_))
    di("cos1", (TOK, 32)); di("sin1", (TOK, 32))
    di("masks", (2, 5, 128, 512), BF16)
    di("iden_bf", (128, 128), BF16); di("ones128", (1, 128))
    di("sel32", (32, 32 * 128))

    yc = nc.dram_tensor("yc", [OWN, D_], F32, kind="ExternalOutput").ap()
    DBG = {}

    def dbg(name, shape=None, dt=F32):
        if name in debug and shape is not None and name not in DBG:
            DBG[name] = nc.dram_tensor("dbg_" + name, list(shape), dt,
                                       kind="ExternalOutput").ap()
        return DBG.get(name)

    b = Prog(nc)
    with ExitStack() as ctx:
        emit_l1(b, I, yc, dbg, nc, ctx)
    return b.build(), sorted(DBG)


def emit_l1(b, I, yc, dbg, nc, ctx):
    def sb(name, shape, dt=F32):
        return ctx.enter_context(nc.sbuf_tensor(name, list(shape), dt))

    def psg(st, name, shape, dt=F32):
        return st.enter_context(nc.psum_tensor(name, list(shape), dt))

    C = {}
    K_fm = sb("K1_fm", [128, 4 * TOK], BF16)
    V_tm = sb("V1_tm", [128, NT * 520], BF16)
    maskb = sb("maskb", [128, 2 * 5 * 512], BF16)
    m_own = sb("m1_own", [128, NT]); r_own = sb("r1_own", [128, NT])
    q2acc = sb("q2acc", [128, 32]); k2acc = sb("k2acc", [128, 8])
    negB = sb("negB", [128, 1])
    es = sb("es", [1, 32], F32)
    sink_s = sb("sink_s", [1, 32], F32)
    C["idb"] = sb("idb", [128, 128], BF16)
    onesr = sb("onesr", [1, 128])
    g1 = sb("g1", [128, 8 * D_], BF16)
    cwT = sb("cwT", [32, OWN], BF16)

    b.dma("sync", "c0L1", C["idb"][:], I["iden_bf"], writes=["idb"], bulk=True)
    b.dma("sync", "c0L1", onesr[:], I["ones128"], writes=["onesr"], bulk=True)
    b.dma("sync", "c0L1", sink_s[:], I["sinks"], writes=["sink_s"], bulk=True)
    b.dma("sync", "c0L1", maskb[:].rearrange("p (q m c) -> p q m c", m=5, c=512),
          I["masks"].rearrange("q m p c -> p q m c"), writes=["maskb"], bulk=True)
    b.vmemset(q2acc[:], 0.0, writes=["q2acc"])
    b.vmemset(k2acc[:], 0.0, writes=["k2acc"])

    with ExitStack() as sAQ:
        xn_fm = sAQ.enter_context(nc.sbuf_tensor("xn1_fm", [128, NF * TOK], BF16))
        cosc = sAQ.enter_context(nc.sbuf_tensor("cosc", [128, NT * 32], F32))
        sinc = sAQ.enter_context(nc.sbuf_tensor("sinc", [128, NT * 32], F32))
        b.dma("sync", "c0L1", d3(cosc[:], 32), r3(I["cos1"]), writes=["cosc"], bulk=True)
        b.dma("sync", "c0L1", d3(sinc[:], 32), r3(I["sin1"]), writes=["sinc"], bulk=True)

        with ExitStack() as sA:
            xt = [sA.enter_context(nc.sbuf_tensor("xt%d" % i, [128, D_], F32))
                  for i in range(2)]
            C["lnscr"] = sA.enter_context(nc.sbuf_tensor("lnscr", [128, D_], F32))
            xnbf = sA.enter_context(nc.sbuf_tensor("xnbf0", [128, D_], BF16))
            stat = [sA.enter_context(nc.sbuf_tensor("stat%d" % i, [128, 4], F32))
                    for i in range(2)]
            C["pt0"] = psg(sA, "pt0", [128, 1024], BF16)
            C["pt1"] = psg(sA, "pt1", [128, 1024], BF16)
            for ot in range(NT):
                bi = ot % 2
                xtb, xk = xt[bi], "xt%d" % bi
                b.dma("sync", xk, xtb[:], I["x1c"][ot * 128:(ot + 1) * 128, :],
                      writes=[xk])
                emit_ln_tile(b, C, xtb[:], xk, stat[bi], "stat%d" % bi,
                             xnbf[:], "xnbf0", 1e-5,
                             mcol=m_own[:, ot:ot + 1], rcol=r_own[:, ot:ot + 1])
                xv = fmv(xn_fm, TOK)
                emit_transpose16(
                    b, C, xnbf[:], "xnbf0",
                    (lambda g, n, ot_=ot: xv[:, g * 8:g * 8 + n, ot_ * 128:(ot_ + 1) * 128]),
                    "xn_fm")
        b.fence()

        with ExitStack() as sKV:
            wkv = sKV.enter_context(nc.sbuf_tensor("wkv1", [128, NF * 1024], BF16))
            kb_bc = sKV.enter_context(nc.sbuf_tensor("kb_bc", [128, 512], F32))
            vb_bc = sKV.enter_context(nc.sbuf_tensor("vb_bc", [128, 512], F32))
            kvscr = sKV.enter_context(nc.sbuf_tensor("kvscr0", [128, 512], F32))
            kvtmp = sKV.enter_context(nc.sbuf_tensor("kvtmp0", [128, 512], F32))
            krs = [sKV.enter_context(nc.sbuf_tensor("krs%d" % i, [128, 512], BF16))
                   for i in range(2)]
            pkv = [psg(sKV, "pkv%d" % i, [128, 512]) for i in range(2)]
            C["pt0"] = psg(sKV, "pt0k", [128, 1024], BF16)
            C["pt1"] = psg(sKV, "pt1k", [128, 1024], BF16)
            b.dma("gpsimd", "wkv1", d3(wkv[:, 0:NF * 512], 512),
                  r3(I["qkv_t"][:, 2048:2560]), writes=["wkv_k"], bulk=True)
            b.dma("gpsimd", "wkv1", d3(wkv[:, NF * 512:], 512),
                  r3(I["qkv_t"][:, 2560:3072]), writes=["wkv_v"], bulk=True)
            b.dma("sync", "cA1", kb_bc[:], I["qkv_b"][:, 2048:2560].broadcast_to([128, 512]),
                  writes=["kb_bc"], bulk=True)
            b.dma("sync", "cA1", vb_bc[:], I["qkv_b"][:, 2560:3072].broadcast_to([128, 512]),
                  writes=["vb_bc"], bulk=True)
            for ot in range(NT):
                bi = ot % 2
                pk, pkk = pkv[bi], "pkv%d" % bi
                for fc in range(NF):
                    b.mm(pk[:], fmv(xn_fm, TOK)[:, fc, ot * 128:(ot + 1) * 128],
                         wkv[:, fc * 512:(fc + 1) * 512], fc == 0, fc == NF - 1,
                         reads=["xn_fm", "wkv_k"], writes=[pkk])
                b.tt(kvscr[:], pk[:], kb_bc[:], ALU.add, reads=[pkk, "kb_bc"],
                     writes=["kvscr0"])
                emit_rope(b, C, kvscr[:], "kvscr0", krs[bi][:], "krs%d" % bi,
                          cosc[:, ot * 32:(ot + 1) * 32], sinc[:, ot * 32:(ot + 1) * 32],
                          8, kvtmp[:], "kvtmp0", ckeys=("cosc", "sinc"))
                pt, ptk = C["pt%d" % bi], "pt%d" % bi
                for k in range(4):
                    b.trans(pt[:, k * 128:(k + 1) * 128], krs[bi][:, k * 128:(k + 1) * 128],
                            C["idb"][:], reads=["krs%d" % bi, "idb"], writes=[ptk])
                b.vcopy(fmv(K_fm, TOK)[:, 0:4, ot * 128:(ot + 1) * 128],
                        pt[:, 0:512].rearrange("p (f c) -> p f c", c=128),
                        reads=[ptk], writes=["K_fm"])
                b.tt(kvtmp[:], krs[bi][:], krs[bi][:], ALU.mult,
                     reads=["krs%d" % bi], writes=["kvtmp0"])
                b.red(kvscr[:, 0:8], kvtmp[:].rearrange("p (h d) -> p h d", d=64),
                      aluop=ALU.max, reads=["kvtmp0"], writes=["kvscr0"])
                b.tt(k2acc[:], k2acc[:], kvscr[:, 0:8], ALU.max,
                     reads=["kvscr0", "k2acc"], writes=["k2acc"])
                for fc in range(NF):
                    b.mm(pk[:], fmv(xn_fm, TOK)[:, fc, ot * 128:(ot + 1) * 128],
                         wkv[:, NF * 512 + fc * 512:NF * 512 + (fc + 1) * 512],
                         fc == 0, fc == NF - 1,
                         reads=["xn_fm", "wkv_v"], writes=[pkk])
                vt = V_tm[:, ot * 520:(ot + 1) * 520].rearrange("p (h c) -> p h c", c=65)
                b.tt(vt[:, :, 0:64], pk[:].rearrange("p (h c) -> p h c", c=64),
                     vb_bc[:].rearrange("p (h c) -> p h c", c=64), ALU.add,
                     reads=[pkk, "vb_bc"], writes=["V_tm"])
                b.vmemset(vt[:, :, 64:65], 1.0, writes=["V_tm"])
        b.fence()

        with ExitStack() as sR:
            gw_b = sR.enter_context(nc.sbuf_tensor("gw_b", [128, NF * 32], BF16))
            gw_f = sR.enter_context(nc.sbuf_tensor("gw_f", [128, NF * 32], F32))
            gb_bc = sR.enter_context(nc.sbuf_tensor("gb_bc", [128, 32], F32))
            rscr = sR.enter_context(nc.sbuf_tensor("rscr", [128, 192], F32))
            rgb = sR.enter_context(nc.sbuf_tensor("rgb", [128, 32], BF16))
            rpsum_t = psg(sR, "rpsum", [128, 512])
            rpsum = rpsum_t[:, 0:32]
            rtp_t = psg(sR, "rtp", [128, 1024], BF16)
            rtp = rtp_t[0:32, 0:128]
            b.dma("sync", "cRg1", d3(gw_f[:], 32), r3(I["gw_t"]), writes=["gw_f"], bulk=True)
            b.vcopy(gw_b[:], gw_f[:], reads=["gw_f"], writes=["gw_b"])
            b.dma("sync", "cR1", gb_bc[:], I["gb"].broadcast_to([128, 32]),
                  writes=["gb_bc"], bulk=True)
            for ot in range(8):
                lt = ot + 1
                for fc in range(NF):
                    b.mm(rpsum, fmv(xn_fm, TOK)[:, fc, lt * 128:(lt + 1) * 128],
                         gw_b[:, fc * 32:(fc + 1) * 32], fc == 0, fc == NF - 1,
                         reads=["xn_fm", "gw_b"], writes=["rpsum"])
                lk = "rscr"
                lg = rscr[:, 0:32]
                sc_ = rscr[:, 32:64]
                sbv = rscr[:, 64:96]
                tq = rscr[:, 96:128]
                gs = rscr[:, 128:132]
                st1 = rscr[:, 132:133]; st2 = rscr[:, 133:134]
                thg = rscr[:, 134:135]; th8 = rscr[:, 135:136]
                ssum = rscr[:, 136:137]; rs = rscr[:, 137:138]
                msk = rscr[:, 128 + 16:128 + 16]
                b.vcopy(lg, rpsum, reads=["rpsum"], writes=[lk])
                b.act(sc_, lg, AF.Exp, scale=-1.0, reads=[lk], writes=[lk])
                b.ts(sc_, sc_, 1.0, None, ALU.add, reads=[lk], writes=[lk])
                b.vrecip(sc_, sc_, reads=[lk], writes=[lk])
                b.tt(sbv, sc_, gb_bc[:], ALU.add, reads=[lk, "gb_bc"], writes=[lk])
                for gi in range(4):
                    sl = sbv[:, gi * 8:(gi + 1) * 8]
                    b.red(st1, sl, aluop=ALU.max, reads=[lk], writes=[lk])
                    b.ts(tq[:, 0:8], sl, st1, -1e30, ALU.is_equal, ALU.mult,
                         reads=[lk], writes=[lk])
                    b.tt(tq[:, 0:8], tq[:, 0:8], sl, ALU.add, reads=[lk], writes=[lk])
                    b.red(st2, tq[:, 0:8], aluop=ALU.max, reads=[lk], writes=[lk])
                    b.tt(gs[:, gi:gi + 1], st1, st2, ALU.add, reads=[lk], writes=[lk])
                b.red(st1, gs, aluop=ALU.max, reads=[lk], writes=[lk])
                b.ts(tq[:, 0:4], gs, st1, -1e30, ALU.is_equal, ALU.mult,
                     reads=[lk], writes=[lk])
                b.tt(tq[:, 0:4], tq[:, 0:4], gs, ALU.add, reads=[lk], writes=[lk])
                b.red(thg, tq[:, 0:4], aluop=ALU.max, reads=[lk], writes=[lk])
                b.ts(gs, gs, thg, None, ALU.is_ge, reads=[lk], writes=[lk])
                gm_exp = gs.rearrange("p (g o) -> p g o", o=1).broadcast_to([128, 4, 8])
                sb3 = sbv.rearrange("p (g e) -> p g e", e=8)
                tq3 = tq.rearrange("p (g e) -> p g e", e=8)
                b.tt(tq3, sb3, gm_exp, ALU.mult, reads=[lk], writes=[lk])
                b.ts(lg, gs.rearrange("p (g o) -> p g o", o=1).broadcast_to([128, 4, 8]),
                     1.0, -1e30, ALU.subtract, ALU.mult, reads=[lk], writes=[lk])
                b.tt(tq, tq, lg, ALU.subtract, reads=[lk], writes=[lk])
                mcp = rscr[:, 96 + 32:96 + 32] if False else None
                b.vcopy(lg, tq, reads=[lk], writes=[lk])
                for rnd in range(8):
                    b.red(th8, tq, aluop=ALU.max, reads=[lk], writes=[lk])
                    if rnd < 7:
                        b.ts(rscr[:, 160:192], tq, th8, -1e30,
                             ALU.is_equal, ALU.mult, reads=[lk], writes=[lk])
                        b.tt(tq, tq, rscr[:, 160:192], ALU.add,
                             reads=[lk], writes=[lk])
                b.ts(lg, lg, th8, None, ALU.is_ge, reads=[lk], writes=[lk])
                b.tt(lg, lg, sc_, ALU.mult, reads=[lk], writes=[lk])
                b.red(ssum, lg, reads=[lk], writes=[lk])
                b.vrecip(rs, ssum, reads=[lk], writes=[lk])
                b.ts(lg, lg, rs, 2.5, ALU.mult, ALU.mult, reads=[lk], writes=[lk])
                b.vcopy(rgb[:], lg, reads=[lk], writes=["rgb"])
                b.trans(rtp, rgb[:], C["idb"][:], reads=["rgb", "idb"], writes=["rtp"])
                b.vcopy(cwT[:, ot * 128:(ot + 1) * 128], rtp, reads=["rtp"], writes=["cwT"])
        b.fence()

        TCH = 256
        NCH = OWN // TCH
        with ExitStack() as sF:
            xlat = sF.enter_context(nc.sbuf_tensor("xlat", [128, 8 * OWN], BF16))
            wstr = [sF.enter_context(nc.sbuf_tensor("wstr%d" % i, [128, 16 * 128], BF16))
                    for i in range(2)]
            wfc1 = [sF.enter_context(nc.sbuf_tensor("wfc1%d" % i, [128, NF * 128], BF16))
                    for i in range(2)]
            hw = sF.enter_context(nc.sbuf_tensor("hw", [128, 8 * TCH], BF16))
            rt = sF.enter_context(nc.sbuf_tensor("rt", [128, TCH], F32))
            rt2 = sF.enter_context(nc.sbuf_tensor("rt2", [128, TCH], F32))
            cwb = sF.enter_context(nc.sbuf_tensor("cwb", [128, TCH], F32))
            sel32b = sF.enter_context(nc.sbuf_tensor("sel32b", [32, 32 * 128], BF16))
            ylat_bf = sF.enter_context(nc.sbuf_tensor("ylat_bf", [128, 8 * TCH], BF16))
            upfm = sF.enter_context(nc.sbuf_tensor("upfm", [128, 32 * TCH], BF16))
            wdn = [sF.enter_context(nc.sbuf_tensor("wdn%d" % i, [128, 32 * 256], BF16))
                   for i in range(2)]
            wf2 = [sF.enter_context(nc.sbuf_tensor("wf20", [128, 8 * 256], BF16))]
            pylat = psg(sF, "pylat", [128, 2048])
            ppre = [psg(sF, "ppre%d" % i, [128, 512]) for i in range(2)]
            pcw = psg(sF, "pcw", [128, 512])
            pdn = ppre
            b.dma("gpsimd", "cF", sel32b[:], I["sel32"], writes=["sel32b"], bulk=True)

            for lt in range(8):
                wi = lt % 2
                wb, wk = wfc1[wi], "wfc1%d" % wi
                b.dma("gpsimd", wk, d3(wb[:], 128),
                      r3(I["fc1_t"][:, lt * 128:(lt + 1) * 128]), writes=[wk])
                for tc in range(2):
                    pp, ppk = ppre[tc % 2], "ppre%d" % (tc % 2)
                    for fc in range(NF):
                        b.mm(pp[:], wb[:, fc * 128:(fc + 1) * 128],
                             fmv(xn_fm, TOK)[:, fc, 128 + tc * 512:128 + (tc + 1) * 512],
                             fc == 0, fc == NF - 1, reads=[wk, "xn_fm"], writes=[ppk])
                    b.vcopy(fmv(xlat, OWN)[:, lt, tc * 512:(tc + 1) * 512][0:128, :],
                            pp[:], reads=[ppk], writes=["xlat"])

            if dbg("xlatd", (128, 8 * OWN), BF16):
                b.dma("sync", "dbgXL", dbg("xlatd"), xlat[:],
                      reads=["xlat"], writes=[])
            for ci in range(NCH):
                t0 = ci * TCH
                for e in range(32):
                    b.mm(pcw[:, 0:TCH], sel32b[:, e * 128:(e + 1) * 128],
                         cwT[:, t0:t0 + TCH], True, True,
                         reads=["sel32b", "cwT"], writes=["pcw"])
                    b.vcopy(cwb[:], pcw[:, 0:TCH], reads=["pcw"], writes=["cwb"])
                    for it_ in range(8):
                        wi = it_ % 2
                        wb, wk = wstr[wi], "wstr%d" % wi
                        b.dma("gpsimd", wk, d3(wb[:, 0:8 * 128], 128),
                              r3(I["w1_t"][e, :, it_ * 128:(it_ + 1) * 128]), writes=[wk])
                        pp, ppk = ppre[it_ % 2], "ppre%d" % (it_ % 2)
                        for lc in range(8):
                            b.mm(pp[:, 0:TCH], wb[:, lc * 128:(lc + 1) * 128],
                                 fmv(xlat, OWN)[:, lc, t0:t0 + TCH],
                                 lc == 0, lc == 7, reads=[wk, "xlat"], writes=[ppk])
                        b.op("scalar",
                             (lambda s, o=rt[:, 0:TCH], i=pp[:, 0:TCH]:
                              s.activation(o, i, AF.Relu)),
                             reads=[ppk], writes=["rt"])
                        b.tt(rt2[:, 0:TCH], rt[:, 0:TCH], rt[:, 0:TCH], ALU.mult,
                             reads=["rt"], writes=["rt2"])
                        b.tt(hw[:, it_ * TCH:(it_ + 1) * TCH], rt2[:, 0:TCH],
                             cwb[:], ALU.mult, reads=["rt2", "cwb"], writes=["hw"])
                    for lt in range(8):
                        wi = lt % 2
                        wb, wk = wstr[wi], "wstr%d" % wi
                        b.dma("gpsimd", wk, d3(wb[:, 0:8 * 128], 128),
                              r3(I["w2_t"][e, :, lt * 128:(lt + 1) * 128]), writes=[wk])
                        for icc in range(8):
                            b.mm(pylat[:, lt * TCH:(lt + 1) * TCH],
                                 wb[:, icc * 128:(icc + 1) * 128],
                                 hw[:, icc * TCH:(icc + 1) * TCH],
                                 (e == 0 and icc == 0 and lt % 2 == 0),
                                 (e == 31 and icc == 7),
                                 reads=[wk, "hw"], writes=["pylat"])
                for lt in range(8):
                    b.vcopy(ylat_bf[:, lt * TCH:(lt + 1) * TCH],
                            pylat[:, lt * TCH:(lt + 1) * TCH],
                            reads=["pylat"], writes=["ylat_bf"])
                if ci == 0 and dbg("ylatd", (128, 8 * 256), BF16):
                    b.dma("sync", "dbgYL", dbg("ylatd"), ylat_bf[:],
                          reads=["ylat_bf"], writes=[])
                for st_ in range(32):
                    wi = st_ % 2
                    wb, wk = wstr[wi], "wstr%d" % wi
                    b.dma("gpsimd", wk, d3(wb[:], 128),
                          r3(I["su_t"][:, st_ * 128:(st_ + 1) * 128]), writes=[wk])
                    pp, ppk = ppre[st_ % 2], "ppre%d" % (st_ % 2)
                    for fc in range(NF):
                        b.mm(pp[:, 0:TCH], wb[:, fc * 128:(fc + 1) * 128],
                             fmv(xn_fm, TOK)[:, fc, 128 + t0:128 + t0 + TCH],
                             fc == 0, fc == NF - 1, reads=[wk, "xn_fm"], writes=[ppk])
                    b.op("scalar",
                         (lambda s, o=rt[:, 0:TCH], i=pp[:, 0:TCH]:
                          s.activation(o, i, AF.Relu)),
                         reads=[ppk], writes=["rt"])
                    b.tt(upfm[:, st_ * TCH:(st_ + 1) * TCH], rt[:, 0:TCH],
                         rt[:, 0:TCH], ALU.mult, reads=["rt"], writes=["upfm"])
                if ci == 0 and dbg("upfmd", (128, 32 * 256), BF16):
                    b.dma("sync", "dbgUP", dbg("upfmd"), upfm[:],
                          reads=["upfm"], writes=[])
                for nc8 in range(8):
                    wi = nc8 % 2
                    wdb, wdk = wdn[wi], "wdn%d" % wi
                    wfb, wfk = wf2[0], "wf20"
                    b.dma("gpsimd", wdk, d3(wdb[:], 256),
                          r3(I["sd_t"][:, nc8 * 256:(nc8 + 1) * 256]),
                          writes=[wdk])
                    b.dma("gpsimd", wfk, d3(wfb[:], 256),
                          r3(I["fc2_t"][:, nc8 * 256:(nc8 + 1) * 256]),
                          writes=[wfk])
                    for it_ in range(TCH // 128):
                        ot = (t0 + it_ * 128) // 128
                        pp, ppk = pdn[it_ % 2], "ppre%d" % (it_ % 2)
                        for sc_ in range(32):
                            b.mm(pp[:, 0:256], upfm[:, sc_ * TCH + it_ * 128:
                                                    sc_ * TCH + (it_ + 1) * 128],
                                 wdb[:, sc_ * 256:(sc_ + 1) * 256],
                                 sc_ == 0, False, reads=["upfm", wdk], writes=[ppk])
                        for lc in range(8):
                            b.mm(pp[:, 0:256], ylat_bf[:, lc * TCH + it_ * 128:
                                                       lc * TCH + (it_ + 1) * 128],
                                 wfb[:, lc * 256:(lc + 1) * 256],
                                 False, lc == 7, reads=["ylat_bf", wfk], writes=[ppk])
                        b.op("scalar",
                             (lambda s, o=g1[:, ot * D_ + nc8 * 256:
                                             ot * D_ + (nc8 + 1) * 256],
                              i=pp[:, 0:256]: s.activation(o, i, AF.Silu)),
                             reads=[ppk], writes=["g1"])
        if dbg("g1d", (128, 8 * D_)):
            b.dma("gpsimd", "dbgG1", dbg("g1d"), g1[:], reads=["g1"], writes=[])
        if dbg("cwTd", (32, OWN)):
            b.dma("gpsimd", "dbgCW", dbg("cwTd"), cwT[:], reads=["cwT"], writes=[])
        b.fence()

        Q_fm = ctx.enter_context(nc.sbuf_tensor("Q1_fm", [128, NF * OWN], BF16,
                                                side="right"))
        with ExitStack() as sQ:
            wqs = [sQ.enter_context(nc.sbuf_tensor("wqs%d" % i, [128, NF * 512], BF16))
                   for i in range(2)]
            qscr = [sQ.enter_context(nc.sbuf_tensor("qscr%d" % i, [128, 512], F32))
                    for i in range(2)]
            qtmp = [sQ.enter_context(nc.sbuf_tensor("qtmp%d" % i, [128, 512], F32))
                    for i in range(2)]
            qrb = [sQ.enter_context(nc.sbuf_tensor("qrb%d" % i, [128, 512], BF16))
                   for i in range(2)]
            qb_bc = sQ.enter_context(nc.sbuf_tensor("qb_bc", [128, D_], F32))
            ppq = [psg(sQ, "ppq%d" % i, [128, 512]) for i in range(2)]
            C["pt0"] = psg(sQ, "pt0b", [128, 1024], BF16)
            C["pt1"] = psg(sQ, "pt1b", [128, 1024], BF16)
            b.dma("sync", "cQ1", qb_bc[:], I["qkv_b"][:, 0:2048].broadcast_to([128, D_]),
                  writes=["qb_bc"], bulk=True)
            for ncc in range(4):
                wi = ncc % 2
                wb, wk = wqs[wi], "wqs%d" % wi
                b.dma("gpsimd", wk, d3(wb[:], 512),
                      r3(I["qkv_t"][:, ncc * 512:(ncc + 1) * 512]), writes=[wk])
                for ot in range(8):
                    lt = ot + 1
                    bi = ot % 2
                    pp, ppk = ppq[bi], "ppq%d" % bi
                    for fc in range(NF):
                        b.mm(pp[:], fmv(xn_fm, TOK)[:, fc, lt * 128:(lt + 1) * 128],
                             wb[:, fc * 512:(fc + 1) * 512], fc == 0, fc == NF - 1,
                             reads=["xn_fm", wk], writes=[ppk])
                    b.tt(qscr[bi][:], pp[:], qb_bc[:, ncc * 512:(ncc + 1) * 512], ALU.add,
                         reads=[ppk, "qb_bc"], writes=["qscr%d" % bi])
                    emit_rope(b, C, qscr[bi][:], "qscr%d" % bi, qrb[bi][:], "qrb%d" % bi,
                              cosc[:, lt * 32:(lt + 1) * 32], sinc[:, lt * 32:(lt + 1) * 32],
                              8, qtmp[bi][:], "qtmp%d" % bi, ckeys=("cosc", "sinc"))
                    b.tt(qtmp[bi][:], qrb[bi][:], qrb[bi][:], ALU.mult,
                         reads=["qrb%d" % bi], writes=["qtmp%d" % bi])
                    b.red(qscr[bi][:, 0:8], qtmp[bi][:].rearrange("p (h d) -> p h d", d=64),
                          aluop=ALU.max, reads=["qtmp%d" % bi], writes=["qscr%d" % bi])
                    b.tt(q2acc[:, ncc * 8:(ncc + 1) * 8], q2acc[:, ncc * 8:(ncc + 1) * 8],
                         qscr[bi][:, 0:8], ALU.max,
                         reads=["qscr%d" % bi, "q2acc"], writes=["q2acc"])
                    pt, ptk = C["pt%d" % bi], "pt%d" % bi
                    for k in range(4):
                        b.trans(pt[:, k * 128:(k + 1) * 128], qrb[bi][:, k * 128:(k + 1) * 128],
                                C["idb"][:], reads=["qrb%d" % bi, "idb"], writes=[ptk])
                    b.vcopy(fmv(Q_fm, OWN)[:, ncc * 4:(ncc + 1) * 4, ot * 128:(ot + 1) * 128],
                            pt[:, 0:512].rearrange("p (f c) -> p f c", c=128),
                            reads=[ptk], writes=["Q_fm"])
        b.fence()

    with ExitStack() as sB:
        bscr = sB.enter_context(nc.sbuf_tensor("bscr", [128, 8], F32))
        brow = sB.enter_context(nc.sbuf_tensor("brow", [1, 128], F32))
        bsc2 = sB.enter_context(nc.sbuf_tensor("bsc2", [1, 4], F32))
        pnb_t = psg(sB, "pnb", [128, 512])
        pnb = pnb_t[:, 0:1]
        b.red(bscr[:, 0:1], q2acc[:], aluop=ALU.max, reads=["q2acc"], writes=["bscr"])
        emit_global_max(b, nc, ctx, "q1", bscr[:, 0:1], "bscr", bsc2[0:1, 0:1], "bsc2",
                        brow[0:1, :], "brow")
        b.red(bscr[:, 1:2], k2acc[:], aluop=ALU.max, reads=["k2acc"], writes=["bscr"])
        emit_global_max(b, nc, ctx, "k1", bscr[:, 1:2], "bscr", bsc2[0:1, 1:2], "bsc2",
                        brow[0:1, :], "brow")
        b.tt(bsc2[0:1, 2:3], bsc2[0:1, 0:1], bsc2[0:1, 1:2], ALU.mult,
             reads=["bsc2"], writes=["bsc2"])
        b.act(bsc2[0:1, 2:3], bsc2[0:1, 2:3], AF.Ln, reads=["bsc2"], writes=["bsc2"])
        b.act(bsc2[0:1, 3:4], bsc2[0:1, 2:3], AF.Exp, scale=0.5, reads=["bsc2"], writes=["bsc2"])
        b.ts(bsc2[0:1, 3:4], bsc2[0:1, 3:4], -0.125, None, ALU.mult,
             reads=["bsc2"], writes=["bsc2"])
        b.mm(pnb, onesr[:], bsc2[0:1, 3:4], True, True,
             reads=["onesr", "bsc2"], writes=["pnb"])
        b.vcopy(negB[:], pnb, reads=["pnb"], writes=["negB"])
        b.act(es[:], sink_s[:], AF.Exp, bias=negB[0:1, 0:1],
              reads=["sink_s", "negB"], writes=["es"])
    b.fence()

    with ExitStack() as sT:
        def tsb(name, shape, dt=F32):
            return sT.enter_context(nc.sbuf_tensor(name, list(shape), dt))

        attnfm = tsb("attnfm", [128, NF * 512], BF16)
        ptl = [tsb("ptl%d" % i, [128, 512], BF16) for i in range(6)]
        rdrow = tsb("rdrow", [1, 512])
        rbc = tsb("rbc", [64, 512])
        wos = [tsb("wos%d" % i, [128, NF * 512], BF16) for i in range(1)]
        ob_bc = tsb("ob_bc", [128, D_])
        sO = [tsb("sO%d" % i, [128, D_]) for i in range(4)]
        gf32 = tsb("gf32", [128, D_])
        xo2 = tsb("xo2", [128, D_])
        pS = [psg(sT, "pS%d" % i, [128, 512]) for i in range(3)]
        pAV = [psg(sT, "pAV%d" % i, [65, 512]) for i in range(2)]
        pbc = psg(sT, "pbc", [64, 512])
        pO = [psg(sT, "pO%d" % i, [128, 512]) for i in range(2)]
        b.dma("sync", "cT1", ob_bc[:], I["out_b"].broadcast_to([128, D_]),
              writes=["ob_bc"], bulk=True)
        mk3 = maskb[:].rearrange("p (q m c) -> p q m c", m=5, c=512)

        for qc in range(2):
            for si in range(32):
                h = SLOT2HEAD[si]
                kvh = h // 4
                av, avk = pAV[si % 2], "pAV%d" % (si % 2)
                qrhs = fmv(Q_fm, OWN)[:, si // 2, qc * 512:(qc + 1) * 512][
                    (si % 2) * 64:(si % 2) * 64 + 64, :]
                for m in range(5):
                    kt = qc * 4 + m
                    sp, spk = pS[m % 2], "pS%d" % (m % 2)
                    klhs = fmv(K_fm, TOK)[:, kvh // 2, kt * 128:(kt + 1) * 128][
                        (kvh % 2) * 64:(kvh % 2) * 64 + 64, :]
                    b.mm(sp[:], klhs, qrhs, True, True,
                         reads=["K_fm", "Q_fm"], writes=[spk])
                    pb_, pbk = ptl[m % 3], "ptl%d" % (m % 3)
                    b.act(pb_[:], sp[:], AF.Exp, bias=negB[:, 0:1], scale=0.125,
                          reads=[spk, "negB"], writes=[pbk])
                    b.tt(pb_[:], pb_[:], mk3[:, qc, m, :], ALU.mult,
                         reads=[pbk, "maskb"], writes=[pbk])
                    b.mm(av[:], V_tm[:, kt * 520 + kvh * 65:kt * 520 + (kvh + 1) * 65],
                         pb_[:], m == 0, m == 4, reads=["V_tm", pbk], writes=[avk])
                b.ts(rdrow[0:1, :], av[64:65, :], es[0:1, h:h + 1], None, ALU.add,
                     reads=[avk, "es"], writes=["rdrow"])
                b.vrecip(rdrow[0:1, :], rdrow[0:1, :], reads=["rdrow"], writes=["rdrow"])
                b.mm(pbc[:], onesr[0:1, 0:64], rdrow[0:1, :], True, True,
                     reads=["onesr", "rdrow"], writes=["pbc"])
                b.vcopy(rbc[:], pbc[:], reads=["pbc"], writes=["rbc"])
                dst = fmv(attnfm, 512)[:, si // 2, :][(si % 2) * 64:(si % 2) * 64 + 64, :]
                b.tt(dst, av[0:64, :], rbc[:], ALU.mult, reads=[avk, "rbc"],
                     writes=["attnfm"])
            for ncc in range(4):
                wb, wk = wos[0], "wos0"
                b.dma("gpsimd", wk, d3(wb[:], 512),
                      r3(I["out_t"][:, ncc * 512:(ncc + 1) * 512]), writes=[wk])
                for it_ in range(4):
                    pp, ppk = pO[it_ % 2], "pO%d" % (it_ % 2)
                    for fc in range(NF):
                        b.mm(pp[:], fmv(attnfm, 512)[:, fc, it_ * 128:(it_ + 1) * 128],
                             wb[:, fc * 512:(fc + 1) * 512], fc == 0, fc == NF - 1,
                             reads=["attnfm", wk], writes=[ppk])
                    b.tt(sO[it_][:, ncc * 512:(ncc + 1) * 512], pp[:],
                         ob_bc[:, ncc * 512:(ncc + 1) * 512], ALU.add,
                         reads=[ppk, "ob_bc"], writes=["sO%d" % it_])
            for it_ in range(4):
                ot = qc * 4 + it_
                b.dma("sync", "xo2", xo2[:], I["x1c"][128 + ot * 128:128 + (ot + 1) * 128, :],
                      writes=["xo2"])
                b.vcopy(gf32[:], g1[:, ot * D_:(ot + 1) * D_], reads=["g1"], writes=["gf32"])
                b.tt(sO[it_][:], sO[it_][:], gf32[:], ALU.mult,
                     reads=["sO%d" % it_, "gf32"], writes=["sO%d" % it_])
                b.tt(sO[it_][:], sO[it_][:], xo2[:], ALU.add,
                     reads=["sO%d" % it_, "xo2"], writes=["sO%d" % it_])
                b.dma("sync", "st%d" % it_, yc[ot * 128:(ot + 1) * 128, :], sO[it_][:],
                      reads=["sO%d" % it_], writes=[])



def build_fused():
    nc = bass.Bass("TRN2", target_bir_lowering=False, debug=False)
    I0, I1 = {}, {}

    def di(d, name, shape, dt=F32):
        d[name] = nc.dram_tensor(name, list(shape), dt, kind="ExternalInput").ap()

    di(I0, "xb", (T_, D_)); di(I0, "xc", (TOK, D_))
    di(I0, "wq_t", (D_, D_)); di(I0, "wk_t", (D_, 512)); di(I0, "wv_t", (D_, 512))
    di(I0, "wo_t", (D_, D_))
    di(I0, "bq", (1, D_)); di(I0, "bk", (1, 512)); di(I0, "bv", (1, 512))
    di(I0, "bo", (1, D_))
    di(I0, "rw", (D_, 8)); di(I0, "rb", (1, 8)); di(I0, "ew", (8, D_, D_))
    di(I0, "eb_cols", (128, 128)); di(I0, "pw", (D_, D_)); di(I0, "pb", (1, D_))
    di(I0, "nw", (1, D_))
    di(I0, "cos0", (T_, 32)); di(I0, "sin0", (T_, 32))
    di(I0, "cos0c", (TOK, 32)); di(I0, "sin0c", (TOK, 32))
    di(I0, "iden_bf", (128, 128), BF16); di(I0, "iden_f32", (128, 128))
    di(I0, "ones128", (1, 128)); di(I0, "sel8", (8, 8 * 128))

    di(I1, "qkv_t", (D_, 3072)); di(I1, "qkv_b", (1, 3072))
    di(I1, "out_t", (D_, D_)); di(I1, "out_b", (1, D_)); di(I1, "sinks", (1, 32))
    di(I1, "gw_t", (D_, 32)); di(I1, "gb", (1, 32))
    di(I1, "fc1_t", (D_, 1024)); di(I1, "fc2_t", (1024, D_))
    di(I1, "w1_t", (32, 1024, 1024)); di(I1, "w2_t", (32, 1024, 1024))
    di(I1, "su_t", (D_, 4096)); di(I1, "sd_t", (4096, D_))
    di(I1, "cos1", (TOK, 32)); di(I1, "sin1", (TOK, 32))
    di(I1, "masks", (2, 5, 128, 512), BF16)
    di(I1, "sel32", (32, 32 * 128))
    I1["iden_bf"] = I0["iden_bf"]
    I1["ones128"] = I0["ones128"]

    x1 = nc.dram_tensor("x1", [TOK, D_], F32, kind="Internal").ap()
    gdram = nc.dram_tensor("gscr", [TOK, D_], BF16, kind="Internal").ap()
    yc = nc.dram_tensor("yc", [OWN, D_], F32, kind="ExternalOutput").ap()
    I1["x1c"] = x1

    nodbg = lambda name, shape=None, dt=F32: None

    class _NCProxy:

        def __init__(self, real):
            self._nc = real

        def __getattr__(self, k):
            return getattr(self._nc, k)

        def sbuf_tensor(self, name, shape, dt, **kw):
            kw.setdefault("allow_name_mangling", True)
            return self._nc.sbuf_tensor(name, shape, dt, **kw)

        def psum_tensor(self, name, shape, dt=F32, **kw):
            kw.setdefault("allow_name_mangling", True)
            return self._nc.psum_tensor(name, shape, dt, **kw)

    ncp = _NCProxy(nc)
    b = Prog(nc)
    with ExitStack() as ctx0:
        emit_l0(b, I0, x1, gdram, nodbg, ncp, ctx0)
    b.fence()
    with ExitStack() as ctx1:
        emit_l1(b, I1, yc, nodbg, ncp, ctx1)
    return b.build()



def kernel(**inputs):
    maps0, maps1 = prep_inputs(inputs)
    nc = build_fused()
    in_maps = []
    for c in range(N_CORES):
        m = dict(maps0[c])
        m.update(maps1[c])
        in_maps.append(m)
    res = run_bass_kernel_spmd(nc, in_maps, core_ids=list(range(N_CORES)))
    out = np.zeros((B_, T_, D_), np.float32)
    for c in range(N_CORES):
        bb, Q0 = c // 4, (c % 4) * OWN
        out[bb, Q0:Q0 + OWN] = np.asarray(res.results[c]["yc"], np.float32)
    return out


# version 04
# speedup vs baseline: 1.2939x; 1.2794x over previous
import math
from contextlib import ExitStack

import numpy as np
import ml_dtypes

import concourse.bass as bass
import concourse.mybir as mybir
from concourse.bass_utils import run_bass_kernel_spmd

F32 = mybir.dt.float32
BF16 = mybir.dt.bfloat16
AF = mybir.ActivationFunctionType
ALU = mybir.AluOpType
AX = mybir.AxisListType
BF = ml_dtypes.bfloat16

N_CORES = 8
PHASES = 99
B_, T_, D_ = 2, 4096, 2048
OWN, HALO = 1024, 128
TOK = OWN + HALO
NT = TOK // 128
NTB = T_ // 128
NF = D_ // 128
CHW = [512, 512, 128]

_EV = [h for h in range(32) if (h // 4) % 2 == 0]
_OD = [h for h in range(32) if (h // 4) % 2 == 1]
SLOT2HEAD = []
for _t in range(16):
    SLOT2HEAD.append(_EV[_t])
    SLOT2HEAD.append(_OD[_t])


COMPUTE_ENGS = ("tensor", "vector", "scalar")
DMA_QUEUES = ("sync", "gpsimd")


class Prog:
    def __init__(self, nc):
        self.nc = nc
        self.ops = []
        self.last_w = {}
        self.readers = {}
        self.slot_fills = {}
        self.bulk_slots = set()
        self._last_per_key = {}
        self._fence_need = None
        self._fence_engines = set()

    def op(self, engine, fn, reads=(), writes=()):
        assert engine in COMPUTE_ENGS, engine
        return self._add(engine, fn, reads, writes, None)

    def fence(self):
        self._fence_need = set(self._last_per_key.values())
        self._fence_engines = set()

    def dma(self, queue, slot, out, in_, reads=(), writes=(), bulk=False):
        assert queue in DMA_QUEUES, queue
        if bulk:
            self.bulk_slots.add(slot)
        self.slot_fills[slot] = self.slot_fills.get(slot, 0) + 1
        fn = lambda e, o=out, i=in_: e.dma_start(o, i)
        return self._add(queue, fn, reads, writes, (slot, self.slot_fills[slot]))

    def mm(self, out, lhsT, rhs, start, stop, reads=(), writes=()):
        fn = lambda t, o=out, l=lhsT, r=rhs, a=start, c=stop: t.matmul(
            o, l, r, start=a, stop=c, skip_group_check=True)
        return self.op("tensor", fn, reads, writes)

    def trans(self, out, in_, iden, reads=(), writes=()):
        fn = lambda t, o=out, i=in_, d=iden: t.transpose(o, i, d)
        return self.op("tensor", fn, reads, writes)

    def tt(self, out, a, c, aluop, reads=(), writes=()):
        fn = lambda v, o=out, x=a, y=c, p=aluop: v.tensor_tensor(o, x, y, p)
        return self.op("vector", fn, reads, writes)

    def ts(self, out, in0, s1, s2, op0, op1=None, reads=(), writes=()):
        if s2 is None and op1 is None:
            fn = lambda v, o=out, i=in0, a=s1, p0=op0: v.tensor_single_scalar(o, i, a, p0)
        else:
            fn = lambda v, o=out, i=in0, a=s1, c=s2, p0=op0, p1=op1: v.tensor_scalar(
                o, i, a, c, p0, p1)
        return self.op("vector", fn, reads, writes)

    def red(self, out, in_, aluop=ALU.add, negate=False, reads=(), writes=()):
        fn = lambda v, o=out, i=in_, p=aluop, n=negate: v.tensor_reduce(
            o, i, axis=AX.X, op=p, negate=n)
        return self.op("vector", fn, reads, writes)

    def vcopy(self, out, in_, reads=(), writes=()):
        fn = lambda v, o=out, i=in_: v.tensor_copy(o, i)
        return self.op("vector", fn, reads, writes)

    def vmemset(self, ap, val, reads=(), writes=()):
        fn = lambda v, a=ap, x=val: v.memset(a, x)
        return self.op("vector", fn, reads, writes)

    def vrecip(self, out, in_, reads=(), writes=()):
        fn = lambda v, o=out, i=in_: v.reciprocal(o, i)
        return self.op("vector", fn, reads, writes)

    def act(self, out, in_, func, bias=0.0, scale=1.0, reads=(), writes=()):
        fn = lambda s, o=out, i=in_, f=func, bb=bias, c=scale: s.activation(
            o, i, f, bias=bb, scale=c)
        return self.op("scalar", fn, reads, writes)

    def _verify(self, ops):
        streams = {}
        for i, o in enumerate(ops):
            streams.setdefault(o["engine"], []).append(i)
        heads = {e: 0 for e in streams}
        sem = {}
        done = 0
        while done < len(ops):
            progressed = False
            for e, lst in streams.items():
                while heads[e] < len(lst):
                    o = ops[lst[heads[e]]]
                    if any(sem.get(k, 0) < v for k, v in o["waits"]):
                        break
                    if o["inc"]:
                        if o["slot"] is not None:
                            k = "slot:" + o["slot"][0]
                            sem[k] = sem.get(k, 0) + 16
                        else:
                            k = "eng:" + e
                            sem[k] = sem.get(k, 0) + 1
                    heads[e] += 1
                    done += 1
                    progressed = True
            if not progressed:
                msg = []
                for e, lst in streams.items():
                    if heads[e] < len(lst):
                        i = lst[heads[e]]
                        o = ops[i]
                        blocked = [(k, v, sem.get(k, 0)) for k, v in o["waits"]
                                   if sem.get(k, 0) < v]
                        msg.append(f"{e} stuck at op {i} waits {blocked}")
                raise RuntimeError("DEADLOCK:\n" + "\n".join(msg))

    def _add(self, engine, fn, reads, writes, dma_slot):
        i = len(self.ops)
        deps = set()
        if self._fence_need is not None and engine not in self._fence_engines:
            deps |= self._fence_need
            self._fence_engines.add(engine)
        for r in reads:
            j = self.last_w.get(r)
            if j is not None:
                deps.add(j)
        for w in writes:
            j = self.last_w.get(w)
            if j is not None:
                deps.add(j)
            deps.update(self.readers.get(w, ()))
        deps.discard(i)
        self.ops.append(dict(engine=engine, fn=fn, deps=deps, slot=dma_slot,
                             inc=False))
        for r in reads:
            self.readers.setdefault(r, []).append(i)
        for w in writes:
            self.last_w[w] = i
            self.readers[w] = []
        self._last_per_key[dma_slot[0] if dma_slot else engine] = i
        return i

    def build(self):
        nc = self.nc
        ops = self.ops
        for o in ops:
            pruned = {}
            for j in o["deps"]:
                oj = ops[j]
                if (oj["slot"] is None and o["slot"] is None
                        and oj["engine"] == "tensor" and o["engine"] == "tensor"):
                    continue
                key = oj["slot"][0] if oj["slot"] else oj["engine"]
                if key not in pruned or j > pruned[key]:
                    pruned[key] = j
            o["rdeps"] = pruned
        for o in ops:
            for j in o["rdeps"].values():
                ops[j]["inc"] = True
        for o in ops:
            if o["slot"] is not None:
                o["inc"] = True
        cnt, run = {}, {}
        for i, o in enumerate(ops):
            if o["slot"] is None and o["inc"]:
                e = o["engine"]
                run[e] = run.get(e, 0) + 1
                cnt[i] = run[e]
        final_slot = dict(self.slot_fills)
        for o in ops:
            waits = []
            for key, j in sorted(o["rdeps"].items()):
                oj = ops[j]
                if oj["slot"] is not None:
                    slot, fill = oj["slot"]
                    v = 16 * (final_slot[slot] if slot in self.bulk_slots else fill)
                    waits.append(("slot:" + slot, v))
                else:
                    waits.append(("eng:" + oj["engine"], cnt[j]))
            o["waits"] = waits

        self._verify(ops)
        with ExitStack() as ctx:
            sems = {}
            for e in COMPUTE_ENGS:
                sems["eng:" + e] = ctx.enter_context(nc.semaphore("sem_" + e))
            for s in self.slot_fills:
                sems["slot:" + s] = ctx.enter_context(nc.semaphore("sem_" + s))
            block = ctx.enter_context(nc.Block())
            water = {}

            def emit_for(engine_name, eng):
                for o in ops:
                    if o["engine"] != engine_name:
                        continue
                    for key, v in o["waits"]:
                        wk = (engine_name, key)
                        if water.get(wk, 0) >= v:
                            continue
                        water[wk] = v
                        eng.wait_ge(sems[key], v)
                    inst = o["fn"](eng)
                    if o["inc"]:
                        if o["slot"] is not None:
                            inst.then_inc(sems["slot:" + o["slot"][0]], 16)
                        else:
                            inst.then_inc(sems["eng:" + engine_name], 1)
                if engine_name == "sync":
                    for s, n in sorted(self.slot_fills.items()):
                        wk = (engine_name, "slot:" + s)
                        if water.get(wk, 0) < 16 * n:
                            eng.wait_ge(sems["slot:" + s], 16 * n)

            @block.tensor
            def _(t):
                emit_for("tensor", t)

            @block.vector
            def _(v):
                emit_for("vector", v)

            @block.scalar
            def _(s):
                emit_for("scalar", s)

            @block.sync
            def _(s):
                emit_for("sync", s)

            @block.gpsimd
            def _(g):
                emit_for("gpsimd", g)
        return nc



def _rope_tables_l0():
    HD = 64
    inv_freq = 1.0 / (10000.0 ** (np.arange(0, HD, 2, dtype=np.float32) / HD))
    ang = np.arange(T_, dtype=np.float32)[:, None] * inv_freq[None, :]
    return np.cos(ang).astype(np.float32), np.sin(ang).astype(np.float32)


def _rope_tables_l1():
    HEAD_DIM = 64
    ROPE_THETA, SCALING, NTK_ALPHA, NTK_BETA, ICL = 150000.0, 32.0, 1.0, 32.0, 1024
    d_half = HEAD_DIM // 2
    freq = ROPE_THETA ** (np.arange(0, HEAD_DIM, 2, dtype=np.float32) / HEAD_DIM)
    concentration = 0.1 * math.log(SCALING) + 1.0
    low = d_half * math.log(ICL / (NTK_BETA * 2 * math.pi)) / math.log(ROPE_THETA)
    high = d_half * math.log(ICL / (NTK_ALPHA * 2 * math.pi)) / math.log(ROPE_THETA)
    interpolation = 1.0 / (SCALING * freq)
    extrapolation = 1.0 / freq
    ramp = np.clip((np.arange(d_half, dtype=np.float32) - low) / (high - low), 0.0, 1.0)
    mask = 1.0 - ramp
    inv_freq = interpolation * (1.0 - mask) + extrapolation * mask
    freqs = np.outer(np.arange(T_, dtype=np.float32), inv_freq)
    return ((np.cos(freqs) * concentration).astype(np.float32),
            (np.sin(freqs) * concentration).astype(np.float32))


def _l1_masks(core):
    Q0 = (core % 4) * OWN
    qp = np.arange(512)[None, :]
    kp = np.arange(128)[:, None]
    out = np.zeros((2, 5, 128, 512), dtype=BF)
    for qc in range(2):
        for m in range(5):
            r = 128 * m + kp - qp
            valid = (r >= 1) & (r <= 128)
            if Q0 == 0:
                kl = qc * 512 + 128 * m + kp + 0 * qp
                valid = valid & (kl >= 128)
            out[qc, m] = valid.astype(np.float32).astype(BF)
    return out


def _pad_slice(x, lo, hi):
    pad = max(0, -lo)
    body = x[max(lo, 0):hi]
    if pad:
        return np.concatenate([np.zeros((pad,) + x.shape[1:], x.dtype), body], 0)
    return np.ascontiguousarray(body)


def prep_inputs(inp):
    f32 = np.float32
    x = np.asarray(inp["L0_attn_x"], f32)
    cos0, sin0 = _rope_tables_l0()
    cos1, sin1 = _rope_tables_l1()
    iden_bf = np.eye(128, dtype=BF)
    iden_f32 = np.eye(128, dtype=f32)
    ones128 = np.ones((1, 128), f32)
    sel8 = np.zeros((8, 8 * 128), f32)
    for e in range(8):
        sel8[e, e * 128:(e + 1) * 128] = 1.0
    sel32 = np.zeros((32, 32 * 128), f32)
    for e in range(32):
        sel32[e, e * 128:(e + 1) * 128] = 1.0

    ebm = np.asarray(inp["L0_ffn_expert_b"], f32)
    eb_cols = np.ascontiguousarray(
        ebm.reshape(8, 16, 128).transpose(2, 1, 0).reshape(128, 128))

    wq_t0 = np.asarray(inp["L0_attn_Wq"], f32).T.reshape(D_, 32, 64)
    wq_tp = np.ascontiguousarray(wq_t0[:, SLOT2HEAD, :].reshape(D_, D_))
    bq0 = np.asarray(inp["L0_attn_bq"], f32).reshape(32, 64)
    bqp = np.ascontiguousarray(bq0[SLOT2HEAD].reshape(1, D_))
    wo_t0 = np.asarray(inp["L0_attn_Wo"], f32).T.reshape(32, 64, D_)
    wo_tp = np.ascontiguousarray(wo_t0[SLOT2HEAD].reshape(D_, D_))
    shared0 = dict(
        wq_t=wq_tp,
        wk_t=np.ascontiguousarray(np.asarray(inp["L0_attn_Wk"], f32).T),
        wv_t=np.ascontiguousarray(np.asarray(inp["L0_attn_Wv"], f32).T),
        wo_t=wo_tp,
        bq=bqp,
        bk=np.asarray(inp["L0_attn_bk"], f32).reshape(1, -1),
        bv=np.asarray(inp["L0_attn_bv"], f32).reshape(1, -1),
        bo=np.asarray(inp["L0_attn_bo"], f32).reshape(1, -1),
        rw=np.ascontiguousarray(np.asarray(inp["L0_ffn_router_w"], f32)),
        rb=np.asarray(inp["L0_ffn_router_b"], f32).reshape(1, -1),
        ew=np.ascontiguousarray(np.asarray(inp["L0_ffn_expert_w"], f32)),
        eb_cols=eb_cols,
        pw=np.ascontiguousarray(np.asarray(inp["L0_ffn_proj_w"], f32)),
        pb=np.asarray(inp["L0_ffn_proj_b"], f32).reshape(1, -1),
        nw=np.asarray(inp["L0_ffn_norm_w"], f32).reshape(1, -1),
        iden_bf=iden_bf, iden_f32=iden_f32, ones128=ones128, sel8=sel8,
        cos0=cos0, sin0=sin0)
    qkv_t0 = np.asarray(inp["L1_attn_qkv_w"], f32).T
    qp = qkv_t0[:, 0:2048].reshape(D_, 32, 64)
    qkv_tp = np.concatenate([qp[:, SLOT2HEAD, :].reshape(D_, 2048),
                             qkv_t0[:, 2048:]], axis=1)
    qb0 = np.asarray(inp["L1_attn_qkv_b"], f32).reshape(-1)
    qbq = qb0[0:2048].reshape(32, 64)
    qkv_bp = np.concatenate([qbq[SLOT2HEAD].reshape(-1), qb0[2048:]])
    out_t0 = np.asarray(inp["L1_attn_out_w"], f32).T.reshape(32, 64, D_)
    out_tp = np.ascontiguousarray(out_t0[SLOT2HEAD].reshape(D_, D_))
    shared1 = dict(
        qkv_t=np.ascontiguousarray(qkv_tp),
        qkv_b=np.ascontiguousarray(qkv_bp.reshape(1, -1)),
        out_t=out_tp,
        out_b=np.asarray(inp["L1_attn_out_b"], f32).reshape(1, -1),
        sinks=np.asarray(inp["L1_attn_sinks"], f32).reshape(1, -1),
        gw_t=np.ascontiguousarray(np.asarray(inp["L1_ffn_gate_w"], f32).T),
        gb=np.asarray(inp["L1_ffn_gate_bias"], f32).reshape(1, -1),
        fc1_t=np.ascontiguousarray(np.asarray(inp["L1_ffn_fc1_w"], f32).T),
        fc2_t=np.ascontiguousarray(np.asarray(inp["L1_ffn_fc2_w"], f32).T),
        w1_t=np.ascontiguousarray(np.asarray(inp["L1_ffn_w1"], f32).transpose(0, 2, 1)),
        w2_t=np.ascontiguousarray(np.asarray(inp["L1_ffn_w2"], f32).transpose(0, 2, 1)),
        su_t=np.ascontiguousarray(np.asarray(inp["L1_ffn_shared_up_w"], f32).T),
        sd_t=np.ascontiguousarray(np.asarray(inp["L1_ffn_shared_down_w"], f32).T),
        iden_bf=iden_bf, iden_f32=iden_f32, ones128=ones128, sel32=sel32)

    maps0, maps1 = [], []
    for c in range(N_CORES):
        bb, Q0 = c // 4, (c % 4) * OWN
        m0 = dict(shared0)
        m0["xb"] = np.ascontiguousarray(x[bb])
        m0["xc"] = _pad_slice(x[bb], Q0 - HALO, Q0 + OWN)
        m0["cos0c"] = _pad_slice(cos0, Q0 - HALO, Q0 + OWN)
        m0["sin0c"] = _pad_slice(sin0, Q0 - HALO, Q0 + OWN)
        maps0.append(m0)
        m1 = dict(shared1)
        m1["cos1"] = _pad_slice(cos1, Q0 - HALO, Q0 + OWN)
        m1["sin1"] = _pad_slice(sin1, Q0 - HALO, Q0 + OWN)
        m1["masks"] = _l1_masks(c)
        maps1.append(m1)
    return maps0, maps1



def r3(src2d, p=128):
    return src2d.rearrange("(n p) d -> p n d", p=p)


def d3(dstbuf_slice, d):
    return dstbuf_slice.rearrange("p (n d) -> p n d", d=d)



def fmv(t, total):
    return t[:, :].rearrange("p (f c) -> p f c", c=total)


def emit_ln_tile(b, C, xap, xkey, st, stk, out_bf, outkey, eps,
                 mcol=None, rcol=None, statkeys=("m_own", "r_own")):
    scr, sck = C["lnscr"], "lnscr"
    b.red(st[:, 0:1], xap, reads=[xkey], writes=[stk])
    b.tt(scr[:, 0:D_], xap, xap, ALU.mult, reads=[xkey], writes=[sck])
    b.red(st[:, 1:2], scr[:, 0:D_], reads=[sck], writes=[stk])
    b.ts(st[:, 0:1], st[:, 0:1], 1.0 / D_, None, ALU.mult, reads=[stk], writes=[stk])
    b.tt(st[:, 2:3], st[:, 0:1], st[:, 0:1], ALU.mult, reads=[stk], writes=[stk])
    b.ts(st[:, 1:2], st[:, 1:2], 1.0 / D_, None, ALU.mult, reads=[stk], writes=[stk])
    b.tt(st[:, 1:2], st[:, 1:2], st[:, 2:3], ALU.subtract, reads=[stk], writes=[stk])
    b.ts(st[:, 1:2], st[:, 1:2], float(eps), None, ALU.add, reads=[stk], writes=[stk])
    b.act(st[:, 2:3], st[:, 1:2], AF.Ln, reads=[stk], writes=[stk])
    b.act(st[:, 3:4], st[:, 2:3], AF.Exp, scale=-0.5, reads=[stk], writes=[stk])
    b.ts(out_bf, xap, st[:, 0:1], st[:, 3:4], ALU.subtract, ALU.mult,
         reads=[xkey, stk], writes=[outkey])
    if mcol is not None:
        b.vcopy(mcol, st[:, 0:1], reads=[stk], writes=[statkeys[0]])
        b.vcopy(rcol, st[:, 3:4], reads=[stk], writes=[statkeys[1]])


def emit_transpose16(b, C, src_bf, srckey, dstv, dstkey, nft=16):
    for g in range((nft + 7) // 8):
        n = min(8, nft - g * 8)
        pt, ptk = C["pt%d" % (g % 2)], "pt%d" % (g % 2)
        for k in range(n):
            ft = g * 8 + k
            b.trans(pt[:, k * 128:(k + 1) * 128], src_bf[:, ft * 128:(ft + 1) * 128],
                    C["idb"][:], reads=[srckey, "idb"], writes=[ptk])
        b.vcopy(dstv(g, n), pt[:, 0:n * 128].rearrange("p (f c) -> p f c", c=128),
                reads=[ptk], writes=[dstkey])


def emit_rope(b, C, src, srckey, dst, dstkey, costile, sintile, nheads, tmp, tmpkey,
              ckeys=("cosb", "sinb")):
    ca = costile.rearrange("p (o d) -> p o d", o=1).broadcast_to([128, nheads, 32])
    sa = sintile.rearrange("p (o d) -> p o d", o=1).broadcast_to([128, nheads, 32])
    s3 = src.rearrange("p (h d) -> p h d", d=64)
    d3 = dst.rearrange("p (h d) -> p h d", d=64)
    t3 = tmp.rearrange("p (h d) -> p h d", d=64)
    x1, x2 = s3[:, :, 0:32], s3[:, :, 32:64]
    b.tt(t3[:, :, 0:32], x1, ca, ALU.mult, reads=[srckey, ckeys[0]], writes=[tmpkey])
    b.tt(t3[:, :, 32:64], x2, sa, ALU.mult, reads=[srckey, ckeys[1]], writes=[tmpkey])
    b.tt(d3[:, :, 0:32], t3[:, :, 0:32], t3[:, :, 32:64], ALU.subtract,
         reads=[tmpkey], writes=[dstkey])
    b.tt(t3[:, :, 0:32], x2, ca, ALU.mult, reads=[srckey, ckeys[0]], writes=[tmpkey])
    b.tt(t3[:, :, 32:64], x1, sa, ALU.mult, reads=[srckey, ckeys[1]], writes=[tmpkey])
    b.tt(d3[:, :, 32:64], t3[:, :, 0:32], t3[:, :, 32:64], ALU.add,
         reads=[tmpkey], writes=[dstkey])


def emit_global_max(b, nc, ctx, tag, col, colkey, out11, outkey, row, rowkey):
    dscr = nc.dram_tensor("gmax_" + tag, [1, 128], F32, kind="Internal").ap()
    b.dma("sync", "gm" + tag, dscr.rearrange("o p -> p o"), col,
          reads=[colkey], writes=["gmd" + tag])
    b.dma("sync", "gm" + tag + "b", row, dscr, reads=["gmd" + tag], writes=[rowkey])
    b.red(out11, row, aluop=ALU.max, reads=[rowkey], writes=[outkey])



def build_l0(debug=()):
    nc = bass.Bass("TRN2", target_bir_lowering=False, debug=False)
    I = {}

    def di(name, shape, dt=F32):
        I[name] = nc.dram_tensor(name, list(shape), dt, kind="ExternalInput").ap()

    di("xb", (T_, D_)); di("xc", (TOK, D_))
    di("wq_t", (D_, D_)); di("wk_t", (D_, 512)); di("wv_t", (D_, 512))
    di("wo_t", (D_, D_))
    di("bq", (1, D_)); di("bk", (1, 512)); di("bv", (1, 512)); di("bo", (1, D_))
    di("rw", (D_, 8)); di("rb", (1, 8)); di("ew", (8, D_, D_))
    di("eb_cols", (128, 128)); di("pw", (D_, D_)); di("pb", (1, D_)); di("nw", (1, D_))
    di("cos0", (T_, 32)); di("sin0", (T_, 32))
    di("cos0c", (TOK, 32)); di("sin0c", (TOK, 32))
    di("iden_bf", (128, 128), BF16); di("iden_f32", (128, 128))
    di("ones128", (1, 128)); di("sel8", (8, 8 * 128))

    x1 = nc.dram_tensor("x1", [TOK, D_], F32, kind="ExternalOutput").ap()
    gdram = nc.dram_tensor("gscr", [TOK, D_], BF16, kind="Internal").ap()
    DBG = {}

    def dbg(name, shape=None, dt=F32):
        if name in debug and shape is not None and name not in DBG:
            DBG[name] = nc.dram_tensor("dbg_" + name, list(shape), dt,
                                       kind="ExternalOutput").ap()
        return DBG.get(name)

    b = Prog(nc)
    with ExitStack() as ctx:
        emit_l0(b, I, x1, gdram, dbg, nc, ctx)
    return b.build(), sorted(DBG)


def emit_l0(b, I, x1, gdram, dbg, nc, ctx):
    def sb(name, shape, dt=F32):
        return ctx.enter_context(nc.sbuf_tensor(name, list(shape), dt))

    def psg(st, name, shape, dt=F32):
        return st.enter_context(nc.psum_tensor(name, list(shape), dt))

    C = {}
    K_fm = sb("K_fm", [128, 4 * T_], BF16)
    V_tm = sb("V_tm", [128, NTB * 520], BF16)
    m_own = sb("m_own", [128, NT]); r_own = sb("r_own", [128, NT])
    q2acc = sb("q2acc", [128, 32]); k2acc = sb("k2acc", [128, 8])
    negB = sb("negB", [128, 1])
    C["idb"] = sb("idb", [128, 128], BF16)
    onesr = sb("onesr", [1, 128])

    b.dma("sync", "c0", C["idb"][:], I["iden_bf"], writes=["idb"], bulk=True)
    b.dma("sync", "c0", onesr[:], I["ones128"], writes=["onesr"], bulk=True)
    b.vmemset(q2acc[:], 0.0, writes=["q2acc"])
    b.vmemset(k2acc[:], 0.0, writes=["k2acc"])

    sComb = ExitStack()
    with ExitStack() as sAQ:
        xn_fm = sAQ.enter_context(nc.sbuf_tensor("xn_fm", [128, NF * TOK], BF16))
        wT = sAQ.enter_context(nc.sbuf_tensor("wT", [8, TOK], F32))
        scs = sAQ.enter_context(ExitStack())
        cosb = scs.enter_context(nc.sbuf_tensor("cosb", [128, NTB * 32], F32))
        sinb = scs.enter_context(nc.sbuf_tensor("sinb", [128, NTB * 32], F32))
        b.dma("sync", "c0", d3(cosb[:], 32), r3(I["cos0"]),
              writes=["cosb"], bulk=True)
        b.dma("sync", "c0", d3(sinb[:], 32), r3(I["sin0"]),
              writes=["sinb"], bulk=True)

        with ExitStack() as sA:
            def asb(name, shape, dt=F32):
                return sA.enter_context(nc.sbuf_tensor(name, list(shape), dt))

            xt = [asb("xt%d" % i, [128, D_]) for i in range(2)]
            C["lnscr"] = asb("lnscr", [128, D_])
            wkv = asb("wkv", [128, NF * 1024], BF16)
            xnbf = [asb("xnbf0", [128, D_], BF16)]
            stat = [asb("stat%d" % i, [128, 4]) for i in range(2)]
            xnfm_kv = asb("xnfm_kv", [128, NF * 512], BF16)
            kvscr = [asb("kvscr0", [128, 512])]
            kvtmp = [asb("kvtmp0", [128, 512])]
            krs = [asb("krs%d" % i, [128, 512], BF16) for i in range(2)]
            bk_bc = asb("bk_bc", [128, 512])
            bv_bc = asb("bv_bc", [128, 512])
            pkv = [psg(sA, "pkv%d" % i, [128, 512]) for i in range(2)]
            C["pt0"] = psg(sA, "pt0", [128, 1024], BF16)
            C["pt1"] = psg(sA, "pt1", [128, 1024], BF16)

            b.dma("gpsimd", "wkv", d3(wkv[:, 0:NF * 512], 512), r3(I["wk_t"]),
                  writes=["wkv_k"], bulk=True)
            b.dma("gpsimd", "wkv", d3(wkv[:, NF * 512:], 512), r3(I["wv_t"]),
                  writes=["wkv_v"], bulk=True)
            b.dma("sync", "cA", bk_bc[:], I["bk"].broadcast_to([128, 512]),
                  writes=["bk_bc"], bulk=True)
            b.dma("sync", "cA", bv_bc[:], I["bv"].broadcast_to([128, 512]),
                  writes=["bv_bc"], bulk=True)

            for ch in range(NTB // 4):
                for it in range(4):
                    tt = ch * 4 + it
                    bi = tt % 2
                    xtb, xk = xt[bi], "xt%d" % bi
                    b.dma("sync", xk, xtb[:], I["xb"][tt * 128:(tt + 1) * 128, :],
                          writes=[xk])
                    emit_ln_tile(b, C, xtb[:], xk, stat[bi], "stat%d" % bi,
                                 xnbf[0][:], "xnbf0", 1e-5)
                    kvv = fmv(xnfm_kv, 512)
                    emit_transpose16(
                        b, C, xnbf[0][:], "xnbf0",
                        (lambda g, n, it_=it: kvv[:, g * 8:g * 8 + n,
                                                  it_ * 128:(it_ + 1) * 128]),
                        "xnfm_kv")
                for it in range(4):
                    tt = ch * 4 + it
                    bi = it % 2
                    pk, pkk = pkv[bi], "pkv%d" % bi
                    scr, sck = kvscr[0], "kvscr0"
                    tmp, tmk = kvtmp[0], "kvtmp0"
                    for fc in range(NF):
                        b.mm(pk[:], xnfm_kv[:, fc * 512 + it * 128:fc * 512 + (it + 1) * 128],
                             wkv[:, fc * 512:(fc + 1) * 512], fc == 0, fc == NF - 1,
                             reads=["xnfm_kv", "wkv_k"], writes=[pkk])
                    b.tt(scr[:], pk[:], bk_bc[:], ALU.add, reads=[pkk, "bk_bc"], writes=[sck])
                    emit_rope(b, C, scr[:], sck, krs[bi][:], "krs%d" % bi,
                              cosb[:, tt * 32:(tt + 1) * 32], sinb[:, tt * 32:(tt + 1) * 32],
                              8, tmp[:], tmk)
                    pt, ptk = C["pt%d" % bi], "pt%d" % bi
                    for k in range(4):
                        b.trans(pt[:, k * 128:(k + 1) * 128], krs[bi][:, k * 128:(k + 1) * 128],
                                C["idb"][:], reads=["krs%d" % bi, "idb"], writes=[ptk])
                    b.vcopy(fmv(K_fm, T_)[:, 0:4, tt * 128:(tt + 1) * 128],
                            pt[:, 0:512].rearrange("p (f c) -> p f c", c=128),
                            reads=[ptk], writes=["K_fm"])
                    b.tt(tmp[:], krs[bi][:], krs[bi][:], ALU.mult,
                         reads=["krs%d" % bi], writes=[tmk])
                    b.red(scr[:, 0:8], tmp[:].rearrange("p (h d) -> p h d", d=64),
                          aluop=ALU.max, reads=[tmk], writes=[sck])
                    b.tt(k2acc[:], k2acc[:], scr[:, 0:8], ALU.max,
                         reads=[sck, "k2acc"], writes=["k2acc"])
                    for fc in range(NF):
                        b.mm(pk[:], xnfm_kv[:, fc * 512 + it * 128:fc * 512 + (it + 1) * 128],
                             wkv[:, NF * 512 + fc * 512:NF * 512 + (fc + 1) * 512],
                             fc == 0, fc == NF - 1,
                             reads=["xnfm_kv", "wkv_v"], writes=[pkk])
                    vt = V_tm[:, tt * 520:(tt + 1) * 520].rearrange("p (h c) -> p h c", c=65)
                    b.tt(vt[:, :, 0:64], pk[:].rearrange("p (h c) -> p h c", c=64),
                         bv_bc[:].rearrange("p (h c) -> p h c", c=64), ALU.add,
                         reads=[pkk, "bv_bc"], writes=["V_tm"])
                    b.vmemset(vt[:, :, 64:65], 1.0, writes=["V_tm"])

            for ot in range(NT):
                bi = ot % 2
                xtb, xk = xt[bi], "xt%d" % bi
                b.dma("sync", xk, xtb[:], I["xc"][ot * 128:(ot + 1) * 128, :],
                      writes=[xk])
                emit_ln_tile(b, C, xtb[:], xk, stat[bi], "stat%d" % bi,
                             xnbf[0][:], "xnbf0", 1e-5,
                             mcol=m_own[:, ot:ot + 1], rcol=r_own[:, ot:ot + 1])
                xv = fmv(xn_fm, TOK)
                emit_transpose16(
                    b, C, xnbf[0][:], "xnbf0",
                    (lambda g, n, ot_=ot: xv[:, g * 8:g * 8 + n, ot_ * 128:(ot_ + 1) * 128]),
                    "xn_fm")

        b.fence()
        if dbg("xn_fm", (128, NF * TOK)):
            b.dma("gpsimd", "dbg1", dbg("xn_fm"), xn_fm[:], reads=["xn_fm"], writes=[])
        if dbg("K_fm", (128, 4 * T_)):
            b.dma("gpsimd", "dbg2", dbg("K_fm"), K_fm[:], reads=["K_fm"], writes=[])
        if dbg("V_tm", (128, NTB * 520)):
            b.dma("gpsimd", "dbg3", dbg("V_tm"), V_tm[:], reads=["V_tm"], writes=[])

        if PHASES < 2:
            return
        with ExitStack() as sR:
            rw_b = sR.enter_context(nc.sbuf_tensor("rw_b", [128, NF * 8], BF16))
            rw_f = sR.enter_context(nc.sbuf_tensor("rw_f", [128, NF * 8], F32))
            rb_bc = sR.enter_context(nc.sbuf_tensor("rb_bc", [128, 8], F32))
            rgb = sR.enter_context(nc.sbuf_tensor("rgb", [128, 8], BF16))
            rscr = sR.enter_context(nc.sbuf_tensor("rscr", [128, 32], F32))
            rpsum_t = psg(sR, "rpsum", [128, 512])
            rpsum = rpsum_t[:, 0:8]
            rtp_t = psg(sR, "rtp", [128, 1024], BF16)
            rtp = rtp_t[0:8, 0:128]
            b.dma("sync", "cRg", d3(rw_f[:], 8), r3(I["rw"]),
                  writes=["rw_f"], bulk=True)
            b.vcopy(rw_b[:], rw_f[:], reads=["rw_f"], writes=["rw_b"])
            b.dma("sync", "cR", rb_bc[:], I["rb"].broadcast_to([128, 8]),
                  writes=["rb_bc"], bulk=True)

            for ot in range(NT):
                for fc in range(NF):
                    b.mm(rpsum, fmv(xn_fm, TOK)[:, fc, ot * 128:(ot + 1) * 128],
                         rw_b[:, fc * 8:(fc + 1) * 8], fc == 0, fc == NF - 1,
                         reads=["xn_fm", "rw_b"], writes=["rpsum"])
                l, lk = rscr[:, 0:8], "rscr"
                b.tt(l, rpsum, rb_bc[:], ALU.add, reads=["rpsum", "rb_bc"], writes=[lk])
                m1, nm1, m2 = rscr[:, 8:9], rscr[:, 9:10], rscr[:, 10:11]
                e21, den, rcp = rscr[:, 11:12], rscr[:, 12:13], rscr[:, 13:14]
                tq, ge = rscr[:, 16:24], rscr[:, 24:32]
                b.red(m1, l, aluop=ALU.max, reads=[lk], writes=[lk])
                b.red(nm1, l, aluop=ALU.max, negate=True, reads=[lk], writes=[lk])
                b.ts(tq, l, m1, -1e30, ALU.is_equal, ALU.mult, reads=[lk], writes=[lk])
                b.tt(tq, tq, l, ALU.add, reads=[lk], writes=[lk])
                b.red(m2, tq, aluop=ALU.max, reads=[lk], writes=[lk])
                b.act(e21, m2, AF.Exp, bias=nm1, reads=[lk], writes=[lk])
                b.ts(den, e21, 1.0, None, ALU.add, reads=[lk], writes=[lk])
                b.vrecip(rcp, den, reads=[lk], writes=[lk])
                b.act(tq, l, AF.Exp, bias=nm1, reads=[lk], writes=[lk])
                b.ts(ge, l, m2, None, ALU.is_ge, reads=[lk], writes=[lk])
                b.tt(ge, ge, tq, ALU.mult, reads=[lk], writes=[lk])
                b.ts(ge, ge, rcp, None, ALU.mult, reads=[lk], writes=[lk])
                if dbg("rdump", (128, NT * 32)):
                    b.dma("sync", "dbgR", dbg("rdump")[:, ot * 32:(ot + 1) * 32],
                          rscr[:], reads=[lk], writes=[])
                b.vcopy(rgb[:], ge, reads=[lk], writes=["rgb"])
                b.trans(rtp, rgb[:], C["idb"][:], reads=["rgb", "idb"], writes=["rtp"])
                b.vcopy(wT[:, ot * 128:(ot + 1) * 128], rtp, reads=["rtp"], writes=["wT"])

        if dbg("wT", (8, TOK)):
            b.dma("sync", "dbg4", dbg("wT"), wT[:], reads=["wT"], writes=[])

        b.fence()
        if PHASES < 3:
            return
        Q_fm = ctx.enter_context(nc.sbuf_tensor("Q_fm", [128, NF * TOK], BF16,
                                                side="right"))
        with ExitStack() as sQ:
            wqs = [sQ.enter_context(nc.sbuf_tensor("wqs%d" % i, [128, NF * 512], BF16))
                   for i in range(2)]
            qscr = [sQ.enter_context(nc.sbuf_tensor("qscr%d" % i, [128, 512], F32))
                    for i in range(2)]
            qtmp = [sQ.enter_context(nc.sbuf_tensor("qtmp%d" % i, [128, 512], F32))
                    for i in range(2)]
            qrb = [sQ.enter_context(nc.sbuf_tensor("qrb%d" % i, [128, 512], BF16))
                   for i in range(2)]
            bq_bc = sQ.enter_context(nc.sbuf_tensor("bq_bc", [128, D_], F32))
            cosc = sQ.enter_context(nc.sbuf_tensor("cosc", [128, NT * 32], F32))
            sinc = sQ.enter_context(nc.sbuf_tensor("sinc", [128, NT * 32], F32))
            ppq = [psg(sQ, "ppq%d" % i, [128, 512]) for i in range(2)]
            C["pt0"] = psg(sQ, "pt0b", [128, 1024], BF16)
            C["pt1"] = psg(sQ, "pt1b", [128, 1024], BF16)
            b.dma("sync", "cQ", bq_bc[:], I["bq"].broadcast_to([128, D_]),
                  writes=["bq_bc"], bulk=True)
            b.dma("sync", "cQ", d3(cosc[:], 32), r3(I["cos0c"]),
                  writes=["cosc"], bulk=True)
            b.dma("sync", "cQ", d3(sinc[:], 32), r3(I["sin0c"]),
                  writes=["sinc"], bulk=True)
            for ncc in range(4):
                wi = ncc % 2
                wb, wk = wqs[wi], "wqs%d" % wi
                b.dma("gpsimd", wk, d3(wb[:], 512), r3(I["wq_t"][:, ncc * 512:(ncc + 1) * 512]),
                      writes=[wk])
                for ot in range(NT):
                    bi = ot % 2
                    pp, ppk = ppq[bi], "ppq%d" % bi
                    for fc in range(NF):
                        b.mm(pp[:], fmv(xn_fm, TOK)[:, fc, ot * 128:(ot + 1) * 128],
                             wb[:, fc * 512:(fc + 1) * 512], fc == 0, fc == NF - 1,
                             reads=["xn_fm", wk], writes=[ppk])
                    b.tt(qscr[bi][:], pp[:], bq_bc[:, ncc * 512:(ncc + 1) * 512], ALU.add,
                         reads=[ppk, "bq_bc"], writes=["qscr%d" % bi])
                    emit_rope(b, C, qscr[bi][:], "qscr%d" % bi, qrb[bi][:], "qrb%d" % bi,
                              cosc[:, ot * 32:(ot + 1) * 32], sinc[:, ot * 32:(ot + 1) * 32],
                              8, qtmp[bi][:], "qtmp%d" % bi, ckeys=("cosc", "sinc"))
                    b.tt(qtmp[bi][:], qrb[bi][:], qrb[bi][:], ALU.mult,
                         reads=["qrb%d" % bi], writes=["qtmp%d" % bi])
                    b.red(qscr[bi][:, 0:8], qtmp[bi][:].rearrange("p (h d) -> p h d", d=64),
                          aluop=ALU.max, reads=["qtmp%d" % bi], writes=["qscr%d" % bi])
                    b.tt(q2acc[:, ncc * 8:(ncc + 1) * 8], q2acc[:, ncc * 8:(ncc + 1) * 8],
                         qscr[bi][:, 0:8], ALU.max,
                         reads=["qscr%d" % bi, "q2acc"], writes=["q2acc"])
                    pt, ptk = C["pt%d" % bi], "pt%d" % bi
                    for k in range(4):
                        b.trans(pt[:, k * 128:(k + 1) * 128], qrb[bi][:, k * 128:(k + 1) * 128],
                                C["idb"][:], reads=["qrb%d" % bi, "idb"], writes=[ptk])
                    b.vcopy(fmv(Q_fm, TOK)[:, ncc * 4:(ncc + 1) * 4, ot * 128:(ot + 1) * 128],
                            pt[:, 0:512].rearrange("p (f c) -> p f c", c=128),
                            reads=[ptk], writes=["Q_fm"])

        b.fence()
        scs.close()

        if PHASES < 4:
            return
        comb = sComb.enter_context(nc.sbuf_tensor("comb", [128, NF * TOK], BF16,
                                                   side="right"))
        combv = fmv(comb, TOK)
        with ExitStack() as sM:
            wes = [sM.enter_context(nc.sbuf_tensor("wes%d" % i, [128, NF * 256], BF16))
                   for i in range(2)]
            she = [sM.enter_context(nc.sbuf_tensor("she0", [128, 512], BF16))] * 2
            mtmp = [sM.enter_context(nc.sbuf_tensor("mtmp0", [128, 512], BF16))] * 2
            wbcs = sM.enter_context(nc.sbuf_tensor("wbcs", [128, TOK], BF16))
            sel8b = sM.enter_context(nc.sbuf_tensor("sel8b", [8, 8 * 128], F32))
            eb_c = sM.enter_context(nc.sbuf_tensor("eb_c", [128, 128], F32))
            pH = [psg(sM, "pH%d" % i, [128, 512]) for i in range(2)]
            pwb = psg(sM, "pwb", [128, 512])
            b.dma("sync", "cM", sel8b[:], I["sel8"], writes=["sel8b"], bulk=True)
            b.dma("sync", "cM", eb_c[:], I["eb_cols"], writes=["eb_c"], bulk=True)

            for e in range(8):
                for ci, tw in enumerate(CHW):
                    b.mm(pwb[:, 0:tw], sel8b[:, e * 128:(e + 1) * 128],
                         wT[:, ci * 512:ci * 512 + tw], True, True,
                         reads=["sel8b", "wT"], writes=["pwb"])
                    b.vcopy(wbcs[:, ci * 512:ci * 512 + tw], pwb[:, 0:tw],
                            reads=["pwb"], writes=["wbcs"])
                for ft4 in range(8):
                    wi = (e * 8 + ft4) % 2
                    wb, wk = wes[wi], "wes%d" % wi
                    b.dma("gpsimd", wk, d3(wb[:], 256),
                          r3(I["ew"][e, :, ft4 * 256:(ft4 + 1) * 256]), writes=[wk])
                    for ftl in range(2):
                        ft = ft4 * 2 + ftl
                        for ci, tw in enumerate(CHW):
                            t0 = ci * 512
                            pi = (ft * 3 + ci) % 2
                            hb, hbk = pH[pi], "pH%d" % pi
                            for fc in range(NF):
                                b.mm(hb[:, 0:tw],
                                     wb[:, fc * 256 + ftl * 128:fc * 256 + (ftl + 1) * 128],
                                     fmv(xn_fm, TOK)[:, fc, t0:t0 + tw], fc == 0, fc == NF - 1,
                                     reads=[wk, "xn_fm"], writes=[hbk])
                            b.act(she[0][:, 0:tw], hb[:, 0:tw], AF.Silu,
                                  bias=eb_c[:, ft * 8 + e:ft * 8 + e + 1],
                                  reads=[hbk, "eb_c"], writes=["she0"])
                            dst = combv[:, ft, t0:t0 + tw]
                            ck = "comb%d_%d" % (ft, ci)
                            if e == 0:
                                b.tt(dst, she[0][:, 0:tw], wbcs[:, t0:t0 + tw], ALU.mult,
                                     reads=["she0", "wbcs"], writes=[ck])
                            else:
                                b.tt(mtmp[0][:, 0:tw], she[0][:, 0:tw], wbcs[:, t0:t0 + tw],
                                     ALU.mult, reads=["she0", "wbcs"],
                                     writes=["mtmp0"])
                                b.tt(dst, dst, mtmp[0][:, 0:tw], ALU.add,
                                     reads=["mtmp0", ck], writes=[ck])

    dbg("g_out", (TOK, D_))
    b.fence()
    if dbg("comb", (128, NF * TOK)):
        b.dma("gpsimd", "dbgC", dbg("comb"), comb[:], reads=["comb0_0"], writes=[])
    if PHASES < 5:
        sComb.close()
        return
    with ExitStack() as sG:
        wpj = [sG.enter_context(nc.sbuf_tensor("wpj%d" % i, [128, NF * 512], BF16))
               for i in range(2)]
        pb_bc = sG.enter_context(nc.sbuf_tensor("pb_bc", [128, D_], F32))
        nw_bc = sG.enter_context(nc.sbuf_tensor("nw_bc", [128, D_], F32))
        scrY = [sG.enter_context(nc.sbuf_tensor("scrY0", [128, D_], F32))]
        xo = [sG.enter_context(nc.sbuf_tensor("xo0", [128, D_], F32))]
        gt = [sG.enter_context(nc.sbuf_tensor("gt0", [128, D_], BF16))]
        yst = [sG.enter_context(nc.sbuf_tensor("yst0", [128, 4], F32))]
        pP = [psg(sG, "pP%d" % i, [128, 512]) for i in range(2)]
        b.dma("sync", "cG", pb_bc[:], I["pb"].broadcast_to([128, D_]),
              writes=["pb_bc"], bulk=True)
        b.dma("sync", "cG", nw_bc[:], I["nw"].broadcast_to([128, D_]),
              writes=["nw_bc"], bulk=True)
        for ot in range(NT):
            bi = 0
            sy, syk = scrY[0], "scrY0"
            xb_, xbk = xo[0], "xo0"
            b.dma("sync", xbk, xb_[:], I["xc"][ot * 128:(ot + 1) * 128, :],
                  writes=[xbk])
            for ncc in range(4):
                wi = ncc % 2
                wb, wk = wpj[wi], "wpj%d" % wi
                b.dma("gpsimd", wk, d3(wb[:], 512), r3(I["pw"][:, ncc * 512:(ncc + 1) * 512]),
                      writes=[wk])
                pp, ppk = pP[wi], "pP%d" % wi
                for fc in range(NF):
                    b.mm(pp[:], combv[:, fc, ot * 128:(ot + 1) * 128],
                         wb[:, fc * 512:(fc + 1) * 512], fc == 0, fc == NF - 1,
                         reads=["comb%d_%d" % (fc, min(ot // 4, 2)), wk], writes=[ppk])
                b.tt(sy[:, ncc * 512:(ncc + 1) * 512], pp[:],
                     pb_bc[:, ncc * 512:(ncc + 1) * 512], ALU.add,
                     reads=[ppk, "pb_bc"], writes=[syk])
            b.ts(xb_[:], xb_[:], m_own[:, ot:ot + 1], r_own[:, ot:ot + 1],
                 ALU.subtract, ALU.mult, reads=[xbk, "m_own", "r_own"], writes=[xbk])
            b.tt(sy[:], sy[:], xb_[:], ALU.add, reads=[syk, xbk], writes=[syk])
            st, stk = yst[0], "yst0"
            b.tt(xb_[:], sy[:], sy[:], ALU.mult, reads=[syk, xbk], writes=[xbk])
            b.red(st[:, 0:1], xb_[:], reads=[xbk], writes=[stk])
            b.ts(st[:, 0:1], st[:, 0:1], 1.0 / D_, 1e-6, ALU.mult, ALU.add,
                 reads=[stk], writes=[stk])
            b.act(st[:, 1:2], st[:, 0:1], AF.Ln, reads=[stk], writes=[stk])
            b.act(st[:, 2:3], st[:, 1:2], AF.Exp, scale=-0.5, reads=[stk], writes=[stk])
            b.ts(sy[:], sy[:], st[:, 2:3], None, ALU.mult, reads=[syk, stk], writes=[syk])
            b.tt(sy[:], sy[:], nw_bc[:], ALU.mult, reads=[syk, "nw_bc"], writes=[syk])
            b.act(gt[0][:], sy[:], AF.Silu, reads=[syk], writes=["gt0"])
            b.dma("sync", "gst0", gdram[ot * 128:(ot + 1) * 128, :], gt[0][:],
                  reads=["gt0"], writes=["gd%d" % ot])
            if dbg("g_out"):
                b.dma("gpsimd", "dbgG", dbg("g_out")[ot * 128:(ot + 1) * 128, :],
                      gt[0][:], reads=["gt0"], writes=[])

    b.fence()
    sComb.close()

    if dbg("Q_fm", (128, NF * TOK)):
        b.dma("gpsimd", "dbg5", dbg("Q_fm"), Q_fm[:], reads=["Q_fm"], writes=[])

    if PHASES < 6:
        return
    with ExitStack() as sB:
        bscr = sB.enter_context(nc.sbuf_tensor("bscr", [128, 8], F32))
        brow = sB.enter_context(nc.sbuf_tensor("brow", [1, 128], F32))
        bsc2 = sB.enter_context(nc.sbuf_tensor("bsc2", [1, 4], F32))
        pnb_t = psg(sB, "pnb", [128, 512])
        pnb = pnb_t[:, 0:1]
        b.red(bscr[:, 0:1], q2acc[:], aluop=ALU.max, reads=["q2acc"], writes=["bscr"])
        emit_global_max(b, nc, ctx, "q", bscr[:, 0:1], "bscr", bsc2[0:1, 0:1], "bsc2",
                        brow[0:1, :], "brow")
        b.red(bscr[:, 1:2], k2acc[:], aluop=ALU.max, reads=["k2acc"], writes=["bscr"])
        emit_global_max(b, nc, ctx, "k", bscr[:, 1:2], "bscr", bsc2[0:1, 1:2], "bsc2",
                        brow[0:1, :], "brow")
        b.tt(bsc2[0:1, 2:3], bsc2[0:1, 0:1], bsc2[0:1, 1:2], ALU.mult,
             reads=["bsc2"], writes=["bsc2"])
        b.act(bsc2[0:1, 2:3], bsc2[0:1, 2:3], AF.Ln, reads=["bsc2"], writes=["bsc2"])
        b.act(bsc2[0:1, 3:4], bsc2[0:1, 2:3], AF.Exp, scale=0.5, reads=["bsc2"], writes=["bsc2"])
        b.ts(bsc2[0:1, 3:4], bsc2[0:1, 3:4], -0.125, None, ALU.mult,
             reads=["bsc2"], writes=["bsc2"])
        b.mm(pnb, onesr[:], bsc2[0:1, 3:4], True, True,
             reads=["onesr", "bsc2"], writes=["pnb"])
        b.vcopy(negB[:], pnb, reads=["pnb"], writes=["negB"])
    if dbg("negB", (128, 1)):
        b.dma("sync", "dbg6", dbg("negB"), negB[:], reads=["negB"], writes=[])

    b.fence()
    if PHASES < 7:
        return
    with ExitStack() as sT:
        def tsb(name, shape, dt=F32):
            return sT.enter_context(nc.sbuf_tensor(name, list(shape), dt))

        attnfm = tsb("attnfm", [128, NF * 512], BF16)
        ptl = [tsb("ptl%d" % i, [128, 512], BF16) for i in range(6)]
        rdrow = tsb("rdrow", [1, 512])
        rbc = tsb("rbc", [64, 512])
        wos = [tsb("wos%d" % i, [128, NF * 512], BF16) for i in range(1)]
        bo_bc = tsb("bo_bc", [128, D_])
        sO = [tsb("sO%d" % i, [128, D_]) for i in range(4)]
        gld = [tsb("gld%d" % i, [128, D_], BF16) for i in range(1)]
        gf32 = [tsb("gf32%d" % i, [128, D_]) for i in range(1)]
        xo2 = [tsb("xo2%d" % i, [128, D_]) for i in range(1)]
        pS = [psg(sT, "pS%d" % i, [128, 512]) for i in range(3)]
        pAV = [psg(sT, "pAV%d" % i, [65, 512]) for i in range(2)]
        pbc = psg(sT, "pbc", [64, 512])
        pO = [psg(sT, "pO%d" % i, [128, 512]) for i in range(2)]
        b.dma("sync", "cT", bo_bc[:], I["bo"].broadcast_to([128, D_]),
              writes=["bo_bc"], bulk=True)

        for ci, qw in enumerate(CHW):
            q0 = ci * 512
            for pr in range(16):
                pair = (2 * pr, 2 * pr + 1)

                def _score(kt, j, si):
                    h = SLOT2HEAD[si]
                    kvh = h // 4
                    qrhs = fmv(Q_fm, TOK)[:, si // 2, q0:q0 + qw][
                        (si % 2) * 64:(si % 2) * 64 + 64, :]
                    ii = 2 * kt + j
                    sp, spk = pS[ii % 3], "pS%d" % (ii % 3)
                    klhs = fmv(K_fm, T_)[:, kvh // 2, kt * 128:(kt + 1) * 128][
                        (kvh % 2) * 64:(kvh % 2) * 64 + 64, :]
                    b.mm(sp[:, 0:qw], klhs, qrhs, True, True,
                         reads=["K_fm", "Q_fm"], writes=[spk])
                    pb_, pbk = ptl[ii % 6], "ptl%d" % (ii % 6)
                    b.act(pb_[:, 0:qw], sp[:, 0:qw], AF.Exp, bias=negB[:, 0:1],
                          scale=0.125, reads=[spk, "negB"], writes=[pbk])

                def _av(kt, j, si):
                    h = SLOT2HEAD[si]
                    kvh = h // 4
                    ii = 2 * kt + j
                    pb_, pbk = ptl[ii % 6], "ptl%d" % (ii % 6)
                    b.mm(pAV[j][:, 0:qw],
                         V_tm[:, kt * 520 + kvh * 65:kt * 520 + (kvh + 1) * 65],
                         pb_[:, 0:qw], kt == 0, kt == NTB - 1,
                         reads=["V_tm", pbk], writes=["pAV%d" % j])

                for j, si in enumerate(pair):
                    _score(0, j, si)
                for kt in range(NTB):
                    for j, si in enumerate(pair):
                        if kt + 1 < NTB:
                            _score(kt + 1, j, si)
                        _av(kt, j, si)
                for j, si in enumerate(pair):
                    av, avk = pAV[j], "pAV%d" % j
                    b.vrecip(rdrow[0:1, 0:qw], av[64:65, 0:qw], reads=[avk], writes=["rdrow"])
                    b.mm(pbc[:, 0:qw], onesr[0:1, 0:64], rdrow[0:1, 0:qw], True, True,
                         reads=["onesr", "rdrow"], writes=["pbc"])
                    b.vcopy(rbc[:, 0:qw], pbc[:, 0:qw], reads=["pbc"], writes=["rbc"])
                    dst = fmv(attnfm, 512)[:, si // 2, 0:qw][(si % 2) * 64:(si % 2) * 64 + 64, :]
                    b.tt(dst, av[0:64, 0:qw], rbc[:, 0:qw], ALU.mult,
                         reads=[avk, "rbc"], writes=["attnfm"])
            for ncc in range(4):
                wb, wk = wos[0], "wos0"
                b.dma("gpsimd", wk, d3(wb[:], 512), r3(I["wo_t"][:, ncc * 512:(ncc + 1) * 512]),
                      writes=[wk])
                for it in range(qw // 128):
                    pp, ppk = pO[it % 2], "pO%d" % (it % 2)
                    for fc in range(NF):
                        b.mm(pp[:], fmv(attnfm, 512)[:, fc, it * 128:(it + 1) * 128],
                             wb[:, fc * 512:(fc + 1) * 512], fc == 0, fc == NF - 1,
                             reads=["attnfm", wk], writes=[ppk])
                    b.tt(sO[it][:, ncc * 512:(ncc + 1) * 512], pp[:],
                         bo_bc[:, ncc * 512:(ncc + 1) * 512], ALU.add,
                         reads=[ppk, "bo_bc"], writes=["sO%d" % it])
            for it in range(qw // 128):
                ot = ci * 4 + it
                b.dma("sync", "gld0", gld[0][:], gdram[ot * 128:(ot + 1) * 128, :],
                      reads=["gd%d" % ot], writes=["gld0"])
                b.dma("sync", "xo20", xo2[0][:], I["xc"][ot * 128:(ot + 1) * 128, :],
                      writes=["xo20"])
                b.vcopy(gf32[0][:], gld[0][:], reads=["gld0"], writes=["gf320"])
                b.tt(sO[it][:], sO[it][:], gf32[0][:], ALU.mult,
                     reads=["sO%d" % it, "gf320"], writes=["sO%d" % it])
                b.tt(sO[it][:], sO[it][:], xo2[0][:], ALU.add,
                     reads=["sO%d" % it, "xo20"], writes=["sO%d" % it])
                b.dma("sync", "st%d" % it, x1[ot * 128:(ot + 1) * 128, :], sO[it][:],
                      reads=["sO%d" % it], writes=[])



def build_l1(debug=()):
    nc = bass.Bass("TRN2", target_bir_lowering=False, debug=False)
    I = {}

    def di(name, shape, dt=F32):
        I[name] = nc.dram_tensor(name, list(shape), dt, kind="ExternalInput").ap()

    di("x1c", (TOK, D_))
    di("qkv_t", (D_, 3072)); di("qkv_b", (1, 3072))
    di("out_t", (D_, D_)); di("out_b", (1, D_)); di("sinks", (1, 32))
    di("gw_t", (D_, 32)); di("gb", (1, 32))
    di("fc1_t", (D_, 1024)); di("fc2_t", (1024, D_))
    di("w1_t", (32, 1024, 1024)); di("w2_t", (32, 1024, 1024))
    di("su_t", (D_, 4096)); di("sd_t", (4096, D_))
    di("cos1", (TOK, 32)); di("sin1", (TOK, 32))
    di("masks", (2, 5, 128, 512), BF16)
    di("iden_bf", (128, 128), BF16); di("ones128", (1, 128))
    di("sel32", (32, 32 * 128))

    yc = nc.dram_tensor("yc", [OWN, D_], F32, kind="ExternalOutput").ap()
    DBG = {}

    def dbg(name, shape=None, dt=F32):
        if name in debug and shape is not None and name not in DBG:
            DBG[name] = nc.dram_tensor("dbg_" + name, list(shape), dt,
                                       kind="ExternalOutput").ap()
        return DBG.get(name)

    b = Prog(nc)
    with ExitStack() as ctx:
        emit_l1(b, I, yc, dbg, nc, ctx)
    return b.build(), sorted(DBG)


def emit_l1(b, I, yc, dbg, nc, ctx):
    def sb(name, shape, dt=F32):
        return ctx.enter_context(nc.sbuf_tensor(name, list(shape), dt))

    def psg(st, name, shape, dt=F32):
        return st.enter_context(nc.psum_tensor(name, list(shape), dt))

    C = {}
    K_fm = sb("K1_fm", [128, 4 * TOK], BF16)
    V_tm = sb("V1_tm", [128, NT * 520], BF16)
    maskb = sb("maskb", [128, 2 * 5 * 512], BF16)
    m_own = sb("m1_own", [128, NT]); r_own = sb("r1_own", [128, NT])
    q2acc = sb("q2acc", [128, 32]); k2acc = sb("k2acc", [128, 8])
    negB = sb("negB", [128, 1])
    es = sb("es", [1, 32], F32)
    sink_s = sb("sink_s", [1, 32], F32)
    C["idb"] = sb("idb", [128, 128], BF16)
    onesr = sb("onesr", [1, 128])
    g1 = sb("g1", [128, 8 * D_], BF16)
    cwT = sb("cwT", [32, OWN], BF16)

    b.dma("sync", "c0L1", C["idb"][:], I["iden_bf"], writes=["idb"], bulk=True)
    b.dma("sync", "c0L1", onesr[:], I["ones128"], writes=["onesr"], bulk=True)
    b.dma("sync", "c0L1", sink_s[:], I["sinks"], writes=["sink_s"], bulk=True)
    b.dma("sync", "c0L1", maskb[:].rearrange("p (q m c) -> p q m c", m=5, c=512),
          I["masks"].rearrange("q m p c -> p q m c"), writes=["maskb"], bulk=True)
    b.vmemset(q2acc[:], 0.0, writes=["q2acc"])
    b.vmemset(k2acc[:], 0.0, writes=["k2acc"])

    with ExitStack() as sAQ:
        xn_fm = sAQ.enter_context(nc.sbuf_tensor("xn1_fm", [128, NF * TOK], BF16))
        cosc = sAQ.enter_context(nc.sbuf_tensor("cosc", [128, NT * 32], F32))
        sinc = sAQ.enter_context(nc.sbuf_tensor("sinc", [128, NT * 32], F32))
        b.dma("sync", "c0L1", d3(cosc[:], 32), r3(I["cos1"]), writes=["cosc"], bulk=True)
        b.dma("sync", "c0L1", d3(sinc[:], 32), r3(I["sin1"]), writes=["sinc"], bulk=True)

        with ExitStack() as sA:
            xt = [sA.enter_context(nc.sbuf_tensor("xt%d" % i, [128, D_], F32))
                  for i in range(2)]
            C["lnscr"] = sA.enter_context(nc.sbuf_tensor("lnscr", [128, D_], F32))
            xnbf = sA.enter_context(nc.sbuf_tensor("xnbf0", [128, D_], BF16))
            stat = [sA.enter_context(nc.sbuf_tensor("stat%d" % i, [128, 4], F32))
                    for i in range(2)]
            C["pt0"] = psg(sA, "pt0", [128, 1024], BF16)
            C["pt1"] = psg(sA, "pt1", [128, 1024], BF16)
            for ot in range(NT):
                bi = ot % 2
                xtb, xk = xt[bi], "xt%d" % bi
                b.dma("sync", xk, xtb[:], I["x1c"][ot * 128:(ot + 1) * 128, :],
                      writes=[xk])
                emit_ln_tile(b, C, xtb[:], xk, stat[bi], "stat%d" % bi,
                             xnbf[:], "xnbf0", 1e-5,
                             mcol=m_own[:, ot:ot + 1], rcol=r_own[:, ot:ot + 1])
                xv = fmv(xn_fm, TOK)
                emit_transpose16(
                    b, C, xnbf[:], "xnbf0",
                    (lambda g, n, ot_=ot: xv[:, g * 8:g * 8 + n, ot_ * 128:(ot_ + 1) * 128]),
                    "xn_fm")
        b.fence()

        with ExitStack() as sKV:
            wkv = sKV.enter_context(nc.sbuf_tensor("wkv1", [128, NF * 1024], BF16))
            kb_bc = sKV.enter_context(nc.sbuf_tensor("kb_bc", [128, 512], F32))
            vb_bc = sKV.enter_context(nc.sbuf_tensor("vb_bc", [128, 512], F32))
            kvscr = sKV.enter_context(nc.sbuf_tensor("kvscr0", [128, 512], F32))
            kvtmp = sKV.enter_context(nc.sbuf_tensor("kvtmp0", [128, 512], F32))
            krs = [sKV.enter_context(nc.sbuf_tensor("krs%d" % i, [128, 512], BF16))
                   for i in range(2)]
            pkv = [psg(sKV, "pkv%d" % i, [128, 512]) for i in range(2)]
            C["pt0"] = psg(sKV, "pt0k", [128, 1024], BF16)
            C["pt1"] = psg(sKV, "pt1k", [128, 1024], BF16)
            b.dma("gpsimd", "wkv1", d3(wkv[:, 0:NF * 512], 512),
                  r3(I["qkv_t"][:, 2048:2560]), writes=["wkv_k"], bulk=True)
            b.dma("gpsimd", "wkv1", d3(wkv[:, NF * 512:], 512),
                  r3(I["qkv_t"][:, 2560:3072]), writes=["wkv_v"], bulk=True)
            b.dma("sync", "cA1", kb_bc[:], I["qkv_b"][:, 2048:2560].broadcast_to([128, 512]),
                  writes=["kb_bc"], bulk=True)
            b.dma("sync", "cA1", vb_bc[:], I["qkv_b"][:, 2560:3072].broadcast_to([128, 512]),
                  writes=["vb_bc"], bulk=True)
            for ot in range(NT):
                bi = ot % 2
                pk, pkk = pkv[bi], "pkv%d" % bi
                for fc in range(NF):
                    b.mm(pk[:], fmv(xn_fm, TOK)[:, fc, ot * 128:(ot + 1) * 128],
                         wkv[:, fc * 512:(fc + 1) * 512], fc == 0, fc == NF - 1,
                         reads=["xn_fm", "wkv_k"], writes=[pkk])
                b.tt(kvscr[:], pk[:], kb_bc[:], ALU.add, reads=[pkk, "kb_bc"],
                     writes=["kvscr0"])
                emit_rope(b, C, kvscr[:], "kvscr0", krs[bi][:], "krs%d" % bi,
                          cosc[:, ot * 32:(ot + 1) * 32], sinc[:, ot * 32:(ot + 1) * 32],
                          8, kvtmp[:], "kvtmp0", ckeys=("cosc", "sinc"))
                pt, ptk = C["pt%d" % bi], "pt%d" % bi
                for k in range(4):
                    b.trans(pt[:, k * 128:(k + 1) * 128], krs[bi][:, k * 128:(k + 1) * 128],
                            C["idb"][:], reads=["krs%d" % bi, "idb"], writes=[ptk])
                b.vcopy(fmv(K_fm, TOK)[:, 0:4, ot * 128:(ot + 1) * 128],
                        pt[:, 0:512].rearrange("p (f c) -> p f c", c=128),
                        reads=[ptk], writes=["K_fm"])
                b.tt(kvtmp[:], krs[bi][:], krs[bi][:], ALU.mult,
                     reads=["krs%d" % bi], writes=["kvtmp0"])
                b.red(kvscr[:, 0:8], kvtmp[:].rearrange("p (h d) -> p h d", d=64),
                      aluop=ALU.max, reads=["kvtmp0"], writes=["kvscr0"])
                b.tt(k2acc[:], k2acc[:], kvscr[:, 0:8], ALU.max,
                     reads=["kvscr0", "k2acc"], writes=["k2acc"])
                for fc in range(NF):
                    b.mm(pk[:], fmv(xn_fm, TOK)[:, fc, ot * 128:(ot + 1) * 128],
                         wkv[:, NF * 512 + fc * 512:NF * 512 + (fc + 1) * 512],
                         fc == 0, fc == NF - 1,
                         reads=["xn_fm", "wkv_v"], writes=[pkk])
                vt = V_tm[:, ot * 520:(ot + 1) * 520].rearrange("p (h c) -> p h c", c=65)
                b.tt(vt[:, :, 0:64], pk[:].rearrange("p (h c) -> p h c", c=64),
                     vb_bc[:].rearrange("p (h c) -> p h c", c=64), ALU.add,
                     reads=[pkk, "vb_bc"], writes=["V_tm"])
                b.vmemset(vt[:, :, 64:65], 1.0, writes=["V_tm"])
        b.fence()

        with ExitStack() as sR:
            gw_b = sR.enter_context(nc.sbuf_tensor("gw_b", [128, NF * 32], BF16))
            gw_f = sR.enter_context(nc.sbuf_tensor("gw_f", [128, NF * 32], F32))
            gb_bc = sR.enter_context(nc.sbuf_tensor("gb_bc", [128, 32], F32))
            rscr = sR.enter_context(nc.sbuf_tensor("rscr", [128, 192], F32))
            rgb = sR.enter_context(nc.sbuf_tensor("rgb", [128, 32], BF16))
            rpsum_t = psg(sR, "rpsum", [128, 512])
            rpsum = rpsum_t[:, 0:32]
            rtp_t = psg(sR, "rtp", [128, 1024], BF16)
            rtp = rtp_t[0:32, 0:128]
            b.dma("sync", "cRg1", d3(gw_f[:], 32), r3(I["gw_t"]), writes=["gw_f"], bulk=True)
            b.vcopy(gw_b[:], gw_f[:], reads=["gw_f"], writes=["gw_b"])
            b.dma("sync", "cR1", gb_bc[:], I["gb"].broadcast_to([128, 32]),
                  writes=["gb_bc"], bulk=True)
            for ot in range(8):
                lt = ot + 1
                for fc in range(NF):
                    b.mm(rpsum, fmv(xn_fm, TOK)[:, fc, lt * 128:(lt + 1) * 128],
                         gw_b[:, fc * 32:(fc + 1) * 32], fc == 0, fc == NF - 1,
                         reads=["xn_fm", "gw_b"], writes=["rpsum"])
                lk = "rscr"
                lg = rscr[:, 0:32]
                sc_ = rscr[:, 32:64]
                sbv = rscr[:, 64:96]
                tq = rscr[:, 96:128]
                gs = rscr[:, 128:132]
                st1 = rscr[:, 132:133]; st2 = rscr[:, 133:134]
                thg = rscr[:, 134:135]; th8 = rscr[:, 135:136]
                ssum = rscr[:, 136:137]; rs = rscr[:, 137:138]
                msk = rscr[:, 128 + 16:128 + 16]
                b.vcopy(lg, rpsum, reads=["rpsum"], writes=[lk])
                b.act(sc_, lg, AF.Exp, scale=-1.0, reads=[lk], writes=[lk])
                b.ts(sc_, sc_, 1.0, None, ALU.add, reads=[lk], writes=[lk])
                b.vrecip(sc_, sc_, reads=[lk], writes=[lk])
                b.tt(sbv, sc_, gb_bc[:], ALU.add, reads=[lk, "gb_bc"], writes=[lk])
                for gi in range(4):
                    sl = sbv[:, gi * 8:(gi + 1) * 8]
                    b.red(st1, sl, aluop=ALU.max, reads=[lk], writes=[lk])
                    b.ts(tq[:, 0:8], sl, st1, -1e30, ALU.is_equal, ALU.mult,
                         reads=[lk], writes=[lk])
                    b.tt(tq[:, 0:8], tq[:, 0:8], sl, ALU.add, reads=[lk], writes=[lk])
                    b.red(st2, tq[:, 0:8], aluop=ALU.max, reads=[lk], writes=[lk])
                    b.tt(gs[:, gi:gi + 1], st1, st2, ALU.add, reads=[lk], writes=[lk])
                b.red(st1, gs, aluop=ALU.max, reads=[lk], writes=[lk])
                b.ts(tq[:, 0:4], gs, st1, -1e30, ALU.is_equal, ALU.mult,
                     reads=[lk], writes=[lk])
                b.tt(tq[:, 0:4], tq[:, 0:4], gs, ALU.add, reads=[lk], writes=[lk])
                b.red(thg, tq[:, 0:4], aluop=ALU.max, reads=[lk], writes=[lk])
                b.ts(gs, gs, thg, None, ALU.is_ge, reads=[lk], writes=[lk])
                gm_exp = gs.rearrange("p (g o) -> p g o", o=1).broadcast_to([128, 4, 8])
                sb3 = sbv.rearrange("p (g e) -> p g e", e=8)
                tq3 = tq.rearrange("p (g e) -> p g e", e=8)
                b.tt(tq3, sb3, gm_exp, ALU.mult, reads=[lk], writes=[lk])
                b.ts(lg, gs.rearrange("p (g o) -> p g o", o=1).broadcast_to([128, 4, 8]),
                     1.0, -1e30, ALU.subtract, ALU.mult, reads=[lk], writes=[lk])
                b.tt(tq, tq, lg, ALU.subtract, reads=[lk], writes=[lk])
                mcp = rscr[:, 96 + 32:96 + 32] if False else None
                b.vcopy(lg, tq, reads=[lk], writes=[lk])
                for rnd in range(8):
                    b.red(th8, tq, aluop=ALU.max, reads=[lk], writes=[lk])
                    if rnd < 7:
                        b.ts(rscr[:, 160:192], tq, th8, -1e30,
                             ALU.is_equal, ALU.mult, reads=[lk], writes=[lk])
                        b.tt(tq, tq, rscr[:, 160:192], ALU.add,
                             reads=[lk], writes=[lk])
                b.ts(lg, lg, th8, None, ALU.is_ge, reads=[lk], writes=[lk])
                b.tt(lg, lg, sc_, ALU.mult, reads=[lk], writes=[lk])
                b.red(ssum, lg, reads=[lk], writes=[lk])
                b.vrecip(rs, ssum, reads=[lk], writes=[lk])
                b.ts(lg, lg, rs, 2.5, ALU.mult, ALU.mult, reads=[lk], writes=[lk])
                b.vcopy(rgb[:], lg, reads=[lk], writes=["rgb"])
                b.trans(rtp, rgb[:], C["idb"][:], reads=["rgb", "idb"], writes=["rtp"])
                b.vcopy(cwT[:, ot * 128:(ot + 1) * 128], rtp, reads=["rtp"], writes=["cwT"])
        b.fence()

        TCH = 256
        NCH = OWN // TCH
        with ExitStack() as sF:
            xlat = sF.enter_context(nc.sbuf_tensor("xlat", [128, 8 * OWN], BF16))
            wstr = [sF.enter_context(nc.sbuf_tensor("wstr%d" % i, [128, 16 * 128], BF16))
                    for i in range(2)]
            wfc1 = [sF.enter_context(nc.sbuf_tensor("wfc1%d" % i, [128, NF * 128], BF16))
                    for i in range(2)]
            hw = sF.enter_context(nc.sbuf_tensor("hw", [128, 8 * TCH], BF16))
            rt = sF.enter_context(nc.sbuf_tensor("rt", [128, TCH], F32))
            rt2 = sF.enter_context(nc.sbuf_tensor("rt2", [128, TCH], F32))
            cwb = sF.enter_context(nc.sbuf_tensor("cwb", [128, TCH], F32))
            sel32b = sF.enter_context(nc.sbuf_tensor("sel32b", [32, 32 * 128], BF16))
            ylat_bf = sF.enter_context(nc.sbuf_tensor("ylat_bf", [128, 8 * TCH], BF16))
            upfm = sF.enter_context(nc.sbuf_tensor("upfm", [128, 32 * TCH], BF16))
            wdn = [sF.enter_context(nc.sbuf_tensor("wdn%d" % i, [128, 32 * 256], BF16))
                   for i in range(2)]
            wf2 = [sF.enter_context(nc.sbuf_tensor("wf20", [128, 8 * 256], BF16))]
            pylat = psg(sF, "pylat", [128, 2048])
            ppre = [psg(sF, "ppre%d" % i, [128, 512]) for i in range(2)]
            pcw = psg(sF, "pcw", [128, 512])
            pdn = ppre
            b.dma("gpsimd", "cF", sel32b[:], I["sel32"], writes=["sel32b"], bulk=True)

            for lt in range(8):
                wi = lt % 2
                wb, wk = wfc1[wi], "wfc1%d" % wi
                b.dma("gpsimd", wk, d3(wb[:], 128),
                      r3(I["fc1_t"][:, lt * 128:(lt + 1) * 128]), writes=[wk])
                for tc in range(2):
                    pp, ppk = ppre[tc % 2], "ppre%d" % (tc % 2)
                    for fc in range(NF):
                        b.mm(pp[:], wb[:, fc * 128:(fc + 1) * 128],
                             fmv(xn_fm, TOK)[:, fc, 128 + tc * 512:128 + (tc + 1) * 512],
                             fc == 0, fc == NF - 1, reads=[wk, "xn_fm"], writes=[ppk])
                    b.vcopy(fmv(xlat, OWN)[:, lt, tc * 512:(tc + 1) * 512][0:128, :],
                            pp[:], reads=[ppk], writes=["xlat"])

            if dbg("xlatd", (128, 8 * OWN), BF16):
                b.dma("sync", "dbgXL", dbg("xlatd"), xlat[:],
                      reads=["xlat"], writes=[])
            for ci in range(NCH):
                t0 = ci * TCH
                for e in range(32):
                    b.mm(pcw[:, 0:TCH], sel32b[:, e * 128:(e + 1) * 128],
                         cwT[:, t0:t0 + TCH], True, True,
                         reads=["sel32b", "cwT"], writes=["pcw"])
                    b.vcopy(cwb[:], pcw[:, 0:TCH], reads=["pcw"], writes=["cwb"])
                    for ig in range(4):
                        wi = ig % 2
                        wb, wk = wstr[wi], "wstr%d" % wi
                        b.dma("gpsimd", wk, d3(wb[:, 0:8 * 256], 256),
                              r3(I["w1_t"][e, :, ig * 256:(ig + 1) * 256]), writes=[wk])
                        for il in range(2):
                            it_ = ig * 2 + il
                            pp, ppk = ppre[it_ % 2], "ppre%d" % (it_ % 2)
                            for lc in range(8):
                                b.mm(pp[:, 0:TCH],
                                     wb[:, lc * 256 + il * 128:lc * 256 + (il + 1) * 128],
                                     fmv(xlat, OWN)[:, lc, t0:t0 + TCH],
                                     lc == 0, lc == 7, reads=[wk, "xlat"], writes=[ppk])
                            b.op("scalar",
                                 (lambda s, o=rt[:, 0:TCH], i=pp[:, 0:TCH]:
                                  s.activation(o, i, AF.Relu)),
                                 reads=[ppk], writes=["rt"])
                            b.tt(rt2[:, 0:TCH], rt[:, 0:TCH], rt[:, 0:TCH], ALU.mult,
                                 reads=["rt"], writes=["rt2"])
                            b.tt(hw[:, it_ * TCH:(it_ + 1) * TCH], rt2[:, 0:TCH],
                                 cwb[:], ALU.mult, reads=["rt2", "cwb"], writes=["hw"])
                    for lg in range(4):
                        wi = lg % 2
                        wb, wk = wstr[wi], "wstr%d" % wi
                        b.dma("gpsimd", wk, d3(wb[:, 0:8 * 256], 256),
                              r3(I["w2_t"][e, :, lg * 256:(lg + 1) * 256]), writes=[wk])
                        for ll in range(2):
                            lt = lg * 2 + ll
                            for icc in range(8):
                                b.mm(pylat[:, lt * TCH:(lt + 1) * TCH],
                                     wb[:, icc * 256 + ll * 128:icc * 256 + (ll + 1) * 128],
                                     hw[:, icc * TCH:(icc + 1) * TCH],
                                     (e == 0 and icc == 0 and lt % 2 == 0),
                                     (e == 31 and icc == 7),
                                     reads=[wk, "hw"], writes=["pylat"])
                for lt in range(8):
                    b.vcopy(ylat_bf[:, lt * TCH:(lt + 1) * TCH],
                            pylat[:, lt * TCH:(lt + 1) * TCH],
                            reads=["pylat"], writes=["ylat_bf"])
                if ci == 0 and dbg("ylatd", (128, 8 * 256), BF16):
                    b.dma("sync", "dbgYL", dbg("ylatd"), ylat_bf[:],
                          reads=["ylat_bf"], writes=[])
                for st_ in range(32):
                    wi = st_ % 2
                    wb, wk = wstr[wi], "wstr%d" % wi
                    b.dma("gpsimd", wk, d3(wb[:], 128),
                          r3(I["su_t"][:, st_ * 128:(st_ + 1) * 128]), writes=[wk])
                    pp, ppk = ppre[st_ % 2], "ppre%d" % (st_ % 2)
                    for fc in range(NF):
                        b.mm(pp[:, 0:TCH], wb[:, fc * 128:(fc + 1) * 128],
                             fmv(xn_fm, TOK)[:, fc, 128 + t0:128 + t0 + TCH],
                             fc == 0, fc == NF - 1, reads=[wk, "xn_fm"], writes=[ppk])
                    b.op("scalar",
                         (lambda s, o=rt[:, 0:TCH], i=pp[:, 0:TCH]:
                          s.activation(o, i, AF.Relu)),
                         reads=[ppk], writes=["rt"])
                    b.tt(upfm[:, st_ * TCH:(st_ + 1) * TCH], rt[:, 0:TCH],
                         rt[:, 0:TCH], ALU.mult, reads=["rt"], writes=["upfm"])
                if ci == 0 and dbg("upfmd", (128, 32 * 256), BF16):
                    b.dma("sync", "dbgUP", dbg("upfmd"), upfm[:],
                          reads=["upfm"], writes=[])
                for nc8 in range(8):
                    wi = nc8 % 2
                    wdb, wdk = wdn[wi], "wdn%d" % wi
                    wfb, wfk = wf2[0], "wf20"
                    b.dma("gpsimd", wdk, d3(wdb[:], 256),
                          r3(I["sd_t"][:, nc8 * 256:(nc8 + 1) * 256]),
                          writes=[wdk])
                    b.dma("gpsimd", wfk, d3(wfb[:], 256),
                          r3(I["fc2_t"][:, nc8 * 256:(nc8 + 1) * 256]),
                          writes=[wfk])
                    for it_ in range(TCH // 128):
                        ot = (t0 + it_ * 128) // 128
                        pp, ppk = pdn[it_ % 2], "ppre%d" % (it_ % 2)
                        for sc_ in range(32):
                            b.mm(pp[:, 0:256], upfm[:, sc_ * TCH + it_ * 128:
                                                    sc_ * TCH + (it_ + 1) * 128],
                                 wdb[:, sc_ * 256:(sc_ + 1) * 256],
                                 sc_ == 0, False, reads=["upfm", wdk], writes=[ppk])
                        for lc in range(8):
                            b.mm(pp[:, 0:256], ylat_bf[:, lc * TCH + it_ * 128:
                                                       lc * TCH + (it_ + 1) * 128],
                                 wfb[:, lc * 256:(lc + 1) * 256],
                                 False, lc == 7, reads=["ylat_bf", wfk], writes=[ppk])
                        b.op("scalar",
                             (lambda s, o=g1[:, ot * D_ + nc8 * 256:
                                             ot * D_ + (nc8 + 1) * 256],
                              i=pp[:, 0:256]: s.activation(o, i, AF.Silu)),
                             reads=[ppk], writes=["g1"])
        if dbg("g1d", (128, 8 * D_)):
            b.dma("gpsimd", "dbgG1", dbg("g1d"), g1[:], reads=["g1"], writes=[])
        if dbg("cwTd", (32, OWN)):
            b.dma("gpsimd", "dbgCW", dbg("cwTd"), cwT[:], reads=["cwT"], writes=[])
        b.fence()

        Q_fm = ctx.enter_context(nc.sbuf_tensor("Q1_fm", [128, NF * OWN], BF16,
                                                side="right"))
        with ExitStack() as sQ:
            wqs = [sQ.enter_context(nc.sbuf_tensor("wqs%d" % i, [128, NF * 512], BF16))
                   for i in range(2)]
            qscr = [sQ.enter_context(nc.sbuf_tensor("qscr%d" % i, [128, 512], F32))
                    for i in range(2)]
            qtmp = [sQ.enter_context(nc.sbuf_tensor("qtmp%d" % i, [128, 512], F32))
                    for i in range(2)]
            qrb = [sQ.enter_context(nc.sbuf_tensor("qrb%d" % i, [128, 512], BF16))
                   for i in range(2)]
            qb_bc = sQ.enter_context(nc.sbuf_tensor("qb_bc", [128, D_], F32))
            ppq = [psg(sQ, "ppq%d" % i, [128, 512]) for i in range(2)]
            C["pt0"] = psg(sQ, "pt0b", [128, 1024], BF16)
            C["pt1"] = psg(sQ, "pt1b", [128, 1024], BF16)
            b.dma("sync", "cQ1", qb_bc[:], I["qkv_b"][:, 0:2048].broadcast_to([128, D_]),
                  writes=["qb_bc"], bulk=True)
            for ncc in range(4):
                wi = ncc % 2
                wb, wk = wqs[wi], "wqs%d" % wi
                b.dma("gpsimd", wk, d3(wb[:], 512),
                      r3(I["qkv_t"][:, ncc * 512:(ncc + 1) * 512]), writes=[wk])
                for ot in range(8):
                    lt = ot + 1
                    bi = ot % 2
                    pp, ppk = ppq[bi], "ppq%d" % bi
                    for fc in range(NF):
                        b.mm(pp[:], fmv(xn_fm, TOK)[:, fc, lt * 128:(lt + 1) * 128],
                             wb[:, fc * 512:(fc + 1) * 512], fc == 0, fc == NF - 1,
                             reads=["xn_fm", wk], writes=[ppk])
                    b.tt(qscr[bi][:], pp[:], qb_bc[:, ncc * 512:(ncc + 1) * 512], ALU.add,
                         reads=[ppk, "qb_bc"], writes=["qscr%d" % bi])
                    emit_rope(b, C, qscr[bi][:], "qscr%d" % bi, qrb[bi][:], "qrb%d" % bi,
                              cosc[:, lt * 32:(lt + 1) * 32], sinc[:, lt * 32:(lt + 1) * 32],
                              8, qtmp[bi][:], "qtmp%d" % bi, ckeys=("cosc", "sinc"))
                    b.tt(qtmp[bi][:], qrb[bi][:], qrb[bi][:], ALU.mult,
                         reads=["qrb%d" % bi], writes=["qtmp%d" % bi])
                    b.red(qscr[bi][:, 0:8], qtmp[bi][:].rearrange("p (h d) -> p h d", d=64),
                          aluop=ALU.max, reads=["qtmp%d" % bi], writes=["qscr%d" % bi])
                    b.tt(q2acc[:, ncc * 8:(ncc + 1) * 8], q2acc[:, ncc * 8:(ncc + 1) * 8],
                         qscr[bi][:, 0:8], ALU.max,
                         reads=["qscr%d" % bi, "q2acc"], writes=["q2acc"])
                    pt, ptk = C["pt%d" % bi], "pt%d" % bi
                    for k in range(4):
                        b.trans(pt[:, k * 128:(k + 1) * 128], qrb[bi][:, k * 128:(k + 1) * 128],
                                C["idb"][:], reads=["qrb%d" % bi, "idb"], writes=[ptk])
                    b.vcopy(fmv(Q_fm, OWN)[:, ncc * 4:(ncc + 1) * 4, ot * 128:(ot + 1) * 128],
                            pt[:, 0:512].rearrange("p (f c) -> p f c", c=128),
                            reads=[ptk], writes=["Q_fm"])
        b.fence()

    with ExitStack() as sB:
        bscr = sB.enter_context(nc.sbuf_tensor("bscr", [128, 8], F32))
        brow = sB.enter_context(nc.sbuf_tensor("brow", [1, 128], F32))
        bsc2 = sB.enter_context(nc.sbuf_tensor("bsc2", [1, 4], F32))
        pnb_t = psg(sB, "pnb", [128, 512])
        pnb = pnb_t[:, 0:1]
        b.red(bscr[:, 0:1], q2acc[:], aluop=ALU.max, reads=["q2acc"], writes=["bscr"])
        emit_global_max(b, nc, ctx, "q1", bscr[:, 0:1], "bscr", bsc2[0:1, 0:1], "bsc2",
                        brow[0:1, :], "brow")
        b.red(bscr[:, 1:2], k2acc[:], aluop=ALU.max, reads=["k2acc"], writes=["bscr"])
        emit_global_max(b, nc, ctx, "k1", bscr[:, 1:2], "bscr", bsc2[0:1, 1:2], "bsc2",
                        brow[0:1, :], "brow")
        b.tt(bsc2[0:1, 2:3], bsc2[0:1, 0:1], bsc2[0:1, 1:2], ALU.mult,
             reads=["bsc2"], writes=["bsc2"])
        b.act(bsc2[0:1, 2:3], bsc2[0:1, 2:3], AF.Ln, reads=["bsc2"], writes=["bsc2"])
        b.act(bsc2[0:1, 3:4], bsc2[0:1, 2:3], AF.Exp, scale=0.5, reads=["bsc2"], writes=["bsc2"])
        b.ts(bsc2[0:1, 3:4], bsc2[0:1, 3:4], -0.125, None, ALU.mult,
             reads=["bsc2"], writes=["bsc2"])
        b.mm(pnb, onesr[:], bsc2[0:1, 3:4], True, True,
             reads=["onesr", "bsc2"], writes=["pnb"])
        b.vcopy(negB[:], pnb, reads=["pnb"], writes=["negB"])
        b.act(es[:], sink_s[:], AF.Exp, bias=negB[0:1, 0:1],
              reads=["sink_s", "negB"], writes=["es"])
    b.fence()

    with ExitStack() as sT:
        def tsb(name, shape, dt=F32):
            return sT.enter_context(nc.sbuf_tensor(name, list(shape), dt))

        attnfm = tsb("attnfm", [128, NF * 512], BF16)
        ptl = [tsb("ptl%d" % i, [128, 512], BF16) for i in range(6)]
        rdrow = tsb("rdrow", [1, 512])
        rbc = tsb("rbc", [64, 512])
        wos = [tsb("wos%d" % i, [128, NF * 512], BF16) for i in range(1)]
        ob_bc = tsb("ob_bc", [128, D_])
        sO = [tsb("sO%d" % i, [128, D_]) for i in range(4)]
        gf32 = tsb("gf32", [128, D_])
        xo2 = tsb("xo2", [128, D_])
        pS = [psg(sT, "pS%d" % i, [128, 512]) for i in range(3)]
        pAV = [psg(sT, "pAV%d" % i, [65, 512]) for i in range(2)]
        pbc = psg(sT, "pbc", [64, 512])
        pO = [psg(sT, "pO%d" % i, [128, 512]) for i in range(2)]
        b.dma("sync", "cT1", ob_bc[:], I["out_b"].broadcast_to([128, D_]),
              writes=["ob_bc"], bulk=True)
        mk3 = maskb[:].rearrange("p (q m c) -> p q m c", m=5, c=512)

        for qc in range(2):
            for si in range(32):
                h = SLOT2HEAD[si]
                kvh = h // 4
                av, avk = pAV[si % 2], "pAV%d" % (si % 2)
                qrhs = fmv(Q_fm, OWN)[:, si // 2, qc * 512:(qc + 1) * 512][
                    (si % 2) * 64:(si % 2) * 64 + 64, :]
                for m in range(5):
                    kt = qc * 4 + m
                    sp, spk = pS[m % 2], "pS%d" % (m % 2)
                    klhs = fmv(K_fm, TOK)[:, kvh // 2, kt * 128:(kt + 1) * 128][
                        (kvh % 2) * 64:(kvh % 2) * 64 + 64, :]
                    b.mm(sp[:], klhs, qrhs, True, True,
                         reads=["K_fm", "Q_fm"], writes=[spk])
                    pb_, pbk = ptl[m % 3], "ptl%d" % (m % 3)
                    b.act(pb_[:], sp[:], AF.Exp, bias=negB[:, 0:1], scale=0.125,
                          reads=[spk, "negB"], writes=[pbk])
                    b.tt(pb_[:], pb_[:], mk3[:, qc, m, :], ALU.mult,
                         reads=[pbk, "maskb"], writes=[pbk])
                    b.mm(av[:], V_tm[:, kt * 520 + kvh * 65:kt * 520 + (kvh + 1) * 65],
                         pb_[:], m == 0, m == 4, reads=["V_tm", pbk], writes=[avk])
                b.ts(rdrow[0:1, :], av[64:65, :], es[0:1, h:h + 1], None, ALU.add,
                     reads=[avk, "es"], writes=["rdrow"])
                b.vrecip(rdrow[0:1, :], rdrow[0:1, :], reads=["rdrow"], writes=["rdrow"])
                b.mm(pbc[:], onesr[0:1, 0:64], rdrow[0:1, :], True, True,
                     reads=["onesr", "rdrow"], writes=["pbc"])
                b.vcopy(rbc[:], pbc[:], reads=["pbc"], writes=["rbc"])
                dst = fmv(attnfm, 512)[:, si // 2, :][(si % 2) * 64:(si % 2) * 64 + 64, :]
                b.tt(dst, av[0:64, :], rbc[:], ALU.mult, reads=[avk, "rbc"],
                     writes=["attnfm"])
            for ncc in range(4):
                wb, wk = wos[0], "wos0"
                b.dma("gpsimd", wk, d3(wb[:], 512),
                      r3(I["out_t"][:, ncc * 512:(ncc + 1) * 512]), writes=[wk])
                for it_ in range(4):
                    pp, ppk = pO[it_ % 2], "pO%d" % (it_ % 2)
                    for fc in range(NF):
                        b.mm(pp[:], fmv(attnfm, 512)[:, fc, it_ * 128:(it_ + 1) * 128],
                             wb[:, fc * 512:(fc + 1) * 512], fc == 0, fc == NF - 1,
                             reads=["attnfm", wk], writes=[ppk])
                    b.tt(sO[it_][:, ncc * 512:(ncc + 1) * 512], pp[:],
                         ob_bc[:, ncc * 512:(ncc + 1) * 512], ALU.add,
                         reads=[ppk, "ob_bc"], writes=["sO%d" % it_])
            for it_ in range(4):
                ot = qc * 4 + it_
                b.dma("sync", "xo2", xo2[:], I["x1c"][128 + ot * 128:128 + (ot + 1) * 128, :],
                      writes=["xo2"])
                b.vcopy(gf32[:], g1[:, ot * D_:(ot + 1) * D_], reads=["g1"], writes=["gf32"])
                b.tt(sO[it_][:], sO[it_][:], gf32[:], ALU.mult,
                     reads=["sO%d" % it_, "gf32"], writes=["sO%d" % it_])
                b.tt(sO[it_][:], sO[it_][:], xo2[:], ALU.add,
                     reads=["sO%d" % it_, "xo2"], writes=["sO%d" % it_])
                b.dma("sync", "st%d" % it_, yc[ot * 128:(ot + 1) * 128, :], sO[it_][:],
                      reads=["sO%d" % it_], writes=[])



def build_fused():
    nc = bass.Bass("TRN2", target_bir_lowering=False, debug=False)
    I0, I1 = {}, {}

    def di(d, name, shape, dt=F32):
        d[name] = nc.dram_tensor(name, list(shape), dt, kind="ExternalInput").ap()

    di(I0, "xb", (T_, D_)); di(I0, "xc", (TOK, D_))
    di(I0, "wq_t", (D_, D_)); di(I0, "wk_t", (D_, 512)); di(I0, "wv_t", (D_, 512))
    di(I0, "wo_t", (D_, D_))
    di(I0, "bq", (1, D_)); di(I0, "bk", (1, 512)); di(I0, "bv", (1, 512))
    di(I0, "bo", (1, D_))
    di(I0, "rw", (D_, 8)); di(I0, "rb", (1, 8)); di(I0, "ew", (8, D_, D_))
    di(I0, "eb_cols", (128, 128)); di(I0, "pw", (D_, D_)); di(I0, "pb", (1, D_))
    di(I0, "nw", (1, D_))
    di(I0, "cos0", (T_, 32)); di(I0, "sin0", (T_, 32))
    di(I0, "cos0c", (TOK, 32)); di(I0, "sin0c", (TOK, 32))
    di(I0, "iden_bf", (128, 128), BF16); di(I0, "iden_f32", (128, 128))
    di(I0, "ones128", (1, 128)); di(I0, "sel8", (8, 8 * 128))

    di(I1, "qkv_t", (D_, 3072)); di(I1, "qkv_b", (1, 3072))
    di(I1, "out_t", (D_, D_)); di(I1, "out_b", (1, D_)); di(I1, "sinks", (1, 32))
    di(I1, "gw_t", (D_, 32)); di(I1, "gb", (1, 32))
    di(I1, "fc1_t", (D_, 1024)); di(I1, "fc2_t", (1024, D_))
    di(I1, "w1_t", (32, 1024, 1024)); di(I1, "w2_t", (32, 1024, 1024))
    di(I1, "su_t", (D_, 4096)); di(I1, "sd_t", (4096, D_))
    di(I1, "cos1", (TOK, 32)); di(I1, "sin1", (TOK, 32))
    di(I1, "masks", (2, 5, 128, 512), BF16)
    di(I1, "sel32", (32, 32 * 128))
    I1["iden_bf"] = I0["iden_bf"]
    I1["ones128"] = I0["ones128"]

    x1 = nc.dram_tensor("x1", [TOK, D_], F32, kind="Internal").ap()
    gdram = nc.dram_tensor("gscr", [TOK, D_], BF16, kind="Internal").ap()
    yc = nc.dram_tensor("yc", [OWN, D_], F32, kind="ExternalOutput").ap()
    I1["x1c"] = x1

    nodbg = lambda name, shape=None, dt=F32: None

    class _NCProxy:

        def __init__(self, real):
            self._nc = real

        def __getattr__(self, k):
            return getattr(self._nc, k)

        def sbuf_tensor(self, name, shape, dt, **kw):
            kw.setdefault("allow_name_mangling", True)
            return self._nc.sbuf_tensor(name, shape, dt, **kw)

        def psum_tensor(self, name, shape, dt=F32, **kw):
            kw.setdefault("allow_name_mangling", True)
            return self._nc.psum_tensor(name, shape, dt, **kw)

    ncp = _NCProxy(nc)
    b = Prog(nc)
    with ExitStack() as ctx0:
        emit_l0(b, I0, x1, gdram, nodbg, ncp, ctx0)
    b.fence()
    with ExitStack() as ctx1:
        emit_l1(b, I1, yc, nodbg, ncp, ctx1)
    return b.build()



def kernel(**inputs):
    maps0, maps1 = prep_inputs(inputs)
    nc = build_fused()
    in_maps = []
    for c in range(N_CORES):
        m = dict(maps0[c])
        m.update(maps1[c])
        in_maps.append(m)
    res = run_bass_kernel_spmd(nc, in_maps, core_ids=list(range(N_CORES)))
    out = np.zeros((B_, T_, D_), np.float32)
    for c in range(N_CORES):
        bb, Q0 = c // 4, (c % 4) * OWN
        out[bb, Q0:Q0 + OWN] = np.asarray(res.results[c]["yc"], np.float32)
    return out


# version 05 (final)
# speedup vs baseline: 1.3461x; 1.0403x over previous
import math
from contextlib import ExitStack

import numpy as np
import ml_dtypes

import concourse.bass as bass
import concourse.mybir as mybir
from concourse.bass_utils import run_bass_kernel_spmd

F32 = mybir.dt.float32
BF16 = mybir.dt.bfloat16
AF = mybir.ActivationFunctionType
ALU = mybir.AluOpType
AX = mybir.AxisListType
BF = ml_dtypes.bfloat16

N_CORES = 8
PHASES = 99
B_, T_, D_ = 2, 4096, 2048
OWN, HALO = 1024, 128
TOK = OWN + HALO
NT = TOK // 128
NTB = T_ // 128
NF = D_ // 128
CHW = [512, 512, 128]

_EV = [h for h in range(32) if (h // 4) % 2 == 0]
_OD = [h for h in range(32) if (h // 4) % 2 == 1]
SLOT2HEAD = []
for _t in range(16):
    SLOT2HEAD.append(_EV[_t])
    SLOT2HEAD.append(_OD[_t])


COMPUTE_ENGS = ("tensor", "vector", "scalar")
DMA_QUEUES = ("sync", "gpsimd")


class Prog:
    def __init__(self, nc):
        self.nc = nc
        self.ops = []
        self.last_w = {}
        self.readers = {}
        self.slot_fills = {}
        self.bulk_slots = set()
        self._last_per_key = {}
        self._fence_need = None
        self._fence_engines = set()

    def op(self, engine, fn, reads=(), writes=()):
        assert engine in COMPUTE_ENGS, engine
        return self._add(engine, fn, reads, writes, None)

    def fence(self):
        self._fence_need = set(self._last_per_key.values())
        self._fence_engines = set()

    def dma(self, queue, slot, out, in_, reads=(), writes=(), bulk=False):
        assert queue in DMA_QUEUES, queue
        if bulk:
            self.bulk_slots.add(slot)
        self.slot_fills[slot] = self.slot_fills.get(slot, 0) + 1
        fn = lambda e, o=out, i=in_: e.dma_start(o, i)
        return self._add(queue, fn, reads, writes, (slot, self.slot_fills[slot]))

    def mm(self, out, lhsT, rhs, start, stop, reads=(), writes=()):
        fn = lambda t, o=out, l=lhsT, r=rhs, a=start, c=stop: t.matmul(
            o, l, r, start=a, stop=c, skip_group_check=True)
        return self.op("tensor", fn, reads, writes)

    def trans(self, out, in_, iden, reads=(), writes=()):
        fn = lambda t, o=out, i=in_, d=iden: t.transpose(o, i, d)
        return self.op("tensor", fn, reads, writes)

    def tt(self, out, a, c, aluop, reads=(), writes=()):
        fn = lambda v, o=out, x=a, y=c, p=aluop: v.tensor_tensor(o, x, y, p)
        return self.op("vector", fn, reads, writes)

    def ts(self, out, in0, s1, s2, op0, op1=None, reads=(), writes=()):
        if s2 is None and op1 is None:
            fn = lambda v, o=out, i=in0, a=s1, p0=op0: v.tensor_single_scalar(o, i, a, p0)
        else:
            fn = lambda v, o=out, i=in0, a=s1, c=s2, p0=op0, p1=op1: v.tensor_scalar(
                o, i, a, c, p0, p1)
        return self.op("vector", fn, reads, writes)

    def red(self, out, in_, aluop=ALU.add, negate=False, reads=(), writes=()):
        fn = lambda v, o=out, i=in_, p=aluop, n=negate: v.tensor_reduce(
            o, i, axis=AX.X, op=p, negate=n)
        return self.op("vector", fn, reads, writes)

    def vcopy(self, out, in_, reads=(), writes=()):
        fn = lambda v, o=out, i=in_: v.tensor_copy(o, i)
        return self.op("vector", fn, reads, writes)

    def vmemset(self, ap, val, reads=(), writes=()):
        fn = lambda v, a=ap, x=val: v.memset(a, x)
        return self.op("vector", fn, reads, writes)

    def vrecip(self, out, in_, reads=(), writes=()):
        fn = lambda v, o=out, i=in_: v.reciprocal(o, i)
        return self.op("vector", fn, reads, writes)

    def act(self, out, in_, func, bias=0.0, scale=1.0, reads=(), writes=()):
        fn = lambda s, o=out, i=in_, f=func, bb=bias, c=scale: s.activation(
            o, i, f, bias=bb, scale=c)
        return self.op("scalar", fn, reads, writes)

    def _verify(self, ops):
        streams = {}
        for i, o in enumerate(ops):
            streams.setdefault(o["engine"], []).append(i)
        heads = {e: 0 for e in streams}
        sem = {}
        done = 0
        while done < len(ops):
            progressed = False
            for e, lst in streams.items():
                while heads[e] < len(lst):
                    o = ops[lst[heads[e]]]
                    if any(sem.get(k, 0) < v for k, v in o["waits"]):
                        break
                    if o["inc"]:
                        if o["slot"] is not None:
                            k = "slot:" + o["slot"][0]
                            sem[k] = sem.get(k, 0) + 16
                        else:
                            k = "eng:" + e
                            sem[k] = sem.get(k, 0) + 1
                    heads[e] += 1
                    done += 1
                    progressed = True
            if not progressed:
                msg = []
                for e, lst in streams.items():
                    if heads[e] < len(lst):
                        i = lst[heads[e]]
                        o = ops[i]
                        blocked = [(k, v, sem.get(k, 0)) for k, v in o["waits"]
                                   if sem.get(k, 0) < v]
                        msg.append(f"{e} stuck at op {i} waits {blocked}")
                raise RuntimeError("DEADLOCK:\n" + "\n".join(msg))

    def _add(self, engine, fn, reads, writes, dma_slot):
        i = len(self.ops)
        deps = set()
        if self._fence_need is not None and engine not in self._fence_engines:
            deps |= self._fence_need
            self._fence_engines.add(engine)
        for r in reads:
            j = self.last_w.get(r)
            if j is not None:
                deps.add(j)
        for w in writes:
            j = self.last_w.get(w)
            if j is not None:
                deps.add(j)
            deps.update(self.readers.get(w, ()))
        deps.discard(i)
        self.ops.append(dict(engine=engine, fn=fn, deps=deps, slot=dma_slot,
                             inc=False))
        for r in reads:
            self.readers.setdefault(r, []).append(i)
        for w in writes:
            self.last_w[w] = i
            self.readers[w] = []
        self._last_per_key[dma_slot[0] if dma_slot else engine] = i
        return i

    def build(self):
        nc = self.nc
        ops = self.ops
        for o in ops:
            pruned = {}
            for j in o["deps"]:
                oj = ops[j]
                if (oj["slot"] is None and o["slot"] is None
                        and oj["engine"] == "tensor" and o["engine"] == "tensor"):
                    continue
                key = oj["slot"][0] if oj["slot"] else oj["engine"]
                if key not in pruned or j > pruned[key]:
                    pruned[key] = j
            o["rdeps"] = pruned
        for o in ops:
            for j in o["rdeps"].values():
                ops[j]["inc"] = True
        for o in ops:
            if o["slot"] is not None:
                o["inc"] = True
        cnt, run = {}, {}
        for i, o in enumerate(ops):
            if o["slot"] is None and o["inc"]:
                e = o["engine"]
                run[e] = run.get(e, 0) + 1
                cnt[i] = run[e]
        final_slot = dict(self.slot_fills)
        for o in ops:
            waits = []
            for key, j in sorted(o["rdeps"].items()):
                oj = ops[j]
                if oj["slot"] is not None:
                    slot, fill = oj["slot"]
                    v = 16 * (final_slot[slot] if slot in self.bulk_slots else fill)
                    waits.append(("slot:" + slot, v))
                else:
                    waits.append(("eng:" + oj["engine"], cnt[j]))
            o["waits"] = waits

        self._verify(ops)
        with ExitStack() as ctx:
            sems = {}
            for e in COMPUTE_ENGS:
                sems["eng:" + e] = ctx.enter_context(nc.semaphore("sem_" + e))
            for s in self.slot_fills:
                sems["slot:" + s] = ctx.enter_context(nc.semaphore("sem_" + s))
            block = ctx.enter_context(nc.Block())
            water = {}

            def emit_for(engine_name, eng):
                for o in ops:
                    if o["engine"] != engine_name:
                        continue
                    for key, v in o["waits"]:
                        wk = (engine_name, key)
                        if water.get(wk, 0) >= v:
                            continue
                        water[wk] = v
                        eng.wait_ge(sems[key], v)
                    inst = o["fn"](eng)
                    if o["inc"]:
                        if o["slot"] is not None:
                            inst.then_inc(sems["slot:" + o["slot"][0]], 16)
                        else:
                            inst.then_inc(sems["eng:" + engine_name], 1)
                if engine_name == "sync":
                    for s, n in sorted(self.slot_fills.items()):
                        wk = (engine_name, "slot:" + s)
                        if water.get(wk, 0) < 16 * n:
                            eng.wait_ge(sems["slot:" + s], 16 * n)

            @block.tensor
            def _(t):
                emit_for("tensor", t)

            @block.vector
            def _(v):
                emit_for("vector", v)

            @block.scalar
            def _(s):
                emit_for("scalar", s)

            @block.sync
            def _(s):
                emit_for("sync", s)

            @block.gpsimd
            def _(g):
                emit_for("gpsimd", g)
        return nc



def _rope_tables_l0():
    HD = 64
    inv_freq = 1.0 / (10000.0 ** (np.arange(0, HD, 2, dtype=np.float32) / HD))
    ang = np.arange(T_, dtype=np.float32)[:, None] * inv_freq[None, :]
    return np.cos(ang).astype(np.float32), np.sin(ang).astype(np.float32)


def _rope_tables_l1():
    HEAD_DIM = 64
    ROPE_THETA, SCALING, NTK_ALPHA, NTK_BETA, ICL = 150000.0, 32.0, 1.0, 32.0, 1024
    d_half = HEAD_DIM // 2
    freq = ROPE_THETA ** (np.arange(0, HEAD_DIM, 2, dtype=np.float32) / HEAD_DIM)
    concentration = 0.1 * math.log(SCALING) + 1.0
    low = d_half * math.log(ICL / (NTK_BETA * 2 * math.pi)) / math.log(ROPE_THETA)
    high = d_half * math.log(ICL / (NTK_ALPHA * 2 * math.pi)) / math.log(ROPE_THETA)
    interpolation = 1.0 / (SCALING * freq)
    extrapolation = 1.0 / freq
    ramp = np.clip((np.arange(d_half, dtype=np.float32) - low) / (high - low), 0.0, 1.0)
    mask = 1.0 - ramp
    inv_freq = interpolation * (1.0 - mask) + extrapolation * mask
    freqs = np.outer(np.arange(T_, dtype=np.float32), inv_freq)
    return ((np.cos(freqs) * concentration).astype(np.float32),
            (np.sin(freqs) * concentration).astype(np.float32))


def _l1_masks(core):
    Q0 = (core % 4) * OWN
    qp = np.arange(512)[None, :]
    kp = np.arange(128)[:, None]
    out = np.zeros((2, 5, 128, 512), dtype=BF)
    for qc in range(2):
        for m in range(5):
            r = 128 * m + kp - qp
            valid = (r >= 1) & (r <= 128)
            if Q0 == 0:
                kl = qc * 512 + 128 * m + kp + 0 * qp
                valid = valid & (kl >= 128)
            out[qc, m] = valid.astype(np.float32).astype(BF)
    return out


def _pad_slice(x, lo, hi):
    pad = max(0, -lo)
    body = x[max(lo, 0):hi]
    if pad:
        return np.concatenate([np.zeros((pad,) + x.shape[1:], x.dtype), body], 0)
    return np.ascontiguousarray(body)


def prep_inputs(inp):
    f32 = np.float32
    x = np.asarray(inp["L0_attn_x"], f32)
    cos0, sin0 = _rope_tables_l0()
    cos1, sin1 = _rope_tables_l1()
    iden_bf = np.eye(128, dtype=BF)
    iden_f32 = np.eye(128, dtype=f32)
    ones128 = np.ones((1, 128), f32)
    sel8 = np.zeros((8, 8 * 128), f32)
    for e in range(8):
        sel8[e, e * 128:(e + 1) * 128] = 1.0
    sel32 = np.zeros((32, 32 * 128), f32)
    for e in range(32):
        sel32[e, e * 128:(e + 1) * 128] = 1.0

    ebm = np.asarray(inp["L0_ffn_expert_b"], f32)
    eb_cols = np.ascontiguousarray(
        ebm.reshape(8, 16, 128).transpose(2, 1, 0).reshape(128, 128))

    wq_t0 = np.asarray(inp["L0_attn_Wq"], f32).T.reshape(D_, 32, 64)
    wq_tp = np.ascontiguousarray(wq_t0[:, SLOT2HEAD, :].reshape(D_, D_))
    bq0 = np.asarray(inp["L0_attn_bq"], f32).reshape(32, 64)
    bqp = np.ascontiguousarray(bq0[SLOT2HEAD].reshape(1, D_))
    wo_t0 = np.asarray(inp["L0_attn_Wo"], f32).T.reshape(32, 64, D_)
    wo_tp = np.ascontiguousarray(wo_t0[SLOT2HEAD].reshape(D_, D_))
    shared0 = dict(
        wq_t=wq_tp,
        wk_t=np.ascontiguousarray(np.asarray(inp["L0_attn_Wk"], f32).T),
        wv_t=np.ascontiguousarray(np.asarray(inp["L0_attn_Wv"], f32).T),
        wo_t=wo_tp,
        bq=bqp,
        bk=np.asarray(inp["L0_attn_bk"], f32).reshape(1, -1),
        bv=np.asarray(inp["L0_attn_bv"], f32).reshape(1, -1),
        bo=np.asarray(inp["L0_attn_bo"], f32).reshape(1, -1),
        rw=np.ascontiguousarray(np.asarray(inp["L0_ffn_router_w"], f32)),
        rb=np.asarray(inp["L0_ffn_router_b"], f32).reshape(1, -1),
        ew=np.ascontiguousarray(np.asarray(inp["L0_ffn_expert_w"], f32)),
        eb_cols=eb_cols,
        pw=np.ascontiguousarray(np.asarray(inp["L0_ffn_proj_w"], f32)),
        pb=np.asarray(inp["L0_ffn_proj_b"], f32).reshape(1, -1),
        nw=np.asarray(inp["L0_ffn_norm_w"], f32).reshape(1, -1),
        iden_bf=iden_bf, iden_f32=iden_f32, ones128=ones128, sel8=sel8,
        cos0=cos0, sin0=sin0)
    qkv_t0 = np.asarray(inp["L1_attn_qkv_w"], f32).T
    qp = qkv_t0[:, 0:2048].reshape(D_, 32, 64)
    qkv_tp = np.concatenate([qp[:, SLOT2HEAD, :].reshape(D_, 2048),
                             qkv_t0[:, 2048:]], axis=1)
    qb0 = np.asarray(inp["L1_attn_qkv_b"], f32).reshape(-1)
    qbq = qb0[0:2048].reshape(32, 64)
    qkv_bp = np.concatenate([qbq[SLOT2HEAD].reshape(-1), qb0[2048:]])
    out_t0 = np.asarray(inp["L1_attn_out_w"], f32).T.reshape(32, 64, D_)
    out_tp = np.ascontiguousarray(out_t0[SLOT2HEAD].reshape(D_, D_))
    shared1 = dict(
        qkv_t=np.ascontiguousarray(qkv_tp),
        qkv_b=np.ascontiguousarray(qkv_bp.reshape(1, -1)),
        out_t=out_tp,
        out_b=np.asarray(inp["L1_attn_out_b"], f32).reshape(1, -1),
        sinks=np.asarray(inp["L1_attn_sinks"], f32).reshape(1, -1),
        gw_t=np.ascontiguousarray(np.asarray(inp["L1_ffn_gate_w"], f32).T),
        gb=np.asarray(inp["L1_ffn_gate_bias"], f32).reshape(1, -1),
        fc1_t=np.ascontiguousarray(np.asarray(inp["L1_ffn_fc1_w"], f32).T),
        fc2_t=np.ascontiguousarray(np.asarray(inp["L1_ffn_fc2_w"], f32).T),
        w1_t=np.ascontiguousarray(np.asarray(inp["L1_ffn_w1"], f32).transpose(0, 2, 1)),
        w2_t=np.ascontiguousarray(np.asarray(inp["L1_ffn_w2"], f32).transpose(0, 2, 1)),
        su_t=np.ascontiguousarray(np.asarray(inp["L1_ffn_shared_up_w"], f32).T),
        sd_t=np.ascontiguousarray(np.asarray(inp["L1_ffn_shared_down_w"], f32).T),
        iden_bf=iden_bf, iden_f32=iden_f32, ones128=ones128, sel32=sel32)

    maps0, maps1 = [], []
    for c in range(N_CORES):
        bb, Q0 = c // 4, (c % 4) * OWN
        m0 = dict(shared0)
        m0["xb"] = np.ascontiguousarray(x[bb])
        m0["xc"] = _pad_slice(x[bb], Q0 - HALO, Q0 + OWN)
        m0["cos0c"] = _pad_slice(cos0, Q0 - HALO, Q0 + OWN)
        m0["sin0c"] = _pad_slice(sin0, Q0 - HALO, Q0 + OWN)
        maps0.append(m0)
        m1 = dict(shared1)
        m1["cos1"] = _pad_slice(cos1, Q0 - HALO, Q0 + OWN)
        m1["sin1"] = _pad_slice(sin1, Q0 - HALO, Q0 + OWN)
        m1["masks"] = _l1_masks(c)
        maps1.append(m1)
    return maps0, maps1



def r3(src2d, p=128):
    return src2d.rearrange("(n p) d -> p n d", p=p)


def d3(dstbuf_slice, d):
    return dstbuf_slice.rearrange("p (n d) -> p n d", d=d)



def fmv(t, total):
    return t[:, :].rearrange("p (f c) -> p f c", c=total)


def emit_ln_tile(b, C, xap, xkey, st, stk, out_bf, outkey, eps,
                 mcol=None, rcol=None, statkeys=("m_own", "r_own")):
    scr, sck = C["lnscr"], "lnscr"
    b.red(st[:, 0:1], xap, reads=[xkey], writes=[stk])
    b.tt(scr[:, 0:D_], xap, xap, ALU.mult, reads=[xkey], writes=[sck])
    b.red(st[:, 1:2], scr[:, 0:D_], reads=[sck], writes=[stk])
    b.ts(st[:, 0:1], st[:, 0:1], 1.0 / D_, None, ALU.mult, reads=[stk], writes=[stk])
    b.tt(st[:, 2:3], st[:, 0:1], st[:, 0:1], ALU.mult, reads=[stk], writes=[stk])
    b.ts(st[:, 1:2], st[:, 1:2], 1.0 / D_, None, ALU.mult, reads=[stk], writes=[stk])
    b.tt(st[:, 1:2], st[:, 1:2], st[:, 2:3], ALU.subtract, reads=[stk], writes=[stk])
    b.ts(st[:, 1:2], st[:, 1:2], float(eps), None, ALU.add, reads=[stk], writes=[stk])
    b.act(st[:, 2:3], st[:, 1:2], AF.Ln, reads=[stk], writes=[stk])
    b.act(st[:, 3:4], st[:, 2:3], AF.Exp, scale=-0.5, reads=[stk], writes=[stk])
    b.ts(out_bf, xap, st[:, 0:1], st[:, 3:4], ALU.subtract, ALU.mult,
         reads=[xkey, stk], writes=[outkey])
    if mcol is not None:
        b.vcopy(mcol, st[:, 0:1], reads=[stk], writes=[statkeys[0]])
        b.vcopy(rcol, st[:, 3:4], reads=[stk], writes=[statkeys[1]])


def emit_transpose16(b, C, src_bf, srckey, dstv, dstkey, nft=16):
    for g in range((nft + 7) // 8):
        n = min(8, nft - g * 8)
        pt, ptk = C["pt%d" % (g % 2)], "pt%d" % (g % 2)
        for k in range(n):
            ft = g * 8 + k
            b.trans(pt[:, k * 128:(k + 1) * 128], src_bf[:, ft * 128:(ft + 1) * 128],
                    C["idb"][:], reads=[srckey, "idb"], writes=[ptk])
        b.vcopy(dstv(g, n), pt[:, 0:n * 128].rearrange("p (f c) -> p f c", c=128),
                reads=[ptk], writes=[dstkey])


def emit_rope(b, C, src, srckey, dst, dstkey, costile, sintile, nheads, tmp, tmpkey,
              ckeys=("cosb", "sinb")):
    ca = costile.rearrange("p (o d) -> p o d", o=1).broadcast_to([128, nheads, 32])
    sa = sintile.rearrange("p (o d) -> p o d", o=1).broadcast_to([128, nheads, 32])
    s3 = src.rearrange("p (h d) -> p h d", d=64)
    d3 = dst.rearrange("p (h d) -> p h d", d=64)
    t3 = tmp.rearrange("p (h d) -> p h d", d=64)
    x1, x2 = s3[:, :, 0:32], s3[:, :, 32:64]
    b.tt(t3[:, :, 0:32], x1, ca, ALU.mult, reads=[srckey, ckeys[0]], writes=[tmpkey])
    b.tt(t3[:, :, 32:64], x2, sa, ALU.mult, reads=[srckey, ckeys[1]], writes=[tmpkey])
    b.tt(d3[:, :, 0:32], t3[:, :, 0:32], t3[:, :, 32:64], ALU.subtract,
         reads=[tmpkey], writes=[dstkey])
    b.tt(t3[:, :, 0:32], x2, ca, ALU.mult, reads=[srckey, ckeys[0]], writes=[tmpkey])
    b.tt(t3[:, :, 32:64], x1, sa, ALU.mult, reads=[srckey, ckeys[1]], writes=[tmpkey])
    b.tt(d3[:, :, 32:64], t3[:, :, 0:32], t3[:, :, 32:64], ALU.add,
         reads=[tmpkey], writes=[dstkey])


def emit_global_max(b, nc, ctx, tag, col, colkey, out11, outkey, row, rowkey):
    dscr = nc.dram_tensor("gmax_" + tag, [1, 128], F32, kind="Internal").ap()
    b.dma("sync", "gm" + tag, dscr.rearrange("o p -> p o"), col,
          reads=[colkey], writes=["gmd" + tag])
    b.dma("sync", "gm" + tag + "b", row, dscr, reads=["gmd" + tag], writes=[rowkey])
    b.red(out11, row, aluop=ALU.max, reads=[rowkey], writes=[outkey])



def build_l0(debug=()):
    nc = bass.Bass("TRN2", target_bir_lowering=False, debug=False)
    I = {}

    def di(name, shape, dt=F32):
        I[name] = nc.dram_tensor(name, list(shape), dt, kind="ExternalInput").ap()

    di("xb", (T_, D_)); di("xc", (TOK, D_))
    di("wq_t", (D_, D_)); di("wk_t", (D_, 512)); di("wv_t", (D_, 512))
    di("wo_t", (D_, D_))
    di("bq", (1, D_)); di("bk", (1, 512)); di("bv", (1, 512)); di("bo", (1, D_))
    di("rw", (D_, 8)); di("rb", (1, 8)); di("ew", (8, D_, D_))
    di("eb_cols", (128, 128)); di("pw", (D_, D_)); di("pb", (1, D_)); di("nw", (1, D_))
    di("cos0", (T_, 32)); di("sin0", (T_, 32))
    di("cos0c", (TOK, 32)); di("sin0c", (TOK, 32))
    di("iden_bf", (128, 128), BF16); di("iden_f32", (128, 128))
    di("ones128", (1, 128)); di("sel8", (8, 8 * 128))

    x1 = nc.dram_tensor("x1", [TOK, D_], F32, kind="ExternalOutput").ap()
    gdram = nc.dram_tensor("gscr", [TOK, D_], BF16, kind="Internal").ap()
    DBG = {}

    def dbg(name, shape=None, dt=F32):
        if name in debug and shape is not None and name not in DBG:
            DBG[name] = nc.dram_tensor("dbg_" + name, list(shape), dt,
                                       kind="ExternalOutput").ap()
        return DBG.get(name)

    b = Prog(nc)
    with ExitStack() as ctx:
        emit_l0(b, I, x1, gdram, dbg, nc, ctx)
    return b.build(), sorted(DBG)


def emit_l0(b, I, x1, gdram, dbg, nc, ctx):
    def sb(name, shape, dt=F32):
        return ctx.enter_context(nc.sbuf_tensor(name, list(shape), dt))

    def psg(st, name, shape, dt=F32):
        return st.enter_context(nc.psum_tensor(name, list(shape), dt))

    C = {}
    K_fm = sb("K_fm", [128, 4 * T_], BF16)
    V_tm = sb("V_tm", [128, NTB * 520], BF16)
    m_own = sb("m_own", [128, NT]); r_own = sb("r_own", [128, NT])
    q2acc = sb("q2acc", [128, 32]); k2acc = sb("k2acc", [128, 8])
    negB = sb("negB", [128, 1])
    C["idb"] = sb("idb", [128, 128], BF16)
    onesr = sb("onesr", [1, 128])

    b.dma("sync", "c0", C["idb"][:], I["iden_bf"], writes=["idb"], bulk=True)
    b.dma("sync", "c0", onesr[:], I["ones128"], writes=["onesr"], bulk=True)
    b.vmemset(q2acc[:], 0.0, writes=["q2acc"])
    b.vmemset(k2acc[:], 0.0, writes=["k2acc"])

    sComb = ExitStack()
    with ExitStack() as sAQ:
        xn_fm = sAQ.enter_context(nc.sbuf_tensor("xn_fm", [128, NF * TOK], BF16))
        wT = sAQ.enter_context(nc.sbuf_tensor("wT", [8, TOK], F32))
        scs = sAQ.enter_context(ExitStack())
        cosb = scs.enter_context(nc.sbuf_tensor("cosb", [128, NTB * 32], F32))
        sinb = scs.enter_context(nc.sbuf_tensor("sinb", [128, NTB * 32], F32))
        b.dma("sync", "c0", d3(cosb[:], 32), r3(I["cos0"]),
              writes=["cosb"], bulk=True)
        b.dma("sync", "c0", d3(sinb[:], 32), r3(I["sin0"]),
              writes=["sinb"], bulk=True)

        with ExitStack() as sA:
            def asb(name, shape, dt=F32):
                return sA.enter_context(nc.sbuf_tensor(name, list(shape), dt))

            xt = [asb("xt%d" % i, [128, D_]) for i in range(2)]
            C["lnscr"] = asb("lnscr", [128, D_])
            wkv = asb("wkv", [128, NF * 1024], BF16)
            xnbf = [asb("xnbf0", [128, D_], BF16)]
            stat = [asb("stat%d" % i, [128, 4]) for i in range(2)]
            xnfm_kv = asb("xnfm_kv", [128, NF * 512], BF16)
            kvscr = [asb("kvscr0", [128, 512])]
            kvtmp = [asb("kvtmp0", [128, 512])]
            krs = [asb("krs%d" % i, [128, 512], BF16) for i in range(2)]
            bk_bc = asb("bk_bc", [128, 512])
            bv_bc = asb("bv_bc", [128, 512])
            pkv = [psg(sA, "pkv%d" % i, [128, 512]) for i in range(2)]
            C["pt0"] = psg(sA, "pt0", [128, 1024], BF16)
            C["pt1"] = psg(sA, "pt1", [128, 1024], BF16)

            b.dma("gpsimd", "wkv", d3(wkv[:, 0:NF * 512], 512), r3(I["wk_t"]),
                  writes=["wkv_k"], bulk=True)
            b.dma("gpsimd", "wkv", d3(wkv[:, NF * 512:], 512), r3(I["wv_t"]),
                  writes=["wkv_v"], bulk=True)
            b.dma("sync", "cA", bk_bc[:], I["bk"].broadcast_to([128, 512]),
                  writes=["bk_bc"], bulk=True)
            b.dma("sync", "cA", bv_bc[:], I["bv"].broadcast_to([128, 512]),
                  writes=["bv_bc"], bulk=True)

            for ch in range(NTB // 4):
                for it in range(4):
                    tt = ch * 4 + it
                    bi = tt % 2
                    xtb, xk = xt[bi], "xt%d" % bi
                    b.dma("sync", xk, xtb[:], I["xb"][tt * 128:(tt + 1) * 128, :],
                          writes=[xk])
                    emit_ln_tile(b, C, xtb[:], xk, stat[bi], "stat%d" % bi,
                                 xnbf[0][:], "xnbf0", 1e-5)
                    kvv = fmv(xnfm_kv, 512)
                    emit_transpose16(
                        b, C, xnbf[0][:], "xnbf0",
                        (lambda g, n, it_=it: kvv[:, g * 8:g * 8 + n,
                                                  it_ * 128:(it_ + 1) * 128]),
                        "xnfm_kv")
                for it in range(4):
                    tt = ch * 4 + it
                    bi = it % 2
                    pk, pkk = pkv[bi], "pkv%d" % bi
                    scr, sck = kvscr[0], "kvscr0"
                    tmp, tmk = kvtmp[0], "kvtmp0"
                    for fc in range(NF):
                        b.mm(pk[:], xnfm_kv[:, fc * 512 + it * 128:fc * 512 + (it + 1) * 128],
                             wkv[:, fc * 512:(fc + 1) * 512], fc == 0, fc == NF - 1,
                             reads=["xnfm_kv", "wkv_k"], writes=[pkk])
                    b.tt(scr[:], pk[:], bk_bc[:], ALU.add, reads=[pkk, "bk_bc"], writes=[sck])
                    emit_rope(b, C, scr[:], sck, krs[bi][:], "krs%d" % bi,
                              cosb[:, tt * 32:(tt + 1) * 32], sinb[:, tt * 32:(tt + 1) * 32],
                              8, tmp[:], tmk)
                    pt, ptk = C["pt%d" % bi], "pt%d" % bi
                    for k in range(4):
                        b.trans(pt[:, k * 128:(k + 1) * 128], krs[bi][:, k * 128:(k + 1) * 128],
                                C["idb"][:], reads=["krs%d" % bi, "idb"], writes=[ptk])
                    b.vcopy(fmv(K_fm, T_)[:, 0:4, tt * 128:(tt + 1) * 128],
                            pt[:, 0:512].rearrange("p (f c) -> p f c", c=128),
                            reads=[ptk], writes=["K_fm"])
                    b.tt(tmp[:], krs[bi][:], krs[bi][:], ALU.mult,
                         reads=["krs%d" % bi], writes=[tmk])
                    b.red(scr[:, 0:8], tmp[:].rearrange("p (h d) -> p h d", d=64),
                          aluop=ALU.max, reads=[tmk], writes=[sck])
                    b.tt(k2acc[:], k2acc[:], scr[:, 0:8], ALU.max,
                         reads=[sck, "k2acc"], writes=["k2acc"])
                    for fc in range(NF):
                        b.mm(pk[:], xnfm_kv[:, fc * 512 + it * 128:fc * 512 + (it + 1) * 128],
                             wkv[:, NF * 512 + fc * 512:NF * 512 + (fc + 1) * 512],
                             fc == 0, fc == NF - 1,
                             reads=["xnfm_kv", "wkv_v"], writes=[pkk])
                    vt = V_tm[:, tt * 520:(tt + 1) * 520].rearrange("p (h c) -> p h c", c=65)
                    b.tt(vt[:, :, 0:64], pk[:].rearrange("p (h c) -> p h c", c=64),
                         bv_bc[:].rearrange("p (h c) -> p h c", c=64), ALU.add,
                         reads=[pkk, "bv_bc"], writes=["V_tm"])
                    b.vmemset(vt[:, :, 64:65], 1.0, writes=["V_tm"])

            for ot in range(NT):
                bi = ot % 2
                xtb, xk = xt[bi], "xt%d" % bi
                b.dma("sync", xk, xtb[:], I["xc"][ot * 128:(ot + 1) * 128, :],
                      writes=[xk])
                emit_ln_tile(b, C, xtb[:], xk, stat[bi], "stat%d" % bi,
                             xnbf[0][:], "xnbf0", 1e-5,
                             mcol=m_own[:, ot:ot + 1], rcol=r_own[:, ot:ot + 1])
                xv = fmv(xn_fm, TOK)
                emit_transpose16(
                    b, C, xnbf[0][:], "xnbf0",
                    (lambda g, n, ot_=ot: xv[:, g * 8:g * 8 + n, ot_ * 128:(ot_ + 1) * 128]),
                    "xn_fm")

        b.fence()
        if dbg("xn_fm", (128, NF * TOK)):
            b.dma("gpsimd", "dbg1", dbg("xn_fm"), xn_fm[:], reads=["xn_fm"], writes=[])
        if dbg("K_fm", (128, 4 * T_)):
            b.dma("gpsimd", "dbg2", dbg("K_fm"), K_fm[:], reads=["K_fm"], writes=[])
        if dbg("V_tm", (128, NTB * 520)):
            b.dma("gpsimd", "dbg3", dbg("V_tm"), V_tm[:], reads=["V_tm"], writes=[])

        if PHASES < 2:
            return
        with ExitStack() as sR:
            rw_b = sR.enter_context(nc.sbuf_tensor("rw_b", [128, NF * 8], BF16))
            rw_f = sR.enter_context(nc.sbuf_tensor("rw_f", [128, NF * 8], F32))
            rb_bc = sR.enter_context(nc.sbuf_tensor("rb_bc", [128, 8], F32))
            rgb = sR.enter_context(nc.sbuf_tensor("rgb", [128, 8], BF16))
            rscr = sR.enter_context(nc.sbuf_tensor("rscr", [128, 32], F32))
            rpsum_t = psg(sR, "rpsum", [128, 512])
            rpsum = rpsum_t[:, 0:8]
            rtp_t = psg(sR, "rtp", [128, 1024], BF16)
            rtp = rtp_t[0:8, 0:128]
            b.dma("sync", "cRg", d3(rw_f[:], 8), r3(I["rw"]),
                  writes=["rw_f"], bulk=True)
            b.vcopy(rw_b[:], rw_f[:], reads=["rw_f"], writes=["rw_b"])
            b.dma("sync", "cR", rb_bc[:], I["rb"].broadcast_to([128, 8]),
                  writes=["rb_bc"], bulk=True)

            for ot in range(NT):
                for fc in range(NF):
                    b.mm(rpsum, fmv(xn_fm, TOK)[:, fc, ot * 128:(ot + 1) * 128],
                         rw_b[:, fc * 8:(fc + 1) * 8], fc == 0, fc == NF - 1,
                         reads=["xn_fm", "rw_b"], writes=["rpsum"])
                l, lk = rscr[:, 0:8], "rscr"
                b.tt(l, rpsum, rb_bc[:], ALU.add, reads=["rpsum", "rb_bc"], writes=[lk])
                m1, nm1, m2 = rscr[:, 8:9], rscr[:, 9:10], rscr[:, 10:11]
                e21, den, rcp = rscr[:, 11:12], rscr[:, 12:13], rscr[:, 13:14]
                tq, ge = rscr[:, 16:24], rscr[:, 24:32]
                b.red(m1, l, aluop=ALU.max, reads=[lk], writes=[lk])
                b.red(nm1, l, aluop=ALU.max, negate=True, reads=[lk], writes=[lk])
                b.ts(tq, l, m1, -1e30, ALU.is_equal, ALU.mult, reads=[lk], writes=[lk])
                b.tt(tq, tq, l, ALU.add, reads=[lk], writes=[lk])
                b.red(m2, tq, aluop=ALU.max, reads=[lk], writes=[lk])
                b.act(e21, m2, AF.Exp, bias=nm1, reads=[lk], writes=[lk])
                b.ts(den, e21, 1.0, None, ALU.add, reads=[lk], writes=[lk])
                b.vrecip(rcp, den, reads=[lk], writes=[lk])
                b.act(tq, l, AF.Exp, bias=nm1, reads=[lk], writes=[lk])
                b.ts(ge, l, m2, None, ALU.is_ge, reads=[lk], writes=[lk])
                b.tt(ge, ge, tq, ALU.mult, reads=[lk], writes=[lk])
                b.ts(ge, ge, rcp, None, ALU.mult, reads=[lk], writes=[lk])
                if dbg("rdump", (128, NT * 32)):
                    b.dma("sync", "dbgR", dbg("rdump")[:, ot * 32:(ot + 1) * 32],
                          rscr[:], reads=[lk], writes=[])
                b.vcopy(rgb[:], ge, reads=[lk], writes=["rgb"])
                b.trans(rtp, rgb[:], C["idb"][:], reads=["rgb", "idb"], writes=["rtp"])
                b.vcopy(wT[:, ot * 128:(ot + 1) * 128], rtp, reads=["rtp"], writes=["wT"])

        if dbg("wT", (8, TOK)):
            b.dma("sync", "dbg4", dbg("wT"), wT[:], reads=["wT"], writes=[])

        b.fence()
        if PHASES < 3:
            return
        Q_fm = ctx.enter_context(nc.sbuf_tensor("Q_fm", [128, NF * TOK], BF16,
                                                side="right"))
        with ExitStack() as sQ:
            wqs = [sQ.enter_context(nc.sbuf_tensor("wqs%d" % i, [128, NF * 512], BF16))
                   for i in range(2)]
            qscr = [sQ.enter_context(nc.sbuf_tensor("qscr%d" % i, [128, 512], F32))
                    for i in range(2)]
            qtmp = [sQ.enter_context(nc.sbuf_tensor("qtmp%d" % i, [128, 512], F32))
                    for i in range(2)]
            qrb = [sQ.enter_context(nc.sbuf_tensor("qrb%d" % i, [128, 512], BF16))
                   for i in range(2)]
            bq_bc = sQ.enter_context(nc.sbuf_tensor("bq_bc", [128, D_], F32))
            cosc = sQ.enter_context(nc.sbuf_tensor("cosc", [128, NT * 32], F32))
            sinc = sQ.enter_context(nc.sbuf_tensor("sinc", [128, NT * 32], F32))
            ppq = [psg(sQ, "ppq%d" % i, [128, 512]) for i in range(2)]
            C["pt0"] = psg(sQ, "pt0b", [128, 1024], BF16)
            C["pt1"] = psg(sQ, "pt1b", [128, 1024], BF16)
            b.dma("sync", "cQ", bq_bc[:], I["bq"].broadcast_to([128, D_]),
                  writes=["bq_bc"], bulk=True)
            b.dma("sync", "cQ", d3(cosc[:], 32), r3(I["cos0c"]),
                  writes=["cosc"], bulk=True)
            b.dma("sync", "cQ", d3(sinc[:], 32), r3(I["sin0c"]),
                  writes=["sinc"], bulk=True)
            for ncc in range(4):
                wi = ncc % 2
                wb, wk = wqs[wi], "wqs%d" % wi
                b.dma("gpsimd", wk, d3(wb[:], 512), r3(I["wq_t"][:, ncc * 512:(ncc + 1) * 512]),
                      writes=[wk])
                for ot in range(NT):
                    bi = ot % 2
                    pp, ppk = ppq[bi], "ppq%d" % bi
                    for fc in range(NF):
                        b.mm(pp[:], fmv(xn_fm, TOK)[:, fc, ot * 128:(ot + 1) * 128],
                             wb[:, fc * 512:(fc + 1) * 512], fc == 0, fc == NF - 1,
                             reads=["xn_fm", wk], writes=[ppk])
                    b.tt(qscr[bi][:], pp[:], bq_bc[:, ncc * 512:(ncc + 1) * 512], ALU.add,
                         reads=[ppk, "bq_bc"], writes=["qscr%d" % bi])
                    emit_rope(b, C, qscr[bi][:], "qscr%d" % bi, qrb[bi][:], "qrb%d" % bi,
                              cosc[:, ot * 32:(ot + 1) * 32], sinc[:, ot * 32:(ot + 1) * 32],
                              8, qtmp[bi][:], "qtmp%d" % bi, ckeys=("cosc", "sinc"))
                    b.tt(qtmp[bi][:], qrb[bi][:], qrb[bi][:], ALU.mult,
                         reads=["qrb%d" % bi], writes=["qtmp%d" % bi])
                    b.red(qscr[bi][:, 0:8], qtmp[bi][:].rearrange("p (h d) -> p h d", d=64),
                          aluop=ALU.max, reads=["qtmp%d" % bi], writes=["qscr%d" % bi])
                    b.tt(q2acc[:, ncc * 8:(ncc + 1) * 8], q2acc[:, ncc * 8:(ncc + 1) * 8],
                         qscr[bi][:, 0:8], ALU.max,
                         reads=["qscr%d" % bi, "q2acc"], writes=["q2acc"])
                    pt, ptk = C["pt%d" % bi], "pt%d" % bi
                    for k in range(4):
                        b.trans(pt[:, k * 128:(k + 1) * 128], qrb[bi][:, k * 128:(k + 1) * 128],
                                C["idb"][:], reads=["qrb%d" % bi, "idb"], writes=[ptk])
                    b.vcopy(fmv(Q_fm, TOK)[:, ncc * 4:(ncc + 1) * 4, ot * 128:(ot + 1) * 128],
                            pt[:, 0:512].rearrange("p (f c) -> p f c", c=128),
                            reads=[ptk], writes=["Q_fm"])

        b.fence()
        scs.close()

        if PHASES < 4:
            return
        comb = sComb.enter_context(nc.sbuf_tensor("comb", [128, NF * TOK], BF16,
                                                   side="right"))
        combv = fmv(comb, TOK)
        with ExitStack() as sM:
            wes = [sM.enter_context(nc.sbuf_tensor("wes%d" % i, [128, NF * 256], BF16))
                   for i in range(2)]
            she = [sM.enter_context(nc.sbuf_tensor("she0", [128, 512], BF16))] * 2
            mtmp = [sM.enter_context(nc.sbuf_tensor("mtmp0", [128, 512], BF16))] * 2
            wbcs = sM.enter_context(nc.sbuf_tensor("wbcs", [128, TOK], BF16))
            sel8b = sM.enter_context(nc.sbuf_tensor("sel8b", [8, 8 * 128], F32))
            eb_c = sM.enter_context(nc.sbuf_tensor("eb_c", [128, 128], F32))
            pH = [psg(sM, "pH%d" % i, [128, 512]) for i in range(2)]
            pwb = psg(sM, "pwb", [128, 512])
            b.dma("sync", "cM", sel8b[:], I["sel8"], writes=["sel8b"], bulk=True)
            b.dma("sync", "cM", eb_c[:], I["eb_cols"], writes=["eb_c"], bulk=True)

            for e in range(8):
                for ci, tw in enumerate(CHW):
                    b.mm(pwb[:, 0:tw], sel8b[:, e * 128:(e + 1) * 128],
                         wT[:, ci * 512:ci * 512 + tw], True, True,
                         reads=["sel8b", "wT"], writes=["pwb"])
                    b.vcopy(wbcs[:, ci * 512:ci * 512 + tw], pwb[:, 0:tw],
                            reads=["pwb"], writes=["wbcs"])
                for ft4 in range(8):
                    wi = (e * 8 + ft4) % 2
                    wb, wk = wes[wi], "wes%d" % wi
                    b.dma("gpsimd", wk, d3(wb[:], 256),
                          r3(I["ew"][e, :, ft4 * 256:(ft4 + 1) * 256]), writes=[wk])
                    for ftl in range(2):
                        ft = ft4 * 2 + ftl
                        for ci, tw in enumerate(CHW):
                            t0 = ci * 512
                            pi = (ft * 3 + ci) % 2
                            hb, hbk = pH[pi], "pH%d" % pi
                            for fc in range(NF):
                                b.mm(hb[:, 0:tw],
                                     wb[:, fc * 256 + ftl * 128:fc * 256 + (ftl + 1) * 128],
                                     fmv(xn_fm, TOK)[:, fc, t0:t0 + tw], fc == 0, fc == NF - 1,
                                     reads=[wk, "xn_fm"], writes=[hbk])
                            b.act(she[0][:, 0:tw], hb[:, 0:tw], AF.Silu,
                                  bias=eb_c[:, ft * 8 + e:ft * 8 + e + 1],
                                  reads=[hbk, "eb_c"], writes=["she0"])
                            dst = combv[:, ft, t0:t0 + tw]
                            ck = "comb%d_%d" % (ft, ci)
                            if e == 0:
                                b.tt(dst, she[0][:, 0:tw], wbcs[:, t0:t0 + tw], ALU.mult,
                                     reads=["she0", "wbcs"], writes=[ck])
                            else:
                                b.tt(mtmp[0][:, 0:tw], she[0][:, 0:tw], wbcs[:, t0:t0 + tw],
                                     ALU.mult, reads=["she0", "wbcs"],
                                     writes=["mtmp0"])
                                b.tt(dst, dst, mtmp[0][:, 0:tw], ALU.add,
                                     reads=["mtmp0", ck], writes=[ck])

    dbg("g_out", (TOK, D_))
    b.fence()
    if dbg("comb", (128, NF * TOK)):
        b.dma("gpsimd", "dbgC", dbg("comb"), comb[:], reads=["comb0_0"], writes=[])
    if PHASES < 5:
        sComb.close()
        return
    with ExitStack() as sG:
        wpj = [sG.enter_context(nc.sbuf_tensor("wpj%d" % i, [128, NF * 512], BF16))
               for i in range(2)]
        pb_bc = sG.enter_context(nc.sbuf_tensor("pb_bc", [128, D_], F32))
        nw_bc = sG.enter_context(nc.sbuf_tensor("nw_bc", [128, D_], F32))
        scrY = [sG.enter_context(nc.sbuf_tensor("scrY0", [128, D_], F32))]
        xo = [sG.enter_context(nc.sbuf_tensor("xo0", [128, D_], F32))]
        gt = [sG.enter_context(nc.sbuf_tensor("gt0", [128, D_], BF16))]
        yst = [sG.enter_context(nc.sbuf_tensor("yst0", [128, 4], F32))]
        pP = [psg(sG, "pP%d" % i, [128, 512]) for i in range(2)]
        b.dma("sync", "cG", pb_bc[:], I["pb"].broadcast_to([128, D_]),
              writes=["pb_bc"], bulk=True)
        b.dma("sync", "cG", nw_bc[:], I["nw"].broadcast_to([128, D_]),
              writes=["nw_bc"], bulk=True)
        for ot in range(NT):
            bi = 0
            sy, syk = scrY[0], "scrY0"
            xb_, xbk = xo[0], "xo0"
            b.dma("sync", xbk, xb_[:], I["xc"][ot * 128:(ot + 1) * 128, :],
                  writes=[xbk])
            for ncc in range(4):
                wi = ncc % 2
                wb, wk = wpj[wi], "wpj%d" % wi
                b.dma("gpsimd", wk, d3(wb[:], 512), r3(I["pw"][:, ncc * 512:(ncc + 1) * 512]),
                      writes=[wk])
                pp, ppk = pP[wi], "pP%d" % wi
                for fc in range(NF):
                    b.mm(pp[:], combv[:, fc, ot * 128:(ot + 1) * 128],
                         wb[:, fc * 512:(fc + 1) * 512], fc == 0, fc == NF - 1,
                         reads=["comb%d_%d" % (fc, min(ot // 4, 2)), wk], writes=[ppk])
                b.tt(sy[:, ncc * 512:(ncc + 1) * 512], pp[:],
                     pb_bc[:, ncc * 512:(ncc + 1) * 512], ALU.add,
                     reads=[ppk, "pb_bc"], writes=[syk])
            b.ts(xb_[:], xb_[:], m_own[:, ot:ot + 1], r_own[:, ot:ot + 1],
                 ALU.subtract, ALU.mult, reads=[xbk, "m_own", "r_own"], writes=[xbk])
            b.tt(sy[:], sy[:], xb_[:], ALU.add, reads=[syk, xbk], writes=[syk])
            st, stk = yst[0], "yst0"
            b.tt(xb_[:], sy[:], sy[:], ALU.mult, reads=[syk, xbk], writes=[xbk])
            b.red(st[:, 0:1], xb_[:], reads=[xbk], writes=[stk])
            b.ts(st[:, 0:1], st[:, 0:1], 1.0 / D_, 1e-6, ALU.mult, ALU.add,
                 reads=[stk], writes=[stk])
            b.act(st[:, 1:2], st[:, 0:1], AF.Ln, reads=[stk], writes=[stk])
            b.act(st[:, 2:3], st[:, 1:2], AF.Exp, scale=-0.5, reads=[stk], writes=[stk])
            b.ts(sy[:], sy[:], st[:, 2:3], None, ALU.mult, reads=[syk, stk], writes=[syk])
            b.tt(sy[:], sy[:], nw_bc[:], ALU.mult, reads=[syk, "nw_bc"], writes=[syk])
            b.act(gt[0][:], sy[:], AF.Silu, reads=[syk], writes=["gt0"])
            b.dma("sync", "gst0", gdram[ot * 128:(ot + 1) * 128, :], gt[0][:],
                  reads=["gt0"], writes=["gd%d" % ot])
            if dbg("g_out"):
                b.dma("gpsimd", "dbgG", dbg("g_out")[ot * 128:(ot + 1) * 128, :],
                      gt[0][:], reads=["gt0"], writes=[])

    b.fence()
    sComb.close()

    if dbg("Q_fm", (128, NF * TOK)):
        b.dma("gpsimd", "dbg5", dbg("Q_fm"), Q_fm[:], reads=["Q_fm"], writes=[])

    if PHASES < 6:
        return
    with ExitStack() as sB:
        bscr = sB.enter_context(nc.sbuf_tensor("bscr", [128, 8], F32))
        brow = sB.enter_context(nc.sbuf_tensor("brow", [1, 128], F32))
        bsc2 = sB.enter_context(nc.sbuf_tensor("bsc2", [1, 4], F32))
        pnb_t = psg(sB, "pnb", [128, 512])
        pnb = pnb_t[:, 0:1]
        b.red(bscr[:, 0:1], q2acc[:], aluop=ALU.max, reads=["q2acc"], writes=["bscr"])
        emit_global_max(b, nc, ctx, "q", bscr[:, 0:1], "bscr", bsc2[0:1, 0:1], "bsc2",
                        brow[0:1, :], "brow")
        b.red(bscr[:, 1:2], k2acc[:], aluop=ALU.max, reads=["k2acc"], writes=["bscr"])
        emit_global_max(b, nc, ctx, "k", bscr[:, 1:2], "bscr", bsc2[0:1, 1:2], "bsc2",
                        brow[0:1, :], "brow")
        b.tt(bsc2[0:1, 2:3], bsc2[0:1, 0:1], bsc2[0:1, 1:2], ALU.mult,
             reads=["bsc2"], writes=["bsc2"])
        b.act(bsc2[0:1, 2:3], bsc2[0:1, 2:3], AF.Ln, reads=["bsc2"], writes=["bsc2"])
        b.act(bsc2[0:1, 3:4], bsc2[0:1, 2:3], AF.Exp, scale=0.5, reads=["bsc2"], writes=["bsc2"])
        b.ts(bsc2[0:1, 3:4], bsc2[0:1, 3:4], -0.125, None, ALU.mult,
             reads=["bsc2"], writes=["bsc2"])
        b.mm(pnb, onesr[:], bsc2[0:1, 3:4], True, True,
             reads=["onesr", "bsc2"], writes=["pnb"])
        b.vcopy(negB[:], pnb, reads=["pnb"], writes=["negB"])
    if dbg("negB", (128, 1)):
        b.dma("sync", "dbg6", dbg("negB"), negB[:], reads=["negB"], writes=[])

    b.fence()
    if PHASES < 7:
        return
    with ExitStack() as sT:
        def tsb(name, shape, dt=F32):
            return sT.enter_context(nc.sbuf_tensor(name, list(shape), dt))

        attnfm = tsb("attnfm", [128, NF * 512], BF16)
        ptl = [tsb("ptl%d" % i, [128, 512], BF16) for i in range(6)]
        rdrow = tsb("rdrow", [1, 512])
        rbc = tsb("rbc", [64, 512])
        wos = [tsb("wos%d" % i, [128, NF * 512], BF16) for i in range(1)]
        bo_bc = tsb("bo_bc", [128, D_])
        sO = [tsb("sO%d" % i, [128, D_]) for i in range(4)]
        gld = [tsb("gld%d" % i, [128, D_], BF16) for i in range(1)]
        gf32 = [tsb("gf32%d" % i, [128, D_]) for i in range(1)]
        xo2 = [tsb("xo2%d" % i, [128, D_]) for i in range(1)]
        pS = [psg(sT, "pS%d" % i, [128, 512]) for i in range(3)]
        pAV = [psg(sT, "pAV%d" % i, [65, 512]) for i in range(2)]
        pbc = psg(sT, "pbc", [64, 512])
        pO = [psg(sT, "pO%d" % i, [128, 512]) for i in range(2)]
        b.dma("sync", "cT", bo_bc[:], I["bo"].broadcast_to([128, D_]),
              writes=["bo_bc"], bulk=True)

        for ci, qw in enumerate(CHW):
            q0 = ci * 512
            for pr in range(16):
                pair = (2 * pr, 2 * pr + 1)

                def _score(kt, j, si):
                    h = SLOT2HEAD[si]
                    kvh = h // 4
                    qrhs = fmv(Q_fm, TOK)[:, si // 2, q0:q0 + qw][
                        (si % 2) * 64:(si % 2) * 64 + 64, :]
                    ii = 2 * kt + j
                    sp, spk = pS[ii % 3], "pS%d" % (ii % 3)
                    klhs = fmv(K_fm, T_)[:, kvh // 2, kt * 128:(kt + 1) * 128][
                        (kvh % 2) * 64:(kvh % 2) * 64 + 64, :]
                    b.mm(sp[:, 0:qw], klhs, qrhs, True, True,
                         reads=["K_fm", "Q_fm"], writes=[spk])
                    pb_, pbk = ptl[ii % 6], "ptl%d" % (ii % 6)
                    b.act(pb_[:, 0:qw], sp[:, 0:qw], AF.Exp, bias=negB[:, 0:1],
                          scale=0.125, reads=[spk, "negB"], writes=[pbk])

                def _av(kt, j, si):
                    h = SLOT2HEAD[si]
                    kvh = h // 4
                    ii = 2 * kt + j
                    pb_, pbk = ptl[ii % 6], "ptl%d" % (ii % 6)
                    b.mm(pAV[j][:, 0:qw],
                         V_tm[:, kt * 520 + kvh * 65:kt * 520 + (kvh + 1) * 65],
                         pb_[:, 0:qw], kt == 0, kt == NTB - 1,
                         reads=["V_tm", pbk], writes=["pAV%d" % j])

                for j, si in enumerate(pair):
                    _score(0, j, si)
                for kt in range(NTB):
                    for j, si in enumerate(pair):
                        if kt + 1 < NTB:
                            _score(kt + 1, j, si)
                        _av(kt, j, si)
                for j, si in enumerate(pair):
                    av, avk = pAV[j], "pAV%d" % j
                    b.vrecip(rdrow[0:1, 0:qw], av[64:65, 0:qw], reads=[avk], writes=["rdrow"])
                    b.mm(pbc[:, 0:qw], onesr[0:1, 0:64], rdrow[0:1, 0:qw], True, True,
                         reads=["onesr", "rdrow"], writes=["pbc"])
                    b.vcopy(rbc[:, 0:qw], pbc[:, 0:qw], reads=["pbc"], writes=["rbc"])
                    dst = fmv(attnfm, 512)[:, si // 2, 0:qw][(si % 2) * 64:(si % 2) * 64 + 64, :]
                    b.tt(dst, av[0:64, 0:qw], rbc[:, 0:qw], ALU.mult,
                         reads=[avk, "rbc"], writes=["attnfm"])
            for ncc in range(4):
                wb, wk = wos[0], "wos0"
                b.dma("gpsimd", wk, d3(wb[:], 512), r3(I["wo_t"][:, ncc * 512:(ncc + 1) * 512]),
                      writes=[wk])
                for it in range(qw // 128):
                    pp, ppk = pO[it % 2], "pO%d" % (it % 2)
                    for fc in range(NF):
                        b.mm(pp[:], fmv(attnfm, 512)[:, fc, it * 128:(it + 1) * 128],
                             wb[:, fc * 512:(fc + 1) * 512], fc == 0, fc == NF - 1,
                             reads=["attnfm", wk], writes=[ppk])
                    b.tt(sO[it][:, ncc * 512:(ncc + 1) * 512], pp[:],
                         bo_bc[:, ncc * 512:(ncc + 1) * 512], ALU.add,
                         reads=[ppk, "bo_bc"], writes=["sO%d" % it])
            for it in range(qw // 128):
                ot = ci * 4 + it
                b.dma("sync", "gld0", gld[0][:], gdram[ot * 128:(ot + 1) * 128, :],
                      reads=["gd%d" % ot], writes=["gld0"])
                b.dma("sync", "xo20", xo2[0][:], I["xc"][ot * 128:(ot + 1) * 128, :],
                      writes=["xo20"])
                b.vcopy(gf32[0][:], gld[0][:], reads=["gld0"], writes=["gf320"])
                b.tt(sO[it][:], sO[it][:], gf32[0][:], ALU.mult,
                     reads=["sO%d" % it, "gf320"], writes=["sO%d" % it])
                b.tt(sO[it][:], sO[it][:], xo2[0][:], ALU.add,
                     reads=["sO%d" % it, "xo20"], writes=["sO%d" % it])
                b.dma("sync", "st%d" % it, x1[ot * 128:(ot + 1) * 128, :], sO[it][:],
                      reads=["sO%d" % it], writes=[])



def build_l1(debug=()):
    nc = bass.Bass("TRN2", target_bir_lowering=False, debug=False)
    I = {}

    def di(name, shape, dt=F32):
        I[name] = nc.dram_tensor(name, list(shape), dt, kind="ExternalInput").ap()

    di("x1c", (TOK, D_))
    di("qkv_t", (D_, 3072)); di("qkv_b", (1, 3072))
    di("out_t", (D_, D_)); di("out_b", (1, D_)); di("sinks", (1, 32))
    di("gw_t", (D_, 32)); di("gb", (1, 32))
    di("fc1_t", (D_, 1024)); di("fc2_t", (1024, D_))
    di("w1_t", (32, 1024, 1024)); di("w2_t", (32, 1024, 1024))
    di("su_t", (D_, 4096)); di("sd_t", (4096, D_))
    di("cos1", (TOK, 32)); di("sin1", (TOK, 32))
    di("masks", (2, 5, 128, 512), BF16)
    di("iden_bf", (128, 128), BF16); di("ones128", (1, 128))
    di("sel32", (32, 32 * 128))

    yc = nc.dram_tensor("yc", [OWN, D_], F32, kind="ExternalOutput").ap()
    DBG = {}

    def dbg(name, shape=None, dt=F32):
        if name in debug and shape is not None and name not in DBG:
            DBG[name] = nc.dram_tensor("dbg_" + name, list(shape), dt,
                                       kind="ExternalOutput").ap()
        return DBG.get(name)

    b = Prog(nc)
    with ExitStack() as ctx:
        emit_l1(b, I, yc, dbg, nc, ctx)
    return b.build(), sorted(DBG)


def emit_l1(b, I, yc, dbg, nc, ctx):
    def sb(name, shape, dt=F32):
        return ctx.enter_context(nc.sbuf_tensor(name, list(shape), dt))

    def psg(st, name, shape, dt=F32):
        return st.enter_context(nc.psum_tensor(name, list(shape), dt))

    C = {}
    K_fm = sb("K1_fm", [128, 4 * TOK], BF16)
    V_tm = sb("V1_tm", [128, NT * 520], BF16)
    maskb = sb("maskb", [128, 2 * 5 * 512], BF16)
    m_own = sb("m1_own", [128, NT]); r_own = sb("r1_own", [128, NT])
    q2acc = sb("q2acc", [128, 32]); k2acc = sb("k2acc", [128, 8])
    negB = sb("negB", [128, 1])
    es = sb("es", [1, 32], F32)
    sink_s = sb("sink_s", [1, 32], F32)
    C["idb"] = sb("idb", [128, 128], BF16)
    onesr = sb("onesr", [1, 128])
    g1 = sb("g1", [128, 8 * D_], BF16)
    cwT = sb("cwT", [32, OWN], BF16)

    b.dma("sync", "c0L1", C["idb"][:], I["iden_bf"], writes=["idb"], bulk=True)
    b.dma("sync", "c0L1", onesr[:], I["ones128"], writes=["onesr"], bulk=True)
    b.dma("sync", "c0L1", sink_s[:], I["sinks"], writes=["sink_s"], bulk=True)
    b.dma("sync", "c0L1", maskb[:].rearrange("p (q m c) -> p q m c", m=5, c=512),
          I["masks"].rearrange("q m p c -> p q m c"), writes=["maskb"], bulk=True)
    b.vmemset(q2acc[:], 0.0, writes=["q2acc"])
    b.vmemset(k2acc[:], 0.0, writes=["k2acc"])

    with ExitStack() as sAQ:
        xn_fm = sAQ.enter_context(nc.sbuf_tensor("xn1_fm", [128, NF * TOK], BF16))
        cosc = sAQ.enter_context(nc.sbuf_tensor("cosc", [128, NT * 32], F32))
        sinc = sAQ.enter_context(nc.sbuf_tensor("sinc", [128, NT * 32], F32))
        b.dma("sync", "c0L1", d3(cosc[:], 32), r3(I["cos1"]), writes=["cosc"], bulk=True)
        b.dma("sync", "c0L1", d3(sinc[:], 32), r3(I["sin1"]), writes=["sinc"], bulk=True)

        with ExitStack() as sA:
            xt = [sA.enter_context(nc.sbuf_tensor("xt%d" % i, [128, D_], F32))
                  for i in range(2)]
            C["lnscr"] = sA.enter_context(nc.sbuf_tensor("lnscr", [128, D_], F32))
            xnbf = sA.enter_context(nc.sbuf_tensor("xnbf0", [128, D_], BF16))
            stat = [sA.enter_context(nc.sbuf_tensor("stat%d" % i, [128, 4], F32))
                    for i in range(2)]
            C["pt0"] = psg(sA, "pt0", [128, 1024], BF16)
            C["pt1"] = psg(sA, "pt1", [128, 1024], BF16)
            for ot in range(NT):
                bi = ot % 2
                xtb, xk = xt[bi], "xt%d" % bi
                b.dma("sync", xk, xtb[:], I["x1c"][ot * 128:(ot + 1) * 128, :],
                      writes=[xk])
                emit_ln_tile(b, C, xtb[:], xk, stat[bi], "stat%d" % bi,
                             xnbf[:], "xnbf0", 1e-5,
                             mcol=m_own[:, ot:ot + 1], rcol=r_own[:, ot:ot + 1])
                xv = fmv(xn_fm, TOK)
                emit_transpose16(
                    b, C, xnbf[:], "xnbf0",
                    (lambda g, n, ot_=ot: xv[:, g * 8:g * 8 + n, ot_ * 128:(ot_ + 1) * 128]),
                    "xn_fm")
        b.fence()

        with ExitStack() as sKV:
            wkv = sKV.enter_context(nc.sbuf_tensor("wkv1", [128, NF * 1024], BF16))
            kb_bc = sKV.enter_context(nc.sbuf_tensor("kb_bc", [128, 512], F32))
            vb_bc = sKV.enter_context(nc.sbuf_tensor("vb_bc", [128, 512], F32))
            kvscr = sKV.enter_context(nc.sbuf_tensor("kvscr0", [128, 512], F32))
            kvtmp = sKV.enter_context(nc.sbuf_tensor("kvtmp0", [128, 512], F32))
            krs = [sKV.enter_context(nc.sbuf_tensor("krs%d" % i, [128, 512], BF16))
                   for i in range(2)]
            pkv = [psg(sKV, "pkv%d" % i, [128, 512]) for i in range(2)]
            C["pt0"] = psg(sKV, "pt0k", [128, 1024], BF16)
            C["pt1"] = psg(sKV, "pt1k", [128, 1024], BF16)
            b.dma("gpsimd", "wkv1", d3(wkv[:, 0:NF * 512], 512),
                  r3(I["qkv_t"][:, 2048:2560]), writes=["wkv_k"], bulk=True)
            b.dma("gpsimd", "wkv1", d3(wkv[:, NF * 512:], 512),
                  r3(I["qkv_t"][:, 2560:3072]), writes=["wkv_v"], bulk=True)
            b.dma("sync", "cA1", kb_bc[:], I["qkv_b"][:, 2048:2560].broadcast_to([128, 512]),
                  writes=["kb_bc"], bulk=True)
            b.dma("sync", "cA1", vb_bc[:], I["qkv_b"][:, 2560:3072].broadcast_to([128, 512]),
                  writes=["vb_bc"], bulk=True)
            for ot in range(NT):
                bi = ot % 2
                pk, pkk = pkv[bi], "pkv%d" % bi
                for fc in range(NF):
                    b.mm(pk[:], fmv(xn_fm, TOK)[:, fc, ot * 128:(ot + 1) * 128],
                         wkv[:, fc * 512:(fc + 1) * 512], fc == 0, fc == NF - 1,
                         reads=["xn_fm", "wkv_k"], writes=[pkk])
                b.tt(kvscr[:], pk[:], kb_bc[:], ALU.add, reads=[pkk, "kb_bc"],
                     writes=["kvscr0"])
                emit_rope(b, C, kvscr[:], "kvscr0", krs[bi][:], "krs%d" % bi,
                          cosc[:, ot * 32:(ot + 1) * 32], sinc[:, ot * 32:(ot + 1) * 32],
                          8, kvtmp[:], "kvtmp0", ckeys=("cosc", "sinc"))
                pt, ptk = C["pt%d" % bi], "pt%d" % bi
                for k in range(4):
                    b.trans(pt[:, k * 128:(k + 1) * 128], krs[bi][:, k * 128:(k + 1) * 128],
                            C["idb"][:], reads=["krs%d" % bi, "idb"], writes=[ptk])
                b.vcopy(fmv(K_fm, TOK)[:, 0:4, ot * 128:(ot + 1) * 128],
                        pt[:, 0:512].rearrange("p (f c) -> p f c", c=128),
                        reads=[ptk], writes=["K_fm"])
                b.tt(kvtmp[:], krs[bi][:], krs[bi][:], ALU.mult,
                     reads=["krs%d" % bi], writes=["kvtmp0"])
                b.red(kvscr[:, 0:8], kvtmp[:].rearrange("p (h d) -> p h d", d=64),
                      aluop=ALU.max, reads=["kvtmp0"], writes=["kvscr0"])
                b.tt(k2acc[:], k2acc[:], kvscr[:, 0:8], ALU.max,
                     reads=["kvscr0", "k2acc"], writes=["k2acc"])
                for fc in range(NF):
                    b.mm(pk[:], fmv(xn_fm, TOK)[:, fc, ot * 128:(ot + 1) * 128],
                         wkv[:, NF * 512 + fc * 512:NF * 512 + (fc + 1) * 512],
                         fc == 0, fc == NF - 1,
                         reads=["xn_fm", "wkv_v"], writes=[pkk])
                vt = V_tm[:, ot * 520:(ot + 1) * 520].rearrange("p (h c) -> p h c", c=65)
                b.tt(vt[:, :, 0:64], pk[:].rearrange("p (h c) -> p h c", c=64),
                     vb_bc[:].rearrange("p (h c) -> p h c", c=64), ALU.add,
                     reads=[pkk, "vb_bc"], writes=["V_tm"])
                b.vmemset(vt[:, :, 64:65], 1.0, writes=["V_tm"])
        b.fence()

        with ExitStack() as sR:
            gw_b = sR.enter_context(nc.sbuf_tensor("gw_b", [128, NF * 32], BF16))
            gw_f = sR.enter_context(nc.sbuf_tensor("gw_f", [128, NF * 32], F32))
            gb_bc = sR.enter_context(nc.sbuf_tensor("gb_bc", [128, 32], F32))
            rscr = sR.enter_context(nc.sbuf_tensor("rscr", [128, 192], F32))
            rgb = sR.enter_context(nc.sbuf_tensor("rgb", [128, 32], BF16))
            rpsum_t = psg(sR, "rpsum", [128, 512])
            rpsum = rpsum_t[:, 0:32]
            rtp_t = psg(sR, "rtp", [128, 1024], BF16)
            rtp = rtp_t[0:32, 0:128]
            b.dma("sync", "cRg1", d3(gw_f[:], 32), r3(I["gw_t"]), writes=["gw_f"], bulk=True)
            b.vcopy(gw_b[:], gw_f[:], reads=["gw_f"], writes=["gw_b"])
            b.dma("sync", "cR1", gb_bc[:], I["gb"].broadcast_to([128, 32]),
                  writes=["gb_bc"], bulk=True)
            for ot in range(8):
                lt = ot + 1
                for fc in range(NF):
                    b.mm(rpsum, fmv(xn_fm, TOK)[:, fc, lt * 128:(lt + 1) * 128],
                         gw_b[:, fc * 32:(fc + 1) * 32], fc == 0, fc == NF - 1,
                         reads=["xn_fm", "gw_b"], writes=["rpsum"])
                lk = "rscr"
                lg = rscr[:, 0:32]
                sc_ = rscr[:, 32:64]
                sbv = rscr[:, 64:96]
                tq = rscr[:, 96:128]
                gs = rscr[:, 128:132]
                st1 = rscr[:, 132:133]; st2 = rscr[:, 133:134]
                thg = rscr[:, 134:135]; th8 = rscr[:, 135:136]
                ssum = rscr[:, 136:137]; rs = rscr[:, 137:138]
                msk = rscr[:, 128 + 16:128 + 16]
                b.vcopy(lg, rpsum, reads=["rpsum"], writes=[lk])
                b.act(sc_, lg, AF.Exp, scale=-1.0, reads=[lk], writes=[lk])
                b.ts(sc_, sc_, 1.0, None, ALU.add, reads=[lk], writes=[lk])
                b.vrecip(sc_, sc_, reads=[lk], writes=[lk])
                b.tt(sbv, sc_, gb_bc[:], ALU.add, reads=[lk, "gb_bc"], writes=[lk])
                for gi in range(4):
                    sl = sbv[:, gi * 8:(gi + 1) * 8]
                    b.red(st1, sl, aluop=ALU.max, reads=[lk], writes=[lk])
                    b.ts(tq[:, 0:8], sl, st1, -1e30, ALU.is_equal, ALU.mult,
                         reads=[lk], writes=[lk])
                    b.tt(tq[:, 0:8], tq[:, 0:8], sl, ALU.add, reads=[lk], writes=[lk])
                    b.red(st2, tq[:, 0:8], aluop=ALU.max, reads=[lk], writes=[lk])
                    b.tt(gs[:, gi:gi + 1], st1, st2, ALU.add, reads=[lk], writes=[lk])
                b.red(st1, gs, aluop=ALU.max, reads=[lk], writes=[lk])
                b.ts(tq[:, 0:4], gs, st1, -1e30, ALU.is_equal, ALU.mult,
                     reads=[lk], writes=[lk])
                b.tt(tq[:, 0:4], tq[:, 0:4], gs, ALU.add, reads=[lk], writes=[lk])
                b.red(thg, tq[:, 0:4], aluop=ALU.max, reads=[lk], writes=[lk])
                b.ts(gs, gs, thg, None, ALU.is_ge, reads=[lk], writes=[lk])
                gm_exp = gs.rearrange("p (g o) -> p g o", o=1).broadcast_to([128, 4, 8])
                sb3 = sbv.rearrange("p (g e) -> p g e", e=8)
                tq3 = tq.rearrange("p (g e) -> p g e", e=8)
                b.tt(tq3, sb3, gm_exp, ALU.mult, reads=[lk], writes=[lk])
                b.ts(lg, gs.rearrange("p (g o) -> p g o", o=1).broadcast_to([128, 4, 8]),
                     1.0, -1e30, ALU.subtract, ALU.mult, reads=[lk], writes=[lk])
                b.tt(tq, tq, lg, ALU.subtract, reads=[lk], writes=[lk])
                mcp = rscr[:, 96 + 32:96 + 32] if False else None
                b.vcopy(lg, tq, reads=[lk], writes=[lk])
                for rnd in range(8):
                    b.red(th8, tq, aluop=ALU.max, reads=[lk], writes=[lk])
                    if rnd < 7:
                        b.ts(rscr[:, 160:192], tq, th8, -1e30,
                             ALU.is_equal, ALU.mult, reads=[lk], writes=[lk])
                        b.tt(tq, tq, rscr[:, 160:192], ALU.add,
                             reads=[lk], writes=[lk])
                b.ts(lg, lg, th8, None, ALU.is_ge, reads=[lk], writes=[lk])
                b.tt(lg, lg, sc_, ALU.mult, reads=[lk], writes=[lk])
                b.red(ssum, lg, reads=[lk], writes=[lk])
                b.vrecip(rs, ssum, reads=[lk], writes=[lk])
                b.ts(lg, lg, rs, 2.5, ALU.mult, ALU.mult, reads=[lk], writes=[lk])
                b.vcopy(rgb[:], lg, reads=[lk], writes=["rgb"])
                b.trans(rtp, rgb[:], C["idb"][:], reads=["rgb", "idb"], writes=["rtp"])
                b.vcopy(cwT[:, ot * 128:(ot + 1) * 128], rtp, reads=["rtp"], writes=["cwT"])
        b.fence()

        TCH = 256
        NCH = OWN // TCH
        with ExitStack() as sF:
            xlat = sF.enter_context(nc.sbuf_tensor("xlat", [128, 8 * OWN], BF16))
            wstr = [sF.enter_context(nc.sbuf_tensor("wstr%d" % i, [128, 16 * 128], BF16))
                    for i in range(2)]
            wfc1 = [sF.enter_context(nc.sbuf_tensor("wfc1%d" % i, [128, NF * 128], BF16))
                    for i in range(2)]
            hw = sF.enter_context(nc.sbuf_tensor("hw", [128, 8 * TCH], BF16))
            rt = sF.enter_context(nc.sbuf_tensor("rt", [128, TCH], F32))
            rt2 = sF.enter_context(nc.sbuf_tensor("rt2", [128, TCH], F32))
            cwb = sF.enter_context(nc.sbuf_tensor("cwb", [128, TCH], F32))
            sel32b = sF.enter_context(nc.sbuf_tensor("sel32b", [32, 32 * 128], BF16))
            ylat_bf = sF.enter_context(nc.sbuf_tensor("ylat_bf", [128, 8 * TCH], BF16))
            upfm = sF.enter_context(nc.sbuf_tensor("upfm", [128, 32 * TCH], BF16))
            wdn = [sF.enter_context(nc.sbuf_tensor("wdn%d" % i, [128, 32 * 256], BF16))
                   for i in range(2)]
            wf2 = [sF.enter_context(nc.sbuf_tensor("wf20", [128, 8 * 256], BF16))]
            pylat = psg(sF, "pylat", [128, 2048])
            ppre = [psg(sF, "ppre%d" % i, [128, 512]) for i in range(2)]
            pcw = psg(sF, "pcw", [128, 512])
            pdn = ppre
            b.dma("gpsimd", "cF", sel32b[:], I["sel32"], writes=["sel32b"], bulk=True)

            for lt in range(8):
                wi = lt % 2
                wb, wk = wfc1[wi], "wfc1%d" % wi
                b.dma("gpsimd", wk, d3(wb[:], 128),
                      r3(I["fc1_t"][:, lt * 128:(lt + 1) * 128]), writes=[wk])
                for tc in range(2):
                    pp, ppk = ppre[tc % 2], "ppre%d" % (tc % 2)
                    for fc in range(NF):
                        b.mm(pp[:], wb[:, fc * 128:(fc + 1) * 128],
                             fmv(xn_fm, TOK)[:, fc, 128 + tc * 512:128 + (tc + 1) * 512],
                             fc == 0, fc == NF - 1, reads=[wk, "xn_fm"], writes=[ppk])
                    b.vcopy(fmv(xlat, OWN)[:, lt, tc * 512:(tc + 1) * 512][0:128, :],
                            pp[:], reads=[ppk], writes=["xlat"])

            if dbg("xlatd", (128, 8 * OWN), BF16):
                b.dma("sync", "dbgXL", dbg("xlatd"), xlat[:],
                      reads=["xlat"], writes=[])
            for ci in range(NCH):
                t0 = ci * TCH
                for e in range(32):
                    b.mm(pcw[:, 0:TCH], sel32b[:, e * 128:(e + 1) * 128],
                         cwT[:, t0:t0 + TCH], True, True,
                         reads=["sel32b", "cwT"], writes=["pcw"])
                    b.vcopy(cwb[:], pcw[:, 0:TCH], reads=["pcw"], writes=["cwb"])
                    for ig in range(4):
                        wi = ig % 2
                        wb, wk = wstr[wi], "wstr%d" % wi
                        b.dma("gpsimd", wk, d3(wb[:, 0:8 * 256], 256),
                              r3(I["w1_t"][e, :, ig * 256:(ig + 1) * 256]), writes=[wk])
                        for il in range(2):
                            it_ = ig * 2 + il
                            pp, ppk = ppre[it_ % 2], "ppre%d" % (it_ % 2)
                            for lc in range(8):
                                b.mm(pp[:, 0:TCH],
                                     wb[:, lc * 256 + il * 128:lc * 256 + (il + 1) * 128],
                                     fmv(xlat, OWN)[:, lc, t0:t0 + TCH],
                                     lc == 0, lc == 7, reads=[wk, "xlat"], writes=[ppk])
                            b.op("scalar",
                                 (lambda s, o=rt[:, 0:TCH], i=pp[:, 0:TCH]:
                                  s.activation(o, i, AF.Relu)),
                                 reads=[ppk], writes=["rt"])
                            b.tt(rt2[:, 0:TCH], rt[:, 0:TCH], rt[:, 0:TCH], ALU.mult,
                                 reads=["rt"], writes=["rt2"])
                            b.tt(hw[:, it_ * TCH:(it_ + 1) * TCH], rt2[:, 0:TCH],
                                 cwb[:], ALU.mult, reads=["rt2", "cwb"], writes=["hw"])
                    for lg in range(4):
                        wi = lg % 2
                        wb, wk = wstr[wi], "wstr%d" % wi
                        b.dma("gpsimd", wk, d3(wb[:, 0:8 * 256], 256),
                              r3(I["w2_t"][e, :, lg * 256:(lg + 1) * 256]), writes=[wk])
                        for ll in range(2):
                            lt = lg * 2 + ll
                            for icc in range(8):
                                b.mm(pylat[:, lt * TCH:(lt + 1) * TCH],
                                     wb[:, icc * 256 + ll * 128:icc * 256 + (ll + 1) * 128],
                                     hw[:, icc * TCH:(icc + 1) * TCH],
                                     (e == 0 and icc == 0 and lt % 2 == 0),
                                     (e == 31 and icc == 7),
                                     reads=[wk, "hw"], writes=["pylat"])
                for lt in range(8):
                    b.vcopy(ylat_bf[:, lt * TCH:(lt + 1) * TCH],
                            pylat[:, lt * TCH:(lt + 1) * TCH],
                            reads=["pylat"], writes=["ylat_bf"])
                if ci == 0 and dbg("ylatd", (128, 8 * 256), BF16):
                    b.dma("sync", "dbgYL", dbg("ylatd"), ylat_bf[:],
                          reads=["ylat_bf"], writes=[])
                for st_ in range(32):
                    wi = st_ % 2
                    wb, wk = wstr[wi], "wstr%d" % wi
                    b.dma("gpsimd", wk, d3(wb[:], 128),
                          r3(I["su_t"][:, st_ * 128:(st_ + 1) * 128]), writes=[wk])
                    pp, ppk = ppre[st_ % 2], "ppre%d" % (st_ % 2)
                    for fc in range(NF):
                        b.mm(pp[:, 0:TCH], wb[:, fc * 128:(fc + 1) * 128],
                             fmv(xn_fm, TOK)[:, fc, 128 + t0:128 + t0 + TCH],
                             fc == 0, fc == NF - 1, reads=[wk, "xn_fm"], writes=[ppk])
                    b.op("scalar",
                         (lambda s, o=rt[:, 0:TCH], i=pp[:, 0:TCH]:
                          s.activation(o, i, AF.Relu)),
                         reads=[ppk], writes=["rt"])
                    b.tt(upfm[:, st_ * TCH:(st_ + 1) * TCH], rt[:, 0:TCH],
                         rt[:, 0:TCH], ALU.mult, reads=["rt"], writes=["upfm"])
                if ci == 0 and dbg("upfmd", (128, 32 * 256), BF16):
                    b.dma("sync", "dbgUP", dbg("upfmd"), upfm[:],
                          reads=["upfm"], writes=[])
                for nc8 in range(8):
                    wi = nc8 % 2
                    wdb, wdk = wdn[wi], "wdn%d" % wi
                    wfb, wfk = wf2[0], "wf20"
                    b.dma("gpsimd", wdk, d3(wdb[:], 256),
                          r3(I["sd_t"][:, nc8 * 256:(nc8 + 1) * 256]),
                          writes=[wdk])
                    b.dma("gpsimd", wfk, d3(wfb[:], 256),
                          r3(I["fc2_t"][:, nc8 * 256:(nc8 + 1) * 256]),
                          writes=[wfk])
                    for it_ in range(TCH // 128):
                        ot = (t0 + it_ * 128) // 128
                        pp, ppk = pdn[it_ % 2], "ppre%d" % (it_ % 2)
                        for sc_ in range(32):
                            b.mm(pp[:, 0:256], upfm[:, sc_ * TCH + it_ * 128:
                                                    sc_ * TCH + (it_ + 1) * 128],
                                 wdb[:, sc_ * 256:(sc_ + 1) * 256],
                                 sc_ == 0, False, reads=["upfm", wdk], writes=[ppk])
                        for lc in range(8):
                            b.mm(pp[:, 0:256], ylat_bf[:, lc * TCH + it_ * 128:
                                                       lc * TCH + (it_ + 1) * 128],
                                 wfb[:, lc * 256:(lc + 1) * 256],
                                 False, lc == 7, reads=["ylat_bf", wfk], writes=[ppk])
                        b.op("scalar",
                             (lambda s, o=g1[:, ot * D_ + nc8 * 256:
                                             ot * D_ + (nc8 + 1) * 256],
                              i=pp[:, 0:256]: s.activation(o, i, AF.Silu)),
                             reads=[ppk], writes=["g1"])
        if dbg("g1d", (128, 8 * D_)):
            b.dma("gpsimd", "dbgG1", dbg("g1d"), g1[:], reads=["g1"], writes=[])
        if dbg("cwTd", (32, OWN)):
            b.dma("gpsimd", "dbgCW", dbg("cwTd"), cwT[:], reads=["cwT"], writes=[])
        b.fence()

        Q_fm = ctx.enter_context(nc.sbuf_tensor("Q1_fm", [128, NF * OWN], BF16,
                                                side="right"))
        with ExitStack() as sQ:
            wqs = [sQ.enter_context(nc.sbuf_tensor("wqs%d" % i, [128, NF * 512], BF16))
                   for i in range(2)]
            qscr = [sQ.enter_context(nc.sbuf_tensor("qscr%d" % i, [128, 512], F32))
                    for i in range(2)]
            qtmp = [sQ.enter_context(nc.sbuf_tensor("qtmp%d" % i, [128, 512], F32))
                    for i in range(2)]
            qrb = [sQ.enter_context(nc.sbuf_tensor("qrb%d" % i, [128, 512], BF16))
                   for i in range(2)]
            qb_bc = sQ.enter_context(nc.sbuf_tensor("qb_bc", [128, D_], F32))
            ppq = [psg(sQ, "ppq%d" % i, [128, 512]) for i in range(2)]
            C["pt0"] = psg(sQ, "pt0b", [128, 1024], BF16)
            C["pt1"] = psg(sQ, "pt1b", [128, 1024], BF16)
            b.dma("sync", "cQ1", qb_bc[:], I["qkv_b"][:, 0:2048].broadcast_to([128, D_]),
                  writes=["qb_bc"], bulk=True)
            for ncc in range(4):
                wi = ncc % 2
                wb, wk = wqs[wi], "wqs%d" % wi
                b.dma("gpsimd", wk, d3(wb[:], 512),
                      r3(I["qkv_t"][:, ncc * 512:(ncc + 1) * 512]), writes=[wk])
                for ot in range(8):
                    lt = ot + 1
                    bi = ot % 2
                    pp, ppk = ppq[bi], "ppq%d" % bi
                    for fc in range(NF):
                        b.mm(pp[:], fmv(xn_fm, TOK)[:, fc, lt * 128:(lt + 1) * 128],
                             wb[:, fc * 512:(fc + 1) * 512], fc == 0, fc == NF - 1,
                             reads=["xn_fm", wk], writes=[ppk])
                    b.tt(qscr[bi][:], pp[:], qb_bc[:, ncc * 512:(ncc + 1) * 512], ALU.add,
                         reads=[ppk, "qb_bc"], writes=["qscr%d" % bi])
                    emit_rope(b, C, qscr[bi][:], "qscr%d" % bi, qrb[bi][:], "qrb%d" % bi,
                              cosc[:, lt * 32:(lt + 1) * 32], sinc[:, lt * 32:(lt + 1) * 32],
                              8, qtmp[bi][:], "qtmp%d" % bi, ckeys=("cosc", "sinc"))
                    b.tt(qtmp[bi][:], qrb[bi][:], qrb[bi][:], ALU.mult,
                         reads=["qrb%d" % bi], writes=["qtmp%d" % bi])
                    b.red(qscr[bi][:, 0:8], qtmp[bi][:].rearrange("p (h d) -> p h d", d=64),
                          aluop=ALU.max, reads=["qtmp%d" % bi], writes=["qscr%d" % bi])
                    b.tt(q2acc[:, ncc * 8:(ncc + 1) * 8], q2acc[:, ncc * 8:(ncc + 1) * 8],
                         qscr[bi][:, 0:8], ALU.max,
                         reads=["qscr%d" % bi, "q2acc"], writes=["q2acc"])
                    pt, ptk = C["pt%d" % bi], "pt%d" % bi
                    for k in range(4):
                        b.trans(pt[:, k * 128:(k + 1) * 128], qrb[bi][:, k * 128:(k + 1) * 128],
                                C["idb"][:], reads=["qrb%d" % bi, "idb"], writes=[ptk])
                    b.vcopy(fmv(Q_fm, OWN)[:, ncc * 4:(ncc + 1) * 4, ot * 128:(ot + 1) * 128],
                            pt[:, 0:512].rearrange("p (f c) -> p f c", c=128),
                            reads=[ptk], writes=["Q_fm"])
        b.fence()

    with ExitStack() as sB:
        bscr = sB.enter_context(nc.sbuf_tensor("bscr", [128, 8], F32))
        brow = sB.enter_context(nc.sbuf_tensor("brow", [1, 128], F32))
        bsc2 = sB.enter_context(nc.sbuf_tensor("bsc2", [1, 4], F32))
        pnb_t = psg(sB, "pnb", [128, 512])
        pnb = pnb_t[:, 0:1]
        b.red(bscr[:, 0:1], q2acc[:], aluop=ALU.max, reads=["q2acc"], writes=["bscr"])
        emit_global_max(b, nc, ctx, "q1", bscr[:, 0:1], "bscr", bsc2[0:1, 0:1], "bsc2",
                        brow[0:1, :], "brow")
        b.red(bscr[:, 1:2], k2acc[:], aluop=ALU.max, reads=["k2acc"], writes=["bscr"])
        emit_global_max(b, nc, ctx, "k1", bscr[:, 1:2], "bscr", bsc2[0:1, 1:2], "bsc2",
                        brow[0:1, :], "brow")
        b.tt(bsc2[0:1, 2:3], bsc2[0:1, 0:1], bsc2[0:1, 1:2], ALU.mult,
             reads=["bsc2"], writes=["bsc2"])
        b.act(bsc2[0:1, 2:3], bsc2[0:1, 2:3], AF.Ln, reads=["bsc2"], writes=["bsc2"])
        b.act(bsc2[0:1, 3:4], bsc2[0:1, 2:3], AF.Exp, scale=0.5, reads=["bsc2"], writes=["bsc2"])
        b.ts(bsc2[0:1, 3:4], bsc2[0:1, 3:4], -0.125, None, ALU.mult,
             reads=["bsc2"], writes=["bsc2"])
        b.mm(pnb, onesr[:], bsc2[0:1, 3:4], True, True,
             reads=["onesr", "bsc2"], writes=["pnb"])
        b.vcopy(negB[:], pnb, reads=["pnb"], writes=["negB"])
        b.act(es[:], sink_s[:], AF.Exp, bias=negB[0:1, 0:1],
              reads=["sink_s", "negB"], writes=["es"])
    b.fence()

    with ExitStack() as sT:
        def tsb(name, shape, dt=F32):
            return sT.enter_context(nc.sbuf_tensor(name, list(shape), dt))

        attnfm = tsb("attnfm", [128, NF * 512], BF16)
        ptl = [tsb("ptl%d" % i, [128, 512], BF16) for i in range(6)]
        rdrow = tsb("rdrow", [1, 512])
        rbc = tsb("rbc", [64, 512])
        wos = [tsb("wos%d" % i, [128, NF * 512], BF16) for i in range(1)]
        ob_bc = tsb("ob_bc", [128, D_])
        sO = [tsb("sO%d" % i, [128, D_]) for i in range(4)]
        gf32 = tsb("gf32", [128, D_])
        xo2 = tsb("xo2", [128, D_])
        pS = [psg(sT, "pS%d" % i, [128, 512]) for i in range(3)]
        pAV = [psg(sT, "pAV%d" % i, [65, 512]) for i in range(2)]
        pbc = psg(sT, "pbc", [64, 512])
        pO = [psg(sT, "pO%d" % i, [128, 512]) for i in range(2)]
        b.dma("sync", "cT1", ob_bc[:], I["out_b"].broadcast_to([128, D_]),
              writes=["ob_bc"], bulk=True)
        mk3 = maskb[:].rearrange("p (q m c) -> p q m c", m=5, c=512)

        for qc in range(2):
            for pr in range(16):
                pair = (2 * pr, 2 * pr + 1)

                def _score1(m, j, si):
                    h = SLOT2HEAD[si]
                    kvh = h // 4
                    kt = qc * 4 + m
                    qrhs = fmv(Q_fm, OWN)[:, si // 2, qc * 512:(qc + 1) * 512][
                        (si % 2) * 64:(si % 2) * 64 + 64, :]
                    ii = 2 * m + j
                    sp, spk = pS[ii % 3], "pS%d" % (ii % 3)
                    klhs = fmv(K_fm, TOK)[:, kvh // 2, kt * 128:(kt + 1) * 128][
                        (kvh % 2) * 64:(kvh % 2) * 64 + 64, :]
                    b.mm(sp[:], klhs, qrhs, True, True,
                         reads=["K_fm", "Q_fm"], writes=[spk])
                    pb_, pbk = ptl[ii % 6], "ptl%d" % (ii % 6)
                    b.act(pb_[:], sp[:], AF.Exp, bias=negB[:, 0:1], scale=0.125,
                          reads=[spk, "negB"], writes=[pbk])
                    b.tt(pb_[:], pb_[:], mk3[:, qc, m, :], ALU.mult,
                         reads=[pbk, "maskb"], writes=[pbk])

                def _av1(m, j, si):
                    h = SLOT2HEAD[si]
                    kvh = h // 4
                    kt = qc * 4 + m
                    ii = 2 * m + j
                    pb_, pbk = ptl[ii % 6], "ptl%d" % (ii % 6)
                    b.mm(pAV[j][:], V_tm[:, kt * 520 + kvh * 65:kt * 520 + (kvh + 1) * 65],
                         pb_[:], m == 0, m == 4, reads=["V_tm", pbk],
                         writes=["pAV%d" % j])

                for j, si in enumerate(pair):
                    _score1(0, j, si)
                for m in range(5):
                    for j, si in enumerate(pair):
                        if m + 1 < 5:
                            _score1(m + 1, j, si)
                        _av1(m, j, si)
                for j, si in enumerate(pair):
                    h = SLOT2HEAD[si]
                    av, avk = pAV[j], "pAV%d" % j
                    b.ts(rdrow[0:1, :], av[64:65, :], es[0:1, h:h + 1], None, ALU.add,
                         reads=[avk, "es"], writes=["rdrow"])
                    b.vrecip(rdrow[0:1, :], rdrow[0:1, :], reads=["rdrow"], writes=["rdrow"])
                    b.mm(pbc[:], onesr[0:1, 0:64], rdrow[0:1, :], True, True,
                         reads=["onesr", "rdrow"], writes=["pbc"])
                    b.vcopy(rbc[:], pbc[:], reads=["pbc"], writes=["rbc"])
                    dst = fmv(attnfm, 512)[:, si // 2, :][(si % 2) * 64:(si % 2) * 64 + 64, :]
                    b.tt(dst, av[0:64, :], rbc[:], ALU.mult, reads=[avk, "rbc"],
                         writes=["attnfm"])
            for ncc in range(4):
                wb, wk = wos[0], "wos0"
                b.dma("gpsimd", wk, d3(wb[:], 512),
                      r3(I["out_t"][:, ncc * 512:(ncc + 1) * 512]), writes=[wk])
                for it_ in range(4):
                    pp, ppk = pO[it_ % 2], "pO%d" % (it_ % 2)
                    for fc in range(NF):
                        b.mm(pp[:], fmv(attnfm, 512)[:, fc, it_ * 128:(it_ + 1) * 128],
                             wb[:, fc * 512:(fc + 1) * 512], fc == 0, fc == NF - 1,
                             reads=["attnfm", wk], writes=[ppk])
                    b.tt(sO[it_][:, ncc * 512:(ncc + 1) * 512], pp[:],
                         ob_bc[:, ncc * 512:(ncc + 1) * 512], ALU.add,
                         reads=[ppk, "ob_bc"], writes=["sO%d" % it_])
            for it_ in range(4):
                ot = qc * 4 + it_
                b.dma("sync", "xo2", xo2[:], I["x1c"][128 + ot * 128:128 + (ot + 1) * 128, :],
                      writes=["xo2"])
                b.vcopy(gf32[:], g1[:, ot * D_:(ot + 1) * D_], reads=["g1"], writes=["gf32"])
                b.tt(sO[it_][:], sO[it_][:], gf32[:], ALU.mult,
                     reads=["sO%d" % it_, "gf32"], writes=["sO%d" % it_])
                b.tt(sO[it_][:], sO[it_][:], xo2[:], ALU.add,
                     reads=["sO%d" % it_, "xo2"], writes=["sO%d" % it_])
                b.dma("sync", "st%d" % it_, yc[ot * 128:(ot + 1) * 128, :], sO[it_][:],
                      reads=["sO%d" % it_], writes=[])



def build_fused():
    nc = bass.Bass("TRN2", target_bir_lowering=False, debug=False)
    I0, I1 = {}, {}

    def di(d, name, shape, dt=F32):
        d[name] = nc.dram_tensor(name, list(shape), dt, kind="ExternalInput").ap()

    di(I0, "xb", (T_, D_)); di(I0, "xc", (TOK, D_))
    di(I0, "wq_t", (D_, D_)); di(I0, "wk_t", (D_, 512)); di(I0, "wv_t", (D_, 512))
    di(I0, "wo_t", (D_, D_))
    di(I0, "bq", (1, D_)); di(I0, "bk", (1, 512)); di(I0, "bv", (1, 512))
    di(I0, "bo", (1, D_))
    di(I0, "rw", (D_, 8)); di(I0, "rb", (1, 8)); di(I0, "ew", (8, D_, D_))
    di(I0, "eb_cols", (128, 128)); di(I0, "pw", (D_, D_)); di(I0, "pb", (1, D_))
    di(I0, "nw", (1, D_))
    di(I0, "cos0", (T_, 32)); di(I0, "sin0", (T_, 32))
    di(I0, "cos0c", (TOK, 32)); di(I0, "sin0c", (TOK, 32))
    di(I0, "iden_bf", (128, 128), BF16); di(I0, "iden_f32", (128, 128))
    di(I0, "ones128", (1, 128)); di(I0, "sel8", (8, 8 * 128))

    di(I1, "qkv_t", (D_, 3072)); di(I1, "qkv_b", (1, 3072))
    di(I1, "out_t", (D_, D_)); di(I1, "out_b", (1, D_)); di(I1, "sinks", (1, 32))
    di(I1, "gw_t", (D_, 32)); di(I1, "gb", (1, 32))
    di(I1, "fc1_t", (D_, 1024)); di(I1, "fc2_t", (1024, D_))
    di(I1, "w1_t", (32, 1024, 1024)); di(I1, "w2_t", (32, 1024, 1024))
    di(I1, "su_t", (D_, 4096)); di(I1, "sd_t", (4096, D_))
    di(I1, "cos1", (TOK, 32)); di(I1, "sin1", (TOK, 32))
    di(I1, "masks", (2, 5, 128, 512), BF16)
    di(I1, "sel32", (32, 32 * 128))
    I1["iden_bf"] = I0["iden_bf"]
    I1["ones128"] = I0["ones128"]

    x1 = nc.dram_tensor("x1", [TOK, D_], F32, kind="Internal").ap()
    gdram = nc.dram_tensor("gscr", [TOK, D_], BF16, kind="Internal").ap()
    yc = nc.dram_tensor("yc", [OWN, D_], F32, kind="ExternalOutput").ap()
    I1["x1c"] = x1

    nodbg = lambda name, shape=None, dt=F32: None

    class _NCProxy:

        def __init__(self, real):
            self._nc = real

        def __getattr__(self, k):
            return getattr(self._nc, k)

        def sbuf_tensor(self, name, shape, dt, **kw):
            kw.setdefault("allow_name_mangling", True)
            return self._nc.sbuf_tensor(name, shape, dt, **kw)

        def psum_tensor(self, name, shape, dt=F32, **kw):
            kw.setdefault("allow_name_mangling", True)
            return self._nc.psum_tensor(name, shape, dt, **kw)

    ncp = _NCProxy(nc)
    b = Prog(nc)
    with ExitStack() as ctx0:
        emit_l0(b, I0, x1, gdram, nodbg, ncp, ctx0)
    b.fence()
    with ExitStack() as ctx1:
        emit_l1(b, I1, yc, nodbg, ncp, ctx1)
    return b.build()



def kernel(**inputs):
    maps0, maps1 = prep_inputs(inputs)
    nc = build_fused()
    in_maps = []
    for c in range(N_CORES):
        m = dict(maps0[c])
        m.update(maps1[c])
        in_maps.append(m)
    res = run_bass_kernel_spmd(nc, in_maps, core_ids=list(range(N_CORES)))
    out = np.zeros((B_, T_, D_), np.float32)
    for c in range(N_CORES):
        bb, Q0 = c // 4, (c % 4) * OWN
        out[bb, Q0:Q0 + OWN] = np.asarray(res.results[c]["yc"], np.float32)
    return out
